# version 03
# baseline (speedup 1.0000x reference)
import contextlib
import numpy as np
import ml_dtypes

import concourse.bass as bass
import concourse.mybir as mybir
from concourse.bass_utils import run_bass_kernel_spmd

F32 = mybir.dt.float32
BF16 = mybir.dt.bfloat16
AF = mybir.ActivationFunctionType
ALU = mybir.AluOpType
AX = mybir.AxisListType.X
BF = ml_dtypes.bfloat16

B, T, C = 4, 4096, 1024
BT = 2048
H4 = 4096
DH = 64
NCH = 8
NBLK = 16
MYH = 8
EPS = 1e-5
RC = 12582912.0
EXPB = -30.0
ALLSEM = ("LD", "GD", "TS", "VS", "SS", "CC")


class Ctx:
    def __init__(self, nc, es):
        self.sem = {}
        self.cnt = {}
        self.waited = {}
        for name in ALLSEM:
            self.sem[name] = es.enter_context(nc.semaphore(name))
            self.cnt[name] = 0

    def inc(self, instr, name, n=1):
        instr.then_inc(self.sem[name], n)
        self.cnt[name] += n
        return instr

    def wait(self, eng, ename, names):
        for name in names:
            v = self.cnt[name]
            key = (ename, name)
            if v == 0 or self.waited.get(key, -1) >= v:
                continue
            eng.wait_ge(self.sem[name], v)
            self.waited[key] = v

    def wait_all(self, eng, ename):
        self.wait(eng, ename, ALLSEM)

    def mark(self, name):
        return (name, self.cnt[name])

    def wait_mark(self, eng, ename, mk):
        if mk is None:
            return
        name, v = mk
        key = (ename, name)
        if v == 0 or self.waited.get(key, -1) >= v:
            return
        eng.wait_ge(self.sem[name], v)
        self.waited[key] = v


DEBUG = False


def build_program():
    nc = bass.Bass("TRN2", target_bir_lowering=False, debug=False)
    es = contextlib.ExitStack()
    DT = {}

    def din(name, shape, dt=F32):
        DT[name] = nc.dram_tensor(name, shape, dt, kind="ExternalInput").ap()

    def dout(name, shape, dt=F32):
        DT[name] = nc.dram_tensor(name, shape, dt, kind="ExternalOutput").ap()

    def dint(name, shape, dt=F32):
        DT[name] = nc.dram_tensor(name, shape, dt).ap()

    din("x_fm", [C, BT]); din("x_tok", [BT, C])
    din("WkT", [C, C]); din("WvT", [C, C]); din("WrT", [C, C]); din("WoT", [C, C])
    din("w1T_sl", [C, 512]); din("w2T_sl", [512, C])
    din("WqkvT_sl", [C, 1536]); din("Wo1T", [C, C])
    din("gamA", [128, 8]); din("betA", [128, 8])
    din("decay", [128, 8]); din("first", [128, 8])
    din("b2c", [128, 8]); din("bTTc", [128, 8])
    din("g1bc", [128, C]); din("b1bc", [128, C])
    din("b1fbc", [128, H4]); din("g2bc", [128, H4]); din("be2bc", [128, H4])
    din("tvals", [128, BT])
    din("identf", [128, 128], F32)
    din("identb", [128, 128], BF16)
    din("onesb", [128, 128], BF16)
    din("ropeR", [128, 128], BF16)
    din("onescolf", [128, 1]); din("onesrowf", [1, 128])
    din("selw", [128, 2]); din("carryw", [128, 2]); din("nb30", [128, 1])
    din("cosQ", [128, T], BF16); din("sinQ", [128, T], BF16)
    din("cosK", [128, T], BF16); din("sinK", [128, T], BF16)
    din("dmask", [128, 2048], BF16)
    din("T1r", [512, 2048]); din("T2r", [512, 2048])
    din("T1i", [512, 2048]); din("T2i", [512, 2048])

    dout("x2_fm", [C, BT])
    if DEBUG:
        dout("dbg_xn2a", [C, BT], BF16)
        dout("dbg_int2", [H4, BT], BF16)
        dout("dbg_g0", [C, BT])
        dout("dbg_x1", [C, BT])
        dout("dbg_xn1", [C, BT], BF16)
        dout("dbg_g1", [C, BT])
        dout("dbg_att", [512, T], BF16)

    dint("cc_wsc_in", [1, 2]); dint("cc_wsc_out", [8, 2])
    dint("cc_wq1_in", [C, 512], BF16); dint("cc_wq1_out", [8 * C, 512], BF16)
    dint("cc_wq2_in", [512, C], BF16); dint("cc_wq2_out", [8 * 512, C], BF16)
    dint("cc_carry_in", [1, C]); dint("cc_carry_out", [2, C])
    dint("cc_xn1_in", [C, BT], BF16); dint("cc_xn1_out", [2 * C, BT], BF16)
    dint("cc_out_in", [512, T], BF16); dint("cc_out_out", [1024, T], BF16)
    dint("xn2a_d", [C, BT], BF16)
    dint("int2_d", [H4, BT], BF16)
    dint("g0_d", [C, BT]); dint("g1_d", [C, BT]); dint("x1_d", [C, BT])
    dint("mt_d", [C, C], BF16)

    cx = Ctx(nc, es)
    sb = {}

    def sball(name, shape, dt=F32, stack=None):
        sb[name] = (stack or es).enter_context(nc.sbuf_tensor(name, shape, dt))
        return sb[name]

    for nm, shp, dt in [
        ("c_idf", [128, 128], F32), ("c_idb", [128, 128], BF16),
        ("c_onesb", [128, 128], BF16), ("c_ropeR", [128, 128], BF16),
        ("c_onescolf", [128, 1], F32), ("c_onesrowf", [1, 128], F32),
        ("c_selw", [128, 2], F32), ("c_carryw", [128, 2], F32),
        ("c_nb30", [128, 1], F32),
        ("v_gamA", [128, 8], F32), ("v_betA", [128, 8], F32),
        ("v_decay", [128, 8], F32), ("v_first", [128, 8], F32),
        ("v_b2c", [128, 8], F32), ("v_bTTc", [128, 8], F32),
        ("v_td", [128, 8], F32), ("v_tf", [128, 8], F32), ("v_sp", [128, 8], F32),
        ("st_A", [128, 16], F32), ("st_B", [128, 16], F32),
        ("st_s1", [128, 16], F32), ("st_s1s", [128, 16], F32),
        ("st_s2i", [128, 16], F32),
        ("v_mw", [128, 2], F32), ("v_swc", [128, 2], F32),
        ("row_a", [1, BT], F32),
        ("carry_fix", [128, 8], F32),
        ("colscr", [128, 64], F32),
        ("st16", [16, 256], F32),
        ("abc", [128, BT], F32), ("bbc", [128, BT], F32),
    ]:
        sball(nm, shp, dt)

    block = es.enter_context(nc.Block())

    def on(engname):
        def deco(fn):
            getattr(block, engname)(fn)
        return deco

    def barrier():
        nc.all_engine_barrier()

    def mkstep(eng, ename, own, deps=ALLSEM):
        state = {"first": True}

        def step(f, *a, **k):
            if state["first"]:
                cx.wait(eng, ename, deps)
                state["first"] = False
            return cx.inc(f(*a, **k), own)

        def chain(f, *a, **k):
            if not state["first"]:
                cx.wait(eng, ename, (own,))
            return step(f, *a, **k)

        step.c = chain
        return step

    def mkdma(eng, ename, own, deps=ALLSEM):
        state = {"first": True}

        def dma(out, in_, **k):
            if state["first"]:
                cx.wait(eng, ename, deps)
                state["first"] = False
            return cx.inc(eng.dma_start(out=out, in_=in_, **k), own, 16)
        return dma

    def emit_bc_from_pack(pack_name, pack_sl, dst_name, ppsum):
        @on("tensor")
        def _(tensor):
            S = mkstep(tensor, "t", "TS")
            S(tensor.transpose, ppsum[0:16, 0:128], sb[pack_name][:, pack_sl],
              sb["c_idf"][:])

        @on("scalar")
        def _(scalar):
            S = mkstep(scalar, "s", "SS")
            S(scalar.copy, sb["st16"][:, 0:128], ppsum[0:16, 0:128])

        @on("sync")
        def _(sync):
            D = mkdma(sync, "sy", "LD")
            for tb in range(NBLK):
                D(sb["row_a"][0:1, tb * 128:(tb + 1) * 128],
                  sb["st16"][tb:tb + 1, 0:128])

        @on("tensor")
        def _(tensor):
            S = mkstep(tensor, "t", "TS")
            for tt in range(4):
                S(tensor.matmul, ppsum[:, tt * 512:(tt + 1) * 512],
                  sb["c_onesrowf"][:], sb["row_a"][0:1, tt * 512:(tt + 1) * 512],
                  start=True, stop=True)

        @on("scalar")
        def _(scalar):
            S = mkstep(scalar, "s", "SS")
            S(scalar.copy, sb[dst_name][:], ppsum[:, 0:2048])

    ph0 = contextlib.ExitStack()
    w1s = sball("w1s", [128, 8 * 512], F32, ph0)
    w2s = sball("w2s", [128, 4 * C], F32, ph0)
    wq1 = sball("wq1", [128, 8 * 512], BF16, ph0)
    wq2 = sball("wq2", [128, 4 * C], BF16, ph0)
    wscr = sball("wscr", [128, 4096], F32, ph0)
    wcol = sball("wcol", [128, 8], F32, ph0)
    wg8 = sball("wg8", [8, 2], F32, ph0)
    t1sb = sball("t1sb", [128, 4 * 2048], F32, ph0)
    t2sb = sball("t2sb", [128, 4 * 2048], F32, ph0)
    wTT = sball("wTT", [128, 4 * 512], F32, ph0)
    MT0 = sball("MT0", [128, 8 * C], BF16, ph0)
    pw = ph0.enter_context(nc.psum_tensor("pw", [128, 512], F32))

    @on("sync")
    def _(sync):
        D = mkdma(sync, "sy", "LD")
        for nm, dst in [("identf", "c_idf"), ("identb", "c_idb"),
                        ("onesb", "c_onesb"), ("ropeR", "c_ropeR"),
                        ("onescolf", "c_onescolf"), ("onesrowf", "c_onesrowf"),
                        ("selw", "c_selw"), ("carryw", "c_carryw"), ("nb30", "c_nb30"),
                        ("gamA", "v_gamA"), ("betA", "v_betA"),
                        ("decay", "v_decay"), ("first", "v_first"),
                        ("b2c", "v_b2c"), ("bTTc", "v_bTTc")]:
            D(sb[dst][:], DT[nm])
        D(w1s[:].rearrange("p (a n) -> p a n", a=8),
          DT["w1T_sl"].rearrange("(a p) n -> p a n", p=128))
        D(w2s[:].rearrange("p (a n) -> p a n", a=4),
          DT["w2T_sl"].rearrange("(a p) n -> p a n", p=128))
        D(t1sb[:].rearrange("p (a n) -> p a n", a=4),
          DT["T1r"].rearrange("(a p) n -> p a n", p=128))
        D(t2sb[:].rearrange("p (a n) -> p a n", a=4),
          DT["T2r"].rearrange("(a p) n -> p a n", p=128))

    @on("scalar")
    def _(scalar):
        S = mkstep(scalar, "s", "SS")
        S(scalar.activation, sb["v_td"][:], sb["v_decay"][:], AF.Sigmoid)
        S(scalar.activation, sb["v_tf"][:], sb["v_first"][:], AF.Sigmoid)

    @on("vector")
    def _(vector):
        S = mkstep(vector, "v", "VS")
        S(vector.tensor_scalar, sb["v_sp"][:], sb["v_td"][:], 1e-38, None, ALU.max)

    @on("scalar")
    def _(scalar):
        S = mkstep(scalar, "s", "SS")
        S(scalar.activation, sb["v_sp"][:], sb["v_sp"][:], AF.Ln)

    @on("vector")
    def _(vector):
        S = mkstep(vector, "v", "VS")
        S(vector.tensor_scalar, sb["v_sp"][:], sb["v_sp"][:], -1.0, None, ALU.mult)

    @on("vector")
    def _(vector):
        S = mkstep(vector, "v", "VS")
        S(vector.reduce_sum, wcol[:, 0:1], w1s[:], axis=AX, apply_absolute_value=True)
        S(vector.reduce_sum, wcol[:, 1:2], w2s[:], axis=AX, apply_absolute_value=True)

    @on("tensor")
    def _(tensor):
        S = mkstep(tensor, "t", "TS")
        S(tensor.matmul, pw[0:1, 0:2], sb["c_onescolf"][:], wcol[:, 0:2],
          start=True, stop=True)

    @on("scalar")
    def _(scalar):
        S = mkstep(scalar, "s", "SS")
        S(scalar.copy, wcol[0:1, 2:4], pw[0:1, 0:2])

    @on("gpsimd")
    def _(g):
        D = mkdma(g, "g", "GD")
        S = mkstep(g, "g", "CC")
        D(DT["cc_wsc_in"], wcol[0:1, 2:4])
        cx.wait(g, "g", ("GD",))
        S(g.collective_compute, "AllGather", ALU.bypass,
          replica_groups=[list(range(8))],
          ins=[DT["cc_wsc_in"].opt()], outs=[DT["cc_wsc_out"].opt()])
        cx.wait(g, "g", ("CC",))
        D(wg8[:], DT["cc_wsc_out"])

    @on("vector")
    def _(vector):
        S = mkstep(vector, "v", "VS")
        S(vector.tensor_copy, wcol[0:8, 4:6], wg8[:, 0:2])

    @on("tensor")
    def _(tensor):
        S = mkstep(tensor, "t", "TS")
        S(tensor.matmul, pw[0:1, 2:4], sb["c_onescolf"][0:8, 0:1], wcol[0:8, 4:6],
          start=True, stop=True)

    @on("vector")
    def _(vector):
        S = mkstep(vector, "v", "VS")
        S(vector.tensor_scalar, wcol[0:1, 6:8], pw[0:1, 2:4],
          1.0 / (C * H4), 1e-5, ALU.mult, ALU.max)

    @on("tensor")
    def _(tensor):
        S = mkstep(tensor, "t", "TS")
        S(tensor.matmul, pw[:, 4:6], sb["c_onesrowf"][:], wcol[0:1, 6:8],
          start=True, stop=True)

    @on("scalar")
    def _(scalar):
        S = mkstep(scalar, "s", "SS")
        S(scalar.copy, sb["v_mw"][:], pw[:, 4:6])

    @on("vector")
    def _(vector):
        S = mkstep(vector, "v", "VS")
        S(vector.reciprocal, sb["v_swc"][:], sb["v_mw"][:])
        for wsrc, dst, sc, n in ((w1s, wq1, 0, 8 * 512), (w2s, wq2, 1, 4 * C)):
            S.c(vector.tensor_scalar, wscr[:, 0:n], wsrc[:], sb["v_swc"][:, sc:sc + 1],
                RC, ALU.mult, ALU.add)
            S.c(vector.tensor_scalar, wscr[:, 0:n], wscr[:, 0:n], RC, None,
                ALU.subtract)
            S.c(vector.tensor_scalar, dst[:], wscr[:, 0:n], 1.0, -1.0, ALU.min,
                ALU.max)

    @on("gpsimd")
    def _(g):
        D = mkdma(g, "g", "GD")
        S = mkstep(g, "g", "CC")
        D(DT["cc_wq1_in"].rearrange("(a p) n -> p a n", p=128),
          wq1[:].rearrange("p (a n) -> p a n", a=8))
        D(DT["cc_wq2_in"].rearrange("(a p) n -> p a n", p=128),
          wq2[:].rearrange("p (a n) -> p a n", a=4))
        cx.wait(g, "g", ("GD",))
        S(g.collective_compute, "AllGather", ALU.bypass,
          replica_groups=[list(range(8))],
          ins=[DT["cc_wq1_in"].opt()], outs=[DT["cc_wq1_out"].opt()])
        S(g.collective_compute, "AllGather", ALU.bypass,
          replica_groups=[list(range(8))],
          ins=[DT["cc_wq2_in"].opt()], outs=[DT["cc_wq2_out"].opt()])

    def emit_tt_build(which):
        @on("vector")
        def _(vector):
            S = mkstep(vector, "v", "VS")
            for pc in range(4):
                base = pc * 2048
                S.c(vector.tensor_mul, wTT[:, pc * 512:(pc + 1) * 512],
                  t1sb[:, base:base + 512], t2sb[:, base:base + 512])
                for c in range(1, 4):
                    s0 = base + c * 512
                    S.c(vector.tensor_mul, wscr[:, 0:512], t1sb[:, s0:s0 + 512],
                      t2sb[:, s0:s0 + 512])
                    S.c(vector.tensor_add, wTT[:, pc * 512:(pc + 1) * 512],
                      wTT[:, pc * 512:(pc + 1) * 512], wscr[:, 0:512])
            for pc in range(4):
                w = wTT[:, pc * 512:(pc + 1) * 512]
                if which == 0:
                    S.c(vector.tensor_copy, MT0[:, pc * C + 0:pc * C + 512], w)
                    S.c(vector.tensor_copy,
                      MT0[:, (4 + pc) * C + 512:(4 + pc) * C + 1024], w)
                else:
                    S.c(vector.tensor_copy, MT0[:, pc * C + 512:pc * C + 1024], w)
                    S.c(vector.tensor_scalar, wscr[:, 0:512], w, -1.0, None, ALU.mult)
                    S.c(vector.tensor_copy,
                      MT0[:, (4 + pc) * C + 0:(4 + pc) * C + 512], wscr[:, 0:512])

    emit_tt_build(0)

    @on("sync")
    def _(sync):
        D = mkdma(sync, "sy", "LD")
        D(t1sb[:].rearrange("p (a n) -> p a n", a=4),
          DT["T1i"].rearrange("(a p) n -> p a n", p=128))
        D(t2sb[:].rearrange("p (a n) -> p a n", a=4),
          DT["T2i"].rearrange("(a p) n -> p a n", p=128))

    emit_tt_build(1)

    @on("sync")
    def _(sync):
        D = mkdma(sync, "sy", "LD")
        D(DT["mt_d"].rearrange("(a p) n -> p a n", p=128),
          MT0[:].rearrange("p (a n) -> p a n", a=NCH))

    barrier()
    ph0.close()

    s_int1f = contextlib.ExitStack()
    int1f = sball("int1f", [128, NCH * BT], BF16, s_int1f)
    ph1 = contextlib.ExitStack()
    xtb = sball("xtb", [128, 2 * C], F32, ph1)
    xfb = sball("xfb", [128, 2 * BT], F32, ph1)
    g1bcs = sball("g1bcs", [128, C], F32, ph1)
    b1bcs = sball("b1bcs", [128, C], F32, ph1)
    scr1 = sball("scr1", [128, C], F32, ph1)
    scr2 = sball("scr2", [128, C], F32, ph1)
    int1t = sball("int1t", [128, NBLK * C], BF16, ph1)
    xn2a = sball("xn2a", [128, NCH * BT], BF16, ph1)
    ph1p = contextlib.ExitStack()
    pstA = ph1p.enter_context(nc.psum_tensor("pstA", [128, 2048], F32))
    pstB = ph1p.enter_context(nc.psum_tensor("pstB", [128, 2048], F32))

    @on("sync")
    def _(sync):
        D = mkdma(sync, "sy", "LD")
        D(g1bcs[:], DT["g1bc"])
        D(b1bcs[:], DT["b1bc"])

    for tb in range(NBLK):
        buf = tb % 2

        @on("sync")
        def _(sync, tb=tb, buf=buf):
            D = mkdma(sync, "sy", "LD")
            D(xtb[:, buf * C:(buf + 1) * C],
              DT["x_tok"][tb * 128:(tb + 1) * 128, :])

        @on("scalar")
        def _(scalar, tb=tb, buf=buf):
            S = mkstep(scalar, "s", "SS")
            S(scalar.activation, scr1[:], xtb[:, buf * C:(buf + 1) * C], AF.Square,
              accum_out=sb["colscr"][:, tb:tb + 1])

        @on("vector")
        def _(vector, tb=tb, buf=buf):
            S = mkstep(vector, "v", "VS")
            S(vector.reduce_sum, sb["colscr"][:, 16 + tb:17 + tb],
              xtb[:, buf * C:(buf + 1) * C], axis=AX)

    @on("vector")
    def _(vector):
        S = mkstep(vector, "v", "VS")
        mu = sb["colscr"][:, 32:48]
        var = sb["colscr"][:, 48:64]
        S(vector.tensor_scalar, mu, sb["colscr"][:, 16:32], 1.0 / C, None, ALU.mult)
        S.c(vector.tensor_mul, var, mu, mu)
        S.c(vector.tensor_scalar, sb["st_A"][:], sb["colscr"][:, 0:16], 1.0 / C, None,
            ALU.mult)
        S.c(vector.tensor_sub, var, sb["st_A"][:], var)
        S.c(vector.tensor_scalar, var, var, 1.0 + EPS, EPS * EPS, ALU.mult, ALU.add)

    @on("scalar")
    def _(scalar):
        S = mkstep(scalar, "s", "SS")
        S(scalar.activation, sb["st_B"][:], sb["colscr"][:, 48:64], AF.Sqrt)

    @on("vector")
    def _(vector):
        S = mkstep(vector, "v", "VS")
        S(vector.reciprocal, sb["st_A"][:], sb["st_B"][:])
        S.c(vector.scalar_tensor_tensor, sb["st_B"][:], sb["colscr"][:, 32:48], -1.0,
            sb["st_A"][:], ALU.mult, ALU.mult)

    for tb in range(NBLK):
        buf = tb % 2

        @on("sync")
        def _(sync, tb=tb, buf=buf):
            D = mkdma(sync, "sy", "LD")
            D(xtb[:, buf * C:(buf + 1) * C],
              DT["x_tok"][tb * 128:(tb + 1) * 128, :])

        @on("vector")
        def _(vector, tb=tb, buf=buf):
            S = mkstep(vector, "v", "VS")
            xb = xtb[:, buf * C:(buf + 1) * C]
            S(vector.tensor_scalar, scr1[:], xb, sb["st_A"][:, tb:tb + 1],
              sb["st_B"][:, tb:tb + 1], ALU.mult, ALU.add)
            S.c(vector.scalar_tensor_tensor, scr2[:], scr1[:], 1.0, g1bcs[:],
                ALU.bypass, ALU.mult)
            S.c(vector.tensor_add, scr2[:], scr2[:], b1bcs[:])
            S.c(vector.reduce_max, sb["colscr"][:, tb:tb + 1], scr2[:], axis=AX,
                apply_absolute_value=True)
            S.c(vector.tensor_scalar, sb["colscr"][:, 16 + tb:17 + tb],
                sb["colscr"][:, tb:tb + 1], 1e-5, None, ALU.max)
            S.c(vector.reciprocal, sb["st_s1"][:, tb:tb + 1],
                sb["colscr"][:, 16 + tb:17 + tb])
            S.c(vector.tensor_scalar, sb["st_s1"][:, tb:tb + 1],
                sb["st_s1"][:, tb:tb + 1], 127.0, None, ALU.mult)
            S.c(vector.scalar_tensor_tensor, sb["st_s1s"][:, tb:tb + 1],
                sb["colscr"][:, 16 + tb:17 + tb], 1.0 / 127.0, sb["v_mw"][:, 0:1],
                ALU.mult, ALU.mult)
            S.c(vector.tensor_scalar, scr2[:], scr2[:], sb["st_s1"][:, tb:tb + 1], RC,
                ALU.mult, ALU.add)
            S.c(vector.tensor_scalar, int1t[:, tb * C:(tb + 1) * C], scr2[:], RC,
                None, ALU.subtract)

    emit_bc_from_pack("st_A", slice(0, 16), "abc", pstA)
    emit_bc_from_pack("st_B", slice(0, 16), "bbc", pstB)

    for ca in range(NCH):
        buf = ca % 2

        @on("sync")
        def _(sync, ca=ca, buf=buf):
            D = mkdma(sync, "sy", "LD")
            D(xfb[:, buf * BT:(buf + 1) * BT],
              DT["x_fm"][ca * 128:(ca + 1) * 128, :])

        @on("vector")
        def _(vector, ca=ca, buf=buf):
            S = mkstep(vector, "v", "VS")
            xc = xfb[:, buf * BT:(buf + 1) * BT]
            dst = xn2a[:, ca * BT:(ca + 1) * BT]
            S(vector.tensor_mul, xc, xc, sb["abc"][:])
            S.c(vector.tensor_add, xc, xc, sb["bbc"][:])
            S.c(vector.tensor_scalar, dst, xc, sb["v_gamA"][:, ca:ca + 1],
                sb["v_betA"][:, ca:ca + 1], ALU.mult, ALU.add)

    @on("sync")
    def _(sync):
        D = mkdma(sync, "sy", "LD")
        D(DT["xn2a_d"].rearrange("(a p) n -> p a n", p=128),
          xn2a[:].rearrange("p (a n) -> p a n", a=NCH))

    barrier()
    ph1p.close()
    pT0 = ph1.enter_context(nc.psum_tensor("pT0", [128, 512], BF16))
    pT1 = ph1.enter_context(nc.psum_tensor("pT1", [128, 512], BF16))

    for ca in range(NCH):
        for tg in range(4):
            pT = (pT0, pT1)[tg % 2]

            @on("tensor")
            def _(tensor, ca=ca, tg=tg, pT=pT):
                S = mkstep(tensor, "t", "TS")
                for j in range(4):
                    tb = tg * 4 + j
                    S(tensor.transpose, pT[:, j * 128:(j + 1) * 128],
                      int1t[:, tb * C + ca * 128:tb * C + (ca + 1) * 128],
                      sb["c_idb"][:])

            @on("scalar")
            def _(scalar, ca=ca, tg=tg, pT=pT):
                S = mkstep(scalar, "s", "SS")
                S(scalar.copy,
                  int1f[:, ca * BT + tg * 512:ca * BT + (tg + 1) * 512], pT[:])

    barrier()
    ph1.close()

    phf = contextlib.ExitStack()
    wq1f = sball("wq1f", [128, 64 * 512], BF16, phf)
    b1fs = sball("b1fs", [128, H4], BF16, phf)
    g2s = sball("g2s", [128, H4], F32, phf)
    be2s = sball("be2s", [128, H4], F32, phf)
    h2b = sball("h2b", [128, H4], BF16, phf)
    ln2o = sball("ln2o", [128, H4], F32, phf)
    sqs = sball("sqs", [128, 512], F32, phf)
    fscr = sball("fscr", [128, 1024], F32, phf)
    int2s = sball("int2s", [128, H4], BF16, phf)
    i2st = sball("i2st", [128, 1024], BF16, phf)
    pF0 = phf.enter_context(nc.psum_tensor("pF0", [128, 512], F32))
    pF1 = phf.enter_context(nc.psum_tensor("pF1", [128, 512], F32))
    pX0 = phf.enter_context(nc.psum_tensor("pX0", [128, 512], BF16))
    pX1 = phf.enter_context(nc.psum_tensor("pX1", [128, 512], BF16))

    @on("sync")
    def _(sync):
        D = mkdma(sync, "sy", "LD")
        for r in range(8):
            for ca in range(NCH):
                D(wq1f[:, (ca * 8 + r) * 512:(ca * 8 + r + 1) * 512],
                  DT["cc_wq1_out"][r * C + ca * 128:r * C + (ca + 1) * 128, :])
        D(g2s[:], DT["g2bc"])
        D(be2s[:], DT["be2bc"])

    @on("gpsimd")
    def _(g):
        D = mkdma(g, "g", "GD")
        D(b1fs[:], DT["b1fbc"])

    fmk = {}
    for tb in range(NBLK):
        for hc in range(8):
            pF = (pF0, pF1)[hc % 2]
            fb = hc % 2

            @on("tensor")
            def _(tensor, tb=tb, hc=hc, pF=pF, fb=fb):
                cx.wait(tensor, "t", ("LD", "GD"))
                cx.wait_mark(tensor, "t", fmk.get(("pF", fb)))
                S = mkstep(tensor, "t", "TS", deps=())
                for ca in range(NCH):
                    S(tensor.matmul, pF[:],
                      int1f[:, ca * BT + tb * 128:ca * BT + (tb + 1) * 128],
                      wq1f[:, (ca * 8 + hc) * 512:(ca * 8 + hc + 1) * 512],
                      start=(ca == 0), stop=(ca == NCH - 1))

            @on("vector")
            def _(vector, tb=tb, hc=hc, pF=pF, fb=fb):
                cx.wait(vector, "v", ("TS", "GD"))
                cx.wait_mark(vector, "v", fmk.get(("fscr", fb)))
                S = mkstep(vector, "v", "VS", deps=())
                S(vector.scalar_tensor_tensor, fscr[:, fb * 512:(fb + 1) * 512],
                  pF[:], sb["st_s1s"][:, tb:tb + 1],
                  b1fs[:, hc * 512:(hc + 1) * 512], ALU.mult, ALU.add)
            fmk[("pF", fb)] = cx.mark("VS")

            @on("scalar")
            def _(scalar, tb=tb, hc=hc, fb=fb):
                S = mkstep(scalar, "s", "SS", deps=("VS",))
                S(scalar.activation, h2b[:, hc * 512:(hc + 1) * 512],
                  fscr[:, fb * 512:(fb + 1) * 512], AF.Gelu)
            fmk[("fscr", fb)] = cx.mark("SS")

        @on("vector")
        def _(vector, tb=tb):
            S = mkstep(vector, "v", "VS", deps=("SS",))
            S(vector.reduce_sum, sb["colscr"][:, 0:1], h2b[:], axis=AX)
            for hc in range(8):
                sl = slice(hc * 512, (hc + 1) * 512)
                S.c(vector.tensor_mul, sqs[:], h2b[:, sl], h2b[:, sl])
                S.c(vector.reduce_sum, sb["colscr"][:, 8 + hc:9 + hc], sqs[:],
                    axis=AX)
            cc = sb["colscr"]
            S.c(vector.tensor_add, cc[:, 8:9], cc[:, 8:9], cc[:, 9:10])
            S.c(vector.tensor_add, cc[:, 10:11], cc[:, 10:11], cc[:, 11:12])
            S.c(vector.tensor_add, cc[:, 12:13], cc[:, 12:13], cc[:, 13:14])
            S.c(vector.tensor_add, cc[:, 14:15], cc[:, 14:15], cc[:, 15:16])
            S.c(vector.tensor_add, cc[:, 8:9], cc[:, 8:9], cc[:, 10:11])
            S.c(vector.tensor_add, cc[:, 12:13], cc[:, 12:13], cc[:, 14:15])
            S.c(vector.tensor_add, cc[:, 8:9], cc[:, 8:9], cc[:, 12:13])
            S.c(vector.tensor_scalar, cc[:, 1:2], cc[:, 0:1], 1.0 / H4, None,
                ALU.mult)
            S.c(vector.tensor_mul, cc[:, 2:3], cc[:, 1:2], cc[:, 1:2])
            S.c(vector.tensor_scalar, cc[:, 3:4], cc[:, 8:9], 1.0 / H4, None,
                ALU.mult)
            S.c(vector.tensor_sub, cc[:, 3:4], cc[:, 3:4], cc[:, 2:3])
            S.c(vector.tensor_scalar, cc[:, 3:4], cc[:, 3:4], 1.0, EPS, ALU.mult,
                ALU.add)

        @on("scalar")
        def _(scalar, tb=tb):
            S = mkstep(scalar, "s", "SS", deps=("VS",))
            S(scalar.activation, sb["colscr"][:, 4:5], sb["colscr"][:, 3:4], AF.Sqrt)

        @on("vector")
        def _(vector, tb=tb):
            S = mkstep(vector, "v", "VS", deps=("SS",))
            cc = sb["colscr"]
            S(vector.reciprocal, cc[:, 5:6], cc[:, 4:5])
            S.c(vector.scalar_tensor_tensor, cc[:, 6:7], cc[:, 1:2], -1.0, cc[:, 5:6],
                ALU.mult, ALU.mult)
            for hc in range(8):
                sl = slice(hc * 512, (hc + 1) * 512)
                S.c(vector.tensor_scalar, fscr[:, 0:512], h2b[:, sl], cc[:, 5:6],
                    cc[:, 6:7], ALU.mult, ALU.add)
                S.c(vector.scalar_tensor_tensor, fscr[:, 0:512], fscr[:, 0:512], 1.0,
                    g2s[:, sl], ALU.bypass, ALU.mult)
                S.c(vector.tensor_add, ln2o[:, sl], fscr[:, 0:512], be2s[:, sl])
            S.c(vector.reduce_max, cc[:, 7:8], ln2o[:], axis=AX,
                apply_absolute_value=True)
            S.c(vector.tensor_scalar, cc[:, 7:8], cc[:, 7:8], 1e-5, None, ALU.max)
            S.c(vector.reciprocal, cc[:, 2:3], cc[:, 7:8])
            S.c(vector.tensor_scalar, cc[:, 2:3], cc[:, 2:3], 127.0, None, ALU.mult)
            S.c(vector.scalar_tensor_tensor, sb["st_s2i"][:, tb:tb + 1], cc[:, 7:8],
                1.0 / 127.0, sb["v_mw"][:, 1:2], ALU.mult, ALU.mult)
            S.c(vector.tensor_scalar, ln2o[:], ln2o[:], cc[:, 2:3], RC, ALU.mult,
                ALU.add)
            S.c(vector.tensor_scalar, int2s[:], ln2o[:], RC, None, ALU.subtract)

        for hg in range(8):
            pX = (pX0, pX1)[hg % 2]
            xb = hg % 2

            @on("tensor")
            def _(tensor, tb=tb, hg=hg, pX=pX, xb=xb):
                cx.wait(tensor, "t", ("VS",))
                cx.wait_mark(tensor, "t", fmk.get(("pX", xb)))
                S = mkstep(tensor, "t", "TS", deps=())
                for j in range(4):
                    hh = hg * 4 + j
                    S(tensor.transpose, pX[:, j * 128:(j + 1) * 128],
                      int2s[:, hh * 128:(hh + 1) * 128], sb["c_idb"][:])

            @on("scalar")
            def _(scalar, tb=tb, hg=hg, pX=pX, xb=xb):
                cx.wait(scalar, "s", ("TS",))
                cx.wait_mark(scalar, "s", fmk.get(("i2st", xb)))
                S = mkstep(scalar, "s", "SS", deps=())
                S(scalar.copy, i2st[:, xb * 512:(xb + 1) * 512], pX[:])
            fmk[("pX", xb)] = cx.mark("SS")

            @on("sync")
            def _(sync, tb=tb, hg=hg, xb=xb):
                D = mkdma(sync, "sy", "LD", deps=("SS",))
                for j in range(4):
                    hh = hg * 4 + j
                    D(DT["int2_d"][hh * 128:(hh + 1) * 128, tb * 128:(tb + 1) * 128],
                      i2st[:, xb * 512 + j * 128:xb * 512 + (j + 1) * 128])
            fmk[("i2st", xb)] = cx.mark("LD")

    barrier()
    phf.close()
    s_int1f.close()

    phg = contextlib.ExitStack()
    wq2f = sball("wq2f", [128, 32 * C], BF16, phg)
    i2b = sball("i2b", [128, 32 * 512], BF16, phg)
    s2ib = sball("s2ib", [128, BT], F32, phg)
    gscr = sball("gscr", [128, 1024], F32, phg)
    gst = sball("gst", [128, 1024], F32, phg)
    pG0 = phg.enter_context(nc.psum_tensor("pG0", [128, 512], F32))
    pG1 = phg.enter_context(nc.psum_tensor("pG1", [128, 512], F32))
    pBC = phg.enter_context(nc.psum_tensor("pBC", [128, 2048], F32))

    @on("sync")
    def _(sync):
        D = mkdma(sync, "sy", "LD")
        for hh in range(32):
            r, loc = hh // 4, hh % 4
            D(wq2f[:, hh * C:(hh + 1) * C],
              DT["cc_wq2_out"][r * 512 + loc * 128:r * 512 + (loc + 1) * 128, :])

    emit_bc_from_pack("st_s2i", slice(0, 16), "abc", pBC)

    @on("vector")
    def _(vector):
        S = mkstep(vector, "v", "VS")
        S(vector.tensor_copy, s2ib[:], sb["abc"][:])

    gmk = {}
    for tt in range(4):
        @on("sync")
        def _(sync, tt=tt):
            D = mkdma(sync, "sy", "LD", deps=("TS",))
            for hh in range(32):
                D(i2b[:, hh * 512:(hh + 1) * 512],
                  DT["int2_d"][hh * 128:(hh + 1) * 128, tt * 512:(tt + 1) * 512])

        for dc in range(NCH):
            pG = (pG0, pG1)[dc % 2]
            gb = dc % 2

            @on("tensor")
            def _(tensor, tt=tt, dc=dc, pG=pG, gb=gb):
                cx.wait(tensor, "t", ("LD",))
                cx.wait_mark(tensor, "t", gmk.get(("pG", gb)))
                S = mkstep(tensor, "t", "TS", deps=())
                for hh in range(32):
                    S(tensor.matmul, pG[:],
                      wq2f[:, hh * C + dc * 128:hh * C + (dc + 1) * 128],
                      i2b[:, hh * 512:(hh + 1) * 512],
                      start=(hh == 0), stop=(hh == 31))

            @on("vector")
            def _(vector, tt=tt, dc=dc, pG=pG, gb=gb):
                cx.wait(vector, "v", ("TS",))
                cx.wait_mark(vector, "v", gmk.get(("gscr", gb)))
                S = mkstep(vector, "v", "VS", deps=())
                S(vector.scalar_tensor_tensor, gscr[:, gb * 512:(gb + 1) * 512],
                  pG[:], 1.0, s2ib[:, tt * 512:(tt + 1) * 512], ALU.bypass, ALU.mult)
            gmk[("pG", gb)] = cx.mark("VS")

            @on("scalar")
            def _(scalar, tt=tt, dc=dc, gb=gb):
                cx.wait(scalar, "s", ("VS",))
                cx.wait_mark(scalar, "s", gmk.get(("gst", gb)))
                S = mkstep(scalar, "s", "SS", deps=())
                S(scalar.activation, gst[:, gb * 512:(gb + 1) * 512],
                  gscr[:, gb * 512:(gb + 1) * 512], AF.Silu,
                  bias=sb["v_b2c"][:, dc:dc + 1])
            gmk[("gscr", gb)] = cx.mark("SS")

            @on("sync")
            def _(sync, tt=tt, dc=dc, gb=gb):
                D = mkdma(sync, "sy", "LD", deps=("SS",))
                D(DT["g0_d"][dc * 128:(dc + 1) * 128, tt * 512:(tt + 1) * 512],
                  gst[:, gb * 512:(gb + 1) * 512])
            gmk[("gst", gb)] = cx.mark("LD")

    barrier()
    phg.close()

    s_carry = contextlib.ExitStack()
    rwkv = sball("rwkv", [128, NCH * BT], BF16, s_carry)
    pha = contextlib.ExitStack()
    xn2r = sball("xn2r", [128, NCH * BT], BF16, pha)
    wst = sball("wst", [128, 8 * 128], F32, pha)
    wbt = sball("wbt", [128, 8 * 128], BF16, pha)
    kbuf = sball("kbuf", [128, BT], BF16, pha)
    vbuf = sball("vbuf", [128, BT], BF16, pha)
    kvb = sball("kvb", [128, BT], F32, pha)
    ubuf = sball("ubuf", [128, BT], F32, pha)
    pbuf = sball("pbuf", [128, 512], F32, pha)
    sS = sball("sS", [128, BT + 1], F32, pha)
    wkvb = sball("wkvb", [128, BT], F32, pha)
    tvalsb = sball("tvalsb", [128, BT], F32, pha)
    sigr = sball("sigr", [128, NCH * BT], BF16, pha)
    totc = sball("totc", [128, 8], F32, pha)
    cgat = sball("cgat", [128, 16], F32, pha)
    pK = pha.enter_context(nc.psum_tensor("pK", [128, 512], F32))
    pV = pha.enter_context(nc.psum_tensor("pV", [128, 512], F32))
    pR = pha.enter_context(nc.psum_tensor("pR", [128, 512], F32))

    @on("sync")
    def _(sync):
        D = mkdma(sync, "sy", "LD")
        D(xn2r[:].rearrange("p (a n) -> p a n", a=NCH),
          DT["xn2a_d"].rearrange("(a p) n -> p a n", p=128))
        D(tvalsb[:], DT["tvals"])

    @on("vector")
    def _(vector):
        S = mkstep(vector, "v", "VS")
        S(vector.memset, sS[:, 0:1], 0.0)

    for dc in range(NCH):
        @on("vector")
        def _(vector, dc=dc):
            S = mkstep(vector, "v", "VS")
            S(vector.memset, wkvb[:], 0.0)

        for wi, wname in enumerate(("WkT", "WvT", "WrT")):
            @on("sync")
            def _(sync, dc=dc, wname=wname):
                D = mkdma(sync, "sy", "LD", deps=("VS",))
                D(wst[:].rearrange("p (a n) -> p a n", a=8),
                  DT[wname][:, dc * 128:(dc + 1) * 128].rearrange(
                      "(a p) n -> p a n", p=128))

            @on("vector")
            def _(vector, dc=dc):
                S = mkstep(vector, "v", "VS", deps=("LD", "TS"))
                S(vector.tensor_copy, wbt[:], wst[:])

            for tt in range(4):
                pdst = (pK, pV, pR)[wi]

                @on("tensor")
                def _(tensor, dc=dc, tt=tt, pdst=pdst):
                    S = mkstep(tensor, "t", "TS", deps=("LD", "VS", "SS"))
                    for ca in range(NCH):
                        S(tensor.matmul, pdst[:],
                          wbt[:, ca * 128:(ca + 1) * 128],
                          xn2r[:, ca * BT + tt * 512:ca * BT + (tt + 1) * 512],
                          start=(ca == 0), stop=(ca == NCH - 1))

                if wi == 0:
                    @on("scalar")
                    def _(scalar, tt=tt):
                        S = mkstep(scalar, "s", "SS", deps=("TS", "VS"))
                        S(scalar.copy, kbuf[:, tt * 512:(tt + 1) * 512], pK[:])
                elif wi == 1:
                    @on("scalar")
                    def _(scalar, tt=tt):
                        S = mkstep(scalar, "s", "SS", deps=("TS", "VS"))
                        S(scalar.copy, vbuf[:, tt * 512:(tt + 1) * 512], pV[:])
                else:
                    @on("scalar")
                    def _(scalar, dc=dc, tt=tt):
                        S = mkstep(scalar, "s", "SS", deps=("TS", "VS"))
                        S(scalar.activation,
                          sigr[:, dc * BT + tt * 512:dc * BT + (tt + 1) * 512],
                          pR[:], AF.Sigmoid)

        for tt in range(4):
            sl = slice(tt * 512, (tt + 1) * 512)

            @on("vector")
            def _(vector, dc=dc, sl=sl):
                S = mkstep(vector, "v", "VS", deps=("SS", "LD"))
                S(vector.tensor_mul, kvb[:, sl], kbuf[:, sl], vbuf[:, sl])
                S(vector.tensor_scalar, pbuf[:], tvalsb[:, sl],
                  sb["v_sp"][:, dc:dc + 1], None, ALU.mult)

            @on("scalar")
            def _(scalar):
                S = mkstep(scalar, "s", "SS", deps=("VS",))
                S(scalar.activation, pbuf[:], pbuf[:], AF.Exp, scale=-1.0)

            @on("vector")
            def _(vector, sl=sl):
                S = mkstep(vector, "v", "VS", deps=("SS",))
                S(vector.tensor_mul, ubuf[:, sl], kvb[:, sl], pbuf[:])

        @on("vector")
        def _(vector, dc=dc):
            S = mkstep(vector, "v", "VS", deps=("SS",))
            S(vector.tensor_tensor_scan, sS[:, 1:BT + 1], wkvb[:], ubuf[:], 0.0,
              ALU.add, ALU.add)
            S.c(vector.tensor_copy, totc[:, dc:dc + 1], sS[:, BT:BT + 1])
            S.c(vector.tensor_scalar, wkvb[:], kvb[:], sb["v_tf"][:, dc:dc + 1], None,
                ALU.mult)
            S.c(vector.scalar_tensor_tensor, wkvb[:], sS[:, 0:BT],
                sb["v_td"][:, dc:dc + 1], wkvb[:], ALU.mult, ALU.add)
            S.c(vector.tensor_mul, rwkv[:, dc * BT:(dc + 1) * BT],
                sigr[:, dc * BT:(dc + 1) * BT], wkvb[:])

    @on("gpsimd")
    def _(g):
        D = mkdma(g, "g", "GD")
        S = mkstep(g, "g", "CC")
        D(DT["cc_carry_in"].rearrange("o (a p) -> (o p) a", p=128), totc[:],
          allow_slow_non_contiguous=True)
        cx.wait(g, "g", ("GD",))
        S(g.collective_compute, "AllGather", ALU.bypass,
          replica_groups=[[0, 1], [2, 3], [4, 5], [6, 7]],
          ins=[DT["cc_carry_in"].opt()], outs=[DT["cc_carry_out"].opt()])
        cx.wait(g, "g", ("CC",))
        D(cgat[:, 0:8],
          DT["cc_carry_out"][0:1, :].rearrange("o (a p) -> (o p) a", p=128),
          allow_slow_non_contiguous=True)
        D(cgat[:, 8:16],
          DT["cc_carry_out"][1:2, :].rearrange("o (a p) -> (o p) a", p=128),
          allow_slow_non_contiguous=True)

    @on("vector")
    def _(vector):
        S = mkstep(vector, "v", "VS")
        S(vector.tensor_scalar, sb["carry_fix"][:], cgat[:, 0:8],
          sb["c_carryw"][:, 0:1], None, ALU.mult)
        S.c(vector.scalar_tensor_tensor, sb["carry_fix"][:], cgat[:, 8:16],
            sb["c_carryw"][:, 1:2], sb["carry_fix"][:], ALU.mult, ALU.add)
        S.c(vector.tensor_mul, sb["carry_fix"][:], sb["carry_fix"][:], sb["v_td"][:])
        for dc in range(NCH):
            sl = slice(dc * BT, (dc + 1) * BT)
            S.c(vector.scalar_tensor_tensor, rwkv[:, sl], sigr[:, sl],
                sb["carry_fix"][:, dc:dc + 1], rwkv[:, sl], ALU.mult, ALU.add)

    barrier()
    pha.close()

    s_x1 = contextlib.ExitStack()
    x1f = sball("x1f", [128, NCH * BT], F32, s_x1)
    phr = contextlib.ExitStack()
    woT = sball("woT", [128, NCH * C], F32, phr)
    woTb = sball("woTb", [128, NCH * C], BF16, phr)
    g0t = sball("g0t", [128, 2 * 512], F32, phr)
    xft = sball("xft", [128, 2 * 512], F32, phr)
    pS0 = phr.enter_context(nc.psum_tensor("pS0", [128, 512], F32))
    pS1 = phr.enter_context(nc.psum_tensor("pS1", [128, 512], F32))

    @on("sync")
    def _(sync):
        D = mkdma(sync, "sy", "LD")
        D(woT[:].rearrange("p (a n) -> p a n", a=NCH),
          DT["WoT"].rearrange("(a p) n -> p a n", p=128))

    @on("vector")
    def _(vector):
        S = mkstep(vector, "v", "VS")
        S(vector.tensor_copy, woTb[:], woT[:])

    for dc in range(NCH):
        for tt in range(4):
            pS = (pS0, pS1)[tt % 2]
            buf = tt % 2

            @on("sync")
            def _(sync, dc=dc, tt=tt, buf=buf):
                D = mkdma(sync, "sy", "LD", deps=("VS",))
                D(g0t[:, buf * 512:(buf + 1) * 512],
                  DT["g0_d"][dc * 128:(dc + 1) * 128, tt * 512:(tt + 1) * 512])
                D(xft[:, buf * 512:(buf + 1) * 512],
                  DT["x_fm"][dc * 128:(dc + 1) * 128, tt * 512:(tt + 1) * 512])

            @on("tensor")
            def _(tensor, dc=dc, tt=tt, pS=pS):
                S = mkstep(tensor, "t", "TS", deps=("VS",))
                for ca in range(NCH):
                    S(tensor.matmul, pS[:],
                      woTb[:, ca * C + dc * 128:ca * C + (dc + 1) * 128],
                      rwkv[:, ca * BT + tt * 512:ca * BT + (tt + 1) * 512],
                      start=(ca == 0), stop=(ca == NCH - 1))

            @on("vector")
            def _(vector, dc=dc, tt=tt, pS=pS, buf=buf):
                S = mkstep(vector, "v", "VS", deps=("TS", "LD"))
                dst = x1f[:, dc * BT + tt * 512:dc * BT + (tt + 1) * 512]
                S(vector.tensor_mul, dst, pS[:], g0t[:, buf * 512:(buf + 1) * 512])
                S.c(vector.tensor_add, dst, dst, xft[:, buf * 512:(buf + 1) * 512])

    @on("sync")
    def _(sync):
        D = mkdma(sync, "sy", "LD")
        D(DT["x1_d"].rearrange("(a p) n -> p a n", p=128),
          x1f[:].rearrange("p (a n) -> p a n", a=NCH))

    barrier()
    phr.close()

    ph2 = contextlib.ExitStack()
    xcb = sball("xcb", [128, 2 * 512], BF16, ph2)
    MTl = sball("MTl", [128, 8 * C], BF16, ph2)
    x1sq = sball("x1sq", [128, 2 * 512], BF16, ph2)
    xn1b = sball("xn1b", [128, NCH * BT], BF16, ph2)
    l1scr = sball("l1scr", [128, 512], F32, ph2)
    g1st = sball("g1st", [128, 512], F32, ph2)
    pM0 = ph2.enter_context(nc.psum_tensor("pM0", [128, 512], F32))
    pM1 = ph2.enter_context(nc.psum_tensor("pM1", [128, 512], F32))
    pGa = ph2.enter_context(nc.psum_tensor("pGa", [128, 512], F32))
    pGb = ph2.enter_context(nc.psum_tensor("pGb", [128, 512], F32))

    @on("sync")
    def _(sync):
        D = mkdma(sync, "sy", "LD")
        D(MTl[:].rearrange("p (a n) -> p a n", a=NCH),
          DT["mt_d"].rearrange("(a p) n -> p a n", p=128))

    for tt in range(4):
        for ca in range(NCH):
            buf = ca % 2
            xc = xcb[:, buf * 512:(buf + 1) * 512]
            sq = x1sq[:, buf * 512:(buf + 1) * 512]

            @on("vector")
            def _(vector, tt=tt, ca=ca, xc=xc):
                S = mkstep(vector, "v", "VS", deps=("TS",))
                S(vector.tensor_copy, xc,
                  x1f[:, ca * BT + tt * 512:ca * BT + (tt + 1) * 512])

            @on("scalar")
            def _(scalar, xc=xc, sq=sq):
                S = mkstep(scalar, "s", "SS", deps=("VS", "TS"))
                S(scalar.activation, sq, xc, AF.Square)

            @on("tensor")
            def _(tensor, ca=ca, xc=xc, sq=sq):
                S = mkstep(tensor, "t", "TS", deps=("VS", "SS"))
                S(tensor.matmul, pM0[:], sb["c_onesb"][:], xc,
                  start=(ca == 0), stop=(ca == NCH - 1))
                S(tensor.matmul, pM1[:], sb["c_onesb"][:], sq,
                  start=(ca == 0), stop=(ca == NCH - 1))

        @on("scalar")
        def _(scalar, tt=tt):
            S = mkstep(scalar, "s", "SS")
            S(scalar.activation, l1scr[:], pM0[:], AF.Square, scale=1.0 / C)

        @on("vector")
        def _(vector, tt=tt):
            S = mkstep(vector, "v", "VS", deps=("TS", "SS"))
            sl = slice(tt * 512, (tt + 1) * 512)
            S(vector.tensor_scalar, sb["abc"][:, sl], pM1[:], 1.0 / C, EPS,
              ALU.mult, ALU.add)
            S.c(vector.tensor_sub, sb["abc"][:, sl], sb["abc"][:, sl], l1scr[:])

        @on("scalar")
        def _(scalar, tt=tt):
            S = mkstep(scalar, "s", "SS")
            sl = slice(tt * 512, (tt + 1) * 512)
            S(scalar.activation, sb["bbc"][:, sl], sb["abc"][:, sl], AF.Sqrt)

        @on("vector")
        def _(vector, tt=tt):
            S = mkstep(vector, "v", "VS", deps=("SS",))
            sl = slice(tt * 512, (tt + 1) * 512)
            S(vector.reciprocal, sb["abc"][:, sl], sb["bbc"][:, sl])
            S.c(vector.scalar_tensor_tensor, sb["bbc"][:, sl], pM0[:], -1.0 / C,
                sb["abc"][:, sl], ALU.mult, ALU.mult)

    @on("vector")
    def _(vector):
        S = mkstep(vector, "v", "VS")
        for ca in range(NCH):
            sl = slice(ca * BT, (ca + 1) * BT)
            S.c(vector.tensor_mul, x1f[:, sl], x1f[:, sl], sb["abc"][:])
            S.c(vector.tensor_add, xn1b[:, sl], x1f[:, sl], sb["bbc"][:])

    @on("gpsimd")
    def _(g):
        D = mkdma(g, "g", "GD")
        S = mkstep(g, "g", "CC")
        D(DT["cc_xn1_in"].rearrange("(a p) n -> p a n", p=128),
          xn1b[:].rearrange("p (a n) -> p a n", a=NCH))
        cx.wait(g, "g", ("GD",))
        S(g.collective_compute, "AllGather", ALU.bypass,
          replica_groups=[[0, 1], [2, 3], [4, 5], [6, 7]],
          ins=[DT["cc_xn1_in"].opt()], outs=[DT["cc_xn1_out"].opt()])

    for dc in range(NCH):
        for tt in range(4):
            pG = (pGa, pGb)[tt % 2]

            @on("tensor")
            def _(tensor, dc=dc, tt=tt, pG=pG):
                S = mkstep(tensor, "t", "TS", deps=("VS", "SS", "LD"))
                for ca in range(NCH):
                    S(tensor.matmul, pG[:],
                      MTl[:, ca * C + dc * 128:ca * C + (dc + 1) * 128],
                      xn1b[:, ca * BT + tt * 512:ca * BT + (tt + 1) * 512],
                      start=(ca == 0), stop=(ca == NCH - 1))

            @on("scalar")
            def _(scalar, dc=dc, tt=tt, pG=pG):
                S = mkstep(scalar, "s", "SS", deps=("TS", "LD"))
                S(scalar.activation, g1st[:], pG[:], AF.Silu,
                  bias=sb["v_bTTc"][:, dc:dc + 1])

            @on("sync")
            def _(sync, dc=dc, tt=tt):
                D = mkdma(sync, "sy", "LD", deps=("SS",))
                D(DT["g1_d"][dc * 128:(dc + 1) * 128, tt * 512:(tt + 1) * 512],
                  g1st[:])

    barrier()
    ph2.close()
    s_x1.close()
    s_carry.close()

    s_att = contextlib.ExitStack()
    Qh = sball("Qh", [128, 4 * T], BF16, s_att)
    Kh = sball("Kh", [128, 4 * T], BF16, s_att)
    Vsb = sball("Vsb", [128, 32 * 520], BF16, s_att)
    ph3 = contextlib.ExitStack()
    wqs = sball("wqs", [128, 2 * 1536], F32, ph3)
    wqb = sball("wqb", [128, NCH * 1536], BF16, ph3)
    cosQs = sball("cosQs", [128, T], BF16, ph3)
    sinQs = sball("sinQs", [128, T], BF16, ph3)
    cosKs = sball("cosKs", [128, T], BF16, ph3)
    sinKs = sball("sinKs", [128, T], BF16, ph3)
    xg = sball("xg", [128, NCH * 512], BF16, ph3)
    qraw = sball("qraw", [128, 512], BF16, ph3)
    t1b = sball("t1b", [128, 512], BF16, ph3)
    t2b = sball("t2b", [128, 512], BF16, ph3)
    pQ0 = ph3.enter_context(nc.psum_tensor("pQ0", [128, 512], F32))
    pQ1 = ph3.enter_context(nc.psum_tensor("pQ1", [128, 512], F32))
    pRot = ph3.enter_context(nc.psum_tensor("pRot", [128, 512], F32))
    pVv = ph3.enter_context(nc.psum_tensor("pVv", [128, 512], F32))

    @on("sync")
    def _(sync):
        D = mkdma(sync, "sy", "LD")
        for nm, dst in (("cosQ", cosQs), ("sinQ", sinQs),
                        ("cosK", cosKs), ("sinK", sinKs)):
            D(dst[:], DT[nm])

    for half in range(4):
        @on("sync")
        def _(sync, half=half):
            D = mkdma(sync, "sy", "LD")
            D(wqs[:].rearrange("p (a n) -> p a n", a=2),
              DT["WqkvT_sl"][half * 256:(half + 1) * 256, :].rearrange(
                  "(a p) n -> p a n", p=128))

        @on("vector")
        def _(vector, half=half):
            S = mkstep(vector, "v", "VS")
            S(vector.tensor_copy, wqb[:, half * 2 * 1536:(half + 1) * 2 * 1536],
              wqs[:])

    @on("vector")
    def _(vector):
        S = mkstep(vector, "v", "VS")
        S(vector.memset,
          Vsb[:].rearrange("p (k h e) -> p k h e", k=32, h=8)[:, :, :, 64:65], 1.0)

    for tt in range(8):
        r, tl = tt // 4, tt % 4

        @on("sync")
        def _(sync, r=r, tl=tl):
            D = mkdma(sync, "sy", "LD", deps=("CC", "TS"))
            for ca in range(NCH):
                D(xg[:, ca * 512:(ca + 1) * 512],
                  DT["cc_xn1_out"][r * C + ca * 128:r * C + (ca + 1) * 128,
                                   tl * 512:(tl + 1) * 512])

        for qc in range(8):
            pQ = (pQ0, pQ1)[qc % 2]
            isq = qc < 4
            hp = qc % 4
            cosT, sinT = (cosQs, sinQs) if isq else (cosKs, sinKs)
            dstT = Qh if isq else Kh

            @on("tensor")
            def _(tensor, tt=tt, qc=qc, pQ=pQ):
                S = mkstep(tensor, "t", "TS", deps=("LD", "SS", "VS"))
                for ca in range(NCH):
                    S(tensor.matmul, pQ[:],
                      wqb[:, ca * 1536 + qc * 128:ca * 1536 + (qc + 1) * 128],
                      xg[:, ca * 512:(ca + 1) * 512],
                      start=(ca == 0), stop=(ca == NCH - 1))

            @on("scalar")
            def _(scalar, pQ=pQ):
                S = mkstep(scalar, "s", "SS", deps=("TS", "VS"))
                S(scalar.copy, qraw[:], pQ[:])

            @on("tensor")
            def _(tensor):
                S = mkstep(tensor, "t", "TS", deps=("SS", "VS"))
                S(tensor.matmul, pRot[:], sb["c_ropeR"][:], qraw[:],
                  start=True, stop=True)

            @on("vector")
            def _(vector, tt=tt, hp=hp, cosT=cosT, sinT=sinT, dstT=dstT):
                S = mkstep(vector, "v", "VS", deps=("SS", "TS"))
                sl = slice(tt * 512, (tt + 1) * 512)
                S(vector.tensor_mul, t1b[:], qraw[:], cosT[:, sl])
                S(vector.tensor_mul, t2b[:], pRot[:], sinT[:, sl])
                S.c(vector.tensor_add,
                    dstT[:, hp * T + tt * 512:hp * T + (tt + 1) * 512], t1b[:],
                    t2b[:])

        for tb in range(4):
            kt = tt * 4 + tb

            @on("tensor")
            def _(tensor, tb=tb):
                S = mkstep(tensor, "t", "TS", deps=("LD", "SS", "VS"))
                for ca in range(NCH):
                    S(tensor.matmul, pVv[:],
                      xg[:, ca * 512 + tb * 128:ca * 512 + (tb + 1) * 128],
                      wqb[:, ca * 1536 + 1024:ca * 1536 + 1536],
                      start=(ca == 0), stop=(ca == NCH - 1))

            @on("scalar")
            def _(scalar, kt=kt):
                S = mkstep(scalar, "s", "SS", deps=("TS", "VS"))
                S(scalar.copy,
                  Vsb[:, kt * 520:(kt + 1) * 520].rearrange(
                      "p (h e) -> p h e", h=8)[:, :, 0:64],
                  pVv[:].rearrange("p (h e) -> p h e", h=8))

    barrier()
    ph3.close()

    ph4 = contextlib.ExitStack()
    pTb = sball("pTb", [128, 2048], BF16, ph4)
    dmasks = sball("dmasks", [128, 2048], BF16, ph4)
    lst = sball("lst", [128, 1024], F32, ph4)
    l0r = sball("l0r", [1, 1024], F32, ph4)
    bcs = sball("bcs", [64, 1024], F32, ph4)
    outn = sball("outn", [64, 1024], BF16, ph4)
    pSmA = ph4.enter_context(nc.psum_tensor("pSmA", [128, 1024], F32))
    pSmB = ph4.enter_context(nc.psum_tensor("pSmB", [128, 1024], F32))
    pAV0 = ph4.enter_context(nc.psum_tensor("pAV0", [128, 512], F32))
    pAV1 = ph4.enter_context(nc.psum_tensor("pAV1", [128, 512], F32))
    pDV = ph4.enter_context(nc.psum_tensor("pDV", [128, 512], F32))

    @on("sync")
    def _(sync):
        D = mkdma(sync, "sy", "LD")
        D(dmasks[:], DT["dmask"])

    amk = {}
    for qt in range(8):
        for h in range(MYH):
            hp, sub = h // 2, h % 2
            prt = slice(64 * sub, 64 * sub + 64)
            hb = h % 2
            pAV = (pAV0, pAV1)[hb]
            ngroups = 2 * qt + 2

            for kg in range(ngroups):
                sm = kg % 2
                pSm = (pSmA, pSmB)[sm]
                last = (kg == ngroups - 1)

                @on("tensor")
                def _(tensor, qt=qt, kg=kg, hp=hp, prt=prt, pSm=pSm, sm=sm):
                    cx.wait(tensor, "t", ("LD",))
                    cx.wait_mark(tensor, "t", amk.get(("exp", sm)))
                    S = mkstep(tensor, "t", "TS", deps=())
                    for j in range(2):
                        kt = kg * 2 + j
                        S(tensor.matmul, pSm[:, j * 512:(j + 1) * 512],
                          Kh[prt, hp * T + kt * 128:hp * T + (kt + 1) * 128],
                          Qh[prt, hp * T + qt * 512:hp * T + (qt + 1) * 512],
                          start=True, stop=True)

                @on("scalar")
                def _(scalar, pSm=pSm, sm=sm):
                    cx.wait(scalar, "s", ("TS",))
                    cx.wait_mark(scalar, "s", amk.get(("pTb", sm)))
                    S = mkstep(scalar, "s", "SS", deps=())
                    S(scalar.activation, pTb[:, sm * 1024:(sm + 1) * 1024],
                      pSm[:, 0:1024], AF.Exp, bias=sb["c_nb30"][:, 0:1])
                amk[("exp", sm)] = cx.mark("SS")
                amk[("ptb_ready", sm)] = cx.mark("SS")

                if kg >= ngroups - 2:
                    moff = 0 if kg == ngroups - 2 else 1024

                    @on("vector")
                    def _(vector, sm=sm, moff=moff):
                        S = mkstep(vector, "v", "VS", deps=("SS", "LD"))
                        S(vector.tensor_mul, pTb[:, sm * 1024:(sm + 1) * 1024],
                          pTb[:, sm * 1024:(sm + 1) * 1024],
                          dmasks[:, moff:moff + 1024])
                    amk[("ptb_ready", sm)] = cx.mark("VS")

                @on("tensor")
                def _(tensor, qt=qt, h=h, kg=kg, last=last, pAV=pAV, sm=sm, hb=hb):
                    cx.wait_mark(tensor, "t", amk.get(("ptb_ready", sm)))
                    if kg == 0:
                        cx.wait_mark(tensor, "t", amk.get(("pAV", hb)))
                    S = mkstep(tensor, "t", "TS", deps=())
                    for j in range(2):
                        kt = kg * 2 + j
                        S(tensor.matmul, pAV[0:65, :],
                          Vsb[:, kt * 520 + h * 65:kt * 520 + (h + 1) * 65],
                          pTb[:, sm * 1024 + j * 512:sm * 1024 + (j + 1) * 512],
                          start=(kg == 0 and j == 0), stop=(last and j == 1))
                amk[("pTb", sm)] = cx.mark("TS")

            @on("vector")
            def _(vector, hb=hb, pAV=pAV):
                cx.wait(vector, "v", ("TS",))
                cx.wait_mark(vector, "v", amk.get(("lst", hb)))
                S = mkstep(vector, "v", "VS", deps=())
                S(vector.tensor_copy, lst[64:65, hb * 512:(hb + 1) * 512],
                  pAV[64:65, :])

            @on("sync")
            def _(sync, hb=hb):
                D = mkdma(sync, "sy", "LD", deps=("VS",))
                D(l0r[0:1, hb * 512:(hb + 1) * 512],
                  lst[64:65, hb * 512:(hb + 1) * 512])
            amk[("lst", hb)] = cx.mark("LD")

            @on("vector")
            def _(vector, hb=hb):
                S = mkstep(vector, "v", "VS", deps=("LD",))
                S(vector.reciprocal, l0r[0:1, hb * 512:(hb + 1) * 512],
                  l0r[0:1, hb * 512:(hb + 1) * 512])

            @on("tensor")
            def _(tensor, hb=hb):
                cx.wait(tensor, "t", ("VS",))
                cx.wait_mark(tensor, "t", amk.get(("pDV",)))
                S = mkstep(tensor, "t", "TS", deps=())
                S(tensor.matmul, pDV[0:64, :], sb["c_onesrowf"][0:1, 0:64],
                  l0r[0:1, hb * 512:(hb + 1) * 512], start=True, stop=True)

            @on("vector")
            def _(vector, hb=hb, pAV=pAV):
                cx.wait(vector, "v", ("TS",))
                cx.wait_mark(vector, "v", amk.get(("outn", hb)))
                S = mkstep(vector, "v", "VS", deps=())
                S(vector.tensor_copy, bcs[:, hb * 512:(hb + 1) * 512], pDV[0:64, :])
                S.c(vector.tensor_mul, outn[:, hb * 512:(hb + 1) * 512],
                    pAV[0:64, :], bcs[:, hb * 512:(hb + 1) * 512])
            amk[("pDV",)] = cx.mark("VS")
            amk[("pAV", hb)] = cx.mark("VS")

            @on("sync")
            def _(sync, qt=qt, h=h, hb=hb):
                D = mkdma(sync, "sy", "LD", deps=("VS",))
                D(DT["cc_out_in"][h * 64:(h + 1) * 64, qt * 512:(qt + 1) * 512],
                  outn[:, hb * 512:(hb + 1) * 512])
            amk[("outn", hb)] = cx.mark("LD")

    @on("gpsimd")
    def _(g):
        S = mkstep(g, "g", "CC")
        S(g.collective_compute, "AllGather", ALU.bypass,
          replica_groups=[[0, 1], [2, 3], [4, 5], [6, 7]],
          ins=[DT["cc_out_in"].opt()], outs=[DT["cc_out_out"].opt()])

    barrier()
    ph4.close()
    s_att.close()

    ph5 = contextlib.ExitStack()
    wo1 = sball("wo1", [128, NCH * C], F32, ph5)
    wo1b = sball("wo1b", [128, NCH * C], BF16, ph5)
    gA = sball("gA", [128, BT], BF16, ph5)
    gB = sball("gB", [128, BT], BF16, ph5)
    omy = sball("omy", [128, NCH * BT], BF16, ph5)
    oscr = sball("oscr", [128, BT], F32, ph5)
    g1t = sball("g1t", [128, 2 * 512], F32, ph5)
    x1t = sball("x1t", [128, 2 * 512], F32, ph5)
    x2st = sball("x2st", [128, 2 * 512], F32, ph5)
    pZ0 = ph5.enter_context(nc.psum_tensor("pZ0", [128, 512], F32))
    pZ1 = ph5.enter_context(nc.psum_tensor("pZ1", [128, 512], F32))

    @on("sync")
    def _(sync):
        D = mkdma(sync, "sy", "LD")
        D(wo1[:].rearrange("p (a n) -> p a n", a=NCH),
          DT["Wo1T"].rearrange("(a p) n -> p a n", p=128))

    @on("vector")
    def _(vector):
        S = mkstep(vector, "v", "VS")
        S(vector.tensor_copy, wo1b[:], wo1[:])

    for r in range(2):
        for cc4 in range(4):
            dc = r * 4 + cc4
            row = r * 512 + cc4 * 128

            @on("sync")
            def _(sync, row=row):
                D = mkdma(sync, "sy", "LD", deps=("VS", "CC"))
                D(gA[:], DT["cc_out_out"][row:row + 128, 0:BT])
                D(gB[:], DT["cc_out_out"][row:row + 128, BT:T])

            @on("vector")
            def _(vector, dc=dc):
                S = mkstep(vector, "v", "VS", deps=("LD",))
                S(vector.tensor_scalar, oscr[:], gA[:], sb["c_selw"][:, 0:1], None,
                  ALU.mult)
                S.c(vector.scalar_tensor_tensor, oscr[:], gB[:], sb["c_selw"][:, 1:2],
                    oscr[:], ALU.mult, ALU.add)
                S.c(vector.tensor_copy, omy[:, dc * BT:(dc + 1) * BT], oscr[:])

    for dc in range(NCH):
        for tt in range(4):
            pZ = (pZ0, pZ1)[tt % 2]
            buf = tt % 2

            @on("sync")
            def _(sync, dc=dc, tt=tt, buf=buf):
                D = mkdma(sync, "sy", "LD", deps=("VS",))
                D(g1t[:, buf * 512:(buf + 1) * 512],
                  DT["g1_d"][dc * 128:(dc + 1) * 128, tt * 512:(tt + 1) * 512])
                D(x1t[:, buf * 512:(buf + 1) * 512],
                  DT["x1_d"][dc * 128:(dc + 1) * 128, tt * 512:(tt + 1) * 512])

            @on("tensor")
            def _(tensor, dc=dc, tt=tt, pZ=pZ):
                S = mkstep(tensor, "t", "TS", deps=("VS",))
                for ca in range(NCH):
                    S(tensor.matmul, pZ[:],
                      wo1b[:, ca * C + dc * 128:ca * C + (dc + 1) * 128],
                      omy[:, ca * BT + tt * 512:ca * BT + (tt + 1) * 512],
                      start=(ca == 0), stop=(ca == NCH - 1))

            @on("vector")
            def _(vector, dc=dc, tt=tt, pZ=pZ, buf=buf):
                S = mkstep(vector, "v", "VS", deps=("TS", "LD"))
                d = x2st[:, buf * 512:(buf + 1) * 512]
                S(vector.tensor_mul, d, pZ[:], g1t[:, buf * 512:(buf + 1) * 512])
                S.c(vector.tensor_add, d, d, x1t[:, buf * 512:(buf + 1) * 512])

            @on("sync")
            def _(sync, dc=dc, tt=tt, buf=buf):
                D = mkdma(sync, "sy", "LD", deps=("VS",))
                D(DT["x2_fm"][dc * 128:(dc + 1) * 128, tt * 512:(tt + 1) * 512],
                  x2st[:, buf * 512:(buf + 1) * 512])

    @on("sync")
    def _(sync):
        cx.wait_all(sync, "sy")

    ph5.close()
    es.close()
    return nc


def _host_prep(inputs):
    f32 = np.float32
    x = np.asarray(inputs["L0_attn_x"], f32)
    WkT = np.ascontiguousarray(np.asarray(inputs["L0_attn_Wk"], f32).T)
    WvT = np.ascontiguousarray(np.asarray(inputs["L0_attn_Wv"], f32).T)
    WrT = np.ascontiguousarray(np.asarray(inputs["L0_attn_Wr"], f32).T)
    WoT = np.ascontiguousarray(np.asarray(inputs["L0_attn_Wo"], f32).T)
    w1T = np.ascontiguousarray(np.asarray(inputs["L0_ffn_w1"], f32).T)
    w2T = np.ascontiguousarray(np.asarray(inputs["L0_ffn_w2"], f32).T)
    Wqkv = np.asarray(inputs["L1_attn_Wqkv"], f32)
    Wo1T = np.ascontiguousarray(np.asarray(inputs["L1_attn_Wo"], f32).T)

    def colpack(v, n):
        return np.ascontiguousarray(np.asarray(v, f32).reshape(n, 128).T)

    def rowbc(v):
        v = np.asarray(v, f32)
        return np.ascontiguousarray(np.broadcast_to(v, (128, v.shape[0])))

    gamA = colpack(inputs["L0_attn_ln_gamma"], 8)
    betA = colpack(inputs["L0_attn_ln_beta"], 8)
    decay = colpack(inputs["L0_attn_time_decay"], 8)
    first = colpack(inputs["L0_attn_time_first"], 8)
    b2c = colpack(inputs["L0_ffn_b2"], 8)
    bTT = np.concatenate([np.asarray(inputs["L1_ffn_bias_real"], f32),
                          np.asarray(inputs["L1_ffn_bias_imag"], f32)])
    bTTc = colpack(bTT, 8)
    g1bc = rowbc(inputs["L0_ffn_ln1_g"])
    b1bc = rowbc(inputs["L0_ffn_ln1_b"])
    b1fbc = rowbc(inputs["L0_ffn_b1"])
    g2bc = rowbc(inputs["L0_ffn_ln2_g"])
    be2bc = rowbc(inputs["L0_ffn_ln2_b"])

    identf = np.eye(128, dtype=f32)
    identb = np.eye(128, dtype=f32).astype(BF)
    onesb = np.ones((128, 128), f32).astype(BF)
    onescolf = np.ones((128, 1), f32)
    onesrowf = np.ones((1, 128), f32)
    R = np.zeros((128, 128), f32)
    for i in range(64):
        R[2 * i, 2 * i + 1] = -1.0
        R[2 * i + 1, 2 * i] = 1.0
    ropeRT = np.ascontiguousarray(R.T).astype(BF)
    freqs = np.exp(-np.log(10000.0) * np.arange(0, DH, 2, dtype=f32) / DH)
    ang = np.arange(T, dtype=f32)[:, None] * freqs[None, :]
    ang = np.repeat(ang, 2, axis=1)
    cos64 = np.cos(ang).T.astype(f32)
    sin64 = np.sin(ang).T.astype(f32)
    cos128 = np.concatenate([cos64, cos64], 0)
    sin128 = np.concatenate([sin64, sin64], 0)
    cosQ = (cos128 / 8.0).astype(BF)
    sinQ = (sin128 / 8.0).astype(BF)
    cosK = cos128.astype(BF)
    sinK = sin128.astype(BF)
    k_ = np.arange(128)[:, None]
    q_ = np.arange(512)[None, :]
    dmask = np.concatenate(
        [((k_ + 128 * j) <= q_).astype(f32) for j in range(4)], axis=1).astype(BF)

    G1R = np.asarray(inputs["L1_ffn_G1_R"], f32)
    G2R = np.asarray(inputs["L1_ffn_G2_R"], f32)
    G1I = np.asarray(inputs["L1_ffn_G1_I"], f32)
    G2I = np.asarray(inputs["L1_ffn_G2_I"], f32)

    def ttrep(G1, G2):
        T1 = np.zeros((512, 4, 512), f32)
        T2 = np.zeros((512, 4, 512), f32)
        d_ = np.arange(512) // 16
        g_ = np.arange(512) % 16
        b_ = np.arange(512) // 16
        e_ = np.arange(512) % 16
        for c in range(4):
            A = G1[0, :, c, :]
            Bm = G2[c, :, 0, :]
            T1[:, c, :] = A.T[d_][:, b_]
            T2[:, c, :] = Bm.T[g_][:, e_]
        return (np.ascontiguousarray(T1.reshape(512, 2048)),
                np.ascontiguousarray(T2.reshape(512, 2048)))

    T1r, T2r = ttrep(G1R, G2R)
    T1i, T2i = ttrep(G1I, G2I)

    base = dict(
        WkT=WkT, WvT=WvT, WrT=WrT, WoT=WoT, Wo1T=Wo1T,
        gamA=gamA, betA=betA, decay=decay, first=first, b2c=b2c, bTTc=bTTc,
        g1bc=g1bc, b1bc=b1bc, b1fbc=b1fbc, g2bc=g2bc, be2bc=be2bc,
        identf=identf, identb=identb, onesb=onesb, ropeR=ropeRT,
        onescolf=onescolf, onesrowf=onesrowf,
        nb30=np.full((128, 1), EXPB, f32),
        cosQ=cosQ, sinQ=sinQ, cosK=cosK, sinK=sinK, dmask=dmask,
        T1r=T1r, T2r=T2r, T1i=T1i, T2i=T2i,
    )

    in_maps = []
    for core in range(8):
        b, p = core // 2, core % 2
        m = dict(base)
        xt = np.ascontiguousarray(x[b, p * BT:(p + 1) * BT, :])
        m["x_tok"] = xt
        m["x_fm"] = np.ascontiguousarray(xt.T)
        m["w1T_sl"] = np.ascontiguousarray(w1T[:, core * 512:(core + 1) * 512])
        m["w2T_sl"] = np.ascontiguousarray(w2T[core * 512:(core + 1) * 512, :])
        hrows = np.concatenate([np.arange(p * 512, (p + 1) * 512),
                                C + np.arange(p * 512, (p + 1) * 512),
                                2 * C + np.arange(p * 512, (p + 1) * 512)])
        m["WqkvT_sl"] = np.ascontiguousarray(Wqkv[hrows, :].T)
        m["tvals"] = np.broadcast_to(
            np.arange(p * BT, (p + 1) * BT, dtype=f32), (128, BT)).copy()
        m["selw"] = np.broadcast_to(
            np.array([1.0 - p, float(p)], f32), (128, 2)).copy()
        m["carryw"] = np.broadcast_to(
            np.array([float(p), 0.0], f32), (128, 2)).copy()
        in_maps.append(m)
    return in_maps


def kernel(**inputs):
    global _last_res
    in_maps = _host_prep(inputs)
    nc = build_program()
    res = run_bass_kernel_spmd(nc, in_maps, core_ids=list(range(8)))
    _last_res = res
    out = np.zeros((B, T, C), np.float32)
    for core in range(8):
        b, p = core // 2, core % 2
        out[b, p * BT:(p + 1) * BT, :] = np.asarray(res.results[core]["x2_fm"]).T
    return out


# version 05
# speedup vs baseline: 1.1053x; 1.1053x over previous
import contextlib
import numpy as np
import ml_dtypes

import concourse.bass as bass
import concourse.mybir as mybir
from concourse.bass_utils import run_bass_kernel_spmd

F32 = mybir.dt.float32
BF16 = mybir.dt.bfloat16
AF = mybir.ActivationFunctionType
ALU = mybir.AluOpType
AX = mybir.AxisListType.X
BF = ml_dtypes.bfloat16

B, T, C = 4, 4096, 1024
BT = 2048
H4 = 4096
DH = 64
NCH = 8
NBLK = 16
MYH = 8
EPS = 1e-5
RC = 12582912.0
EXPB = -30.0
ALLSEM = ("LD", "GD", "TS", "VS", "SS", "CC")


class Ctx:
    def __init__(self, nc, es):
        self.sem = {}
        self.cnt = {}
        self.waited = {}
        for name in ALLSEM:
            self.sem[name] = es.enter_context(nc.semaphore(name))
            self.cnt[name] = 0

    def inc(self, instr, name, n=1):
        instr.then_inc(self.sem[name], n)
        self.cnt[name] += n
        return instr

    def wait(self, eng, ename, names):
        for name in names:
            v = self.cnt[name]
            key = (ename, name)
            if v == 0 or self.waited.get(key, -1) >= v:
                continue
            eng.wait_ge(self.sem[name], v)
            self.waited[key] = v

    def wait_all(self, eng, ename):
        self.wait(eng, ename, ALLSEM)

    def mark(self, name):
        return (name, self.cnt[name])

    def wait_mark(self, eng, ename, mk):
        if mk is None:
            return
        name, v = mk
        key = (ename, name)
        if v == 0 or self.waited.get(key, -1) >= v:
            return
        eng.wait_ge(self.sem[name], v)
        self.waited[key] = v


DEBUG = False


def build_program():
    nc = bass.Bass("TRN2", target_bir_lowering=False, debug=False)
    es = contextlib.ExitStack()
    DT = {}

    def din(name, shape, dt=F32):
        DT[name] = nc.dram_tensor(name, shape, dt, kind="ExternalInput").ap()

    def dout(name, shape, dt=F32):
        DT[name] = nc.dram_tensor(name, shape, dt, kind="ExternalOutput").ap()

    def dint(name, shape, dt=F32):
        DT[name] = nc.dram_tensor(name, shape, dt).ap()

    din("x_fm", [C, BT]); din("x_tok", [BT, C])
    din("WkT", [C, C]); din("WvT", [C, C]); din("WrT", [C, C]); din("WoT", [C, C])
    din("w1T_sl", [C, 512]); din("w2T_sl", [512, C])
    din("WqkvT_sl", [C, 1536]); din("Wo1T", [C, C])
    din("gamA", [128, 8]); din("betA", [128, 8])
    din("decay", [128, 8]); din("first", [128, 8])
    din("b2c", [128, 8]); din("bTTc", [128, 8])
    din("g1bc", [128, C]); din("b1bc", [128, C])
    din("b1fbc", [128, H4]); din("g2bc", [128, H4]); din("be2bc", [128, H4])
    din("tvals", [128, BT])
    din("identf", [128, 128], F32)
    din("identb", [128, 128], BF16)
    din("onesb", [128, 128], BF16)
    din("ropeR", [128, 128], BF16)
    din("onescolf", [128, 1]); din("onesrowf", [1, 128])
    din("selw", [128, 2]); din("carryw", [128, 2]); din("nb30", [128, 1])
    din("cosQ", [128, T], BF16); din("sinQ", [128, T], BF16)
    din("cosK", [128, T], BF16); din("sinK", [128, T], BF16)
    din("dmask", [128, 2048], BF16)
    din("T1r", [512, 2048]); din("T2r", [512, 2048])
    din("T1i", [512, 2048]); din("T2i", [512, 2048])

    dout("x2_fm", [C, BT])
    if DEBUG:
        dout("dbg_xn2a", [C, BT], BF16)
        dout("dbg_int2", [H4, BT], BF16)
        dout("dbg_g0", [C, BT])
        dout("dbg_x1", [C, BT])
        dout("dbg_xn1", [C, BT], BF16)
        dout("dbg_g1", [C, BT])
        dout("dbg_att", [512, T], BF16)

    dint("cc_wsc_in", [1, 2]); dint("cc_wsc_out", [8, 2])
    dint("cc_wq1_in", [C, 512], BF16); dint("cc_wq1_out", [8 * C, 512], BF16)
    dint("cc_wq2_in", [512, C], BF16); dint("cc_wq2_out", [8 * 512, C], BF16)
    dint("cc_carry_in", [1, C]); dint("cc_carry_out", [2, C])
    dint("cc_xn1_in", [C, BT], BF16); dint("cc_xn1_out", [2 * C, BT], BF16)
    dint("cc_out_in", [512, T], BF16); dint("cc_out_out", [1024, T], BF16)
    dint("xn2a_d", [C, BT], BF16)
    dint("int2_d", [H4, BT], BF16)
    dint("g0_d", [C, BT]); dint("g1_d", [C, BT]); dint("x1_d", [C, BT])
    dint("mt_d", [C, C], BF16)

    cx = Ctx(nc, es)
    sb = {}

    def sball(name, shape, dt=F32, stack=None):
        sb[name] = (stack or es).enter_context(nc.sbuf_tensor(name, shape, dt))
        return sb[name]

    for nm, shp, dt in [
        ("c_idf", [128, 128], F32), ("c_idb", [128, 128], BF16),
        ("c_onesb", [128, 128], BF16), ("c_ropeR", [128, 128], BF16),
        ("c_onescolf", [128, 1], F32), ("c_onesrowf", [1, 128], F32),
        ("c_selw", [128, 2], F32), ("c_carryw", [128, 2], F32),
        ("c_nb30", [128, 1], F32),
        ("v_gamA", [128, 8], F32), ("v_betA", [128, 8], F32),
        ("v_decay", [128, 8], F32), ("v_first", [128, 8], F32),
        ("v_b2c", [128, 8], F32), ("v_bTTc", [128, 8], F32),
        ("v_td", [128, 8], F32), ("v_tf", [128, 8], F32), ("v_sp", [128, 8], F32),
        ("st_A", [128, 16], F32), ("st_B", [128, 16], F32),
        ("st_s1", [128, 16], F32), ("st_s1s", [128, 16], F32),
        ("st_s2i", [128, 16], F32),
        ("v_mw", [128, 2], F32), ("v_swc", [128, 2], F32),
        ("row_a", [1, BT], F32),
        ("carry_fix", [128, 8], F32),
        ("colscr", [128, 64], F32),
        ("st16", [16, 256], F32),
        ("abc", [128, BT], F32), ("bbc", [128, BT], F32),
    ]:
        sball(nm, shp, dt)

    block = es.enter_context(nc.Block())

    def on(engname):
        def deco(fn):
            getattr(block, engname)(fn)
        return deco

    def barrier():
        nc.all_engine_barrier()

    def mkstep(eng, ename, own, deps=ALLSEM):
        state = {"first": True}

        def step(f, *a, **k):
            if state["first"]:
                cx.wait(eng, ename, deps)
                state["first"] = False
            return cx.inc(f(*a, **k), own)

        def chain(f, *a, **k):
            if not state["first"]:
                cx.wait(eng, ename, (own,))
            return step(f, *a, **k)

        step.c = chain
        return step

    def mkdma(eng, ename, own, deps=ALLSEM):
        state = {"first": True}

        def dma(out, in_, **k):
            if state["first"]:
                cx.wait(eng, ename, deps)
                state["first"] = False
            return cx.inc(eng.dma_start(out=out, in_=in_, **k), own, 16)
        return dma

    def emit_bc_from_pack(pack_name, pack_sl, dst_name, ppsum):
        @on("tensor")
        def _(tensor):
            S = mkstep(tensor, "t", "TS")
            S(tensor.transpose, ppsum[0:16, 0:128], sb[pack_name][:, pack_sl],
              sb["c_idf"][:])

        @on("scalar")
        def _(scalar):
            S = mkstep(scalar, "s", "SS")
            S(scalar.copy, sb["st16"][:, 0:128], ppsum[0:16, 0:128])

        @on("sync")
        def _(sync):
            D = mkdma(sync, "sy", "LD")
            for tb in range(NBLK):
                D(sb["row_a"][0:1, tb * 128:(tb + 1) * 128],
                  sb["st16"][tb:tb + 1, 0:128])

        @on("tensor")
        def _(tensor):
            S = mkstep(tensor, "t", "TS")
            for tt in range(4):
                S(tensor.matmul, ppsum[:, tt * 512:(tt + 1) * 512],
                  sb["c_onesrowf"][:], sb["row_a"][0:1, tt * 512:(tt + 1) * 512],
                  start=True, stop=True)

        @on("scalar")
        def _(scalar):
            S = mkstep(scalar, "s", "SS")
            S(scalar.copy, sb[dst_name][:], ppsum[:, 0:2048])

    ph0 = contextlib.ExitStack()
    w1s = sball("w1s", [128, 8 * 512], F32, ph0)
    w2s = sball("w2s", [128, 4 * C], F32, ph0)
    wq1 = sball("wq1", [128, 8 * 512], BF16, ph0)
    wq2 = sball("wq2", [128, 4 * C], BF16, ph0)
    wscr = sball("wscr", [128, 4096], F32, ph0)
    wcol = sball("wcol", [128, 8], F32, ph0)
    wg8 = sball("wg8", [8, 2], F32, ph0)
    t1sb = sball("t1sb", [128, 4 * 2048], F32, ph0)
    t2sb = sball("t2sb", [128, 4 * 2048], F32, ph0)
    wTT = sball("wTT", [128, 4 * 512], F32, ph0)
    MT0 = sball("MT0", [128, 8 * C], BF16, ph0)
    pw = ph0.enter_context(nc.psum_tensor("pw", [128, 512], F32))

    @on("sync")
    def _(sync):
        D = mkdma(sync, "sy", "LD")
        for nm, dst in [("identf", "c_idf"), ("identb", "c_idb"),
                        ("onesb", "c_onesb"), ("ropeR", "c_ropeR"),
                        ("onescolf", "c_onescolf"), ("onesrowf", "c_onesrowf"),
                        ("selw", "c_selw"), ("carryw", "c_carryw"), ("nb30", "c_nb30"),
                        ("gamA", "v_gamA"), ("betA", "v_betA"),
                        ("decay", "v_decay"), ("first", "v_first"),
                        ("b2c", "v_b2c"), ("bTTc", "v_bTTc")]:
            D(sb[dst][:], DT[nm])
        D(w1s[:].rearrange("p (a n) -> p a n", a=8),
          DT["w1T_sl"].rearrange("(a p) n -> p a n", p=128))
        D(w2s[:].rearrange("p (a n) -> p a n", a=4),
          DT["w2T_sl"].rearrange("(a p) n -> p a n", p=128))
        D(t1sb[:].rearrange("p (a n) -> p a n", a=4),
          DT["T1r"].rearrange("(a p) n -> p a n", p=128))
        D(t2sb[:].rearrange("p (a n) -> p a n", a=4),
          DT["T2r"].rearrange("(a p) n -> p a n", p=128))

    @on("scalar")
    def _(scalar):
        S = mkstep(scalar, "s", "SS")
        S(scalar.activation, sb["v_td"][:], sb["v_decay"][:], AF.Sigmoid)
        S(scalar.activation, sb["v_tf"][:], sb["v_first"][:], AF.Sigmoid)

    @on("vector")
    def _(vector):
        S = mkstep(vector, "v", "VS")
        S(vector.tensor_scalar, sb["v_sp"][:], sb["v_td"][:], 1e-38, None, ALU.max)

    @on("scalar")
    def _(scalar):
        S = mkstep(scalar, "s", "SS")
        S(scalar.activation, sb["v_sp"][:], sb["v_sp"][:], AF.Ln)

    @on("vector")
    def _(vector):
        S = mkstep(vector, "v", "VS")
        S(vector.tensor_scalar, sb["v_sp"][:], sb["v_sp"][:], -1.0, None, ALU.mult)

    @on("vector")
    def _(vector):
        S = mkstep(vector, "v", "VS")
        S(vector.reduce_sum, wcol[:, 0:1], w1s[:], axis=AX, apply_absolute_value=True)
        S(vector.reduce_sum, wcol[:, 1:2], w2s[:], axis=AX, apply_absolute_value=True)

    @on("tensor")
    def _(tensor):
        S = mkstep(tensor, "t", "TS")
        S(tensor.matmul, pw[0:1, 0:2], sb["c_onescolf"][:], wcol[:, 0:2],
          start=True, stop=True)

    @on("scalar")
    def _(scalar):
        S = mkstep(scalar, "s", "SS")
        S(scalar.copy, wcol[0:1, 2:4], pw[0:1, 0:2])

    @on("gpsimd")
    def _(g):
        D = mkdma(g, "g", "GD")
        S = mkstep(g, "g", "CC")
        D(DT["cc_wsc_in"], wcol[0:1, 2:4])
        cx.wait(g, "g", ("GD",))
        S(g.collective_compute, "AllGather", ALU.bypass,
          replica_groups=[list(range(8))],
          ins=[DT["cc_wsc_in"].opt()], outs=[DT["cc_wsc_out"].opt()])
        cx.wait(g, "g", ("CC",))
        D(wg8[:], DT["cc_wsc_out"])

    @on("vector")
    def _(vector):
        S = mkstep(vector, "v", "VS")
        S(vector.tensor_copy, wcol[0:8, 4:6], wg8[:, 0:2])

    @on("tensor")
    def _(tensor):
        S = mkstep(tensor, "t", "TS")
        S(tensor.matmul, pw[0:1, 2:4], sb["c_onescolf"][0:8, 0:1], wcol[0:8, 4:6],
          start=True, stop=True)

    @on("vector")
    def _(vector):
        S = mkstep(vector, "v", "VS")
        S(vector.tensor_scalar, wcol[0:1, 6:8], pw[0:1, 2:4],
          1.0 / (C * H4), 1e-5, ALU.mult, ALU.max)

    @on("tensor")
    def _(tensor):
        S = mkstep(tensor, "t", "TS")
        S(tensor.matmul, pw[:, 4:6], sb["c_onesrowf"][:], wcol[0:1, 6:8],
          start=True, stop=True)

    @on("scalar")
    def _(scalar):
        S = mkstep(scalar, "s", "SS")
        S(scalar.copy, sb["v_mw"][:], pw[:, 4:6])

    @on("vector")
    def _(vector):
        S = mkstep(vector, "v", "VS")
        S(vector.reciprocal, sb["v_swc"][:], sb["v_mw"][:])
        for wsrc, dst, sc, n in ((w1s, wq1, 0, 8 * 512), (w2s, wq2, 1, 4 * C)):
            S.c(vector.tensor_scalar, wscr[:, 0:n], wsrc[:], sb["v_swc"][:, sc:sc + 1],
                RC, ALU.mult, ALU.add)
            S.c(vector.tensor_scalar, wscr[:, 0:n], wscr[:, 0:n], RC, None,
                ALU.subtract)
            S.c(vector.tensor_scalar, dst[:], wscr[:, 0:n], 1.0, -1.0, ALU.min,
                ALU.max)

    @on("gpsimd")
    def _(g):
        D = mkdma(g, "g", "GD")
        S = mkstep(g, "g", "CC")
        D(DT["cc_wq1_in"].rearrange("(a p) n -> p a n", p=128),
          wq1[:].rearrange("p (a n) -> p a n", a=8))
        D(DT["cc_wq2_in"].rearrange("(a p) n -> p a n", p=128),
          wq2[:].rearrange("p (a n) -> p a n", a=4))
        cx.wait(g, "g", ("GD",))
        S(g.collective_compute, "AllGather", ALU.bypass,
          replica_groups=[list(range(8))],
          ins=[DT["cc_wq1_in"].opt()], outs=[DT["cc_wq1_out"].opt()])
        S(g.collective_compute, "AllGather", ALU.bypass,
          replica_groups=[list(range(8))],
          ins=[DT["cc_wq2_in"].opt()], outs=[DT["cc_wq2_out"].opt()])

    def emit_tt_build(which):
        @on("vector")
        def _(vector):
            S = mkstep(vector, "v", "VS")
            for pc in range(4):
                base = pc * 2048
                S.c(vector.tensor_mul, wTT[:, pc * 512:(pc + 1) * 512],
                  t1sb[:, base:base + 512], t2sb[:, base:base + 512])
                for c in range(1, 4):
                    s0 = base + c * 512
                    S.c(vector.tensor_mul, wscr[:, 0:512], t1sb[:, s0:s0 + 512],
                      t2sb[:, s0:s0 + 512])
                    S.c(vector.tensor_add, wTT[:, pc * 512:(pc + 1) * 512],
                      wTT[:, pc * 512:(pc + 1) * 512], wscr[:, 0:512])
            for pc in range(4):
                w = wTT[:, pc * 512:(pc + 1) * 512]
                if which == 0:
                    S.c(vector.tensor_copy, MT0[:, pc * C + 0:pc * C + 512], w)
                    S.c(vector.tensor_copy,
                      MT0[:, (4 + pc) * C + 512:(4 + pc) * C + 1024], w)
                else:
                    S.c(vector.tensor_copy, MT0[:, pc * C + 512:pc * C + 1024], w)
                    S.c(vector.tensor_scalar, wscr[:, 0:512], w, -1.0, None, ALU.mult)
                    S.c(vector.tensor_copy,
                      MT0[:, (4 + pc) * C + 0:(4 + pc) * C + 512], wscr[:, 0:512])

    emit_tt_build(0)

    @on("sync")
    def _(sync):
        D = mkdma(sync, "sy", "LD")
        D(t1sb[:].rearrange("p (a n) -> p a n", a=4),
          DT["T1i"].rearrange("(a p) n -> p a n", p=128))
        D(t2sb[:].rearrange("p (a n) -> p a n", a=4),
          DT["T2i"].rearrange("(a p) n -> p a n", p=128))

    emit_tt_build(1)

    @on("sync")
    def _(sync):
        D = mkdma(sync, "sy", "LD")
        D(DT["mt_d"].rearrange("(a p) n -> p a n", p=128),
          MT0[:].rearrange("p (a n) -> p a n", a=NCH))

    barrier()
    ph0.close()

    s_int1f = contextlib.ExitStack()
    int1f = sball("int1f", [128, NCH * BT], BF16, s_int1f)
    ph1 = contextlib.ExitStack()
    xtb = sball("xtb", [128, 2 * C], F32, ph1)
    xfb = sball("xfb", [128, 2 * BT], F32, ph1)
    g1bcs = sball("g1bcs", [128, C], F32, ph1)
    b1bcs = sball("b1bcs", [128, C], F32, ph1)
    scr1 = sball("scr1", [128, C], F32, ph1)
    scr2 = sball("scr2", [128, C], F32, ph1)
    int1t = sball("int1t", [128, NBLK * C], BF16, ph1)
    xn2a = sball("xn2a", [128, NCH * BT], BF16, ph1)
    ph1p = contextlib.ExitStack()
    pstA = ph1p.enter_context(nc.psum_tensor("pstA", [128, 2048], F32))
    pstB = ph1p.enter_context(nc.psum_tensor("pstB", [128, 2048], F32))

    @on("sync")
    def _(sync):
        D = mkdma(sync, "sy", "LD")
        D(g1bcs[:], DT["g1bc"])
        D(b1bcs[:], DT["b1bc"])

    h1mk = {}
    for tb in range(NBLK):
        buf = tb % 2

        @on("sync")
        def _(sync, tb=tb, buf=buf):
            cx.wait_mark(sync, "sy", h1mk.get(("xtb_s", buf)))
            cx.wait_mark(sync, "sy", h1mk.get(("xtb_v", buf)))
            D = mkdma(sync, "sy", "LD", deps=())
            D(xtb[:, buf * C:(buf + 1) * C],
              DT["x_tok"][tb * 128:(tb + 1) * 128, :])

        @on("scalar")
        def _(scalar, tb=tb, buf=buf):
            S = mkstep(scalar, "s", "SS", deps=("LD", "SS"))
            S(scalar.activation, scr1[:], xtb[:, buf * C:(buf + 1) * C], AF.Square,
              accum_out=sb["colscr"][:, tb:tb + 1])
        h1mk[("xtb_s", buf)] = cx.mark("SS")

        @on("vector")
        def _(vector, tb=tb, buf=buf):
            S = mkstep(vector, "v", "VS", deps=("LD",))
            S(vector.reduce_sum, sb["colscr"][:, 16 + tb:17 + tb],
              xtb[:, buf * C:(buf + 1) * C], axis=AX)
        h1mk[("xtb_v", buf)] = cx.mark("VS")

    @on("vector")
    def _(vector):
        S = mkstep(vector, "v", "VS")
        mu = sb["colscr"][:, 32:48]
        var = sb["colscr"][:, 48:64]
        S(vector.tensor_scalar, mu, sb["colscr"][:, 16:32], 1.0 / C, None, ALU.mult)
        S.c(vector.tensor_mul, var, mu, mu)
        S.c(vector.tensor_scalar, sb["st_A"][:], sb["colscr"][:, 0:16], 1.0 / C, None,
            ALU.mult)
        S.c(vector.tensor_sub, var, sb["st_A"][:], var)
        S.c(vector.tensor_scalar, var, var, 1.0 + EPS, EPS * EPS, ALU.mult, ALU.add)

    @on("scalar")
    def _(scalar):
        S = mkstep(scalar, "s", "SS")
        S(scalar.activation, sb["st_B"][:], sb["colscr"][:, 48:64], AF.Sqrt)

    @on("vector")
    def _(vector):
        S = mkstep(vector, "v", "VS")
        S(vector.reciprocal, sb["st_A"][:], sb["st_B"][:])
        S.c(vector.scalar_tensor_tensor, sb["st_B"][:], sb["colscr"][:, 32:48], -1.0,
            sb["st_A"][:], ALU.mult, ALU.mult)

    for tb in range(NBLK):
        buf = tb % 2

        @on("sync")
        def _(sync, tb=tb, buf=buf):
            cx.wait_mark(sync, "sy", h1mk.get(("xtb2", buf)))
            D = mkdma(sync, "sy", "LD", deps=("SS", "VS"))
            D(xtb[:, buf * C:(buf + 1) * C],
              DT["x_tok"][tb * 128:(tb + 1) * 128, :])

        @on("vector")
        def _(vector, tb=tb, buf=buf):
            S = mkstep(vector, "v", "VS", deps=("LD",))
            xb = xtb[:, buf * C:(buf + 1) * C]
            S(vector.tensor_scalar, scr1[:], xb, sb["st_A"][:, tb:tb + 1],
              sb["st_B"][:, tb:tb + 1], ALU.mult, ALU.add)
            S.c(vector.scalar_tensor_tensor, scr2[:], scr1[:], 1.0, g1bcs[:],
                ALU.bypass, ALU.mult)
            S.c(vector.tensor_add, scr2[:], scr2[:], b1bcs[:])
            S.c(vector.reduce_max, sb["colscr"][:, tb:tb + 1], scr2[:], axis=AX,
                apply_absolute_value=True)
            S.c(vector.tensor_scalar, sb["colscr"][:, 16 + tb:17 + tb],
                sb["colscr"][:, tb:tb + 1], 1e-5, None, ALU.max)
            S.c(vector.reciprocal, sb["st_s1"][:, tb:tb + 1],
                sb["colscr"][:, 16 + tb:17 + tb])
            S.c(vector.tensor_scalar, sb["st_s1"][:, tb:tb + 1],
                sb["st_s1"][:, tb:tb + 1], 127.0, None, ALU.mult)
            S.c(vector.scalar_tensor_tensor, sb["st_s1s"][:, tb:tb + 1],
                sb["colscr"][:, 16 + tb:17 + tb], 1.0 / 127.0, sb["v_mw"][:, 0:1],
                ALU.mult, ALU.mult)
            S.c(vector.tensor_scalar, scr2[:], scr2[:], sb["st_s1"][:, tb:tb + 1], RC,
                ALU.mult, ALU.add)
            S.c(vector.tensor_scalar, int1t[:, tb * C:(tb + 1) * C], scr2[:], RC,
                None, ALU.subtract)
        h1mk[("xtb2", buf)] = cx.mark("VS")

    emit_bc_from_pack("st_A", slice(0, 16), "abc", pstA)
    emit_bc_from_pack("st_B", slice(0, 16), "bbc", pstB)

    for ca in range(NCH):
        buf = ca % 2

        @on("sync")
        def _(sync, ca=ca, buf=buf):
            cx.wait_mark(sync, "sy", h1mk.get(("xfb", buf)))
            D = mkdma(sync, "sy", "LD", deps=("SS",))
            D(xfb[:, buf * BT:(buf + 1) * BT],
              DT["x_fm"][ca * 128:(ca + 1) * 128, :])

        @on("vector")
        def _(vector, ca=ca, buf=buf):
            S = mkstep(vector, "v", "VS", deps=("LD",))
            xc = xfb[:, buf * BT:(buf + 1) * BT]
            dst = xn2a[:, ca * BT:(ca + 1) * BT]
            S(vector.tensor_mul, xc, xc, sb["abc"][:])
            S.c(vector.tensor_add, xc, xc, sb["bbc"][:])
            S.c(vector.tensor_scalar, dst, xc, sb["v_gamA"][:, ca:ca + 1],
                sb["v_betA"][:, ca:ca + 1], ALU.mult, ALU.add)
        h1mk[("xfb", buf)] = cx.mark("VS")

    @on("sync")
    def _(sync):
        D = mkdma(sync, "sy", "LD")
        D(DT["xn2a_d"].rearrange("(a p) n -> p a n", p=128),
          xn2a[:].rearrange("p (a n) -> p a n", a=NCH))

    barrier()
    ph1p.close()
    pT0 = ph1.enter_context(nc.psum_tensor("pT0", [128, 512], BF16))
    pT1 = ph1.enter_context(nc.psum_tensor("pT1", [128, 512], BF16))

    for ca in range(NCH):
        for tg in range(4):
            pT = (pT0, pT1)[tg % 2]

            @on("tensor")
            def _(tensor, ca=ca, tg=tg, pT=pT):
                cx.wait(tensor, "t", ("VS",))
                cx.wait_mark(tensor, "t", h1mk.get(("pT", tg % 2)))
                S = mkstep(tensor, "t", "TS", deps=())
                for j in range(4):
                    tb = tg * 4 + j
                    S(tensor.transpose, pT[:, j * 128:(j + 1) * 128],
                      int1t[:, tb * C + ca * 128:tb * C + (ca + 1) * 128],
                      sb["c_idb"][:])

            @on("scalar")
            def _(scalar, ca=ca, tg=tg, pT=pT):
                S = mkstep(scalar, "s", "SS", deps=("TS",))
                S(scalar.copy,
                  int1f[:, ca * BT + tg * 512:ca * BT + (tg + 1) * 512], pT[:])
            h1mk[("pT", tg % 2)] = cx.mark("SS")

    barrier()
    ph1.close()

    phf = contextlib.ExitStack()
    wq1f = sball("wq1f", [128, 64 * 512], BF16, phf)
    b1fs = sball("b1fs", [128, H4], BF16, phf)
    g2s = sball("g2s", [128, H4], BF16, phf)
    be2s = sball("be2s", [128, H4], BF16, phf)
    h2b = sball("h2b", [128, 2 * H4], BF16, phf)
    ln2o = sball("ln2o", [128, H4], F32, phf)
    sqs = sball("sqs", [128, 512], F32, phf)
    fscr = sball("fscr", [128, 1024], F32, phf)
    int2s = sball("int2s", [128, H4], BF16, phf)
    i2st = sball("i2st", [128, 1024], BF16, phf)
    pF0 = phf.enter_context(nc.psum_tensor("pF0", [128, 512], F32))
    pF1 = phf.enter_context(nc.psum_tensor("pF1", [128, 512], F32))
    pX0 = phf.enter_context(nc.psum_tensor("pX0", [128, 512], BF16))
    pX1 = phf.enter_context(nc.psum_tensor("pX1", [128, 512], BF16))

    @on("sync")
    def _(sync):
        D = mkdma(sync, "sy", "LD")
        for r in range(8):
            for ca in range(NCH):
                D(wq1f[:, (ca * 8 + r) * 512:(ca * 8 + r + 1) * 512],
                  DT["cc_wq1_out"][r * C + ca * 128:r * C + (ca + 1) * 128, :])
    @on("gpsimd")
    def _(g):
        D = mkdma(g, "g", "GD")
        D(b1fs[:], DT["b1fbc"])
        D(g2s[:], DT["g2bc"])
        D(be2s[:], DT["be2bc"])

    fmk = {}
    for tb in range(NBLK):
        h2o = (tb % 2) * H4

        for hc in range(8):
            pF = (pF0, pF1)[hc % 2]
            fb = hc % 2

            @on("tensor")
            def _(tensor, tb=tb, hc=hc, pF=pF, fb=fb):
                cx.wait(tensor, "t", ("LD", "GD"))
                cx.wait_mark(tensor, "t", fmk.get(("pF", fb)))
                S = mkstep(tensor, "t", "TS", deps=())
                for ca in range(NCH):
                    S(tensor.matmul, pF[:],
                      int1f[:, ca * BT + tb * 128:ca * BT + (tb + 1) * 128],
                      wq1f[:, (ca * 8 + hc) * 512:(ca * 8 + hc + 1) * 512],
                      start=(ca == 0), stop=(ca == NCH - 1))

            @on("vector")
            def _(vector, tb=tb, hc=hc, pF=pF, fb=fb):
                cx.wait(vector, "v", ("TS", "GD"))
                cx.wait_mark(vector, "v", fmk.get(("fscr", fb)))
                S = mkstep(vector, "v", "VS", deps=())
                S(vector.scalar_tensor_tensor, fscr[:, fb * 512:(fb + 1) * 512],
                  pF[:], sb["st_s1s"][:, tb:tb + 1],
                  b1fs[:, hc * 512:(hc + 1) * 512], ALU.mult, ALU.add)
            fmk[("pF", fb)] = cx.mark("VS")

            @on("scalar")
            def _(scalar, tb=tb, hc=hc, fb=fb, h2o=h2o):
                cx.wait(scalar, "s", ("VS",))
                cx.wait_mark(scalar, "s", fmk.get(("h2b", tb % 2)))
                S = mkstep(scalar, "s", "SS", deps=())
                S(scalar.activation, h2b[:, h2o + hc * 512:h2o + (hc + 1) * 512],
                  fscr[:, fb * 512:(fb + 1) * 512], AF.Gelu)
            fmk[("fscr", fb)] = cx.mark("SS")

        @on("vector")
        def _(vector, tb=tb, h2o=h2o):
            cx.wait(vector, "v", ("SS",))
            cx.wait_mark(vector, "v", fmk.get(("ln2o",)))
            S = mkstep(vector, "v", "VS", deps=())
            h2 = h2b[:, h2o:h2o + H4]
            cc = sb["colscr"]
            S(vector.reduce_sum, cc[:, 0:1], h2, axis=AX)
            S.c(vector.tensor_mul, ln2o[:], h2, h2)
            S.c(vector.reduce_sum, cc[:, 8:9], ln2o[:], axis=AX)
            S.c(vector.tensor_scalar, cc[:, 1:2], cc[:, 0:1], 1.0 / H4, None,
                ALU.mult)
            S.c(vector.tensor_mul, cc[:, 2:3], cc[:, 1:2], cc[:, 1:2])
            S.c(vector.tensor_scalar, cc[:, 3:4], cc[:, 8:9], 1.0 / H4, None,
                ALU.mult)
            S.c(vector.tensor_sub, cc[:, 3:4], cc[:, 3:4], cc[:, 2:3])
            S.c(vector.tensor_scalar, cc[:, 3:4], cc[:, 3:4], 1.0, EPS, ALU.mult,
                ALU.add)

        @on("scalar")
        def _(scalar, tb=tb):
            S = mkstep(scalar, "s", "SS", deps=("VS",))
            S(scalar.activation, sb["colscr"][:, 4:5], sb["colscr"][:, 3:4], AF.Sqrt)

        @on("vector")
        def _(vector, tb=tb, h2o=h2o):
            S = mkstep(vector, "v", "VS", deps=("SS",))
            h2 = h2b[:, h2o:h2o + H4]
            cc = sb["colscr"]
            S(vector.reciprocal, cc[:, 5:6], cc[:, 4:5])
            S.c(vector.scalar_tensor_tensor, cc[:, 6:7], cc[:, 1:2], -1.0, cc[:, 5:6],
                ALU.mult, ALU.mult)
            S.c(vector.tensor_scalar, ln2o[:], h2, cc[:, 5:6], cc[:, 6:7],
                ALU.mult, ALU.add)
            S.c(vector.scalar_tensor_tensor, ln2o[:], ln2o[:], 1.0, g2s[:],
                ALU.bypass, ALU.mult)
            S.c(vector.tensor_add, ln2o[:], ln2o[:], be2s[:])
            S.c(vector.reduce_max, cc[:, 7:8], ln2o[:], axis=AX,
                apply_absolute_value=True)
            S.c(vector.tensor_scalar, cc[:, 7:8], cc[:, 7:8], 1e-5, None, ALU.max)
            S.c(vector.reciprocal, cc[:, 2:3], cc[:, 7:8])
            S.c(vector.tensor_scalar, cc[:, 2:3], cc[:, 2:3], 127.0, None, ALU.mult)
            S.c(vector.scalar_tensor_tensor, sb["st_s2i"][:, tb:tb + 1], cc[:, 7:8],
                1.0 / 127.0, sb["v_mw"][:, 1:2], ALU.mult, ALU.mult)
            S.c(vector.tensor_scalar, ln2o[:], ln2o[:], cc[:, 2:3], RC, ALU.mult,
                ALU.add)
            S.c(vector.tensor_scalar, int2s[:], ln2o[:], RC, None, ALU.subtract)
        fmk[("ln2o",)] = cx.mark("VS")
        fmk[("h2b", tb % 2)] = cx.mark("VS")

        for hg in range(8):
            pX = (pX0, pX1)[hg % 2]
            xb = hg % 2

            @on("tensor")
            def _(tensor, tb=tb, hg=hg, pX=pX, xb=xb):
                cx.wait(tensor, "t", ("VS",))
                cx.wait_mark(tensor, "t", fmk.get(("pX", xb)))
                S = mkstep(tensor, "t", "TS", deps=())
                for j in range(4):
                    hh = hg * 4 + j
                    S(tensor.transpose, pX[:, j * 128:(j + 1) * 128],
                      int2s[:, hh * 128:(hh + 1) * 128], sb["c_idb"][:])

            @on("scalar")
            def _(scalar, tb=tb, hg=hg, pX=pX, xb=xb):
                cx.wait(scalar, "s", ("TS",))
                cx.wait_mark(scalar, "s", fmk.get(("i2st", xb)))
                S = mkstep(scalar, "s", "SS", deps=())
                S(scalar.copy, i2st[:, xb * 512:(xb + 1) * 512], pX[:])
            fmk[("pX", xb)] = cx.mark("SS")

            @on("sync")
            def _(sync, tb=tb, hg=hg, xb=xb):
                D = mkdma(sync, "sy", "LD", deps=("SS",))
                for j in range(4):
                    hh = hg * 4 + j
                    D(DT["int2_d"][hh * 128:(hh + 1) * 128, tb * 128:(tb + 1) * 128],
                      i2st[:, xb * 512 + j * 128:xb * 512 + (j + 1) * 128])
            fmk[("i2st", xb)] = cx.mark("LD")

    barrier()
    phf.close()
    s_int1f.close()

    phg = contextlib.ExitStack()
    wq2f = sball("wq2f", [128, 32 * C], BF16, phg)
    i2b = sball("i2b", [128, 32 * 512], BF16, phg)
    s2ib = sball("s2ib", [128, BT], F32, phg)
    gscr = sball("gscr", [128, 1024], F32, phg)
    gst = sball("gst", [128, 1024], F32, phg)
    pG0 = phg.enter_context(nc.psum_tensor("pG0", [128, 512], F32))
    pG1 = phg.enter_context(nc.psum_tensor("pG1", [128, 512], F32))
    pBC = phg.enter_context(nc.psum_tensor("pBC", [128, 2048], F32))

    @on("sync")
    def _(sync):
        D = mkdma(sync, "sy", "LD")
        for hh in range(32):
            r, loc = hh // 4, hh % 4
            D(wq2f[:, hh * C:(hh + 1) * C],
              DT["cc_wq2_out"][r * 512 + loc * 128:r * 512 + (loc + 1) * 128, :])

    emit_bc_from_pack("st_s2i", slice(0, 16), "abc", pBC)

    @on("vector")
    def _(vector):
        S = mkstep(vector, "v", "VS")
        S(vector.tensor_copy, s2ib[:], sb["abc"][:])

    gmk = {}
    for tt in range(4):
        @on("sync")
        def _(sync, tt=tt):
            D = mkdma(sync, "sy", "LD", deps=("TS",))
            for hh in range(32):
                D(i2b[:, hh * 512:(hh + 1) * 512],
                  DT["int2_d"][hh * 128:(hh + 1) * 128, tt * 512:(tt + 1) * 512])

        for dc in range(NCH):
            pG = (pG0, pG1)[dc % 2]
            gb = dc % 2

            @on("tensor")
            def _(tensor, tt=tt, dc=dc, pG=pG, gb=gb):
                cx.wait(tensor, "t", ("LD",))
                cx.wait_mark(tensor, "t", gmk.get(("pG", gb)))
                S = mkstep(tensor, "t", "TS", deps=())
                for hh in range(32):
                    S(tensor.matmul, pG[:],
                      wq2f[:, hh * C + dc * 128:hh * C + (dc + 1) * 128],
                      i2b[:, hh * 512:(hh + 1) * 512],
                      start=(hh == 0), stop=(hh == 31))

            @on("vector")
            def _(vector, tt=tt, dc=dc, pG=pG, gb=gb):
                cx.wait(vector, "v", ("TS",))
                cx.wait_mark(vector, "v", gmk.get(("gscr", gb)))
                S = mkstep(vector, "v", "VS", deps=())
                S(vector.scalar_tensor_tensor, gscr[:, gb * 512:(gb + 1) * 512],
                  pG[:], 1.0, s2ib[:, tt * 512:(tt + 1) * 512], ALU.bypass, ALU.mult)
            gmk[("pG", gb)] = cx.mark("VS")

            @on("scalar")
            def _(scalar, tt=tt, dc=dc, gb=gb):
                cx.wait(scalar, "s", ("VS",))
                cx.wait_mark(scalar, "s", gmk.get(("gst", gb)))
                S = mkstep(scalar, "s", "SS", deps=())
                S(scalar.activation, gst[:, gb * 512:(gb + 1) * 512],
                  gscr[:, gb * 512:(gb + 1) * 512], AF.Silu,
                  bias=sb["v_b2c"][:, dc:dc + 1])
            gmk[("gscr", gb)] = cx.mark("SS")

            @on("sync")
            def _(sync, tt=tt, dc=dc, gb=gb):
                D = mkdma(sync, "sy", "LD", deps=("SS",))
                D(DT["g0_d"][dc * 128:(dc + 1) * 128, tt * 512:(tt + 1) * 512],
                  gst[:, gb * 512:(gb + 1) * 512])
            gmk[("gst", gb)] = cx.mark("LD")

    barrier()
    phg.close()

    s_carry = contextlib.ExitStack()
    rwkv = sball("rwkv", [128, NCH * BT], BF16, s_carry)
    pha = contextlib.ExitStack()
    xn2r = sball("xn2r", [128, NCH * BT], BF16, pha)
    wst = sball("wst", [128, 2 * 1024], F32, pha)
    wbt = sball("wbt", [128, 2 * 1024], BF16, pha)
    kbuf = sball("kbuf", [128, BT], BF16, pha)
    vbuf = sball("vbuf", [128, BT], BF16, pha)
    kvb = sball("kvb", [128, BT], F32, pha)
    ubuf = sball("ubuf", [128, BT], F32, pha)
    pbuf = sball("pbuf", [128, 512], F32, pha)
    sS = sball("sS", [128, BT + 1], F32, pha)
    wkvb = sball("wkvb", [128, BT], F32, pha)
    tvalsb = sball("tvalsb", [128, BT], F32, pha)
    sigr = sball("sigr", [128, NCH * BT], BF16, pha)
    totc = sball("totc", [128, 8], F32, pha)
    cgat = sball("cgat", [128, 16], F32, pha)
    pK = pha.enter_context(nc.psum_tensor("pK", [128, 512], F32))
    pV = pha.enter_context(nc.psum_tensor("pV", [128, 512], F32))
    pR = pha.enter_context(nc.psum_tensor("pR", [128, 512], F32))

    @on("sync")
    def _(sync):
        D = mkdma(sync, "sy", "LD")
        D(xn2r[:].rearrange("p (a n) -> p a n", a=NCH),
          DT["xn2a_d"].rearrange("(a p) n -> p a n", p=128))
        D(tvalsb[:], DT["tvals"])

    pmk = {}

    @on("vector")
    def _(vector):
        S = mkstep(vector, "v", "VS")
        S(vector.memset, sS[:, 0:1], 0.0)

    for dc in range(NCH):
        @on("vector")
        def _(vector, dc=dc):
            S = mkstep(vector, "v", "VS")
            S(vector.memset, wkvb[:], 0.0)

        for wi, wname in enumerate(("WkT", "WvT", "WrT")):
            wb = (dc * 3 + wi) % 2

            @on("sync")
            def _(sync, dc=dc, wname=wname, wb=wb):
                cx.wait_mark(sync, "sy", pmk.get(("wst", wb)))
                D = mkdma(sync, "sy", "LD", deps=())
                D(wst[:, wb * 1024:(wb + 1) * 1024].rearrange(
                      "p (a n) -> p a n", a=8),
                  DT[wname][:, dc * 128:(dc + 1) * 128].rearrange(
                      "(a p) n -> p a n", p=128))

            @on("vector")
            def _(vector, wb=wb):
                cx.wait(vector, "v", ("LD",))
                cx.wait_mark(vector, "v", pmk.get(("wbt", wb)))
                S = mkstep(vector, "v", "VS", deps=())
                S(vector.tensor_copy, wbt[:, wb * 1024:(wb + 1) * 1024],
                  wst[:, wb * 1024:(wb + 1) * 1024])
            pmk[("wst", wb)] = cx.mark("VS")

            for tt in range(4):
                pdst = (pK, pV, pR)[wi]

                @on("tensor")
                def _(tensor, dc=dc, tt=tt, pdst=pdst, wb=wb, wi=wi):
                    cx.wait(tensor, "t", ("VS",))
                    cx.wait_mark(tensor, "t", pmk.get(("pkevac", wi)))
                    S = mkstep(tensor, "t", "TS", deps=())
                    for ca in range(NCH):
                        S(tensor.matmul, pdst[:],
                          wbt[:, wb * 1024 + ca * 128:wb * 1024 + (ca + 1) * 128],
                          xn2r[:, ca * BT + tt * 512:ca * BT + (tt + 1) * 512],
                          start=(ca == 0), stop=(ca == NCH - 1))

                if wi == 0:
                    @on("scalar")
                    def _(scalar, tt=tt):
                        S = mkstep(scalar, "s", "SS", deps=("TS", "VS"))
                        S(scalar.copy, kbuf[:, tt * 512:(tt + 1) * 512], pK[:])
                elif wi == 1:
                    @on("scalar")
                    def _(scalar, tt=tt):
                        S = mkstep(scalar, "s", "SS", deps=("TS", "VS"))
                        S(scalar.copy, vbuf[:, tt * 512:(tt + 1) * 512], pV[:])
                else:
                    @on("scalar")
                    def _(scalar, dc=dc, tt=tt):
                        S = mkstep(scalar, "s", "SS", deps=("TS", "VS"))
                        S(scalar.activation,
                          sigr[:, dc * BT + tt * 512:dc * BT + (tt + 1) * 512],
                          pR[:], AF.Sigmoid)
                pmk[("pkevac", wi)] = cx.mark("SS")
            pmk[("wbt", wb)] = cx.mark("TS")

        for tt in range(4):
            sl = slice(tt * 512, (tt + 1) * 512)

            @on("vector")
            def _(vector, dc=dc, sl=sl):
                S = mkstep(vector, "v", "VS", deps=("SS", "LD"))
                S(vector.tensor_mul, kvb[:, sl], kbuf[:, sl], vbuf[:, sl])
                S(vector.tensor_scalar, pbuf[:], tvalsb[:, sl],
                  sb["v_sp"][:, dc:dc + 1], None, ALU.mult)

            @on("scalar")
            def _(scalar):
                S = mkstep(scalar, "s", "SS", deps=("VS",))
                S(scalar.activation, pbuf[:], pbuf[:], AF.Exp, scale=-1.0)

            @on("vector")
            def _(vector, sl=sl):
                S = mkstep(vector, "v", "VS", deps=("SS",))
                S(vector.tensor_mul, ubuf[:, sl], kvb[:, sl], pbuf[:])

        @on("vector")
        def _(vector, dc=dc):
            S = mkstep(vector, "v", "VS", deps=("SS",))
            S(vector.tensor_tensor_scan, sS[:, 1:BT + 1], wkvb[:], ubuf[:], 0.0,
              ALU.add, ALU.add)
            S.c(vector.tensor_copy, totc[:, dc:dc + 1], sS[:, BT:BT + 1])
            S.c(vector.tensor_scalar, wkvb[:], kvb[:], sb["v_tf"][:, dc:dc + 1], None,
                ALU.mult)
            S.c(vector.scalar_tensor_tensor, wkvb[:], sS[:, 0:BT],
                sb["v_td"][:, dc:dc + 1], wkvb[:], ALU.mult, ALU.add)
            S.c(vector.tensor_mul, rwkv[:, dc * BT:(dc + 1) * BT],
                sigr[:, dc * BT:(dc + 1) * BT], wkvb[:])

    @on("gpsimd")
    def _(g):
        D = mkdma(g, "g", "GD")
        S = mkstep(g, "g", "CC")
        D(DT["cc_carry_in"].rearrange("o (a p) -> (o p) a", p=128), totc[:],
          allow_slow_non_contiguous=True)
        cx.wait(g, "g", ("GD",))
        S(g.collective_compute, "AllGather", ALU.bypass,
          replica_groups=[[0, 1], [2, 3], [4, 5], [6, 7]],
          ins=[DT["cc_carry_in"].opt()], outs=[DT["cc_carry_out"].opt()])
        cx.wait(g, "g", ("CC",))
        D(cgat[:, 0:8],
          DT["cc_carry_out"][0:1, :].rearrange("o (a p) -> (o p) a", p=128),
          allow_slow_non_contiguous=True)
        D(cgat[:, 8:16],
          DT["cc_carry_out"][1:2, :].rearrange("o (a p) -> (o p) a", p=128),
          allow_slow_non_contiguous=True)

    @on("vector")
    def _(vector):
        S = mkstep(vector, "v", "VS")
        S(vector.tensor_scalar, sb["carry_fix"][:], cgat[:, 0:8],
          sb["c_carryw"][:, 0:1], None, ALU.mult)
        S.c(vector.scalar_tensor_tensor, sb["carry_fix"][:], cgat[:, 8:16],
            sb["c_carryw"][:, 1:2], sb["carry_fix"][:], ALU.mult, ALU.add)
        S.c(vector.tensor_mul, sb["carry_fix"][:], sb["carry_fix"][:], sb["v_td"][:])
        for dc in range(NCH):
            sl = slice(dc * BT, (dc + 1) * BT)
            S.c(vector.scalar_tensor_tensor, rwkv[:, sl], sigr[:, sl],
                sb["carry_fix"][:, dc:dc + 1], rwkv[:, sl], ALU.mult, ALU.add)

    barrier()
    pha.close()

    s_x1 = contextlib.ExitStack()
    x1f = sball("x1f", [128, NCH * BT], F32, s_x1)
    phr = contextlib.ExitStack()
    woT = sball("woT", [128, NCH * C], F32, phr)
    woTb = sball("woTb", [128, NCH * C], BF16, phr)
    g0t = sball("g0t", [128, 2 * 512], F32, phr)
    xft = sball("xft", [128, 2 * 512], F32, phr)
    pS0 = phr.enter_context(nc.psum_tensor("pS0", [128, 512], F32))
    pS1 = phr.enter_context(nc.psum_tensor("pS1", [128, 512], F32))

    @on("sync")
    def _(sync):
        D = mkdma(sync, "sy", "LD")
        D(woT[:].rearrange("p (a n) -> p a n", a=NCH),
          DT["WoT"].rearrange("(a p) n -> p a n", p=128))

    @on("vector")
    def _(vector):
        S = mkstep(vector, "v", "VS")
        S(vector.tensor_copy, woTb[:], woT[:])

    for dc in range(NCH):
        for tt in range(4):
            pS = (pS0, pS1)[tt % 2]
            buf = tt % 2

            @on("sync")
            def _(sync, dc=dc, tt=tt, buf=buf):
                D = mkdma(sync, "sy", "LD", deps=("VS",))
                D(g0t[:, buf * 512:(buf + 1) * 512],
                  DT["g0_d"][dc * 128:(dc + 1) * 128, tt * 512:(tt + 1) * 512])
                D(xft[:, buf * 512:(buf + 1) * 512],
                  DT["x_fm"][dc * 128:(dc + 1) * 128, tt * 512:(tt + 1) * 512])

            @on("tensor")
            def _(tensor, dc=dc, tt=tt, pS=pS):
                S = mkstep(tensor, "t", "TS", deps=("VS",))
                for ca in range(NCH):
                    S(tensor.matmul, pS[:],
                      woTb[:, ca * C + dc * 128:ca * C + (dc + 1) * 128],
                      rwkv[:, ca * BT + tt * 512:ca * BT + (tt + 1) * 512],
                      start=(ca == 0), stop=(ca == NCH - 1))

            @on("vector")
            def _(vector, dc=dc, tt=tt, pS=pS, buf=buf):
                S = mkstep(vector, "v", "VS", deps=("TS", "LD"))
                dst = x1f[:, dc * BT + tt * 512:dc * BT + (tt + 1) * 512]
                S(vector.tensor_mul, dst, pS[:], g0t[:, buf * 512:(buf + 1) * 512])
                S.c(vector.tensor_add, dst, dst, xft[:, buf * 512:(buf + 1) * 512])

    @on("sync")
    def _(sync):
        D = mkdma(sync, "sy", "LD")
        D(DT["x1_d"].rearrange("(a p) n -> p a n", p=128),
          x1f[:].rearrange("p (a n) -> p a n", a=NCH))

    barrier()
    phr.close()

    ph2 = contextlib.ExitStack()
    xcb = sball("xcb", [128, 2 * 512], BF16, ph2)
    MTl = sball("MTl", [128, 8 * C], BF16, ph2)
    x1sq = sball("x1sq", [128, 2 * 512], BF16, ph2)
    xn1b = sball("xn1b", [128, NCH * BT], BF16, ph2)
    l1scr = sball("l1scr", [128, 512], F32, ph2)
    g1st = sball("g1st", [128, 512], F32, ph2)
    pM0 = ph2.enter_context(nc.psum_tensor("pM0", [128, 512], F32))
    pM1 = ph2.enter_context(nc.psum_tensor("pM1", [128, 512], F32))
    pGa = ph2.enter_context(nc.psum_tensor("pGa", [128, 512], F32))
    pGb = ph2.enter_context(nc.psum_tensor("pGb", [128, 512], F32))

    @on("sync")
    def _(sync):
        D = mkdma(sync, "sy", "LD")
        D(MTl[:].rearrange("p (a n) -> p a n", a=NCH),
          DT["mt_d"].rearrange("(a p) n -> p a n", p=128))

    for tt in range(4):
        for ca in range(NCH):
            buf = ca % 2
            xc = xcb[:, buf * 512:(buf + 1) * 512]
            sq = x1sq[:, buf * 512:(buf + 1) * 512]

            @on("vector")
            def _(vector, tt=tt, ca=ca, xc=xc):
                S = mkstep(vector, "v", "VS", deps=("TS",))
                S(vector.tensor_copy, xc,
                  x1f[:, ca * BT + tt * 512:ca * BT + (tt + 1) * 512])

            @on("scalar")
            def _(scalar, xc=xc, sq=sq):
                S = mkstep(scalar, "s", "SS", deps=("VS", "TS"))
                S(scalar.activation, sq, xc, AF.Square)

            @on("tensor")
            def _(tensor, ca=ca, xc=xc, sq=sq):
                S = mkstep(tensor, "t", "TS", deps=("VS", "SS"))
                S(tensor.matmul, pM0[:], sb["c_onesb"][:], xc,
                  start=(ca == 0), stop=(ca == NCH - 1))
                S(tensor.matmul, pM1[:], sb["c_onesb"][:], sq,
                  start=(ca == 0), stop=(ca == NCH - 1))

        @on("scalar")
        def _(scalar, tt=tt):
            S = mkstep(scalar, "s", "SS")
            S(scalar.activation, l1scr[:], pM0[:], AF.Square, scale=1.0 / C)

        @on("vector")
        def _(vector, tt=tt):
            S = mkstep(vector, "v", "VS", deps=("TS", "SS"))
            sl = slice(tt * 512, (tt + 1) * 512)
            S(vector.tensor_scalar, sb["abc"][:, sl], pM1[:], 1.0 / C, EPS,
              ALU.mult, ALU.add)
            S.c(vector.tensor_sub, sb["abc"][:, sl], sb["abc"][:, sl], l1scr[:])

        @on("scalar")
        def _(scalar, tt=tt):
            S = mkstep(scalar, "s", "SS")
            sl = slice(tt * 512, (tt + 1) * 512)
            S(scalar.activation, sb["bbc"][:, sl], sb["abc"][:, sl], AF.Sqrt)

        @on("vector")
        def _(vector, tt=tt):
            S = mkstep(vector, "v", "VS", deps=("SS",))
            sl = slice(tt * 512, (tt + 1) * 512)
            S(vector.reciprocal, sb["abc"][:, sl], sb["bbc"][:, sl])
            S.c(vector.scalar_tensor_tensor, sb["bbc"][:, sl], pM0[:], -1.0 / C,
                sb["abc"][:, sl], ALU.mult, ALU.mult)

    @on("vector")
    def _(vector):
        S = mkstep(vector, "v", "VS")
        for ca in range(NCH):
            sl = slice(ca * BT, (ca + 1) * BT)
            S.c(vector.tensor_mul, x1f[:, sl], x1f[:, sl], sb["abc"][:])
            S.c(vector.tensor_add, xn1b[:, sl], x1f[:, sl], sb["bbc"][:])

    @on("gpsimd")
    def _(g):
        D = mkdma(g, "g", "GD")
        S = mkstep(g, "g", "CC")
        D(DT["cc_xn1_in"].rearrange("(a p) n -> p a n", p=128),
          xn1b[:].rearrange("p (a n) -> p a n", a=NCH))
        cx.wait(g, "g", ("GD",))
        S(g.collective_compute, "AllGather", ALU.bypass,
          replica_groups=[[0, 1], [2, 3], [4, 5], [6, 7]],
          ins=[DT["cc_xn1_in"].opt()], outs=[DT["cc_xn1_out"].opt()])

    for dc in range(NCH):
        for tt in range(4):
            pG = (pGa, pGb)[tt % 2]

            @on("tensor")
            def _(tensor, dc=dc, tt=tt, pG=pG):
                S = mkstep(tensor, "t", "TS", deps=("VS", "SS", "LD"))
                for ca in range(NCH):
                    S(tensor.matmul, pG[:],
                      MTl[:, ca * C + dc * 128:ca * C + (dc + 1) * 128],
                      xn1b[:, ca * BT + tt * 512:ca * BT + (tt + 1) * 512],
                      start=(ca == 0), stop=(ca == NCH - 1))

            @on("scalar")
            def _(scalar, dc=dc, tt=tt, pG=pG):
                S = mkstep(scalar, "s", "SS", deps=("TS", "LD"))
                S(scalar.activation, g1st[:], pG[:], AF.Silu,
                  bias=sb["v_bTTc"][:, dc:dc + 1])

            @on("sync")
            def _(sync, dc=dc, tt=tt):
                D = mkdma(sync, "sy", "LD", deps=("SS",))
                D(DT["g1_d"][dc * 128:(dc + 1) * 128, tt * 512:(tt + 1) * 512],
                  g1st[:])

    barrier()
    ph2.close()
    s_x1.close()
    s_carry.close()

    s_att = contextlib.ExitStack()
    Qh = sball("Qh", [128, 4 * T], BF16, s_att)
    Kh = sball("Kh", [128, 4 * T], BF16, s_att)
    Vsb = sball("Vsb", [128, 32 * 520], BF16, s_att)
    ph3 = contextlib.ExitStack()
    wqs = sball("wqs", [128, 2 * 1536], F32, ph3)
    wqb = sball("wqb", [128, NCH * 1536], BF16, ph3)
    cosQs = sball("cosQs", [128, T], BF16, ph3)
    sinQs = sball("sinQs", [128, T], BF16, ph3)
    cosKs = sball("cosKs", [128, T], BF16, ph3)
    sinKs = sball("sinKs", [128, T], BF16, ph3)
    xg = sball("xg", [128, NCH * 512], BF16, ph3)
    qraw = sball("qraw", [128, 512], BF16, ph3)
    t1b = sball("t1b", [128, 512], BF16, ph3)
    t2b = sball("t2b", [128, 512], BF16, ph3)
    pQ0 = ph3.enter_context(nc.psum_tensor("pQ0", [128, 512], F32))
    pQ1 = ph3.enter_context(nc.psum_tensor("pQ1", [128, 512], F32))
    pRot = ph3.enter_context(nc.psum_tensor("pRot", [128, 512], F32))
    pVv = ph3.enter_context(nc.psum_tensor("pVv", [128, 512], F32))

    @on("sync")
    def _(sync):
        D = mkdma(sync, "sy", "LD")
        for nm, dst in (("cosQ", cosQs), ("sinQ", sinQs),
                        ("cosK", cosKs), ("sinK", sinKs)):
            D(dst[:], DT[nm])

    for half in range(4):
        @on("sync")
        def _(sync, half=half):
            D = mkdma(sync, "sy", "LD")
            D(wqs[:].rearrange("p (a n) -> p a n", a=2),
              DT["WqkvT_sl"][half * 256:(half + 1) * 256, :].rearrange(
                  "(a p) n -> p a n", p=128))

        @on("vector")
        def _(vector, half=half):
            S = mkstep(vector, "v", "VS")
            S(vector.tensor_copy, wqb[:, half * 2 * 1536:(half + 1) * 2 * 1536],
              wqs[:])

    @on("vector")
    def _(vector):
        S = mkstep(vector, "v", "VS")
        S(vector.memset,
          Vsb[:].rearrange("p (k h e) -> p k h e", k=32, h=8)[:, :, :, 64:65], 1.0)

    for tt in range(8):
        r, tl = tt // 4, tt % 4

        @on("sync")
        def _(sync, r=r, tl=tl):
            D = mkdma(sync, "sy", "LD", deps=("CC", "TS"))
            for ca in range(NCH):
                D(xg[:, ca * 512:(ca + 1) * 512],
                  DT["cc_xn1_out"][r * C + ca * 128:r * C + (ca + 1) * 128,
                                   tl * 512:(tl + 1) * 512])

        for qc in range(8):
            pQ = (pQ0, pQ1)[qc % 2]
            isq = qc < 4
            hp = qc % 4
            cosT, sinT = (cosQs, sinQs) if isq else (cosKs, sinKs)
            dstT = Qh if isq else Kh

            @on("tensor")
            def _(tensor, tt=tt, qc=qc, pQ=pQ):
                S = mkstep(tensor, "t", "TS", deps=("LD", "SS", "VS"))
                for ca in range(NCH):
                    S(tensor.matmul, pQ[:],
                      wqb[:, ca * 1536 + qc * 128:ca * 1536 + (qc + 1) * 128],
                      xg[:, ca * 512:(ca + 1) * 512],
                      start=(ca == 0), stop=(ca == NCH - 1))

            @on("scalar")
            def _(scalar, pQ=pQ):
                S = mkstep(scalar, "s", "SS", deps=("TS", "VS"))
                S(scalar.copy, qraw[:], pQ[:])

            @on("tensor")
            def _(tensor):
                S = mkstep(tensor, "t", "TS", deps=("SS", "VS"))
                S(tensor.matmul, pRot[:], sb["c_ropeR"][:], qraw[:],
                  start=True, stop=True)

            @on("vector")
            def _(vector, tt=tt, hp=hp, cosT=cosT, sinT=sinT, dstT=dstT):
                S = mkstep(vector, "v", "VS", deps=("SS", "TS"))
                sl = slice(tt * 512, (tt + 1) * 512)
                S(vector.tensor_mul, t1b[:], qraw[:], cosT[:, sl])
                S(vector.tensor_mul, t2b[:], pRot[:], sinT[:, sl])
                S.c(vector.tensor_add,
                    dstT[:, hp * T + tt * 512:hp * T + (tt + 1) * 512], t1b[:],
                    t2b[:])

        for tb in range(4):
            kt = tt * 4 + tb

            @on("tensor")
            def _(tensor, tb=tb):
                S = mkstep(tensor, "t", "TS", deps=("LD", "SS", "VS"))
                for ca in range(NCH):
                    S(tensor.matmul, pVv[:],
                      xg[:, ca * 512 + tb * 128:ca * 512 + (tb + 1) * 128],
                      wqb[:, ca * 1536 + 1024:ca * 1536 + 1536],
                      start=(ca == 0), stop=(ca == NCH - 1))

            @on("scalar")
            def _(scalar, kt=kt):
                S = mkstep(scalar, "s", "SS", deps=("TS", "VS"))
                S(scalar.copy,
                  Vsb[:, kt * 520:(kt + 1) * 520].rearrange(
                      "p (h e) -> p h e", h=8)[:, :, 0:64],
                  pVv[:].rearrange("p (h e) -> p h e", h=8))

    barrier()
    ph3.close()

    ph4 = contextlib.ExitStack()
    pTb = sball("pTb", [128, 2048], BF16, ph4)
    dmasks = sball("dmasks", [128, 2048], BF16, ph4)
    lst = sball("lst", [128, 1024], F32, ph4)
    l0r = sball("l0r", [1, 1024], F32, ph4)
    bcs = sball("bcs", [64, 1024], F32, ph4)
    outn = sball("outn", [64, 1024], BF16, ph4)
    pSmA = ph4.enter_context(nc.psum_tensor("pSmA", [128, 1024], F32))
    pSmB = ph4.enter_context(nc.psum_tensor("pSmB", [128, 1024], F32))
    pAV0 = ph4.enter_context(nc.psum_tensor("pAV0", [128, 512], F32))
    pAV1 = ph4.enter_context(nc.psum_tensor("pAV1", [128, 512], F32))
    pDV = ph4.enter_context(nc.psum_tensor("pDV", [128, 512], F32))

    @on("sync")
    def _(sync):
        D = mkdma(sync, "sy", "LD")
        D(dmasks[:], DT["dmask"])

    amk = {}

    def emit_st(qt, hp, prt, kg):
        sm = kg % 2
        pSm = (pSmA, pSmB)[sm]

        @on("tensor")
        def _(tensor, qt=qt, kg=kg, hp=hp, prt=prt, pSm=pSm, sm=sm):
            cx.wait(tensor, "t", ("LD",))
            cx.wait_mark(tensor, "t", amk.get(("exp", sm)))
            S = mkstep(tensor, "t", "TS", deps=())
            for j in range(2):
                kt = kg * 2 + j
                S(tensor.matmul, pSm[:, j * 512:(j + 1) * 512],
                  Kh[prt, hp * T + kt * 128:hp * T + (kt + 1) * 128],
                  Qh[prt, hp * T + qt * 512:hp * T + (qt + 1) * 512],
                  start=True, stop=True)

    for qt in range(8):
        for h in range(MYH):
            hp, sub = h // 2, h % 2
            prt = slice(64 * sub, 64 * sub + 64)
            hb = h % 2
            pAV = (pAV0, pAV1)[hb]
            ngroups = 2 * qt + 2

            emit_st(qt, hp, prt, 0)
            for kg in range(ngroups):
                sm = kg % 2
                pSm = (pSmA, pSmB)[sm]
                last = (kg == ngroups - 1)

                @on("scalar")
                def _(scalar, pSm=pSm, sm=sm):
                    cx.wait(scalar, "s", ("TS",))
                    cx.wait_mark(scalar, "s", amk.get(("pTb", sm)))
                    S = mkstep(scalar, "s", "SS", deps=())
                    S(scalar.activation, pTb[:, sm * 1024:(sm + 1) * 1024],
                      pSm[:, 0:1024], AF.Exp, bias=sb["c_nb30"][:, 0:1])
                amk[("exp", sm)] = cx.mark("SS")
                amk[("ptb_ready", sm)] = cx.mark("SS")

                if kg >= ngroups - 2:
                    moff = 0 if kg == ngroups - 2 else 1024

                    @on("vector")
                    def _(vector, sm=sm, moff=moff):
                        S = mkstep(vector, "v", "VS", deps=("SS", "LD"))
                        S(vector.tensor_mul, pTb[:, sm * 1024:(sm + 1) * 1024],
                          pTb[:, sm * 1024:(sm + 1) * 1024],
                          dmasks[:, moff:moff + 1024])
                    amk[("ptb_ready", sm)] = cx.mark("VS")

                if not last:
                    emit_st(qt, hp, prt, kg + 1)

                @on("tensor")
                def _(tensor, qt=qt, h=h, kg=kg, last=last, pAV=pAV, sm=sm, hb=hb):
                    cx.wait_mark(tensor, "t", amk.get(("ptb_ready", sm)))
                    if kg == 0:
                        cx.wait_mark(tensor, "t", amk.get(("pAV", hb)))
                    S = mkstep(tensor, "t", "TS", deps=())
                    for j in range(2):
                        kt = kg * 2 + j
                        S(tensor.matmul, pAV[0:65, :],
                          Vsb[:, kt * 520 + h * 65:kt * 520 + (h + 1) * 65],
                          pTb[:, sm * 1024 + j * 512:sm * 1024 + (j + 1) * 512],
                          start=(kg == 0 and j == 0), stop=(last and j == 1))
                amk[("pTb", sm)] = cx.mark("TS")

            @on("vector")
            def _(vector, hb=hb, pAV=pAV):
                cx.wait(vector, "v", ("TS",))
                cx.wait_mark(vector, "v", amk.get(("lst", hb)))
                S = mkstep(vector, "v", "VS", deps=())
                S(vector.tensor_copy, lst[64:65, hb * 512:(hb + 1) * 512],
                  pAV[64:65, :])

            @on("sync")
            def _(sync, hb=hb):
                D = mkdma(sync, "sy", "LD", deps=("VS",))
                D(l0r[0:1, hb * 512:(hb + 1) * 512],
                  lst[64:65, hb * 512:(hb + 1) * 512])
            amk[("lst", hb)] = cx.mark("LD")

            @on("vector")
            def _(vector, hb=hb):
                S = mkstep(vector, "v", "VS", deps=("LD",))
                S(vector.reciprocal, l0r[0:1, hb * 512:(hb + 1) * 512],
                  l0r[0:1, hb * 512:(hb + 1) * 512])

            @on("tensor")
            def _(tensor, hb=hb):
                cx.wait(tensor, "t", ("VS",))
                cx.wait_mark(tensor, "t", amk.get(("pDV",)))
                S = mkstep(tensor, "t", "TS", deps=())
                S(tensor.matmul, pDV[0:64, :], sb["c_onesrowf"][0:1, 0:64],
                  l0r[0:1, hb * 512:(hb + 1) * 512], start=True, stop=True)

            @on("vector")
            def _(vector, hb=hb, pAV=pAV):
                cx.wait(vector, "v", ("TS",))
                cx.wait_mark(vector, "v", amk.get(("outn", hb)))
                S = mkstep(vector, "v", "VS", deps=())
                S(vector.tensor_copy, bcs[:, hb * 512:(hb + 1) * 512], pDV[0:64, :])
                S.c(vector.tensor_mul, outn[:, hb * 512:(hb + 1) * 512],
                    pAV[0:64, :], bcs[:, hb * 512:(hb + 1) * 512])
            amk[("pDV",)] = cx.mark("VS")
            amk[("pAV", hb)] = cx.mark("VS")

            @on("sync")
            def _(sync, qt=qt, h=h, hb=hb):
                D = mkdma(sync, "sy", "LD", deps=("VS",))
                D(DT["cc_out_in"][h * 64:(h + 1) * 64, qt * 512:(qt + 1) * 512],
                  outn[:, hb * 512:(hb + 1) * 512])
            amk[("outn", hb)] = cx.mark("LD")

    @on("gpsimd")
    def _(g):
        S = mkstep(g, "g", "CC")
        S(g.collective_compute, "AllGather", ALU.bypass,
          replica_groups=[[0, 1], [2, 3], [4, 5], [6, 7]],
          ins=[DT["cc_out_in"].opt()], outs=[DT["cc_out_out"].opt()])

    barrier()
    ph4.close()
    s_att.close()

    ph5 = contextlib.ExitStack()
    wo1 = sball("wo1", [128, NCH * C], F32, ph5)
    wo1b = sball("wo1b", [128, NCH * C], BF16, ph5)
    gA = sball("gA", [128, BT], BF16, ph5)
    gB = sball("gB", [128, BT], BF16, ph5)
    omy = sball("omy", [128, NCH * BT], BF16, ph5)
    oscr = sball("oscr", [128, BT], F32, ph5)
    g1t = sball("g1t", [128, 2 * 512], F32, ph5)
    x1t = sball("x1t", [128, 2 * 512], F32, ph5)
    x2st = sball("x2st", [128, 2 * 512], F32, ph5)
    pZ0 = ph5.enter_context(nc.psum_tensor("pZ0", [128, 512], F32))
    pZ1 = ph5.enter_context(nc.psum_tensor("pZ1", [128, 512], F32))

    @on("sync")
    def _(sync):
        D = mkdma(sync, "sy", "LD")
        D(wo1[:].rearrange("p (a n) -> p a n", a=NCH),
          DT["Wo1T"].rearrange("(a p) n -> p a n", p=128))

    @on("vector")
    def _(vector):
        S = mkstep(vector, "v", "VS")
        S(vector.tensor_copy, wo1b[:], wo1[:])

    for r in range(2):
        for cc4 in range(4):
            dc = r * 4 + cc4
            row = r * 512 + cc4 * 128

            @on("sync")
            def _(sync, row=row):
                D = mkdma(sync, "sy", "LD", deps=("VS", "CC"))
                D(gA[:], DT["cc_out_out"][row:row + 128, 0:BT])
                D(gB[:], DT["cc_out_out"][row:row + 128, BT:T])

            @on("vector")
            def _(vector, dc=dc):
                S = mkstep(vector, "v", "VS", deps=("LD",))
                S(vector.tensor_scalar, oscr[:], gA[:], sb["c_selw"][:, 0:1], None,
                  ALU.mult)
                S.c(vector.scalar_tensor_tensor, oscr[:], gB[:], sb["c_selw"][:, 1:2],
                    oscr[:], ALU.mult, ALU.add)
                S.c(vector.tensor_copy, omy[:, dc * BT:(dc + 1) * BT], oscr[:])

    for dc in range(NCH):
        for tt in range(4):
            pZ = (pZ0, pZ1)[tt % 2]
            buf = tt % 2

            @on("sync")
            def _(sync, dc=dc, tt=tt, buf=buf):
                D = mkdma(sync, "sy", "LD", deps=("VS",))
                D(g1t[:, buf * 512:(buf + 1) * 512],
                  DT["g1_d"][dc * 128:(dc + 1) * 128, tt * 512:(tt + 1) * 512])
                D(x1t[:, buf * 512:(buf + 1) * 512],
                  DT["x1_d"][dc * 128:(dc + 1) * 128, tt * 512:(tt + 1) * 512])

            @on("tensor")
            def _(tensor, dc=dc, tt=tt, pZ=pZ):
                S = mkstep(tensor, "t", "TS", deps=("VS",))
                for ca in range(NCH):
                    S(tensor.matmul, pZ[:],
                      wo1b[:, ca * C + dc * 128:ca * C + (dc + 1) * 128],
                      omy[:, ca * BT + tt * 512:ca * BT + (tt + 1) * 512],
                      start=(ca == 0), stop=(ca == NCH - 1))

            @on("vector")
            def _(vector, dc=dc, tt=tt, pZ=pZ, buf=buf):
                S = mkstep(vector, "v", "VS", deps=("TS", "LD"))
                d = x2st[:, buf * 512:(buf + 1) * 512]
                S(vector.tensor_mul, d, pZ[:], g1t[:, buf * 512:(buf + 1) * 512])
                S.c(vector.tensor_add, d, d, x1t[:, buf * 512:(buf + 1) * 512])

            @on("sync")
            def _(sync, dc=dc, tt=tt, buf=buf):
                D = mkdma(sync, "sy", "LD", deps=("VS",))
                D(DT["x2_fm"][dc * 128:(dc + 1) * 128, tt * 512:(tt + 1) * 512],
                  x2st[:, buf * 512:(buf + 1) * 512])

    @on("sync")
    def _(sync):
        cx.wait_all(sync, "sy")

    ph5.close()
    es.close()
    return nc


def _host_prep(inputs):
    f32 = np.float32
    x = np.asarray(inputs["L0_attn_x"], f32)
    WkT = np.ascontiguousarray(np.asarray(inputs["L0_attn_Wk"], f32).T)
    WvT = np.ascontiguousarray(np.asarray(inputs["L0_attn_Wv"], f32).T)
    WrT = np.ascontiguousarray(np.asarray(inputs["L0_attn_Wr"], f32).T)
    WoT = np.ascontiguousarray(np.asarray(inputs["L0_attn_Wo"], f32).T)
    w1T = np.ascontiguousarray(np.asarray(inputs["L0_ffn_w1"], f32).T)
    w2T = np.ascontiguousarray(np.asarray(inputs["L0_ffn_w2"], f32).T)
    Wqkv = np.asarray(inputs["L1_attn_Wqkv"], f32)
    Wo1T = np.ascontiguousarray(np.asarray(inputs["L1_attn_Wo"], f32).T)

    def colpack(v, n):
        return np.ascontiguousarray(np.asarray(v, f32).reshape(n, 128).T)

    def rowbc(v):
        v = np.asarray(v, f32)
        return np.ascontiguousarray(np.broadcast_to(v, (128, v.shape[0])))

    gamA = colpack(inputs["L0_attn_ln_gamma"], 8)
    betA = colpack(inputs["L0_attn_ln_beta"], 8)
    decay = colpack(inputs["L0_attn_time_decay"], 8)
    first = colpack(inputs["L0_attn_time_first"], 8)
    b2c = colpack(inputs["L0_ffn_b2"], 8)
    bTT = np.concatenate([np.asarray(inputs["L1_ffn_bias_real"], f32),
                          np.asarray(inputs["L1_ffn_bias_imag"], f32)])
    bTTc = colpack(bTT, 8)
    g1bc = rowbc(inputs["L0_ffn_ln1_g"])
    b1bc = rowbc(inputs["L0_ffn_ln1_b"])
    b1fbc = rowbc(inputs["L0_ffn_b1"])
    g2bc = rowbc(inputs["L0_ffn_ln2_g"])
    be2bc = rowbc(inputs["L0_ffn_ln2_b"])

    identf = np.eye(128, dtype=f32)
    identb = np.eye(128, dtype=f32).astype(BF)
    onesb = np.ones((128, 128), f32).astype(BF)
    onescolf = np.ones((128, 1), f32)
    onesrowf = np.ones((1, 128), f32)
    R = np.zeros((128, 128), f32)
    for i in range(64):
        R[2 * i, 2 * i + 1] = -1.0
        R[2 * i + 1, 2 * i] = 1.0
    ropeRT = np.ascontiguousarray(R.T).astype(BF)
    freqs = np.exp(-np.log(10000.0) * np.arange(0, DH, 2, dtype=f32) / DH)
    ang = np.arange(T, dtype=f32)[:, None] * freqs[None, :]
    ang = np.repeat(ang, 2, axis=1)
    cos64 = np.cos(ang).T.astype(f32)
    sin64 = np.sin(ang).T.astype(f32)
    cos128 = np.concatenate([cos64, cos64], 0)
    sin128 = np.concatenate([sin64, sin64], 0)
    cosQ = (cos128 / 8.0).astype(BF)
    sinQ = (sin128 / 8.0).astype(BF)
    cosK = cos128.astype(BF)
    sinK = sin128.astype(BF)
    k_ = np.arange(128)[:, None]
    q_ = np.arange(512)[None, :]
    dmask = np.concatenate(
        [((k_ + 128 * j) <= q_).astype(f32) for j in range(4)], axis=1).astype(BF)

    G1R = np.asarray(inputs["L1_ffn_G1_R"], f32)
    G2R = np.asarray(inputs["L1_ffn_G2_R"], f32)
    G1I = np.asarray(inputs["L1_ffn_G1_I"], f32)
    G2I = np.asarray(inputs["L1_ffn_G2_I"], f32)

    def ttrep(G1, G2):
        T1 = np.zeros((512, 4, 512), f32)
        T2 = np.zeros((512, 4, 512), f32)
        d_ = np.arange(512) // 16
        g_ = np.arange(512) % 16
        b_ = np.arange(512) // 16
        e_ = np.arange(512) % 16
        for c in range(4):
            A = G1[0, :, c, :]
            Bm = G2[c, :, 0, :]
            T1[:, c, :] = A.T[d_][:, b_]
            T2[:, c, :] = Bm.T[g_][:, e_]
        return (np.ascontiguousarray(T1.reshape(512, 2048)),
                np.ascontiguousarray(T2.reshape(512, 2048)))

    T1r, T2r = ttrep(G1R, G2R)
    T1i, T2i = ttrep(G1I, G2I)

    base = dict(
        WkT=WkT, WvT=WvT, WrT=WrT, WoT=WoT, Wo1T=Wo1T,
        gamA=gamA, betA=betA, decay=decay, first=first, b2c=b2c, bTTc=bTTc,
        g1bc=g1bc, b1bc=b1bc, b1fbc=b1fbc, g2bc=g2bc, be2bc=be2bc,
        identf=identf, identb=identb, onesb=onesb, ropeR=ropeRT,
        onescolf=onescolf, onesrowf=onesrowf,
        nb30=np.full((128, 1), EXPB, f32),
        cosQ=cosQ, sinQ=sinQ, cosK=cosK, sinK=sinK, dmask=dmask,
        T1r=T1r, T2r=T2r, T1i=T1i, T2i=T2i,
    )

    in_maps = []
    for core in range(8):
        b, p = core // 2, core % 2
        m = dict(base)
        xt = np.ascontiguousarray(x[b, p * BT:(p + 1) * BT, :])
        m["x_tok"] = xt
        m["x_fm"] = np.ascontiguousarray(xt.T)
        m["w1T_sl"] = np.ascontiguousarray(w1T[:, core * 512:(core + 1) * 512])
        m["w2T_sl"] = np.ascontiguousarray(w2T[core * 512:(core + 1) * 512, :])
        hrows = np.concatenate([np.arange(p * 512, (p + 1) * 512),
                                C + np.arange(p * 512, (p + 1) * 512),
                                2 * C + np.arange(p * 512, (p + 1) * 512)])
        m["WqkvT_sl"] = np.ascontiguousarray(Wqkv[hrows, :].T)
        m["tvals"] = np.broadcast_to(
            np.arange(p * BT, (p + 1) * BT, dtype=f32), (128, BT)).copy()
        m["selw"] = np.broadcast_to(
            np.array([1.0 - p, float(p)], f32), (128, 2)).copy()
        m["carryw"] = np.broadcast_to(
            np.array([float(p), 0.0], f32), (128, 2)).copy()
        in_maps.append(m)
    return in_maps


def kernel(**inputs):
    global _last_res
    in_maps = _host_prep(inputs)
    nc = build_program()
    res = run_bass_kernel_spmd(nc, in_maps, core_ids=list(range(8)))
    _last_res = res
    out = np.zeros((B, T, C), np.float32)
    for core in range(8):
        b, p = core // 2, core % 2
        out[b, p * BT:(p + 1) * BT, :] = np.asarray(res.results[core]["x2_fm"]).T
    return out


# version 06
# speedup vs baseline: 1.1255x; 1.0183x over previous
import contextlib
import numpy as np
import ml_dtypes

import concourse.bass as bass
import concourse.mybir as mybir
from concourse.bass_utils import run_bass_kernel_spmd

F32 = mybir.dt.float32
BF16 = mybir.dt.bfloat16
AF = mybir.ActivationFunctionType
ALU = mybir.AluOpType
AX = mybir.AxisListType.X
BF = ml_dtypes.bfloat16

B, T, C = 4, 4096, 1024
BT = 2048
H4 = 4096
DH = 64
NCH = 8
NBLK = 16
MYH = 8
EPS = 1e-5
RC = 12582912.0
EXPB = -30.0
ALLSEM = ("LD", "GD", "TS", "VS", "SS", "CC")


class Ctx:
    def __init__(self, nc, es):
        self.sem = {}
        self.cnt = {}
        self.waited = {}
        for name in ALLSEM:
            self.sem[name] = es.enter_context(nc.semaphore(name))
            self.cnt[name] = 0

    def inc(self, instr, name, n=1):
        instr.then_inc(self.sem[name], n)
        self.cnt[name] += n
        return instr

    def wait(self, eng, ename, names):
        for name in names:
            v = self.cnt[name]
            key = (ename, name)
            if v == 0 or self.waited.get(key, -1) >= v:
                continue
            eng.wait_ge(self.sem[name], v)
            self.waited[key] = v

    def wait_all(self, eng, ename):
        self.wait(eng, ename, ALLSEM)

    def mark(self, name):
        return (name, self.cnt[name])

    def wait_mark(self, eng, ename, mk):
        if mk is None:
            return
        name, v = mk
        key = (ename, name)
        if v == 0 or self.waited.get(key, -1) >= v:
            return
        eng.wait_ge(self.sem[name], v)
        self.waited[key] = v


DEBUG = False


def build_program():
    nc = bass.Bass("TRN2", target_bir_lowering=False, debug=False)
    es = contextlib.ExitStack()
    DT = {}

    def din(name, shape, dt=F32):
        DT[name] = nc.dram_tensor(name, shape, dt, kind="ExternalInput").ap()

    def dout(name, shape, dt=F32):
        DT[name] = nc.dram_tensor(name, shape, dt, kind="ExternalOutput").ap()

    def dint(name, shape, dt=F32):
        DT[name] = nc.dram_tensor(name, shape, dt).ap()

    din("x_fm", [C, BT]); din("x_tok", [BT, C])
    din("WkT", [C, C]); din("WvT", [C, C]); din("WrT", [C, C]); din("WoT", [C, C])
    din("w1T_sl", [C, 512]); din("w2T_sl", [512, C])
    din("WqkvT_sl", [C, 1536]); din("Wo1T", [C, C])
    din("gamA", [128, 8]); din("betA", [128, 8])
    din("decay", [128, 8]); din("first", [128, 8])
    din("b2c", [128, 8]); din("bTTc", [128, 8])
    din("g1bc", [128, C]); din("b1bc", [128, C])
    din("b1fbc", [128, H4]); din("g2bc", [128, H4]); din("be2bc", [128, H4])
    din("tvals", [128, BT])
    din("identf", [128, 128], F32)
    din("identb", [128, 128], BF16)
    din("onesb", [128, 128], BF16)
    din("ropeR", [128, 128], BF16)
    din("onescolf", [128, 1]); din("onesrowf", [1, 128])
    din("selw", [128, 2]); din("carryw", [128, 2]); din("nb30", [128, 1])
    din("cosQ", [128, T], BF16); din("sinQ", [128, T], BF16)
    din("cosK", [128, T], BF16); din("sinK", [128, T], BF16)
    din("dmask", [128, 2048], BF16); din("dmaskP", [128, 4096], BF16)
    din("T1r", [512, 2048]); din("T2r", [512, 2048])
    din("T1i", [512, 2048]); din("T2i", [512, 2048])

    dout("x2_fm", [C, BT])
    if DEBUG:
        dout("dbg_xn2a", [C, BT], BF16)
        dout("dbg_int2", [H4, BT], BF16)
        dout("dbg_g0", [C, BT])
        dout("dbg_x1", [C, BT])
        dout("dbg_xn1", [C, BT], BF16)
        dout("dbg_g1", [C, BT])
        dout("dbg_att", [512, T], BF16)

    dint("cc_wsc_in", [1, 2]); dint("cc_wsc_out", [8, 2])
    dint("cc_wq1_in", [C, 512], BF16); dint("cc_wq1_out", [8 * C, 512], BF16)
    dint("cc_wq2_in", [512, C], BF16); dint("cc_wq2_out", [8 * 512, C], BF16)
    dint("cc_carry_in", [1, C]); dint("cc_carry_out", [2, C])
    dint("cc_xn1_in", [C, BT], BF16); dint("cc_xn1_out", [2 * C, BT], BF16)
    dint("cc_out_in", [512, T], BF16); dint("cc_out_out", [1024, T], BF16)
    dint("xn2a_d", [C, BT], BF16)
    dint("int2_d", [H4, BT], BF16)
    dint("g0_d", [C, BT]); dint("g1_d", [C, BT]); dint("x1_d", [C, BT])
    dint("mt_d", [C, C], BF16)

    cx = Ctx(nc, es)
    sb = {}

    def sball(name, shape, dt=F32, stack=None):
        sb[name] = (stack or es).enter_context(nc.sbuf_tensor(name, shape, dt))
        return sb[name]

    for nm, shp, dt in [
        ("c_idf", [128, 128], F32), ("c_idb", [128, 128], BF16),
        ("c_onesb", [128, 128], BF16), ("c_ropeR", [128, 128], BF16),
        ("c_onescolf", [128, 1], F32), ("c_onesrowf", [1, 128], F32),
        ("c_selw", [128, 2], F32), ("c_carryw", [128, 2], F32),
        ("c_nb30", [128, 1], F32),
        ("v_gamA", [128, 8], F32), ("v_betA", [128, 8], F32),
        ("v_decay", [128, 8], F32), ("v_first", [128, 8], F32),
        ("v_b2c", [128, 8], F32), ("v_bTTc", [128, 8], F32),
        ("v_td", [128, 8], F32), ("v_tf", [128, 8], F32), ("v_sp", [128, 8], F32),
        ("st_A", [128, 16], F32), ("st_B", [128, 16], F32),
        ("st_s1", [128, 16], F32), ("st_s1s", [128, 16], F32),
        ("st_s2i", [128, 16], F32),
        ("v_mw", [128, 2], F32), ("v_swc", [128, 2], F32),
        ("row_a", [1, BT], F32),
        ("carry_fix", [128, 8], F32),
        ("colscr", [128, 64], F32),
        ("st16", [16, 256], F32),
        ("abc", [128, BT], F32), ("bbc", [128, BT], F32),
    ]:
        sball(nm, shp, dt)

    block = es.enter_context(nc.Block())

    def on(engname):
        def deco(fn):
            getattr(block, engname)(fn)
        return deco

    def barrier():
        nc.all_engine_barrier()

    def mkstep(eng, ename, own, deps=ALLSEM):
        state = {"first": True}

        def step(f, *a, **k):
            if state["first"]:
                cx.wait(eng, ename, deps)
                state["first"] = False
            return cx.inc(f(*a, **k), own)

        def chain(f, *a, **k):
            if not state["first"]:
                cx.wait(eng, ename, (own,))
            return step(f, *a, **k)

        step.c = chain
        return step

    def mkdma(eng, ename, own, deps=ALLSEM):
        state = {"first": True}

        def dma(out, in_, **k):
            if state["first"]:
                cx.wait(eng, ename, deps)
                state["first"] = False
            return cx.inc(eng.dma_start(out=out, in_=in_, **k), own, 16)
        return dma

    def emit_bc_from_pack(pack_name, pack_sl, dst_name, ppsum):
        @on("tensor")
        def _(tensor):
            S = mkstep(tensor, "t", "TS")
            S(tensor.transpose, ppsum[0:16, 0:128], sb[pack_name][:, pack_sl],
              sb["c_idf"][:])

        @on("scalar")
        def _(scalar):
            S = mkstep(scalar, "s", "SS")
            S(scalar.copy, sb["st16"][:, 0:128], ppsum[0:16, 0:128])

        @on("sync")
        def _(sync):
            D = mkdma(sync, "sy", "LD")
            for tb in range(NBLK):
                D(sb["row_a"][0:1, tb * 128:(tb + 1) * 128],
                  sb["st16"][tb:tb + 1, 0:128])

        @on("tensor")
        def _(tensor):
            S = mkstep(tensor, "t", "TS")
            for tt in range(4):
                S(tensor.matmul, ppsum[:, tt * 512:(tt + 1) * 512],
                  sb["c_onesrowf"][:], sb["row_a"][0:1, tt * 512:(tt + 1) * 512],
                  start=True, stop=True)

        @on("scalar")
        def _(scalar):
            S = mkstep(scalar, "s", "SS")
            S(scalar.copy, sb[dst_name][:], ppsum[:, 0:2048])

    ph0 = contextlib.ExitStack()
    w1s = sball("w1s", [128, 8 * 512], F32, ph0)
    w2s = sball("w2s", [128, 4 * C], F32, ph0)
    wq1 = sball("wq1", [128, 8 * 512], BF16, ph0)
    wq2 = sball("wq2", [128, 4 * C], BF16, ph0)
    wscr = sball("wscr", [128, 4096], F32, ph0)
    wcol = sball("wcol", [128, 8], F32, ph0)
    wg8 = sball("wg8", [8, 2], F32, ph0)
    t1sb = sball("t1sb", [128, 4 * 2048], F32, ph0)
    t2sb = sball("t2sb", [128, 4 * 2048], F32, ph0)
    wTT = sball("wTT", [128, 4 * 512], F32, ph0)
    MT0 = sball("MT0", [128, 8 * C], BF16, ph0)
    pw = ph0.enter_context(nc.psum_tensor("pw", [128, 512], F32))

    @on("sync")
    def _(sync):
        D = mkdma(sync, "sy", "LD")
        for nm, dst in [("identf", "c_idf"), ("identb", "c_idb"),
                        ("onesb", "c_onesb"), ("ropeR", "c_ropeR"),
                        ("onescolf", "c_onescolf"), ("onesrowf", "c_onesrowf"),
                        ("selw", "c_selw"), ("carryw", "c_carryw"), ("nb30", "c_nb30"),
                        ("gamA", "v_gamA"), ("betA", "v_betA"),
                        ("decay", "v_decay"), ("first", "v_first"),
                        ("b2c", "v_b2c"), ("bTTc", "v_bTTc")]:
            D(sb[dst][:], DT[nm])
        D(w1s[:].rearrange("p (a n) -> p a n", a=8),
          DT["w1T_sl"].rearrange("(a p) n -> p a n", p=128))
        D(w2s[:].rearrange("p (a n) -> p a n", a=4),
          DT["w2T_sl"].rearrange("(a p) n -> p a n", p=128))
        D(t1sb[:].rearrange("p (a n) -> p a n", a=4),
          DT["T1r"].rearrange("(a p) n -> p a n", p=128))
        D(t2sb[:].rearrange("p (a n) -> p a n", a=4),
          DT["T2r"].rearrange("(a p) n -> p a n", p=128))

    @on("scalar")
    def _(scalar):
        S = mkstep(scalar, "s", "SS")
        S(scalar.activation, sb["v_td"][:], sb["v_decay"][:], AF.Sigmoid)
        S(scalar.activation, sb["v_tf"][:], sb["v_first"][:], AF.Sigmoid)

    @on("vector")
    def _(vector):
        S = mkstep(vector, "v", "VS")
        S(vector.tensor_scalar, sb["v_sp"][:], sb["v_td"][:], 1e-38, None, ALU.max)

    @on("scalar")
    def _(scalar):
        S = mkstep(scalar, "s", "SS")
        S(scalar.activation, sb["v_sp"][:], sb["v_sp"][:], AF.Ln)

    @on("vector")
    def _(vector):
        S = mkstep(vector, "v", "VS")
        S(vector.tensor_scalar, sb["v_sp"][:], sb["v_sp"][:], -1.0, None, ALU.mult)

    @on("vector")
    def _(vector):
        S = mkstep(vector, "v", "VS")
        S(vector.reduce_sum, wcol[:, 0:1], w1s[:], axis=AX, apply_absolute_value=True)
        S(vector.reduce_sum, wcol[:, 1:2], w2s[:], axis=AX, apply_absolute_value=True)

    @on("tensor")
    def _(tensor):
        S = mkstep(tensor, "t", "TS")
        S(tensor.matmul, pw[0:1, 0:2], sb["c_onescolf"][:], wcol[:, 0:2],
          start=True, stop=True)

    @on("scalar")
    def _(scalar):
        S = mkstep(scalar, "s", "SS")
        S(scalar.copy, wcol[0:1, 2:4], pw[0:1, 0:2])

    @on("gpsimd")
    def _(g):
        D = mkdma(g, "g", "GD")
        S = mkstep(g, "g", "CC")
        D(DT["cc_wsc_in"], wcol[0:1, 2:4])
        cx.wait(g, "g", ("GD",))
        S(g.collective_compute, "AllGather", ALU.bypass,
          replica_groups=[list(range(8))],
          ins=[DT["cc_wsc_in"].opt()], outs=[DT["cc_wsc_out"].opt()])
        cx.wait(g, "g", ("CC",))
        D(wg8[:], DT["cc_wsc_out"])

    @on("vector")
    def _(vector):
        S = mkstep(vector, "v", "VS")
        S(vector.tensor_copy, wcol[0:8, 4:6], wg8[:, 0:2])

    @on("tensor")
    def _(tensor):
        S = mkstep(tensor, "t", "TS")
        S(tensor.matmul, pw[0:1, 2:4], sb["c_onescolf"][0:8, 0:1], wcol[0:8, 4:6],
          start=True, stop=True)

    @on("vector")
    def _(vector):
        S = mkstep(vector, "v", "VS")
        S(vector.tensor_scalar, wcol[0:1, 6:8], pw[0:1, 2:4],
          1.0 / (C * H4), 1e-5, ALU.mult, ALU.max)

    @on("tensor")
    def _(tensor):
        S = mkstep(tensor, "t", "TS")
        S(tensor.matmul, pw[:, 4:6], sb["c_onesrowf"][:], wcol[0:1, 6:8],
          start=True, stop=True)

    @on("scalar")
    def _(scalar):
        S = mkstep(scalar, "s", "SS")
        S(scalar.copy, sb["v_mw"][:], pw[:, 4:6])

    @on("vector")
    def _(vector):
        S = mkstep(vector, "v", "VS")
        S(vector.reciprocal, sb["v_swc"][:], sb["v_mw"][:])
        for wsrc, dst, sc, n in ((w1s, wq1, 0, 8 * 512), (w2s, wq2, 1, 4 * C)):
            S.c(vector.tensor_scalar, wscr[:, 0:n], wsrc[:], sb["v_swc"][:, sc:sc + 1],
                RC, ALU.mult, ALU.add)
            S.c(vector.tensor_scalar, wscr[:, 0:n], wscr[:, 0:n], RC, None,
                ALU.subtract)
            S.c(vector.tensor_scalar, dst[:], wscr[:, 0:n], 1.0, -1.0, ALU.min,
                ALU.max)

    @on("gpsimd")
    def _(g):
        D = mkdma(g, "g", "GD")
        S = mkstep(g, "g", "CC")
        D(DT["cc_wq1_in"].rearrange("(a p) n -> p a n", p=128),
          wq1[:].rearrange("p (a n) -> p a n", a=8))
        D(DT["cc_wq2_in"].rearrange("(a p) n -> p a n", p=128),
          wq2[:].rearrange("p (a n) -> p a n", a=4))
        cx.wait(g, "g", ("GD",))
        S(g.collective_compute, "AllGather", ALU.bypass,
          replica_groups=[list(range(8))],
          ins=[DT["cc_wq1_in"].opt()], outs=[DT["cc_wq1_out"].opt()])
        S(g.collective_compute, "AllGather", ALU.bypass,
          replica_groups=[list(range(8))],
          ins=[DT["cc_wq2_in"].opt()], outs=[DT["cc_wq2_out"].opt()])

    def emit_tt_build(which):
        @on("vector")
        def _(vector):
            S = mkstep(vector, "v", "VS")
            for pc in range(4):
                base = pc * 2048
                S.c(vector.tensor_mul, wTT[:, pc * 512:(pc + 1) * 512],
                  t1sb[:, base:base + 512], t2sb[:, base:base + 512])
                for c in range(1, 4):
                    s0 = base + c * 512
                    S.c(vector.tensor_mul, wscr[:, 0:512], t1sb[:, s0:s0 + 512],
                      t2sb[:, s0:s0 + 512])
                    S.c(vector.tensor_add, wTT[:, pc * 512:(pc + 1) * 512],
                      wTT[:, pc * 512:(pc + 1) * 512], wscr[:, 0:512])
            for pc in range(4):
                w = wTT[:, pc * 512:(pc + 1) * 512]
                if which == 0:
                    S.c(vector.tensor_copy, MT0[:, pc * C + 0:pc * C + 512], w)
                    S.c(vector.tensor_copy,
                      MT0[:, (4 + pc) * C + 512:(4 + pc) * C + 1024], w)
                else:
                    S.c(vector.tensor_copy, MT0[:, pc * C + 512:pc * C + 1024], w)
                    S.c(vector.tensor_scalar, wscr[:, 0:512], w, -1.0, None, ALU.mult)
                    S.c(vector.tensor_copy,
                      MT0[:, (4 + pc) * C + 0:(4 + pc) * C + 512], wscr[:, 0:512])

    emit_tt_build(0)

    @on("sync")
    def _(sync):
        D = mkdma(sync, "sy", "LD")
        D(t1sb[:].rearrange("p (a n) -> p a n", a=4),
          DT["T1i"].rearrange("(a p) n -> p a n", p=128))
        D(t2sb[:].rearrange("p (a n) -> p a n", a=4),
          DT["T2i"].rearrange("(a p) n -> p a n", p=128))

    emit_tt_build(1)

    @on("sync")
    def _(sync):
        D = mkdma(sync, "sy", "LD")
        D(DT["mt_d"].rearrange("(a p) n -> p a n", p=128),
          MT0[:].rearrange("p (a n) -> p a n", a=NCH))

    barrier()
    ph0.close()

    s_int1f = contextlib.ExitStack()
    int1f = sball("int1f", [128, NCH * BT], BF16, s_int1f)
    ph1 = contextlib.ExitStack()
    xtb = sball("xtb", [128, 2 * C], F32, ph1)
    xfb = sball("xfb", [128, 2 * BT], F32, ph1)
    g1bcs = sball("g1bcs", [128, C], F32, ph1)
    b1bcs = sball("b1bcs", [128, C], F32, ph1)
    scr1 = sball("scr1", [128, C], F32, ph1)
    scr2 = sball("scr2", [128, C], F32, ph1)
    int1t = sball("int1t", [128, NBLK * C], BF16, ph1)
    xn2a = sball("xn2a", [128, NCH * BT], BF16, ph1)
    ph1p = contextlib.ExitStack()
    pstA = ph1p.enter_context(nc.psum_tensor("pstA", [128, 2048], F32))
    pstB = ph1p.enter_context(nc.psum_tensor("pstB", [128, 2048], F32))

    @on("sync")
    def _(sync):
        D = mkdma(sync, "sy", "LD")
        D(g1bcs[:], DT["g1bc"])
        D(b1bcs[:], DT["b1bc"])

    h1mk = {}
    for tb in range(NBLK):
        buf = tb % 2

        @on("sync")
        def _(sync, tb=tb, buf=buf):
            cx.wait_mark(sync, "sy", h1mk.get(("xtb_s", buf)))
            cx.wait_mark(sync, "sy", h1mk.get(("xtb_v", buf)))
            D = mkdma(sync, "sy", "LD", deps=())
            D(xtb[:, buf * C:(buf + 1) * C],
              DT["x_tok"][tb * 128:(tb + 1) * 128, :])

        @on("scalar")
        def _(scalar, tb=tb, buf=buf):
            S = mkstep(scalar, "s", "SS", deps=("LD", "SS"))
            S(scalar.activation, scr1[:], xtb[:, buf * C:(buf + 1) * C], AF.Square,
              accum_out=sb["colscr"][:, tb:tb + 1])
        h1mk[("xtb_s", buf)] = cx.mark("SS")

        @on("vector")
        def _(vector, tb=tb, buf=buf):
            S = mkstep(vector, "v", "VS", deps=("LD",))
            S(vector.reduce_sum, sb["colscr"][:, 16 + tb:17 + tb],
              xtb[:, buf * C:(buf + 1) * C], axis=AX)
        h1mk[("xtb_v", buf)] = cx.mark("VS")

    @on("vector")
    def _(vector):
        S = mkstep(vector, "v", "VS")
        mu = sb["colscr"][:, 32:48]
        var = sb["colscr"][:, 48:64]
        S(vector.tensor_scalar, mu, sb["colscr"][:, 16:32], 1.0 / C, None, ALU.mult)
        S.c(vector.tensor_mul, var, mu, mu)
        S.c(vector.tensor_scalar, sb["st_A"][:], sb["colscr"][:, 0:16], 1.0 / C, None,
            ALU.mult)
        S.c(vector.tensor_sub, var, sb["st_A"][:], var)
        S.c(vector.tensor_scalar, var, var, 1.0 + EPS, EPS * EPS, ALU.mult, ALU.add)

    @on("scalar")
    def _(scalar):
        S = mkstep(scalar, "s", "SS")
        S(scalar.activation, sb["st_B"][:], sb["colscr"][:, 48:64], AF.Sqrt)

    @on("vector")
    def _(vector):
        S = mkstep(vector, "v", "VS")
        S(vector.reciprocal, sb["st_A"][:], sb["st_B"][:])
        S.c(vector.scalar_tensor_tensor, sb["st_B"][:], sb["colscr"][:, 32:48], -1.0,
            sb["st_A"][:], ALU.mult, ALU.mult)

    for tb in range(NBLK):
        buf = tb % 2

        @on("sync")
        def _(sync, tb=tb, buf=buf):
            cx.wait_mark(sync, "sy", h1mk.get(("xtb2", buf)))
            D = mkdma(sync, "sy", "LD", deps=("SS", "VS"))
            D(xtb[:, buf * C:(buf + 1) * C],
              DT["x_tok"][tb * 128:(tb + 1) * 128, :])

        @on("vector")
        def _(vector, tb=tb, buf=buf):
            S = mkstep(vector, "v", "VS", deps=("LD",))
            xb = xtb[:, buf * C:(buf + 1) * C]
            S(vector.tensor_scalar, scr1[:], xb, sb["st_A"][:, tb:tb + 1],
              sb["st_B"][:, tb:tb + 1], ALU.mult, ALU.add)
            S.c(vector.scalar_tensor_tensor, scr2[:], scr1[:], 1.0, g1bcs[:],
                ALU.bypass, ALU.mult)
            S.c(vector.tensor_add, scr2[:], scr2[:], b1bcs[:])
            S.c(vector.reduce_max, sb["colscr"][:, tb:tb + 1], scr2[:], axis=AX,
                apply_absolute_value=True)
            S.c(vector.tensor_scalar, sb["colscr"][:, 16 + tb:17 + tb],
                sb["colscr"][:, tb:tb + 1], 1e-5, None, ALU.max)
            S.c(vector.reciprocal, sb["st_s1"][:, tb:tb + 1],
                sb["colscr"][:, 16 + tb:17 + tb])
            S.c(vector.tensor_scalar, sb["st_s1"][:, tb:tb + 1],
                sb["st_s1"][:, tb:tb + 1], 127.0, None, ALU.mult)
            S.c(vector.scalar_tensor_tensor, sb["st_s1s"][:, tb:tb + 1],
                sb["colscr"][:, 16 + tb:17 + tb], 1.0 / 127.0, sb["v_mw"][:, 0:1],
                ALU.mult, ALU.mult)
            S.c(vector.tensor_scalar, scr2[:], scr2[:], sb["st_s1"][:, tb:tb + 1], RC,
                ALU.mult, ALU.add)
            S.c(vector.tensor_scalar, int1t[:, tb * C:(tb + 1) * C], scr2[:], RC,
                None, ALU.subtract)
        h1mk[("xtb2", buf)] = cx.mark("VS")

    emit_bc_from_pack("st_A", slice(0, 16), "abc", pstA)
    emit_bc_from_pack("st_B", slice(0, 16), "bbc", pstB)

    for ca in range(NCH):
        buf = ca % 2

        @on("sync")
        def _(sync, ca=ca, buf=buf):
            cx.wait_mark(sync, "sy", h1mk.get(("xfb", buf)))
            D = mkdma(sync, "sy", "LD", deps=("SS",))
            D(xfb[:, buf * BT:(buf + 1) * BT],
              DT["x_fm"][ca * 128:(ca + 1) * 128, :])

        @on("vector")
        def _(vector, ca=ca, buf=buf):
            S = mkstep(vector, "v", "VS", deps=("LD",))
            xc = xfb[:, buf * BT:(buf + 1) * BT]
            dst = xn2a[:, ca * BT:(ca + 1) * BT]
            S(vector.tensor_mul, xc, xc, sb["abc"][:])
            S.c(vector.tensor_add, xc, xc, sb["bbc"][:])
            S.c(vector.tensor_scalar, dst, xc, sb["v_gamA"][:, ca:ca + 1],
                sb["v_betA"][:, ca:ca + 1], ALU.mult, ALU.add)
        h1mk[("xfb", buf)] = cx.mark("VS")

    @on("sync")
    def _(sync):
        D = mkdma(sync, "sy", "LD")
        D(DT["xn2a_d"].rearrange("(a p) n -> p a n", p=128),
          xn2a[:].rearrange("p (a n) -> p a n", a=NCH))

    barrier()
    ph1p.close()
    pT0 = ph1.enter_context(nc.psum_tensor("pT0", [128, 512], BF16))
    pT1 = ph1.enter_context(nc.psum_tensor("pT1", [128, 512], BF16))

    for ca in range(NCH):
        for tg in range(4):
            pT = (pT0, pT1)[tg % 2]

            @on("tensor")
            def _(tensor, ca=ca, tg=tg, pT=pT):
                cx.wait(tensor, "t", ("VS",))
                cx.wait_mark(tensor, "t", h1mk.get(("pT", tg % 2)))
                S = mkstep(tensor, "t", "TS", deps=())
                for j in range(4):
                    tb = tg * 4 + j
                    S(tensor.transpose, pT[:, j * 128:(j + 1) * 128],
                      int1t[:, tb * C + ca * 128:tb * C + (ca + 1) * 128],
                      sb["c_idb"][:])

            @on("scalar")
            def _(scalar, ca=ca, tg=tg, pT=pT):
                S = mkstep(scalar, "s", "SS", deps=("TS",))
                S(scalar.copy,
                  int1f[:, ca * BT + tg * 512:ca * BT + (tg + 1) * 512], pT[:])
            h1mk[("pT", tg % 2)] = cx.mark("SS")

    barrier()
    ph1.close()

    phf = contextlib.ExitStack()
    wq1f = sball("wq1f", [128, 64 * 512], BF16, phf)
    b1fs = sball("b1fs", [128, H4], BF16, phf)
    g2s = sball("g2s", [128, H4], BF16, phf)
    be2s = sball("be2s", [128, H4], BF16, phf)
    h2b = sball("h2b", [128, 2 * H4], BF16, phf)
    ln2o = sball("ln2o", [128, H4], F32, phf)
    sqs = sball("sqs", [128, 512], F32, phf)
    fscr = sball("fscr", [128, 1024], F32, phf)
    int2s = sball("int2s", [128, H4], BF16, phf)
    i2st = sball("i2st", [128, 1024], BF16, phf)
    pF0 = phf.enter_context(nc.psum_tensor("pF0", [128, 512], F32))
    pF1 = phf.enter_context(nc.psum_tensor("pF1", [128, 512], F32))
    pX0 = phf.enter_context(nc.psum_tensor("pX0", [128, 512], BF16))
    pX1 = phf.enter_context(nc.psum_tensor("pX1", [128, 512], BF16))

    @on("sync")
    def _(sync):
        D = mkdma(sync, "sy", "LD")
        for r in range(8):
            for ca in range(NCH):
                D(wq1f[:, (ca * 8 + r) * 512:(ca * 8 + r + 1) * 512],
                  DT["cc_wq1_out"][r * C + ca * 128:r * C + (ca + 1) * 128, :])
    @on("gpsimd")
    def _(g):
        D = mkdma(g, "g", "GD")
        D(b1fs[:], DT["b1fbc"])
        D(g2s[:], DT["g2bc"])
        D(be2s[:], DT["be2bc"])

    fmk = {}
    for tb in range(NBLK):
        h2o = (tb % 2) * H4

        for hc in range(8):
            pF = (pF0, pF1)[hc % 2]
            fb = hc % 2

            @on("tensor")
            def _(tensor, tb=tb, hc=hc, pF=pF, fb=fb):
                cx.wait(tensor, "t", ("LD", "GD"))
                cx.wait_mark(tensor, "t", fmk.get(("pF", fb)))
                S = mkstep(tensor, "t", "TS", deps=())
                for ca in range(NCH):
                    S(tensor.matmul, pF[:],
                      int1f[:, ca * BT + tb * 128:ca * BT + (tb + 1) * 128],
                      wq1f[:, (ca * 8 + hc) * 512:(ca * 8 + hc + 1) * 512],
                      start=(ca == 0), stop=(ca == NCH - 1))

            @on("vector")
            def _(vector, tb=tb, hc=hc, pF=pF, fb=fb):
                cx.wait(vector, "v", ("TS", "GD"))
                cx.wait_mark(vector, "v", fmk.get(("fscr", fb)))
                S = mkstep(vector, "v", "VS", deps=())
                S(vector.scalar_tensor_tensor, fscr[:, fb * 512:(fb + 1) * 512],
                  pF[:], sb["st_s1s"][:, tb:tb + 1],
                  b1fs[:, hc * 512:(hc + 1) * 512], ALU.mult, ALU.add)
            fmk[("pF", fb)] = cx.mark("VS")

            @on("scalar")
            def _(scalar, tb=tb, hc=hc, fb=fb, h2o=h2o):
                cx.wait(scalar, "s", ("VS",))
                cx.wait_mark(scalar, "s", fmk.get(("h2b", tb % 2)))
                S = mkstep(scalar, "s", "SS", deps=())
                S(scalar.activation, h2b[:, h2o + hc * 512:h2o + (hc + 1) * 512],
                  fscr[:, fb * 512:(fb + 1) * 512], AF.Gelu)
            fmk[("fscr", fb)] = cx.mark("SS")

        @on("vector")
        def _(vector, tb=tb, h2o=h2o):
            cx.wait(vector, "v", ("SS",))
            cx.wait_mark(vector, "v", fmk.get(("ln2o",)))
            S = mkstep(vector, "v", "VS", deps=())
            h2 = h2b[:, h2o:h2o + H4]
            cc = sb["colscr"]
            S(vector.reduce_sum, cc[:, 0:1], h2, axis=AX)
            S.c(vector.tensor_mul, ln2o[:], h2, h2)
            S.c(vector.reduce_sum, cc[:, 8:9], ln2o[:], axis=AX)
            S.c(vector.tensor_scalar, cc[:, 1:2], cc[:, 0:1], 1.0 / H4, None,
                ALU.mult)
            S.c(vector.tensor_mul, cc[:, 2:3], cc[:, 1:2], cc[:, 1:2])
            S.c(vector.tensor_scalar, cc[:, 3:4], cc[:, 8:9], 1.0 / H4, None,
                ALU.mult)
            S.c(vector.tensor_sub, cc[:, 3:4], cc[:, 3:4], cc[:, 2:3])
            S.c(vector.tensor_scalar, cc[:, 3:4], cc[:, 3:4], 1.0, EPS, ALU.mult,
                ALU.add)

        @on("scalar")
        def _(scalar, tb=tb):
            S = mkstep(scalar, "s", "SS", deps=("VS",))
            S(scalar.activation, sb["colscr"][:, 4:5], sb["colscr"][:, 3:4], AF.Sqrt)

        @on("vector")
        def _(vector, tb=tb, h2o=h2o):
            S = mkstep(vector, "v", "VS", deps=("SS",))
            h2 = h2b[:, h2o:h2o + H4]
            cc = sb["colscr"]
            S(vector.reciprocal, cc[:, 5:6], cc[:, 4:5])
            S.c(vector.scalar_tensor_tensor, cc[:, 6:7], cc[:, 1:2], -1.0, cc[:, 5:6],
                ALU.mult, ALU.mult)
            S.c(vector.tensor_scalar, ln2o[:], h2, cc[:, 5:6], cc[:, 6:7],
                ALU.mult, ALU.add)
            S.c(vector.scalar_tensor_tensor, ln2o[:], ln2o[:], 1.0, g2s[:],
                ALU.bypass, ALU.mult)
            S.c(vector.tensor_add, ln2o[:], ln2o[:], be2s[:])
            S.c(vector.reduce_max, cc[:, 7:8], ln2o[:], axis=AX,
                apply_absolute_value=True)
            S.c(vector.tensor_scalar, cc[:, 7:8], cc[:, 7:8], 1e-5, None, ALU.max)
            S.c(vector.reciprocal, cc[:, 2:3], cc[:, 7:8])
            S.c(vector.tensor_scalar, cc[:, 2:3], cc[:, 2:3], 127.0, None, ALU.mult)
            S.c(vector.scalar_tensor_tensor, sb["st_s2i"][:, tb:tb + 1], cc[:, 7:8],
                1.0 / 127.0, sb["v_mw"][:, 1:2], ALU.mult, ALU.mult)
            S.c(vector.tensor_scalar, ln2o[:], ln2o[:], cc[:, 2:3], RC, ALU.mult,
                ALU.add)
            S.c(vector.tensor_scalar, int2s[:], ln2o[:], RC, None, ALU.subtract)
        fmk[("ln2o",)] = cx.mark("VS")
        fmk[("h2b", tb % 2)] = cx.mark("VS")

        for hg in range(8):
            pX = (pX0, pX1)[hg % 2]
            xb = hg % 2

            @on("tensor")
            def _(tensor, tb=tb, hg=hg, pX=pX, xb=xb):
                cx.wait(tensor, "t", ("VS",))
                cx.wait_mark(tensor, "t", fmk.get(("pX", xb)))
                S = mkstep(tensor, "t", "TS", deps=())
                for j in range(4):
                    hh = hg * 4 + j
                    S(tensor.transpose, pX[:, j * 128:(j + 1) * 128],
                      int2s[:, hh * 128:(hh + 1) * 128], sb["c_idb"][:])

            @on("scalar")
            def _(scalar, tb=tb, hg=hg, pX=pX, xb=xb):
                cx.wait(scalar, "s", ("TS",))
                cx.wait_mark(scalar, "s", fmk.get(("i2st", xb)))
                S = mkstep(scalar, "s", "SS", deps=())
                S(scalar.copy, i2st[:, xb * 512:(xb + 1) * 512], pX[:])
            fmk[("pX", xb)] = cx.mark("SS")

            @on("sync")
            def _(sync, tb=tb, hg=hg, xb=xb):
                D = mkdma(sync, "sy", "LD", deps=("SS",))
                for j in range(4):
                    hh = hg * 4 + j
                    D(DT["int2_d"][hh * 128:(hh + 1) * 128, tb * 128:(tb + 1) * 128],
                      i2st[:, xb * 512 + j * 128:xb * 512 + (j + 1) * 128])
            fmk[("i2st", xb)] = cx.mark("LD")

    barrier()
    phf.close()
    s_int1f.close()

    phg = contextlib.ExitStack()
    wq2f = sball("wq2f", [128, 32 * C], BF16, phg)
    i2b = sball("i2b", [128, 32 * 512], BF16, phg)
    s2ib = sball("s2ib", [128, BT], F32, phg)
    gscr = sball("gscr", [128, 1024], F32, phg)
    gst = sball("gst", [128, 1024], F32, phg)
    pG0 = phg.enter_context(nc.psum_tensor("pG0", [128, 512], F32))
    pG1 = phg.enter_context(nc.psum_tensor("pG1", [128, 512], F32))
    pBC = phg.enter_context(nc.psum_tensor("pBC", [128, 2048], F32))

    @on("sync")
    def _(sync):
        D = mkdma(sync, "sy", "LD")
        for hh in range(32):
            r, loc = hh // 4, hh % 4
            D(wq2f[:, hh * C:(hh + 1) * C],
              DT["cc_wq2_out"][r * 512 + loc * 128:r * 512 + (loc + 1) * 128, :])

    emit_bc_from_pack("st_s2i", slice(0, 16), "abc", pBC)

    @on("vector")
    def _(vector):
        S = mkstep(vector, "v", "VS")
        S(vector.tensor_copy, s2ib[:], sb["abc"][:])

    gmk = {}
    for tt in range(4):
        @on("sync")
        def _(sync, tt=tt):
            D = mkdma(sync, "sy", "LD", deps=("TS",))
            for hh in range(32):
                D(i2b[:, hh * 512:(hh + 1) * 512],
                  DT["int2_d"][hh * 128:(hh + 1) * 128, tt * 512:(tt + 1) * 512])

        for dc in range(NCH):
            pG = (pG0, pG1)[dc % 2]
            gb = dc % 2

            @on("tensor")
            def _(tensor, tt=tt, dc=dc, pG=pG, gb=gb):
                cx.wait(tensor, "t", ("LD",))
                cx.wait_mark(tensor, "t", gmk.get(("pG", gb)))
                S = mkstep(tensor, "t", "TS", deps=())
                for hh in range(32):
                    S(tensor.matmul, pG[:],
                      wq2f[:, hh * C + dc * 128:hh * C + (dc + 1) * 128],
                      i2b[:, hh * 512:(hh + 1) * 512],
                      start=(hh == 0), stop=(hh == 31))

            @on("vector")
            def _(vector, tt=tt, dc=dc, pG=pG, gb=gb):
                cx.wait(vector, "v", ("TS",))
                cx.wait_mark(vector, "v", gmk.get(("gscr", gb)))
                S = mkstep(vector, "v", "VS", deps=())
                S(vector.scalar_tensor_tensor, gscr[:, gb * 512:(gb + 1) * 512],
                  pG[:], 1.0, s2ib[:, tt * 512:(tt + 1) * 512], ALU.bypass, ALU.mult)
            gmk[("pG", gb)] = cx.mark("VS")

            @on("scalar")
            def _(scalar, tt=tt, dc=dc, gb=gb):
                cx.wait(scalar, "s", ("VS",))
                cx.wait_mark(scalar, "s", gmk.get(("gst", gb)))
                S = mkstep(scalar, "s", "SS", deps=())
                S(scalar.activation, gst[:, gb * 512:(gb + 1) * 512],
                  gscr[:, gb * 512:(gb + 1) * 512], AF.Silu,
                  bias=sb["v_b2c"][:, dc:dc + 1])
            gmk[("gscr", gb)] = cx.mark("SS")

            @on("sync")
            def _(sync, tt=tt, dc=dc, gb=gb):
                D = mkdma(sync, "sy", "LD", deps=("SS",))
                D(DT["g0_d"][dc * 128:(dc + 1) * 128, tt * 512:(tt + 1) * 512],
                  gst[:, gb * 512:(gb + 1) * 512])
            gmk[("gst", gb)] = cx.mark("LD")

    barrier()
    phg.close()

    s_carry = contextlib.ExitStack()
    rwkv = sball("rwkv", [128, NCH * BT], BF16, s_carry)
    pha = contextlib.ExitStack()
    xn2r = sball("xn2r", [128, NCH * BT], BF16, pha)
    wst = sball("wst", [128, 2 * 1024], F32, pha)
    wbt = sball("wbt", [128, 2 * 1024], BF16, pha)
    kbuf = sball("kbuf", [128, BT], BF16, pha)
    vbuf = sball("vbuf", [128, BT], BF16, pha)
    kvb = sball("kvb", [128, BT], F32, pha)
    ubuf = sball("ubuf", [128, BT], F32, pha)
    pbuf = sball("pbuf", [128, 512], F32, pha)
    sS = sball("sS", [128, BT + 1], F32, pha)
    wkvb = sball("wkvb", [128, BT], F32, pha)
    tvalsb = sball("tvalsb", [128, BT], F32, pha)
    sigr = sball("sigr", [128, NCH * BT], BF16, pha)
    totc = sball("totc", [128, 8], F32, pha)
    cgat = sball("cgat", [128, 16], F32, pha)
    pK = pha.enter_context(nc.psum_tensor("pK", [128, 512], F32))
    pV = pha.enter_context(nc.psum_tensor("pV", [128, 512], F32))
    pR = pha.enter_context(nc.psum_tensor("pR", [128, 512], F32))

    @on("sync")
    def _(sync):
        D = mkdma(sync, "sy", "LD")
        D(xn2r[:].rearrange("p (a n) -> p a n", a=NCH),
          DT["xn2a_d"].rearrange("(a p) n -> p a n", p=128))
        D(tvalsb[:], DT["tvals"])

    pmk = {}

    @on("vector")
    def _(vector):
        S = mkstep(vector, "v", "VS")
        S(vector.memset, sS[:, 0:1], 0.0)

    for dc in range(NCH):
        @on("vector")
        def _(vector, dc=dc):
            S = mkstep(vector, "v", "VS")
            S(vector.memset, wkvb[:], 0.0)

        for wi, wname in enumerate(("WkT", "WvT", "WrT")):
            wb = (dc * 3 + wi) % 2

            @on("sync")
            def _(sync, dc=dc, wname=wname, wb=wb):
                cx.wait_mark(sync, "sy", pmk.get(("wst", wb)))
                D = mkdma(sync, "sy", "LD", deps=())
                D(wst[:, wb * 1024:(wb + 1) * 1024].rearrange(
                      "p (a n) -> p a n", a=8),
                  DT[wname][:, dc * 128:(dc + 1) * 128].rearrange(
                      "(a p) n -> p a n", p=128))

            @on("vector")
            def _(vector, wb=wb):
                cx.wait(vector, "v", ("LD",))
                cx.wait_mark(vector, "v", pmk.get(("wbt", wb)))
                S = mkstep(vector, "v", "VS", deps=())
                S(vector.tensor_copy, wbt[:, wb * 1024:(wb + 1) * 1024],
                  wst[:, wb * 1024:(wb + 1) * 1024])
            pmk[("wst", wb)] = cx.mark("VS")

            for tt in range(4):
                pdst = (pK, pV, pR)[wi]

                @on("tensor")
                def _(tensor, dc=dc, tt=tt, pdst=pdst, wb=wb, wi=wi):
                    cx.wait(tensor, "t", ("VS",))
                    cx.wait_mark(tensor, "t", pmk.get(("pkevac", wi)))
                    S = mkstep(tensor, "t", "TS", deps=())
                    for ca in range(NCH):
                        S(tensor.matmul, pdst[:],
                          wbt[:, wb * 1024 + ca * 128:wb * 1024 + (ca + 1) * 128],
                          xn2r[:, ca * BT + tt * 512:ca * BT + (tt + 1) * 512],
                          start=(ca == 0), stop=(ca == NCH - 1))

                if wi == 0:
                    @on("scalar")
                    def _(scalar, tt=tt):
                        S = mkstep(scalar, "s", "SS", deps=("TS", "VS"))
                        S(scalar.copy, kbuf[:, tt * 512:(tt + 1) * 512], pK[:])
                elif wi == 1:
                    @on("scalar")
                    def _(scalar, tt=tt):
                        S = mkstep(scalar, "s", "SS", deps=("TS", "VS"))
                        S(scalar.copy, vbuf[:, tt * 512:(tt + 1) * 512], pV[:])
                else:
                    @on("scalar")
                    def _(scalar, dc=dc, tt=tt):
                        S = mkstep(scalar, "s", "SS", deps=("TS", "VS"))
                        S(scalar.activation,
                          sigr[:, dc * BT + tt * 512:dc * BT + (tt + 1) * 512],
                          pR[:], AF.Sigmoid)
                pmk[("pkevac", wi)] = cx.mark("SS")
            pmk[("wbt", wb)] = cx.mark("TS")

        for tt in range(4):
            sl = slice(tt * 512, (tt + 1) * 512)

            @on("vector")
            def _(vector, dc=dc, sl=sl):
                S = mkstep(vector, "v", "VS", deps=("SS", "LD"))
                S(vector.tensor_mul, kvb[:, sl], kbuf[:, sl], vbuf[:, sl])
                S(vector.tensor_scalar, pbuf[:], tvalsb[:, sl],
                  sb["v_sp"][:, dc:dc + 1], None, ALU.mult)

            @on("scalar")
            def _(scalar):
                S = mkstep(scalar, "s", "SS", deps=("VS",))
                S(scalar.activation, pbuf[:], pbuf[:], AF.Exp, scale=-1.0)

            @on("vector")
            def _(vector, sl=sl):
                S = mkstep(vector, "v", "VS", deps=("SS",))
                S(vector.tensor_mul, ubuf[:, sl], kvb[:, sl], pbuf[:])

        @on("vector")
        def _(vector, dc=dc):
            S = mkstep(vector, "v", "VS", deps=("SS",))
            S(vector.tensor_tensor_scan, sS[:, 1:BT + 1], wkvb[:], ubuf[:], 0.0,
              ALU.add, ALU.add)
            S.c(vector.tensor_copy, totc[:, dc:dc + 1], sS[:, BT:BT + 1])
            S.c(vector.tensor_scalar, wkvb[:], kvb[:], sb["v_tf"][:, dc:dc + 1], None,
                ALU.mult)
            S.c(vector.scalar_tensor_tensor, wkvb[:], sS[:, 0:BT],
                sb["v_td"][:, dc:dc + 1], wkvb[:], ALU.mult, ALU.add)
            S.c(vector.tensor_mul, rwkv[:, dc * BT:(dc + 1) * BT],
                sigr[:, dc * BT:(dc + 1) * BT], wkvb[:])

    @on("gpsimd")
    def _(g):
        D = mkdma(g, "g", "GD")
        S = mkstep(g, "g", "CC")
        D(DT["cc_carry_in"].rearrange("o (a p) -> (o p) a", p=128), totc[:],
          allow_slow_non_contiguous=True)
        cx.wait(g, "g", ("GD",))
        S(g.collective_compute, "AllGather", ALU.bypass,
          replica_groups=[[0, 1], [2, 3], [4, 5], [6, 7]],
          ins=[DT["cc_carry_in"].opt()], outs=[DT["cc_carry_out"].opt()])
        cx.wait(g, "g", ("CC",))
        D(cgat[:, 0:8],
          DT["cc_carry_out"][0:1, :].rearrange("o (a p) -> (o p) a", p=128),
          allow_slow_non_contiguous=True)
        D(cgat[:, 8:16],
          DT["cc_carry_out"][1:2, :].rearrange("o (a p) -> (o p) a", p=128),
          allow_slow_non_contiguous=True)

    @on("vector")
    def _(vector):
        S = mkstep(vector, "v", "VS")
        S(vector.tensor_scalar, sb["carry_fix"][:], cgat[:, 0:8],
          sb["c_carryw"][:, 0:1], None, ALU.mult)
        S.c(vector.scalar_tensor_tensor, sb["carry_fix"][:], cgat[:, 8:16],
            sb["c_carryw"][:, 1:2], sb["carry_fix"][:], ALU.mult, ALU.add)
        S.c(vector.tensor_mul, sb["carry_fix"][:], sb["carry_fix"][:], sb["v_td"][:])
        for dc in range(NCH):
            sl = slice(dc * BT, (dc + 1) * BT)
            S.c(vector.scalar_tensor_tensor, rwkv[:, sl], sigr[:, sl],
                sb["carry_fix"][:, dc:dc + 1], rwkv[:, sl], ALU.mult, ALU.add)

    barrier()
    pha.close()

    s_x1 = contextlib.ExitStack()
    x1f = sball("x1f", [128, NCH * BT], F32, s_x1)
    phr = contextlib.ExitStack()
    woT = sball("woT", [128, NCH * C], F32, phr)
    woTb = sball("woTb", [128, NCH * C], BF16, phr)
    g0t = sball("g0t", [128, 2 * 512], F32, phr)
    xft = sball("xft", [128, 2 * 512], F32, phr)
    pS0 = phr.enter_context(nc.psum_tensor("pS0", [128, 512], F32))
    pS1 = phr.enter_context(nc.psum_tensor("pS1", [128, 512], F32))

    @on("sync")
    def _(sync):
        D = mkdma(sync, "sy", "LD")
        D(woT[:].rearrange("p (a n) -> p a n", a=NCH),
          DT["WoT"].rearrange("(a p) n -> p a n", p=128))

    @on("vector")
    def _(vector):
        S = mkstep(vector, "v", "VS")
        S(vector.tensor_copy, woTb[:], woT[:])

    for dc in range(NCH):
        for tt in range(4):
            pS = (pS0, pS1)[tt % 2]
            buf = tt % 2

            @on("sync")
            def _(sync, dc=dc, tt=tt, buf=buf):
                D = mkdma(sync, "sy", "LD", deps=("VS",))
                D(g0t[:, buf * 512:(buf + 1) * 512],
                  DT["g0_d"][dc * 128:(dc + 1) * 128, tt * 512:(tt + 1) * 512])
                D(xft[:, buf * 512:(buf + 1) * 512],
                  DT["x_fm"][dc * 128:(dc + 1) * 128, tt * 512:(tt + 1) * 512])

            @on("tensor")
            def _(tensor, dc=dc, tt=tt, pS=pS):
                S = mkstep(tensor, "t", "TS", deps=("VS",))
                for ca in range(NCH):
                    S(tensor.matmul, pS[:],
                      woTb[:, ca * C + dc * 128:ca * C + (dc + 1) * 128],
                      rwkv[:, ca * BT + tt * 512:ca * BT + (tt + 1) * 512],
                      start=(ca == 0), stop=(ca == NCH - 1))

            @on("vector")
            def _(vector, dc=dc, tt=tt, pS=pS, buf=buf):
                S = mkstep(vector, "v", "VS", deps=("TS", "LD"))
                dst = x1f[:, dc * BT + tt * 512:dc * BT + (tt + 1) * 512]
                S(vector.tensor_mul, dst, pS[:], g0t[:, buf * 512:(buf + 1) * 512])
                S.c(vector.tensor_add, dst, dst, xft[:, buf * 512:(buf + 1) * 512])

    @on("sync")
    def _(sync):
        D = mkdma(sync, "sy", "LD")
        D(DT["x1_d"].rearrange("(a p) n -> p a n", p=128),
          x1f[:].rearrange("p (a n) -> p a n", a=NCH))

    barrier()
    phr.close()

    ph2 = contextlib.ExitStack()
    xcb = sball("xcb", [128, 2 * 512], BF16, ph2)
    MTl = sball("MTl", [128, 8 * C], BF16, ph2)
    x1sq = sball("x1sq", [128, 2 * 512], BF16, ph2)
    xn1b = sball("xn1b", [128, NCH * BT], BF16, ph2)
    l1scr = sball("l1scr", [128, 512], F32, ph2)
    g1st = sball("g1st", [128, 512], F32, ph2)
    pM0 = ph2.enter_context(nc.psum_tensor("pM0", [128, 512], F32))
    pM1 = ph2.enter_context(nc.psum_tensor("pM1", [128, 512], F32))
    pGa = ph2.enter_context(nc.psum_tensor("pGa", [128, 512], F32))
    pGb = ph2.enter_context(nc.psum_tensor("pGb", [128, 512], F32))

    @on("sync")
    def _(sync):
        D = mkdma(sync, "sy", "LD")
        D(MTl[:].rearrange("p (a n) -> p a n", a=NCH),
          DT["mt_d"].rearrange("(a p) n -> p a n", p=128))

    for tt in range(4):
        for ca in range(NCH):
            buf = ca % 2
            xc = xcb[:, buf * 512:(buf + 1) * 512]
            sq = x1sq[:, buf * 512:(buf + 1) * 512]

            @on("vector")
            def _(vector, tt=tt, ca=ca, xc=xc):
                S = mkstep(vector, "v", "VS", deps=("TS",))
                S(vector.tensor_copy, xc,
                  x1f[:, ca * BT + tt * 512:ca * BT + (tt + 1) * 512])

            @on("scalar")
            def _(scalar, xc=xc, sq=sq):
                S = mkstep(scalar, "s", "SS", deps=("VS", "TS"))
                S(scalar.activation, sq, xc, AF.Square)

            @on("tensor")
            def _(tensor, ca=ca, xc=xc, sq=sq):
                S = mkstep(tensor, "t", "TS", deps=("VS", "SS"))
                S(tensor.matmul, pM0[:], sb["c_onesb"][:], xc,
                  start=(ca == 0), stop=(ca == NCH - 1))
                S(tensor.matmul, pM1[:], sb["c_onesb"][:], sq,
                  start=(ca == 0), stop=(ca == NCH - 1))

        @on("scalar")
        def _(scalar, tt=tt):
            S = mkstep(scalar, "s", "SS")
            S(scalar.activation, l1scr[:], pM0[:], AF.Square, scale=1.0 / C)

        @on("vector")
        def _(vector, tt=tt):
            S = mkstep(vector, "v", "VS", deps=("TS", "SS"))
            sl = slice(tt * 512, (tt + 1) * 512)
            S(vector.tensor_scalar, sb["abc"][:, sl], pM1[:], 1.0 / C, EPS,
              ALU.mult, ALU.add)
            S.c(vector.tensor_sub, sb["abc"][:, sl], sb["abc"][:, sl], l1scr[:])

        @on("scalar")
        def _(scalar, tt=tt):
            S = mkstep(scalar, "s", "SS")
            sl = slice(tt * 512, (tt + 1) * 512)
            S(scalar.activation, sb["bbc"][:, sl], sb["abc"][:, sl], AF.Sqrt)

        @on("vector")
        def _(vector, tt=tt):
            S = mkstep(vector, "v", "VS", deps=("SS",))
            sl = slice(tt * 512, (tt + 1) * 512)
            S(vector.reciprocal, sb["abc"][:, sl], sb["bbc"][:, sl])
            S.c(vector.scalar_tensor_tensor, sb["bbc"][:, sl], pM0[:], -1.0 / C,
                sb["abc"][:, sl], ALU.mult, ALU.mult)

    @on("vector")
    def _(vector):
        S = mkstep(vector, "v", "VS")
        for ca in range(NCH):
            sl = slice(ca * BT, (ca + 1) * BT)
            S.c(vector.tensor_mul, x1f[:, sl], x1f[:, sl], sb["abc"][:])
            S.c(vector.tensor_add, xn1b[:, sl], x1f[:, sl], sb["bbc"][:])

    @on("gpsimd")
    def _(g):
        D = mkdma(g, "g", "GD")
        S = mkstep(g, "g", "CC")
        D(DT["cc_xn1_in"].rearrange("(a p) n -> p a n", p=128),
          xn1b[:].rearrange("p (a n) -> p a n", a=NCH))
        cx.wait(g, "g", ("GD",))
        S(g.collective_compute, "AllGather", ALU.bypass,
          replica_groups=[[0, 1], [2, 3], [4, 5], [6, 7]],
          ins=[DT["cc_xn1_in"].opt()], outs=[DT["cc_xn1_out"].opt()])

    for dc in range(NCH):
        for tt in range(4):
            pG = (pGa, pGb)[tt % 2]

            @on("tensor")
            def _(tensor, dc=dc, tt=tt, pG=pG):
                S = mkstep(tensor, "t", "TS", deps=("VS", "SS", "LD"))
                for ca in range(NCH):
                    S(tensor.matmul, pG[:],
                      MTl[:, ca * C + dc * 128:ca * C + (dc + 1) * 128],
                      xn1b[:, ca * BT + tt * 512:ca * BT + (tt + 1) * 512],
                      start=(ca == 0), stop=(ca == NCH - 1))

            @on("scalar")
            def _(scalar, dc=dc, tt=tt, pG=pG):
                S = mkstep(scalar, "s", "SS", deps=("TS", "LD"))
                S(scalar.activation, g1st[:], pG[:], AF.Silu,
                  bias=sb["v_bTTc"][:, dc:dc + 1])

            @on("sync")
            def _(sync, dc=dc, tt=tt):
                D = mkdma(sync, "sy", "LD", deps=("SS",))
                D(DT["g1_d"][dc * 128:(dc + 1) * 128, tt * 512:(tt + 1) * 512],
                  g1st[:])

    barrier()
    ph2.close()
    s_x1.close()
    s_carry.close()

    s_att = contextlib.ExitStack()
    Qh = sball("Qh", [128, 4 * T], BF16, s_att)
    Kh = sball("Kh", [128, 4 * T], BF16, s_att)
    Vsb = sball("Vsb", [128, 32 * 520], BF16, s_att)
    ph3 = contextlib.ExitStack()
    wqs = sball("wqs", [128, 2 * 1536], F32, ph3)
    wqb = sball("wqb", [128, NCH * 1536], BF16, ph3)
    cosQs = sball("cosQs", [128, T], BF16, ph3)
    sinQs = sball("sinQs", [128, T], BF16, ph3)
    cosKs = sball("cosKs", [128, T], BF16, ph3)
    sinKs = sball("sinKs", [128, T], BF16, ph3)
    xg = sball("xg", [128, NCH * 512], BF16, ph3)
    qraw = sball("qraw", [128, 512], BF16, ph3)
    t1b = sball("t1b", [128, 512], BF16, ph3)
    t2b = sball("t2b", [128, 512], BF16, ph3)
    pQ0 = ph3.enter_context(nc.psum_tensor("pQ0", [128, 512], F32))
    pQ1 = ph3.enter_context(nc.psum_tensor("pQ1", [128, 512], F32))
    pRot = ph3.enter_context(nc.psum_tensor("pRot", [128, 512], F32))
    pVv = ph3.enter_context(nc.psum_tensor("pVv", [128, 512], F32))

    @on("sync")
    def _(sync):
        D = mkdma(sync, "sy", "LD")
        for nm, dst in (("cosQ", cosQs), ("sinQ", sinQs),
                        ("cosK", cosKs), ("sinK", sinKs)):
            D(dst[:], DT[nm])

    for half in range(4):
        @on("sync")
        def _(sync, half=half):
            D = mkdma(sync, "sy", "LD")
            D(wqs[:].rearrange("p (a n) -> p a n", a=2),
              DT["WqkvT_sl"][half * 256:(half + 1) * 256, :].rearrange(
                  "(a p) n -> p a n", p=128))

        @on("vector")
        def _(vector, half=half):
            S = mkstep(vector, "v", "VS")
            S(vector.tensor_copy, wqb[:, half * 2 * 1536:(half + 1) * 2 * 1536],
              wqs[:])

    @on("vector")
    def _(vector):
        S = mkstep(vector, "v", "VS")
        S(vector.memset,
          Vsb[:].rearrange("p (k h e) -> p k h e", k=32, h=8)[:, :, :, 64:65], 1.0)

    for tt in range(8):
        r, tl = tt // 4, tt % 4

        @on("sync")
        def _(sync, r=r, tl=tl):
            D = mkdma(sync, "sy", "LD", deps=("CC", "TS"))
            for ca in range(NCH):
                D(xg[:, ca * 512:(ca + 1) * 512],
                  DT["cc_xn1_out"][r * C + ca * 128:r * C + (ca + 1) * 128,
                                   tl * 512:(tl + 1) * 512])

        for qc in range(8):
            pQ = (pQ0, pQ1)[qc % 2]
            isq = qc < 4
            hp = qc % 4
            cosT, sinT = (cosQs, sinQs) if isq else (cosKs, sinKs)
            dstT = Qh if isq else Kh

            @on("tensor")
            def _(tensor, tt=tt, qc=qc, pQ=pQ):
                S = mkstep(tensor, "t", "TS", deps=("LD", "SS", "VS"))
                for ca in range(NCH):
                    S(tensor.matmul, pQ[:],
                      wqb[:, ca * 1536 + qc * 128:ca * 1536 + (qc + 1) * 128],
                      xg[:, ca * 512:(ca + 1) * 512],
                      start=(ca == 0), stop=(ca == NCH - 1))

            @on("scalar")
            def _(scalar, pQ=pQ):
                S = mkstep(scalar, "s", "SS", deps=("TS", "VS"))
                S(scalar.copy, qraw[:], pQ[:])

            @on("tensor")
            def _(tensor):
                S = mkstep(tensor, "t", "TS", deps=("SS", "VS"))
                S(tensor.matmul, pRot[:], sb["c_ropeR"][:], qraw[:],
                  start=True, stop=True)

            @on("vector")
            def _(vector, tt=tt, hp=hp, cosT=cosT, sinT=sinT, dstT=dstT):
                S = mkstep(vector, "v", "VS", deps=("SS", "TS"))
                sl = slice(tt * 512, (tt + 1) * 512)
                S(vector.tensor_mul, t1b[:], qraw[:], cosT[:, sl])
                S(vector.tensor_mul, t2b[:], pRot[:], sinT[:, sl])
                S.c(vector.tensor_add,
                    dstT[:, hp * T + tt * 512:hp * T + (tt + 1) * 512], t1b[:],
                    t2b[:])

        for tb in range(4):
            kt = tt * 4 + tb

            @on("tensor")
            def _(tensor, tb=tb):
                S = mkstep(tensor, "t", "TS", deps=("LD", "SS", "VS"))
                for ca in range(NCH):
                    S(tensor.matmul, pVv[:],
                      xg[:, ca * 512 + tb * 128:ca * 512 + (tb + 1) * 128],
                      wqb[:, ca * 1536 + 1024:ca * 1536 + 1536],
                      start=(ca == 0), stop=(ca == NCH - 1))

            @on("scalar")
            def _(scalar, kt=kt):
                S = mkstep(scalar, "s", "SS", deps=("TS", "VS"))
                S(scalar.copy,
                  Vsb[:, kt * 520:(kt + 1) * 520].rearrange(
                      "p (h e) -> p h e", h=8)[:, :, 0:64],
                  pVv[:].rearrange("p (h e) -> p h e", h=8))

    barrier()
    ph3.close()

    ph4 = contextlib.ExitStack()
    pTb = sball("pTb", [128, 2048], BF16, ph4)
    dmaskP = sball("dmaskP_s", [128, 4096], BF16, ph4)
    lst = sball("lst", [128, 1024], F32, ph4)
    l0r = sball("l0r", [1, 1024], F32, ph4)
    bcs = sball("bcs", [64, 1024], F32, ph4)
    outn = sball("outn", [64, 1024], BF16, ph4)
    pSmA = ph4.enter_context(nc.psum_tensor("pSmA", [128, 1024], F32))
    pSmB = ph4.enter_context(nc.psum_tensor("pSmB", [128, 1024], F32))
    pAV0 = ph4.enter_context(nc.psum_tensor("pAV0", [128, 512], F32))
    pAV1 = ph4.enter_context(nc.psum_tensor("pAV1", [128, 512], F32))
    pDV = ph4.enter_context(nc.psum_tensor("pDV", [128, 512], F32))

    @on("sync")
    def _(sync):
        D = mkdma(sync, "sy", "LD")
        D(dmaskP[:], DT["dmaskP"])

    amk = {}

    def emit_st(qt, hp, kg):
        sm = kg % 2
        pSm = (pSmA, pSmB)[sm]

        @on("tensor")
        def _(tensor, qt=qt, kg=kg, hp=hp, pSm=pSm, sm=sm):
            cx.wait(tensor, "t", ("LD",))
            cx.wait_mark(tensor, "t", amk.get(("exp", sm)))
            S = mkstep(tensor, "t", "TS", deps=())
            for sub in range(2):
                prt = slice(64 * sub, 64 * sub + 64)
                S(tensor.matmul, pSm[:, sub * 512:(sub + 1) * 512],
                  Kh[prt, hp * T + kg * 128:hp * T + (kg + 1) * 128],
                  Qh[prt, hp * T + qt * 512:hp * T + (qt + 1) * 512],
                  start=True, stop=True)

    for qt in range(8):
        for hp in range(4):
            ngroups = 4 * qt + 4

            emit_st(qt, hp, 0)
            for kg in range(ngroups):
                sm = kg % 2
                pSm = (pSmA, pSmB)[sm]
                last = (kg == ngroups - 1)

                @on("scalar")
                def _(scalar, pSm=pSm, sm=sm):
                    cx.wait(scalar, "s", ("TS",))
                    cx.wait_mark(scalar, "s", amk.get(("pTb", sm)))
                    S = mkstep(scalar, "s", "SS", deps=())
                    S(scalar.activation, pTb[:, sm * 1024:(sm + 1) * 1024],
                      pSm[:, 0:1024], AF.Exp, bias=sb["c_nb30"][:, 0:1])
                amk[("exp", sm)] = cx.mark("SS")
                amk[("ptb_ready", sm)] = cx.mark("SS")

                if kg >= ngroups - 4:
                    moff = (kg - (ngroups - 4)) * 1024

                    @on("vector")
                    def _(vector, sm=sm, moff=moff):
                        S = mkstep(vector, "v", "VS", deps=("SS", "LD"))
                        S(vector.tensor_mul, pTb[:, sm * 1024:(sm + 1) * 1024],
                          pTb[:, sm * 1024:(sm + 1) * 1024],
                          dmaskP[:, moff:moff + 1024])
                    amk[("ptb_ready", sm)] = cx.mark("VS")

                if not last:
                    emit_st(qt, hp, kg + 1)

                @on("tensor")
                def _(tensor, qt=qt, hp=hp, kg=kg, last=last, sm=sm):
                    cx.wait_mark(tensor, "t", amk.get(("ptb_ready", sm)))
                    if kg == 0:
                        cx.wait_mark(tensor, "t", amk.get(("pAV", 0)))
                        cx.wait_mark(tensor, "t", amk.get(("pAV", 1)))
                    S = mkstep(tensor, "t", "TS", deps=())
                    for sub in range(2):
                        h = 2 * hp + sub
                        pAV = (pAV0, pAV1)[sub]
                        S(tensor.matmul, pAV[0:65, :],
                          Vsb[:, kg * 520 + h * 65:kg * 520 + (h + 1) * 65],
                          pTb[:, sm * 1024 + sub * 512:sm * 1024 + (sub + 1) * 512],
                          start=(kg == 0), stop=last)
                amk[("pTb", sm)] = cx.mark("TS")

            for sub in range(2):
                h = 2 * hp + sub
                hb = sub
                pAV = (pAV0, pAV1)[sub]

                @on("vector")
                def _(vector, hb=hb, pAV=pAV):
                    cx.wait(vector, "v", ("TS",))
                    cx.wait_mark(vector, "v", amk.get(("lst", hb)))
                    S = mkstep(vector, "v", "VS", deps=())
                    S(vector.tensor_copy, lst[64:65, hb * 512:(hb + 1) * 512],
                      pAV[64:65, :])

                @on("sync")
                def _(sync, hb=hb):
                    D = mkdma(sync, "sy", "LD", deps=("VS",))
                    D(l0r[0:1, hb * 512:(hb + 1) * 512],
                      lst[64:65, hb * 512:(hb + 1) * 512])
                amk[("lst", hb)] = cx.mark("LD")

                @on("vector")
                def _(vector, hb=hb):
                    S = mkstep(vector, "v", "VS", deps=("LD",))
                    S(vector.reciprocal, l0r[0:1, hb * 512:(hb + 1) * 512],
                      l0r[0:1, hb * 512:(hb + 1) * 512])

                @on("tensor")
                def _(tensor, hb=hb):
                    cx.wait(tensor, "t", ("VS",))
                    cx.wait_mark(tensor, "t", amk.get(("pDV",)))
                    S = mkstep(tensor, "t", "TS", deps=())
                    S(tensor.matmul, pDV[0:64, :], sb["c_onesrowf"][0:1, 0:64],
                      l0r[0:1, hb * 512:(hb + 1) * 512], start=True, stop=True)

                @on("vector")
                def _(vector, hb=hb, pAV=pAV):
                    cx.wait(vector, "v", ("TS",))
                    cx.wait_mark(vector, "v", amk.get(("outn", hb)))
                    S = mkstep(vector, "v", "VS", deps=())
                    S(vector.tensor_copy, bcs[:, hb * 512:(hb + 1) * 512],
                      pDV[0:64, :])
                    S.c(vector.tensor_mul, outn[:, hb * 512:(hb + 1) * 512],
                        pAV[0:64, :], bcs[:, hb * 512:(hb + 1) * 512])
                amk[("pDV",)] = cx.mark("VS")
                amk[("pAV", hb)] = cx.mark("VS")

                @on("sync")
                def _(sync, qt=qt, h=h, hb=hb):
                    D = mkdma(sync, "sy", "LD", deps=("VS",))
                    D(DT["cc_out_in"][h * 64:(h + 1) * 64, qt * 512:(qt + 1) * 512],
                      outn[:, hb * 512:(hb + 1) * 512])
                amk[("outn", hb)] = cx.mark("LD")

    @on("gpsimd")
    def _(g):
        S = mkstep(g, "g", "CC")
        S(g.collective_compute, "AllGather", ALU.bypass,
          replica_groups=[[0, 1], [2, 3], [4, 5], [6, 7]],
          ins=[DT["cc_out_in"].opt()], outs=[DT["cc_out_out"].opt()])

    barrier()
    ph4.close()
    s_att.close()

    ph5 = contextlib.ExitStack()
    wo1 = sball("wo1", [128, NCH * C], F32, ph5)
    wo1b = sball("wo1b", [128, NCH * C], BF16, ph5)
    gA = sball("gA", [128, BT], BF16, ph5)
    gB = sball("gB", [128, BT], BF16, ph5)
    omy = sball("omy", [128, NCH * BT], BF16, ph5)
    oscr = sball("oscr", [128, BT], F32, ph5)
    g1t = sball("g1t", [128, 2 * 512], F32, ph5)
    x1t = sball("x1t", [128, 2 * 512], F32, ph5)
    x2st = sball("x2st", [128, 2 * 512], F32, ph5)
    pZ0 = ph5.enter_context(nc.psum_tensor("pZ0", [128, 512], F32))
    pZ1 = ph5.enter_context(nc.psum_tensor("pZ1", [128, 512], F32))

    @on("sync")
    def _(sync):
        D = mkdma(sync, "sy", "LD")
        D(wo1[:].rearrange("p (a n) -> p a n", a=NCH),
          DT["Wo1T"].rearrange("(a p) n -> p a n", p=128))

    @on("vector")
    def _(vector):
        S = mkstep(vector, "v", "VS")
        S(vector.tensor_copy, wo1b[:], wo1[:])

    for r in range(2):
        for cc4 in range(4):
            dc = r * 4 + cc4
            row = r * 512 + cc4 * 128

            @on("sync")
            def _(sync, row=row):
                D = mkdma(sync, "sy", "LD", deps=("VS", "CC"))
                D(gA[:], DT["cc_out_out"][row:row + 128, 0:BT])
                D(gB[:], DT["cc_out_out"][row:row + 128, BT:T])

            @on("vector")
            def _(vector, dc=dc):
                S = mkstep(vector, "v", "VS", deps=("LD",))
                S(vector.tensor_scalar, oscr[:], gA[:], sb["c_selw"][:, 0:1], None,
                  ALU.mult)
                S.c(vector.scalar_tensor_tensor, oscr[:], gB[:], sb["c_selw"][:, 1:2],
                    oscr[:], ALU.mult, ALU.add)
                S.c(vector.tensor_copy, omy[:, dc * BT:(dc + 1) * BT], oscr[:])

    for dc in range(NCH):
        for tt in range(4):
            pZ = (pZ0, pZ1)[tt % 2]
            buf = tt % 2

            @on("sync")
            def _(sync, dc=dc, tt=tt, buf=buf):
                D = mkdma(sync, "sy", "LD", deps=("VS",))
                D(g1t[:, buf * 512:(buf + 1) * 512],
                  DT["g1_d"][dc * 128:(dc + 1) * 128, tt * 512:(tt + 1) * 512])
                D(x1t[:, buf * 512:(buf + 1) * 512],
                  DT["x1_d"][dc * 128:(dc + 1) * 128, tt * 512:(tt + 1) * 512])

            @on("tensor")
            def _(tensor, dc=dc, tt=tt, pZ=pZ):
                S = mkstep(tensor, "t", "TS", deps=("VS",))
                for ca in range(NCH):
                    S(tensor.matmul, pZ[:],
                      wo1b[:, ca * C + dc * 128:ca * C + (dc + 1) * 128],
                      omy[:, ca * BT + tt * 512:ca * BT + (tt + 1) * 512],
                      start=(ca == 0), stop=(ca == NCH - 1))

            @on("vector")
            def _(vector, dc=dc, tt=tt, pZ=pZ, buf=buf):
                S = mkstep(vector, "v", "VS", deps=("TS", "LD"))
                d = x2st[:, buf * 512:(buf + 1) * 512]
                S(vector.tensor_mul, d, pZ[:], g1t[:, buf * 512:(buf + 1) * 512])
                S.c(vector.tensor_add, d, d, x1t[:, buf * 512:(buf + 1) * 512])

            @on("sync")
            def _(sync, dc=dc, tt=tt, buf=buf):
                D = mkdma(sync, "sy", "LD", deps=("VS",))
                D(DT["x2_fm"][dc * 128:(dc + 1) * 128, tt * 512:(tt + 1) * 512],
                  x2st[:, buf * 512:(buf + 1) * 512])

    @on("sync")
    def _(sync):
        cx.wait_all(sync, "sy")

    ph5.close()
    es.close()
    return nc


def _host_prep(inputs):
    f32 = np.float32
    x = np.asarray(inputs["L0_attn_x"], f32)
    WkT = np.ascontiguousarray(np.asarray(inputs["L0_attn_Wk"], f32).T)
    WvT = np.ascontiguousarray(np.asarray(inputs["L0_attn_Wv"], f32).T)
    WrT = np.ascontiguousarray(np.asarray(inputs["L0_attn_Wr"], f32).T)
    WoT = np.ascontiguousarray(np.asarray(inputs["L0_attn_Wo"], f32).T)
    w1T = np.ascontiguousarray(np.asarray(inputs["L0_ffn_w1"], f32).T)
    w2T = np.ascontiguousarray(np.asarray(inputs["L0_ffn_w2"], f32).T)
    Wqkv = np.asarray(inputs["L1_attn_Wqkv"], f32)
    Wo1T = np.ascontiguousarray(np.asarray(inputs["L1_attn_Wo"], f32).T)

    def colpack(v, n):
        return np.ascontiguousarray(np.asarray(v, f32).reshape(n, 128).T)

    def rowbc(v):
        v = np.asarray(v, f32)
        return np.ascontiguousarray(np.broadcast_to(v, (128, v.shape[0])))

    gamA = colpack(inputs["L0_attn_ln_gamma"], 8)
    betA = colpack(inputs["L0_attn_ln_beta"], 8)
    decay = colpack(inputs["L0_attn_time_decay"], 8)
    first = colpack(inputs["L0_attn_time_first"], 8)
    b2c = colpack(inputs["L0_ffn_b2"], 8)
    bTT = np.concatenate([np.asarray(inputs["L1_ffn_bias_real"], f32),
                          np.asarray(inputs["L1_ffn_bias_imag"], f32)])
    bTTc = colpack(bTT, 8)
    g1bc = rowbc(inputs["L0_ffn_ln1_g"])
    b1bc = rowbc(inputs["L0_ffn_ln1_b"])
    b1fbc = rowbc(inputs["L0_ffn_b1"])
    g2bc = rowbc(inputs["L0_ffn_ln2_g"])
    be2bc = rowbc(inputs["L0_ffn_ln2_b"])

    identf = np.eye(128, dtype=f32)
    identb = np.eye(128, dtype=f32).astype(BF)
    onesb = np.ones((128, 128), f32).astype(BF)
    onescolf = np.ones((128, 1), f32)
    onesrowf = np.ones((1, 128), f32)
    R = np.zeros((128, 128), f32)
    for i in range(64):
        R[2 * i, 2 * i + 1] = -1.0
        R[2 * i + 1, 2 * i] = 1.0
    ropeRT = np.ascontiguousarray(R.T).astype(BF)
    freqs = np.exp(-np.log(10000.0) * np.arange(0, DH, 2, dtype=f32) / DH)
    ang = np.arange(T, dtype=f32)[:, None] * freqs[None, :]
    ang = np.repeat(ang, 2, axis=1)
    cos64 = np.cos(ang).T.astype(f32)
    sin64 = np.sin(ang).T.astype(f32)
    cos128 = np.concatenate([cos64, cos64], 0)
    sin128 = np.concatenate([sin64, sin64], 0)
    cosQ = (cos128 / 8.0).astype(BF)
    sinQ = (sin128 / 8.0).astype(BF)
    cosK = cos128.astype(BF)
    sinK = sin128.astype(BF)
    k_ = np.arange(128)[:, None]
    q_ = np.arange(512)[None, :]
    dmask = np.concatenate(
        [((k_ + 128 * j) <= q_).astype(f32) for j in range(4)], axis=1).astype(BF)
    dmaskP = np.concatenate(
        [np.concatenate([((k_ + 128 * j) <= q_).astype(f32)] * 2, axis=1)
         for j in range(4)], axis=1).astype(BF)

    G1R = np.asarray(inputs["L1_ffn_G1_R"], f32)
    G2R = np.asarray(inputs["L1_ffn_G2_R"], f32)
    G1I = np.asarray(inputs["L1_ffn_G1_I"], f32)
    G2I = np.asarray(inputs["L1_ffn_G2_I"], f32)

    def ttrep(G1, G2):
        T1 = np.zeros((512, 4, 512), f32)
        T2 = np.zeros((512, 4, 512), f32)
        d_ = np.arange(512) // 16
        g_ = np.arange(512) % 16
        b_ = np.arange(512) // 16
        e_ = np.arange(512) % 16
        for c in range(4):
            A = G1[0, :, c, :]
            Bm = G2[c, :, 0, :]
            T1[:, c, :] = A.T[d_][:, b_]
            T2[:, c, :] = Bm.T[g_][:, e_]
        return (np.ascontiguousarray(T1.reshape(512, 2048)),
                np.ascontiguousarray(T2.reshape(512, 2048)))

    T1r, T2r = ttrep(G1R, G2R)
    T1i, T2i = ttrep(G1I, G2I)

    base = dict(
        WkT=WkT, WvT=WvT, WrT=WrT, WoT=WoT, Wo1T=Wo1T,
        gamA=gamA, betA=betA, decay=decay, first=first, b2c=b2c, bTTc=bTTc,
        g1bc=g1bc, b1bc=b1bc, b1fbc=b1fbc, g2bc=g2bc, be2bc=be2bc,
        identf=identf, identb=identb, onesb=onesb, ropeR=ropeRT,
        onescolf=onescolf, onesrowf=onesrowf,
        nb30=np.full((128, 1), EXPB, f32),
        cosQ=cosQ, sinQ=sinQ, cosK=cosK, sinK=sinK, dmask=dmask,
        T1r=T1r, T2r=T2r, T1i=T1i, T2i=T2i, dmaskP=dmaskP,
    )

    in_maps = []
    for core in range(8):
        b, p = core // 2, core % 2
        m = dict(base)
        xt = np.ascontiguousarray(x[b, p * BT:(p + 1) * BT, :])
        m["x_tok"] = xt
        m["x_fm"] = np.ascontiguousarray(xt.T)
        m["w1T_sl"] = np.ascontiguousarray(w1T[:, core * 512:(core + 1) * 512])
        m["w2T_sl"] = np.ascontiguousarray(w2T[core * 512:(core + 1) * 512, :])
        hrows = np.concatenate([np.arange(p * 512, (p + 1) * 512),
                                C + np.arange(p * 512, (p + 1) * 512),
                                2 * C + np.arange(p * 512, (p + 1) * 512)])
        m["WqkvT_sl"] = np.ascontiguousarray(Wqkv[hrows, :].T)
        m["tvals"] = np.broadcast_to(
            np.arange(p * BT, (p + 1) * BT, dtype=f32), (128, BT)).copy()
        m["selw"] = np.broadcast_to(
            np.array([1.0 - p, float(p)], f32), (128, 2)).copy()
        m["carryw"] = np.broadcast_to(
            np.array([float(p), 0.0], f32), (128, 2)).copy()
        in_maps.append(m)
    return in_maps


def kernel(**inputs):
    global _last_res
    in_maps = _host_prep(inputs)
    nc = build_program()
    res = run_bass_kernel_spmd(nc, in_maps, core_ids=list(range(8)))
    _last_res = res
    out = np.zeros((B, T, C), np.float32)
    for core in range(8):
        b, p = core // 2, core % 2
        out[b, p * BT:(p + 1) * BT, :] = np.asarray(res.results[core]["x2_fm"]).T
    return out


# version 09
# speedup vs baseline: 1.2875x; 1.1440x over previous
import contextlib
import numpy as np
import ml_dtypes

import concourse.bass as bass
import concourse.mybir as mybir
from concourse.bass_utils import run_bass_kernel_spmd

F32 = mybir.dt.float32
BF16 = mybir.dt.bfloat16
AF = mybir.ActivationFunctionType
ALU = mybir.AluOpType
AX = mybir.AxisListType.X
BF = ml_dtypes.bfloat16

B, T, C = 4, 4096, 1024
BT = 2048
H4 = 4096
DH = 64
NCH = 8
NBLK = 16
MYH = 8
EPS = 1e-5
RC = 12582912.0
EXPB = -30.0
ALLSEM = ("LD", "GD", "TS", "VS", "SS", "CC")


class Ctx:
    def __init__(self, nc, es):
        self.sem = {}
        self.cnt = {}
        self.waited = {}
        for name in ALLSEM:
            self.sem[name] = es.enter_context(nc.semaphore(name))
            self.cnt[name] = 0

    def inc(self, instr, name, n=1):
        instr.then_inc(self.sem[name], n)
        self.cnt[name] += n
        return instr

    def wait(self, eng, ename, names):
        for name in names:
            v = self.cnt[name]
            key = (ename, name)
            if v == 0 or self.waited.get(key, -1) >= v:
                continue
            eng.wait_ge(self.sem[name], v)
            self.waited[key] = v

    def wait_all(self, eng, ename):
        self.wait(eng, ename, ALLSEM)

    def mark(self, name):
        return (name, self.cnt[name])

    def wait_value(self, eng, ename, name, v):
        key = (ename, name)
        if v <= 0 or self.waited.get(key, -1) >= v:
            return
        eng.wait_ge(self.sem[name], v)
        self.waited[key] = v

    def wait_mark(self, eng, ename, mk):
        if mk is None:
            return
        name, v = mk
        key = (ename, name)
        if v == 0 or self.waited.get(key, -1) >= v:
            return
        eng.wait_ge(self.sem[name], v)
        self.waited[key] = v


DEBUG = False


def build_program():
    nc = bass.Bass("TRN2", target_bir_lowering=False, debug=False)
    es = contextlib.ExitStack()
    DT = {}

    def din(name, shape, dt=F32):
        DT[name] = nc.dram_tensor(name, shape, dt, kind="ExternalInput").ap()

    def dout(name, shape, dt=F32):
        DT[name] = nc.dram_tensor(name, shape, dt, kind="ExternalOutput").ap()

    def dint(name, shape, dt=F32):
        DT[name] = nc.dram_tensor(name, shape, dt).ap()

    din("x_fm", [C, BT]); din("x_tok", [BT, C])
    din("WkT", [C, C]); din("WvT", [C, C]); din("WrT", [C, C]); din("WoT", [C, C])
    din("w1T_sl", [C, 512]); din("w2T_sl", [512, C])
    din("WqkvT_sl", [C, 1536]); din("Wo1T", [C, C])
    din("gamA", [128, 8]); din("betA", [128, 8])
    din("decay", [128, 8]); din("first", [128, 8])
    din("b2c", [128, 8]); din("bTTc", [128, 8])
    din("g1bc", [128, C]); din("b1bc", [128, C])
    din("b1fbc", [128, H4]); din("g2bc", [128, H4]); din("be2bc", [128, H4])
    din("tvals", [128, BT])
    din("identf", [128, 128], F32)
    din("identb", [128, 128], BF16)
    din("onesb", [128, 128], BF16)
    din("ropeR", [128, 128], BF16)
    din("onescolf", [128, 1]); din("onesrowf", [1, 128])
    din("selw", [128, 2]); din("carryw", [128, 2]); din("nb30", [128, 1])
    din("cosQ", [128, T], BF16); din("sinQ", [128, T], BF16)
    din("cosK", [128, T], BF16); din("sinK", [128, T], BF16)
    din("dmask", [128, 2048], BF16); din("dmaskP", [128, 4096], BF16)
    din("T1r", [512, 2048]); din("T2r", [512, 2048])
    din("T1i", [512, 2048]); din("T2i", [512, 2048])

    dout("x2_fm", [C, BT])
    if DEBUG:
        dout("dbg_xn2a", [C, BT], BF16)
        dout("dbg_int2", [H4, BT], BF16)
        dout("dbg_g0", [C, BT])
        dout("dbg_x1", [C, BT])
        dout("dbg_xn1", [C, BT], BF16)
        dout("dbg_g1", [C, BT])
        dout("dbg_att", [512, T], BF16)

    dint("cc_wsc_in", [1, 2]); dint("cc_wsc_out", [8, 2])
    dint("cc_wq1_in", [C, 512], BF16); dint("cc_wq1_out", [8 * C, 512], BF16)
    dint("cc_wq2_in", [512, C], BF16); dint("cc_wq2_out", [8 * 512, C], BF16)
    dint("cc_carry_in", [1, C]); dint("cc_carry_out", [2, C])
    dint("cc_xn1_in", [C, BT], BF16); dint("cc_xn1_out", [2 * C, BT], BF16)
    dint("cc_out_in", [512, T], BF16); dint("cc_out_out", [1024, T], BF16)
    dint("xn2a_d", [C, BT], BF16)
    dint("int2_d", [H4, BT], BF16)
    dint("g0_d", [C, BT]); dint("g1_d", [C, BT]); dint("x1_d", [C, BT])
    dint("mt_d", [C, C], BF16)

    cx = Ctx(nc, es)
    sb = {}

    def sball(name, shape, dt=F32, stack=None):
        sb[name] = (stack or es).enter_context(nc.sbuf_tensor(name, shape, dt))
        return sb[name]

    for nm, shp, dt in [
        ("c_idf", [128, 128], F32), ("c_idb", [128, 128], BF16),
        ("c_onesb", [128, 128], BF16), ("c_ropeR", [128, 128], BF16),
        ("c_onescolf", [128, 1], F32), ("c_onesrowf", [1, 128], F32),
        ("c_selw", [128, 2], F32), ("c_carryw", [128, 2], F32),
        ("c_nb30", [128, 1], F32),
        ("v_gamA", [128, 8], F32), ("v_betA", [128, 8], F32),
        ("v_decay", [128, 8], F32), ("v_first", [128, 8], F32),
        ("v_b2c", [128, 8], F32), ("v_bTTc", [128, 8], F32),
        ("v_td", [128, 8], F32), ("v_tf", [128, 8], F32), ("v_sp", [128, 8], F32),
        ("st_A", [128, 16], F32), ("st_B", [128, 16], F32),
        ("st_s1", [128, 16], F32), ("st_s1s", [128, 16], F32),
        ("st_s2i", [128, 16], F32),
        ("v_mw", [128, 2], F32), ("v_swc", [128, 2], F32),
        ("row_a", [1, BT], F32),
        ("carry_fix", [128, 8], F32),
        ("colscr", [128, 64], F32),
        ("st16", [16, 256], F32),
        ("abc", [128, BT], F32), ("bbc", [128, BT], F32),
    ]:
        sball(nm, shp, dt)

    block = es.enter_context(nc.Block())

    def on(engname):
        def deco(fn):
            getattr(block, engname)(fn)
        return deco

    def barrier():
        nc.all_engine_barrier()

    def mkstep(eng, ename, own, deps=ALLSEM):
        state = {"first": True}

        def step(f, *a, **k):
            if state["first"]:
                cx.wait(eng, ename, deps)
                state["first"] = False
            return cx.inc(f(*a, **k), own)

        def chain(f, *a, **k):
            if not state["first"]:
                cx.wait(eng, ename, (own,))
            return step(f, *a, **k)

        step.c = chain
        return step

    def mkdma(eng, ename, own, deps=ALLSEM):
        state = {"first": True}

        def dma(out, in_, **k):
            if state["first"]:
                cx.wait(eng, ename, deps)
                state["first"] = False
            return cx.inc(eng.dma_start(out=out, in_=in_, **k), own, 16)
        return dma

    def emit_bc_from_pack(pack_name, pack_sl, dst_name, ppsum):
        @on("tensor")
        def _(tensor):
            S = mkstep(tensor, "t", "TS")
            S(tensor.transpose, ppsum[0:16, 0:128], sb[pack_name][:, pack_sl],
              sb["c_idf"][:])

        @on("scalar")
        def _(scalar):
            S = mkstep(scalar, "s", "SS")
            S(scalar.copy, sb["st16"][:, 0:128], ppsum[0:16, 0:128])

        @on("sync")
        def _(sync):
            D = mkdma(sync, "sy", "LD")
            for tb in range(NBLK):
                D(sb["row_a"][0:1, tb * 128:(tb + 1) * 128],
                  sb["st16"][tb:tb + 1, 0:128])

        @on("tensor")
        def _(tensor):
            S = mkstep(tensor, "t", "TS")
            for tt in range(4):
                S(tensor.matmul, ppsum[:, tt * 512:(tt + 1) * 512],
                  sb["c_onesrowf"][:], sb["row_a"][0:1, tt * 512:(tt + 1) * 512],
                  start=True, stop=True)

        @on("scalar")
        def _(scalar):
            S = mkstep(scalar, "s", "SS")
            S(scalar.copy, sb[dst_name][:], ppsum[:, 0:2048])

    ph0 = contextlib.ExitStack()
    w1s = sball("w1s", [128, 8 * 512], F32, ph0)
    w2s = sball("w2s", [128, 4 * C], F32, ph0)
    wq1 = sball("wq1", [128, 8 * 512], BF16, ph0)
    wq2 = sball("wq2", [128, 4 * C], BF16, ph0)
    wscr = sball("wscr", [128, 4096], F32, ph0)
    wcol = sball("wcol", [128, 8], F32, ph0)
    wg8 = sball("wg8", [8, 2], F32, ph0)
    t1sb = sball("t1sb", [128, 4 * 2048], F32, ph0)
    t2sb = sball("t2sb", [128, 4 * 2048], F32, ph0)
    wTT = sball("wTT", [128, 4 * 512], F32, ph0)
    MT0 = sball("MT0", [128, 8 * C], BF16, ph0)
    pw = ph0.enter_context(nc.psum_tensor("pw", [128, 512], F32))

    @on("sync")
    def _(sync):
        D = mkdma(sync, "sy", "LD")
        for nm, dst in [("identf", "c_idf"), ("identb", "c_idb"),
                        ("onesb", "c_onesb"), ("ropeR", "c_ropeR"),
                        ("onescolf", "c_onescolf"), ("onesrowf", "c_onesrowf"),
                        ("selw", "c_selw"), ("carryw", "c_carryw"), ("nb30", "c_nb30"),
                        ("gamA", "v_gamA"), ("betA", "v_betA"),
                        ("decay", "v_decay"), ("first", "v_first"),
                        ("b2c", "v_b2c"), ("bTTc", "v_bTTc")]:
            D(sb[dst][:], DT[nm])
        D(w1s[:].rearrange("p (a n) -> p a n", a=8),
          DT["w1T_sl"].rearrange("(a p) n -> p a n", p=128))
        D(w2s[:].rearrange("p (a n) -> p a n", a=4),
          DT["w2T_sl"].rearrange("(a p) n -> p a n", p=128))
        D(t1sb[:].rearrange("p (a n) -> p a n", a=4),
          DT["T1r"].rearrange("(a p) n -> p a n", p=128))
        D(t2sb[:].rearrange("p (a n) -> p a n", a=4),
          DT["T2r"].rearrange("(a p) n -> p a n", p=128))

    @on("scalar")
    def _(scalar):
        S = mkstep(scalar, "s", "SS")
        S(scalar.activation, sb["v_td"][:], sb["v_decay"][:], AF.Sigmoid)
        S(scalar.activation, sb["v_tf"][:], sb["v_first"][:], AF.Sigmoid)

    @on("vector")
    def _(vector):
        S = mkstep(vector, "v", "VS")
        S(vector.tensor_scalar, sb["v_sp"][:], sb["v_td"][:], 1e-38, None, ALU.max)

    @on("scalar")
    def _(scalar):
        S = mkstep(scalar, "s", "SS")
        S(scalar.activation, sb["v_sp"][:], sb["v_sp"][:], AF.Ln)

    @on("vector")
    def _(vector):
        S = mkstep(vector, "v", "VS")
        S(vector.tensor_scalar, sb["v_sp"][:], sb["v_sp"][:], -1.0, None, ALU.mult)

    @on("vector")
    def _(vector):
        S = mkstep(vector, "v", "VS")
        S(vector.reduce_sum, wcol[:, 0:1], w1s[:], axis=AX, apply_absolute_value=True)
        S(vector.reduce_sum, wcol[:, 1:2], w2s[:], axis=AX, apply_absolute_value=True)

    @on("tensor")
    def _(tensor):
        S = mkstep(tensor, "t", "TS")
        S(tensor.matmul, pw[0:1, 0:2], sb["c_onescolf"][:], wcol[:, 0:2],
          start=True, stop=True)

    @on("scalar")
    def _(scalar):
        S = mkstep(scalar, "s", "SS")
        S(scalar.copy, wcol[0:1, 2:4], pw[0:1, 0:2])

    @on("gpsimd")
    def _(g):
        D = mkdma(g, "g", "GD")
        S = mkstep(g, "g", "CC")
        D(DT["cc_wsc_in"], wcol[0:1, 2:4])
        cx.wait(g, "g", ("GD",))
        S(g.collective_compute, "AllGather", ALU.bypass,
          replica_groups=[list(range(8))],
          ins=[DT["cc_wsc_in"].opt()], outs=[DT["cc_wsc_out"].opt()])
        cx.wait(g, "g", ("CC",))
        D(wg8[:], DT["cc_wsc_out"])

    @on("vector")
    def _(vector):
        S = mkstep(vector, "v", "VS")
        S(vector.tensor_copy, wcol[0:8, 4:6], wg8[:, 0:2])

    @on("tensor")
    def _(tensor):
        S = mkstep(tensor, "t", "TS")
        S(tensor.matmul, pw[0:1, 2:4], sb["c_onescolf"][0:8, 0:1], wcol[0:8, 4:6],
          start=True, stop=True)

    @on("vector")
    def _(vector):
        S = mkstep(vector, "v", "VS")
        S(vector.tensor_scalar, wcol[0:1, 6:8], pw[0:1, 2:4],
          1.0 / (C * H4), 1e-5, ALU.mult, ALU.max)

    @on("tensor")
    def _(tensor):
        S = mkstep(tensor, "t", "TS")
        S(tensor.matmul, pw[:, 4:6], sb["c_onesrowf"][:], wcol[0:1, 6:8],
          start=True, stop=True)

    @on("scalar")
    def _(scalar):
        S = mkstep(scalar, "s", "SS")
        S(scalar.copy, sb["v_mw"][:], pw[:, 4:6])

    @on("vector")
    def _(vector):
        S = mkstep(vector, "v", "VS")
        S(vector.reciprocal, sb["v_swc"][:], sb["v_mw"][:])
        for wsrc, dst, sc, n in ((w1s, wq1, 0, 8 * 512), (w2s, wq2, 1, 4 * C)):
            S.c(vector.tensor_scalar, wscr[:, 0:n], wsrc[:], sb["v_swc"][:, sc:sc + 1],
                RC, ALU.mult, ALU.add)
            S.c(vector.tensor_scalar, wscr[:, 0:n], wscr[:, 0:n], RC, None,
                ALU.subtract)
            S.c(vector.tensor_scalar, dst[:], wscr[:, 0:n], 1.0, -1.0, ALU.min,
                ALU.max)

    @on("gpsimd")
    def _(g):
        D = mkdma(g, "g", "GD")
        S = mkstep(g, "g", "CC")
        D(DT["cc_wq1_in"].rearrange("(a p) n -> p a n", p=128),
          wq1[:].rearrange("p (a n) -> p a n", a=8))
        D(DT["cc_wq2_in"].rearrange("(a p) n -> p a n", p=128),
          wq2[:].rearrange("p (a n) -> p a n", a=4))
        cx.wait(g, "g", ("GD",))
        S(g.collective_compute, "AllGather", ALU.bypass,
          replica_groups=[list(range(8))],
          ins=[DT["cc_wq1_in"].opt()], outs=[DT["cc_wq1_out"].opt()])
        S(g.collective_compute, "AllGather", ALU.bypass,
          replica_groups=[list(range(8))],
          ins=[DT["cc_wq2_in"].opt()], outs=[DT["cc_wq2_out"].opt()])

    def emit_tt_build(which):
        @on("vector")
        def _(vector):
            S = mkstep(vector, "v", "VS")
            for pc in range(4):
                base = pc * 2048
                S.c(vector.tensor_mul, wTT[:, pc * 512:(pc + 1) * 512],
                  t1sb[:, base:base + 512], t2sb[:, base:base + 512])
                for c in range(1, 4):
                    s0 = base + c * 512
                    S.c(vector.tensor_mul, wscr[:, 0:512], t1sb[:, s0:s0 + 512],
                      t2sb[:, s0:s0 + 512])
                    S.c(vector.tensor_add, wTT[:, pc * 512:(pc + 1) * 512],
                      wTT[:, pc * 512:(pc + 1) * 512], wscr[:, 0:512])
            for pc in range(4):
                w = wTT[:, pc * 512:(pc + 1) * 512]
                if which == 0:
                    S.c(vector.tensor_copy, MT0[:, pc * C + 0:pc * C + 512], w)
                    S.c(vector.tensor_copy,
                      MT0[:, (4 + pc) * C + 512:(4 + pc) * C + 1024], w)
                else:
                    S.c(vector.tensor_copy, MT0[:, pc * C + 512:pc * C + 1024], w)
                    S.c(vector.tensor_scalar, wscr[:, 0:512], w, -1.0, None, ALU.mult)
                    S.c(vector.tensor_copy,
                      MT0[:, (4 + pc) * C + 0:(4 + pc) * C + 512], wscr[:, 0:512])

    emit_tt_build(0)

    @on("sync")
    def _(sync):
        D = mkdma(sync, "sy", "LD")
        D(t1sb[:].rearrange("p (a n) -> p a n", a=4),
          DT["T1i"].rearrange("(a p) n -> p a n", p=128))
        D(t2sb[:].rearrange("p (a n) -> p a n", a=4),
          DT["T2i"].rearrange("(a p) n -> p a n", p=128))

    emit_tt_build(1)

    @on("sync")
    def _(sync):
        D = mkdma(sync, "sy", "LD")
        D(DT["mt_d"].rearrange("(a p) n -> p a n", p=128),
          MT0[:].rearrange("p (a n) -> p a n", a=NCH))

    barrier()
    ph0.close()

    s_int1f = contextlib.ExitStack()
    int1f = sball("int1f", [128, NCH * BT], BF16, s_int1f)
    ph1 = contextlib.ExitStack()
    xtb = sball("xtb", [128, 2 * C], F32, ph1)
    xfb = sball("xfb", [128, 2 * BT], F32, ph1)
    g1bcs = sball("g1bcs", [128, C], F32, ph1)
    b1bcs = sball("b1bcs", [128, C], F32, ph1)
    scr1 = sball("scr1", [128, C], F32, ph1)
    scr2 = sball("scr2", [128, C], F32, ph1)
    int1t = sball("int1t", [128, NBLK * C], BF16, ph1)
    xn2a = sball("xn2a", [128, NCH * BT], BF16, ph1)
    ph1p = contextlib.ExitStack()
    pstA = ph1p.enter_context(nc.psum_tensor("pstA", [128, 2048], F32))
    pstB = ph1p.enter_context(nc.psum_tensor("pstB", [128, 2048], F32))

    @on("sync")
    def _(sync):
        D = mkdma(sync, "sy", "LD")
        D(g1bcs[:], DT["g1bc"])
        D(b1bcs[:], DT["b1bc"])

    h1mk = {}
    for tb in range(NBLK):
        buf = tb % 2

        @on("sync")
        def _(sync, tb=tb, buf=buf):
            cx.wait_mark(sync, "sy", h1mk.get(("xtb_s", buf)))
            cx.wait_mark(sync, "sy", h1mk.get(("xtb_v", buf)))
            D = mkdma(sync, "sy", "LD", deps=())
            D(xtb[:, buf * C:(buf + 1) * C],
              DT["x_tok"][tb * 128:(tb + 1) * 128, :])

        @on("scalar")
        def _(scalar, tb=tb, buf=buf):
            S = mkstep(scalar, "s", "SS", deps=("LD", "SS"))
            S(scalar.activation, scr1[:], xtb[:, buf * C:(buf + 1) * C], AF.Square,
              accum_out=sb["colscr"][:, tb:tb + 1])
        h1mk[("xtb_s", buf)] = cx.mark("SS")

        @on("vector")
        def _(vector, tb=tb, buf=buf):
            S = mkstep(vector, "v", "VS", deps=("LD",))
            S(vector.reduce_sum, sb["colscr"][:, 16 + tb:17 + tb],
              xtb[:, buf * C:(buf + 1) * C], axis=AX)
        h1mk[("xtb_v", buf)] = cx.mark("VS")

    @on("vector")
    def _(vector):
        S = mkstep(vector, "v", "VS")
        mu = sb["colscr"][:, 32:48]
        var = sb["colscr"][:, 48:64]
        S(vector.tensor_scalar, mu, sb["colscr"][:, 16:32], 1.0 / C, None, ALU.mult)
        S.c(vector.tensor_mul, var, mu, mu)
        S.c(vector.tensor_scalar, sb["st_A"][:], sb["colscr"][:, 0:16], 1.0 / C, None,
            ALU.mult)
        S.c(vector.tensor_sub, var, sb["st_A"][:], var)
        S.c(vector.tensor_scalar, var, var, 1.0 + EPS, EPS * EPS, ALU.mult, ALU.add)

    @on("scalar")
    def _(scalar):
        S = mkstep(scalar, "s", "SS")
        S(scalar.activation, sb["st_B"][:], sb["colscr"][:, 48:64], AF.Sqrt)

    @on("vector")
    def _(vector):
        S = mkstep(vector, "v", "VS")
        S(vector.reciprocal, sb["st_A"][:], sb["st_B"][:])
        S.c(vector.scalar_tensor_tensor, sb["st_B"][:], sb["colscr"][:, 32:48], -1.0,
            sb["st_A"][:], ALU.mult, ALU.mult)

    for tb in range(NBLK):
        buf = tb % 2

        @on("sync")
        def _(sync, tb=tb, buf=buf):
            cx.wait_mark(sync, "sy", h1mk.get(("xtb2", buf)))
            D = mkdma(sync, "sy", "LD", deps=("SS", "VS"))
            D(xtb[:, buf * C:(buf + 1) * C],
              DT["x_tok"][tb * 128:(tb + 1) * 128, :])

        @on("vector")
        def _(vector, tb=tb, buf=buf):
            S = mkstep(vector, "v", "VS", deps=("LD",))
            xb = xtb[:, buf * C:(buf + 1) * C]
            S(vector.tensor_scalar, scr1[:], xb, sb["st_A"][:, tb:tb + 1],
              sb["st_B"][:, tb:tb + 1], ALU.mult, ALU.add)
            S.c(vector.scalar_tensor_tensor, scr2[:], scr1[:], 1.0, g1bcs[:],
                ALU.bypass, ALU.mult)
            S.c(vector.tensor_add, scr2[:], scr2[:], b1bcs[:])
            S.c(vector.reduce_max, sb["colscr"][:, tb:tb + 1], scr2[:], axis=AX,
                apply_absolute_value=True)
            S.c(vector.tensor_scalar, sb["colscr"][:, 16 + tb:17 + tb],
                sb["colscr"][:, tb:tb + 1], 1e-5, None, ALU.max)
            S.c(vector.reciprocal, sb["st_s1"][:, tb:tb + 1],
                sb["colscr"][:, 16 + tb:17 + tb])
            S.c(vector.tensor_scalar, sb["st_s1"][:, tb:tb + 1],
                sb["st_s1"][:, tb:tb + 1], 127.0, None, ALU.mult)
            S.c(vector.scalar_tensor_tensor, sb["st_s1s"][:, tb:tb + 1],
                sb["colscr"][:, 16 + tb:17 + tb], 1.0 / 127.0, sb["v_mw"][:, 0:1],
                ALU.mult, ALU.mult)
            S.c(vector.tensor_scalar, scr2[:], scr2[:], sb["st_s1"][:, tb:tb + 1], RC,
                ALU.mult, ALU.add)
            S.c(vector.tensor_scalar, int1t[:, tb * C:(tb + 1) * C], scr2[:], RC,
                None, ALU.subtract)
        h1mk[("xtb2", buf)] = cx.mark("VS")

    emit_bc_from_pack("st_A", slice(0, 16), "abc", pstA)
    emit_bc_from_pack("st_B", slice(0, 16), "bbc", pstB)

    for ca in range(NCH):
        buf = ca % 2

        @on("sync")
        def _(sync, ca=ca, buf=buf):
            cx.wait_mark(sync, "sy", h1mk.get(("xfb", buf)))
            D = mkdma(sync, "sy", "LD", deps=("SS",))
            D(xfb[:, buf * BT:(buf + 1) * BT],
              DT["x_fm"][ca * 128:(ca + 1) * 128, :])

        @on("vector")
        def _(vector, ca=ca, buf=buf):
            S = mkstep(vector, "v", "VS", deps=("LD",))
            xc = xfb[:, buf * BT:(buf + 1) * BT]
            dst = xn2a[:, ca * BT:(ca + 1) * BT]
            S(vector.tensor_mul, xc, xc, sb["abc"][:])
            S.c(vector.tensor_add, xc, xc, sb["bbc"][:])
            S.c(vector.tensor_scalar, dst, xc, sb["v_gamA"][:, ca:ca + 1],
                sb["v_betA"][:, ca:ca + 1], ALU.mult, ALU.add)
        h1mk[("xfb", buf)] = cx.mark("VS")

    @on("sync")
    def _(sync):
        D = mkdma(sync, "sy", "LD")
        D(DT["xn2a_d"].rearrange("(a p) n -> p a n", p=128),
          xn2a[:].rearrange("p (a n) -> p a n", a=NCH))

    barrier()
    ph1p.close()
    pT0 = ph1.enter_context(nc.psum_tensor("pT0", [128, 512], BF16))
    pT1 = ph1.enter_context(nc.psum_tensor("pT1", [128, 512], BF16))

    for ca in range(NCH):
        for tg in range(4):
            pT = (pT0, pT1)[tg % 2]

            @on("tensor")
            def _(tensor, ca=ca, tg=tg, pT=pT):
                cx.wait(tensor, "t", ("VS",))
                cx.wait_mark(tensor, "t", h1mk.get(("pT", tg % 2)))
                S = mkstep(tensor, "t", "TS", deps=())
                for j in range(4):
                    tb = tg * 4 + j
                    S(tensor.transpose, pT[:, j * 128:(j + 1) * 128],
                      int1t[:, tb * C + ca * 128:tb * C + (ca + 1) * 128],
                      sb["c_idb"][:])

            @on("scalar")
            def _(scalar, ca=ca, tg=tg, pT=pT):
                S = mkstep(scalar, "s", "SS", deps=("TS",))
                S(scalar.copy,
                  int1f[:, ca * BT + tg * 512:ca * BT + (tg + 1) * 512], pT[:])
            h1mk[("pT", tg % 2)] = cx.mark("SS")

    barrier()
    ph1.close()

    phf = contextlib.ExitStack()
    wq1f = sball("wq1f", [128, 64 * 512], BF16, phf)
    b1fs = sball("b1fs", [128, H4], BF16, phf)
    g2s = sball("g2s", [128, H4], BF16, phf)
    be2s = sball("be2s", [128, H4], BF16, phf)
    h2b = sball("h2b", [128, 2 * H4], BF16, phf)
    ln2o = sball("ln2o", [128, H4], F32, phf)
    sqs = sball("sqs", [128, 512], F32, phf)
    fscr = sball("fscr", [128, 4096], BF16, phf)
    int2s = sball("int2s", [128, H4], BF16, phf)
    i2st = sball("i2st", [128, 1024], BF16, phf)
    pF0 = phf.enter_context(nc.psum_tensor("pF0", [128, 512], F32))
    pF1 = phf.enter_context(nc.psum_tensor("pF1", [128, 512], F32))
    pX0 = phf.enter_context(nc.psum_tensor("pX0", [128, 512], BF16))
    pX1 = phf.enter_context(nc.psum_tensor("pX1", [128, 512], BF16))

    @on("sync")
    def _(sync):
        D = mkdma(sync, "sy", "LD")
        for r in range(8):
            for ca in range(NCH):
                D(wq1f[:, (ca * 8 + r) * 512:(ca * 8 + r + 1) * 512],
                  DT["cc_wq1_out"][r * C + ca * 128:r * C + (ca + 1) * 128, :])
    @on("gpsimd")
    def _(g):
        D = mkdma(g, "g", "GD")
        D(b1fs[:], DT["b1fbc"])
        D(g2s[:], DT["g2bc"])
        D(be2s[:], DT["be2bc"])

    fmk = {}
    for tb in range(NBLK):
        h2o = (tb % 2) * H4

        vs_after_stt = {}
        for hc in range(8):
            pF = (pF0, pF1)[hc % 2]

            @on("tensor")
            def _(tensor, tb=tb, hc=hc, pF=pF):
                cx.wait(tensor, "t", ("LD", "GD"))
                cx.wait_mark(tensor, "t", fmk.get(("pF", hc % 2)))
                S = mkstep(tensor, "t", "TS", deps=())
                for ca in range(NCH):
                    S(tensor.matmul, pF[:],
                      int1f[:, ca * BT + tb * 128:ca * BT + (tb + 1) * 128],
                      wq1f[:, (ca * 8 + hc) * 512:(ca * 8 + hc + 1) * 512],
                      start=(ca == 0), stop=(ca == NCH - 1))

            @on("vector")
            def _(vector, tb=tb, hc=hc, pF=pF):
                cx.wait(vector, "v", ("TS", "GD"))
                cx.wait_mark(vector, "v", fmk.get(("fscr",)))
                S = mkstep(vector, "v", "VS", deps=())
                S(vector.scalar_tensor_tensor, fscr[:, hc * 512:(hc + 1) * 512],
                  pF[:], sb["st_s1s"][:, tb:tb + 1],
                  b1fs[:, hc * 512:(hc + 1) * 512], ALU.mult, ALU.add)
            fmk[("pF", hc % 2)] = cx.mark("VS")
            vs_after_stt[hc] = cx.cnt["VS"]

        @on("scalar")
        def _(scalar, tb=tb, h2o=h2o, vs_after_stt=vs_after_stt):
            cx.wait_mark(scalar, "s", fmk.get(("h2b", tb % 2)))
            for hc in range(8):
                cx.wait_value(scalar, "s", "VS", vs_after_stt[hc])
                ins = scalar.activation(
                    h2b[:, h2o + hc * 512:h2o + (hc + 1) * 512],
                    fscr[:, hc * 512:(hc + 1) * 512], AF.Gelu)
                cx.inc(ins, "SS")
        fmk[("fscr",)] = cx.mark("SS")

        @on("vector")
        def _(vector, tb=tb, h2o=h2o):
            cx.wait(vector, "v", ("SS",))
            cx.wait_mark(vector, "v", fmk.get(("ln2o",)))
            S = mkstep(vector, "v", "VS", deps=())
            h2 = h2b[:, h2o:h2o + H4]
            cc = sb["colscr"]
            S(vector.reduce_sum, cc[:, 0:1], h2, axis=AX)
            S.c(vector.tensor_mul, ln2o[:], h2, h2)
            S.c(vector.reduce_sum, cc[:, 8:9], ln2o[:], axis=AX)
            S.c(vector.tensor_scalar, cc[:, 1:2], cc[:, 0:1], 1.0 / H4, None,
                ALU.mult)
            S.c(vector.tensor_mul, cc[:, 2:3], cc[:, 1:2], cc[:, 1:2])
            S.c(vector.tensor_scalar, cc[:, 3:4], cc[:, 8:9], 1.0 / H4, None,
                ALU.mult)
            S.c(vector.tensor_sub, cc[:, 3:4], cc[:, 3:4], cc[:, 2:3])
            S.c(vector.tensor_scalar, cc[:, 3:4], cc[:, 3:4], 1.0, EPS, ALU.mult,
                ALU.add)

        @on("scalar")
        def _(scalar, tb=tb):
            S = mkstep(scalar, "s", "SS", deps=("VS",))
            S(scalar.activation, sb["colscr"][:, 4:5], sb["colscr"][:, 3:4], AF.Sqrt)

        @on("vector")
        def _(vector, tb=tb, h2o=h2o):
            S = mkstep(vector, "v", "VS", deps=("SS",))
            h2 = h2b[:, h2o:h2o + H4]
            cc = sb["colscr"]
            S(vector.reciprocal, cc[:, 5:6], cc[:, 4:5])
            S.c(vector.scalar_tensor_tensor, cc[:, 6:7], cc[:, 1:2], -1.0, cc[:, 5:6],
                ALU.mult, ALU.mult)
            S.c(vector.tensor_scalar, ln2o[:], h2, cc[:, 5:6], cc[:, 6:7],
                ALU.mult, ALU.add)
            S.c(vector.scalar_tensor_tensor, ln2o[:], ln2o[:], 1.0, g2s[:],
                ALU.bypass, ALU.mult)
            S.c(vector.tensor_add, ln2o[:], ln2o[:], be2s[:])
            S.c(vector.reduce_max, cc[:, 7:8], ln2o[:], axis=AX,
                apply_absolute_value=True)
            S.c(vector.tensor_scalar, cc[:, 7:8], cc[:, 7:8], 1e-5, None, ALU.max)
            S.c(vector.reciprocal, cc[:, 2:3], cc[:, 7:8])
            S.c(vector.tensor_scalar, cc[:, 2:3], cc[:, 2:3], 127.0, None, ALU.mult)
            S.c(vector.scalar_tensor_tensor, sb["st_s2i"][:, tb:tb + 1], cc[:, 7:8],
                1.0 / 127.0, sb["v_mw"][:, 1:2], ALU.mult, ALU.mult)
            S.c(vector.tensor_scalar, ln2o[:], ln2o[:], cc[:, 2:3], RC, ALU.mult,
                ALU.add)
            S.c(vector.tensor_scalar, int2s[:], ln2o[:], RC, None, ALU.subtract)
        fmk[("ln2o",)] = cx.mark("VS")
        fmk[("h2b", tb % 2)] = cx.mark("VS")

        for hg in range(8):
            pX = (pX0, pX1)[hg % 2]
            xb = hg % 2

            @on("tensor")
            def _(tensor, tb=tb, hg=hg, pX=pX, xb=xb):
                cx.wait(tensor, "t", ("VS",))
                cx.wait_mark(tensor, "t", fmk.get(("pX", xb)))
                S = mkstep(tensor, "t", "TS", deps=())
                for j in range(4):
                    hh = hg * 4 + j
                    S(tensor.transpose, pX[:, j * 128:(j + 1) * 128],
                      int2s[:, hh * 128:(hh + 1) * 128], sb["c_idb"][:])

            @on("scalar")
            def _(scalar, tb=tb, hg=hg, pX=pX, xb=xb):
                cx.wait(scalar, "s", ("TS",))
                cx.wait_mark(scalar, "s", fmk.get(("i2st", xb)))
                S = mkstep(scalar, "s", "SS", deps=())
                S(scalar.copy, i2st[:, xb * 512:(xb + 1) * 512], pX[:])
            fmk[("pX", xb)] = cx.mark("SS")

            @on("sync")
            def _(sync, tb=tb, hg=hg, xb=xb):
                D = mkdma(sync, "sy", "LD", deps=("SS",))
                for j in range(4):
                    hh = hg * 4 + j
                    D(DT["int2_d"][hh * 128:(hh + 1) * 128, tb * 128:(tb + 1) * 128],
                      i2st[:, xb * 512 + j * 128:xb * 512 + (j + 1) * 128])
            fmk[("i2st", xb)] = cx.mark("LD")

    barrier()
    phf.close()
    s_int1f.close()

    phg = contextlib.ExitStack()
    wq2f = sball("wq2f", [128, 32 * C], BF16, phg)
    i2b = sball("i2b", [128, 32 * 512], BF16, phg)
    s2ib = sball("s2ib", [128, BT], F32, phg)
    gscr = sball("gscr", [128, 1024], F32, phg)
    gst = sball("gst", [128, 1024], F32, phg)
    pG0 = phg.enter_context(nc.psum_tensor("pG0", [128, 512], F32))
    pG1 = phg.enter_context(nc.psum_tensor("pG1", [128, 512], F32))
    pBC = phg.enter_context(nc.psum_tensor("pBC", [128, 2048], F32))

    @on("sync")
    def _(sync):
        D = mkdma(sync, "sy", "LD")
        for hh in range(32):
            r, loc = hh // 4, hh % 4
            D(wq2f[:, hh * C:(hh + 1) * C],
              DT["cc_wq2_out"][r * 512 + loc * 128:r * 512 + (loc + 1) * 128, :])

    emit_bc_from_pack("st_s2i", slice(0, 16), "abc", pBC)

    @on("vector")
    def _(vector):
        S = mkstep(vector, "v", "VS")
        S(vector.tensor_copy, s2ib[:], sb["abc"][:])

    gmk = {}
    for tt in range(4):
        @on("sync")
        def _(sync, tt=tt):
            D = mkdma(sync, "sy", "LD", deps=("TS",))
            for hh in range(32):
                D(i2b[:, hh * 512:(hh + 1) * 512],
                  DT["int2_d"][hh * 128:(hh + 1) * 128, tt * 512:(tt + 1) * 512])

        for dc in range(NCH):
            pG = (pG0, pG1)[dc % 2]
            gb = dc % 2

            @on("tensor")
            def _(tensor, tt=tt, dc=dc, pG=pG, gb=gb):
                cx.wait(tensor, "t", ("LD",))
                cx.wait_mark(tensor, "t", gmk.get(("pG", gb)))
                S = mkstep(tensor, "t", "TS", deps=())
                for hh in range(32):
                    S(tensor.matmul, pG[:],
                      wq2f[:, hh * C + dc * 128:hh * C + (dc + 1) * 128],
                      i2b[:, hh * 512:(hh + 1) * 512],
                      start=(hh == 0), stop=(hh == 31))

            @on("vector")
            def _(vector, tt=tt, dc=dc, pG=pG, gb=gb):
                cx.wait(vector, "v", ("TS",))
                cx.wait_mark(vector, "v", gmk.get(("gscr", gb)))
                S = mkstep(vector, "v", "VS", deps=())
                S(vector.scalar_tensor_tensor, gscr[:, gb * 512:(gb + 1) * 512],
                  pG[:], 1.0, s2ib[:, tt * 512:(tt + 1) * 512], ALU.bypass, ALU.mult)
            gmk[("pG", gb)] = cx.mark("VS")

            @on("scalar")
            def _(scalar, tt=tt, dc=dc, gb=gb):
                cx.wait(scalar, "s", ("VS",))
                cx.wait_mark(scalar, "s", gmk.get(("gst", gb)))
                S = mkstep(scalar, "s", "SS", deps=())
                S(scalar.activation, gst[:, gb * 512:(gb + 1) * 512],
                  gscr[:, gb * 512:(gb + 1) * 512], AF.Silu,
                  bias=sb["v_b2c"][:, dc:dc + 1])
            gmk[("gscr", gb)] = cx.mark("SS")

            @on("sync")
            def _(sync, tt=tt, dc=dc, gb=gb):
                D = mkdma(sync, "sy", "LD", deps=("SS",))
                D(DT["g0_d"][dc * 128:(dc + 1) * 128, tt * 512:(tt + 1) * 512],
                  gst[:, gb * 512:(gb + 1) * 512])
            gmk[("gst", gb)] = cx.mark("LD")

    barrier()
    phg.close()

    s_carry = contextlib.ExitStack()
    rwkv = sball("rwkv", [128, NCH * BT], BF16, s_carry)
    pha = contextlib.ExitStack()
    xn2r = sball("xn2r", [128, NCH * BT], BF16, pha)
    wst = sball("wst", [128, 2 * 1024], F32, pha)
    wbt = sball("wbt", [128, 2 * 1024], BF16, pha)
    kbuf = sball("kbuf", [128, BT], BF16, pha)
    vbuf = sball("vbuf", [128, BT], BF16, pha)
    kvb = sball("kvb", [128, BT], F32, pha)
    ubuf = sball("ubuf", [128, BT], F32, pha)
    pbuf = sball("pbuf", [128, 512], F32, pha)
    sS = sball("sS", [128, BT + 1], F32, pha)
    wkvb = sball("wkvb", [128, BT], F32, pha)
    tvalsb = sball("tvalsb", [128, BT], F32, pha)
    sigr = sball("sigr", [128, NCH * BT], BF16, pha)
    totc = sball("totc", [128, 8], F32, pha)
    cgat = sball("cgat", [128, 16], F32, pha)
    pK = pha.enter_context(nc.psum_tensor("pK", [128, 512], F32))
    pV = pha.enter_context(nc.psum_tensor("pV", [128, 512], F32))
    pR = pha.enter_context(nc.psum_tensor("pR", [128, 512], F32))

    @on("sync")
    def _(sync):
        D = mkdma(sync, "sy", "LD")
        D(xn2r[:].rearrange("p (a n) -> p a n", a=NCH),
          DT["xn2a_d"].rearrange("(a p) n -> p a n", p=128))
        D(tvalsb[:], DT["tvals"])

    pmk = {}

    @on("vector")
    def _(vector):
        S = mkstep(vector, "v", "VS")
        S(vector.memset, sS[:, 0:1], 0.0)

    for dc in range(NCH):
        @on("vector")
        def _(vector, dc=dc):
            S = mkstep(vector, "v", "VS")
            S(vector.memset, wkvb[:], 0.0)

        for wi, wname in enumerate(("WkT", "WvT", "WrT")):
            wb = (dc * 3 + wi) % 2

            @on("sync")
            def _(sync, dc=dc, wname=wname, wb=wb):
                cx.wait_mark(sync, "sy", pmk.get(("wst", wb)))
                D = mkdma(sync, "sy", "LD", deps=())
                D(wst[:, wb * 1024:(wb + 1) * 1024].rearrange(
                      "p (a n) -> p a n", a=8),
                  DT[wname][:, dc * 128:(dc + 1) * 128].rearrange(
                      "(a p) n -> p a n", p=128))

            @on("vector")
            def _(vector, wb=wb):
                cx.wait(vector, "v", ("LD",))
                cx.wait_mark(vector, "v", pmk.get(("wbt", wb)))
                S = mkstep(vector, "v", "VS", deps=())
                S(vector.tensor_copy, wbt[:, wb * 1024:(wb + 1) * 1024],
                  wst[:, wb * 1024:(wb + 1) * 1024])
            pmk[("wst", wb)] = cx.mark("VS")

            for tt in range(4):
                pdst = (pK, pV, pR)[wi]

                @on("tensor")
                def _(tensor, dc=dc, tt=tt, pdst=pdst, wb=wb, wi=wi):
                    cx.wait(tensor, "t", ("VS",))
                    cx.wait_mark(tensor, "t", pmk.get(("pkevac", wi)))
                    S = mkstep(tensor, "t", "TS", deps=())
                    for ca in range(NCH):
                        S(tensor.matmul, pdst[:],
                          wbt[:, wb * 1024 + ca * 128:wb * 1024 + (ca + 1) * 128],
                          xn2r[:, ca * BT + tt * 512:ca * BT + (tt + 1) * 512],
                          start=(ca == 0), stop=(ca == NCH - 1))

                if wi == 0:
                    @on("scalar")
                    def _(scalar, tt=tt):
                        S = mkstep(scalar, "s", "SS", deps=("TS", "VS"))
                        S(scalar.copy, kbuf[:, tt * 512:(tt + 1) * 512], pK[:])
                elif wi == 1:
                    @on("scalar")
                    def _(scalar, tt=tt):
                        S = mkstep(scalar, "s", "SS", deps=("TS", "VS"))
                        S(scalar.copy, vbuf[:, tt * 512:(tt + 1) * 512], pV[:])
                else:
                    @on("scalar")
                    def _(scalar, dc=dc, tt=tt):
                        S = mkstep(scalar, "s", "SS", deps=("TS", "VS"))
                        S(scalar.activation,
                          sigr[:, dc * BT + tt * 512:dc * BT + (tt + 1) * 512],
                          pR[:], AF.Sigmoid)
                pmk[("pkevac", wi)] = cx.mark("SS")
            pmk[("wbt", wb)] = cx.mark("TS")

        for tt in range(4):
            sl = slice(tt * 512, (tt + 1) * 512)

            @on("vector")
            def _(vector, dc=dc, sl=sl):
                S = mkstep(vector, "v", "VS", deps=("SS", "LD"))
                S(vector.tensor_mul, kvb[:, sl], kbuf[:, sl], vbuf[:, sl])
                S(vector.tensor_scalar, pbuf[:], tvalsb[:, sl],
                  sb["v_sp"][:, dc:dc + 1], None, ALU.mult)

            @on("scalar")
            def _(scalar):
                S = mkstep(scalar, "s", "SS", deps=("VS",))
                S(scalar.activation, pbuf[:], pbuf[:], AF.Exp, scale=-1.0)

            @on("vector")
            def _(vector, sl=sl):
                S = mkstep(vector, "v", "VS", deps=("SS",))
                S(vector.tensor_mul, ubuf[:, sl], kvb[:, sl], pbuf[:])

        @on("vector")
        def _(vector, dc=dc):
            S = mkstep(vector, "v", "VS", deps=("SS",))
            S(vector.tensor_tensor_scan, sS[:, 1:BT + 1], wkvb[:], ubuf[:], 0.0,
              ALU.add, ALU.add)
            S.c(vector.tensor_copy, totc[:, dc:dc + 1], sS[:, BT:BT + 1])
            S.c(vector.tensor_scalar, wkvb[:], kvb[:], sb["v_tf"][:, dc:dc + 1], None,
                ALU.mult)
            S.c(vector.scalar_tensor_tensor, wkvb[:], sS[:, 0:BT],
                sb["v_td"][:, dc:dc + 1], wkvb[:], ALU.mult, ALU.add)
            S.c(vector.tensor_mul, rwkv[:, dc * BT:(dc + 1) * BT],
                sigr[:, dc * BT:(dc + 1) * BT], wkvb[:])

    @on("gpsimd")
    def _(g):
        D = mkdma(g, "g", "GD")
        S = mkstep(g, "g", "CC")
        D(DT["cc_carry_in"].rearrange("o (a p) -> (o p) a", p=128), totc[:],
          allow_slow_non_contiguous=True)
        cx.wait(g, "g", ("GD",))
        S(g.collective_compute, "AllGather", ALU.bypass,
          replica_groups=[[0, 1], [2, 3], [4, 5], [6, 7]],
          ins=[DT["cc_carry_in"].opt()], outs=[DT["cc_carry_out"].opt()])
        cx.wait(g, "g", ("CC",))
        D(cgat[:, 0:8],
          DT["cc_carry_out"][0:1, :].rearrange("o (a p) -> (o p) a", p=128),
          allow_slow_non_contiguous=True)
        D(cgat[:, 8:16],
          DT["cc_carry_out"][1:2, :].rearrange("o (a p) -> (o p) a", p=128),
          allow_slow_non_contiguous=True)

    @on("vector")
    def _(vector):
        S = mkstep(vector, "v", "VS")
        S(vector.tensor_scalar, sb["carry_fix"][:], cgat[:, 0:8],
          sb["c_carryw"][:, 0:1], None, ALU.mult)
        S.c(vector.scalar_tensor_tensor, sb["carry_fix"][:], cgat[:, 8:16],
            sb["c_carryw"][:, 1:2], sb["carry_fix"][:], ALU.mult, ALU.add)
        S.c(vector.tensor_mul, sb["carry_fix"][:], sb["carry_fix"][:], sb["v_td"][:])
        for dc in range(NCH):
            sl = slice(dc * BT, (dc + 1) * BT)
            S.c(vector.scalar_tensor_tensor, rwkv[:, sl], sigr[:, sl],
                sb["carry_fix"][:, dc:dc + 1], rwkv[:, sl], ALU.mult, ALU.add)

    barrier()
    pha.close()

    s_x1 = contextlib.ExitStack()
    x1f = sball("x1f", [128, NCH * BT], F32, s_x1)
    phr = contextlib.ExitStack()
    woT = sball("woT", [128, NCH * C], F32, phr)
    woTb = sball("woTb", [128, NCH * C], BF16, phr)
    g0t = sball("g0t", [128, 2 * 512], F32, phr)
    xft = sball("xft", [128, 2 * 512], F32, phr)
    pS0 = phr.enter_context(nc.psum_tensor("pS0", [128, 512], F32))
    pS1 = phr.enter_context(nc.psum_tensor("pS1", [128, 512], F32))

    @on("sync")
    def _(sync):
        D = mkdma(sync, "sy", "LD")
        D(woT[:].rearrange("p (a n) -> p a n", a=NCH),
          DT["WoT"].rearrange("(a p) n -> p a n", p=128))

    @on("vector")
    def _(vector):
        S = mkstep(vector, "v", "VS")
        S(vector.tensor_copy, woTb[:], woT[:])

    for dc in range(NCH):
        for tt in range(4):
            pS = (pS0, pS1)[tt % 2]
            buf = tt % 2

            @on("sync")
            def _(sync, dc=dc, tt=tt, buf=buf):
                D = mkdma(sync, "sy", "LD", deps=("VS",))
                D(g0t[:, buf * 512:(buf + 1) * 512],
                  DT["g0_d"][dc * 128:(dc + 1) * 128, tt * 512:(tt + 1) * 512])
                D(xft[:, buf * 512:(buf + 1) * 512],
                  DT["x_fm"][dc * 128:(dc + 1) * 128, tt * 512:(tt + 1) * 512])

            @on("tensor")
            def _(tensor, dc=dc, tt=tt, pS=pS):
                S = mkstep(tensor, "t", "TS", deps=("VS",))
                for ca in range(NCH):
                    S(tensor.matmul, pS[:],
                      woTb[:, ca * C + dc * 128:ca * C + (dc + 1) * 128],
                      rwkv[:, ca * BT + tt * 512:ca * BT + (tt + 1) * 512],
                      start=(ca == 0), stop=(ca == NCH - 1))

            @on("vector")
            def _(vector, dc=dc, tt=tt, pS=pS, buf=buf):
                S = mkstep(vector, "v", "VS", deps=("TS", "LD"))
                dst = x1f[:, dc * BT + tt * 512:dc * BT + (tt + 1) * 512]
                S(vector.tensor_mul, dst, pS[:], g0t[:, buf * 512:(buf + 1) * 512])
                S.c(vector.tensor_add, dst, dst, xft[:, buf * 512:(buf + 1) * 512])

    @on("sync")
    def _(sync):
        D = mkdma(sync, "sy", "LD")
        D(DT["x1_d"].rearrange("(a p) n -> p a n", p=128),
          x1f[:].rearrange("p (a n) -> p a n", a=NCH))

    barrier()
    phr.close()

    ph2 = contextlib.ExitStack()
    xcb = sball("xcb", [128, 2 * 512], BF16, ph2)
    MTl = sball("MTl", [128, 8 * C], BF16, ph2)
    x1sq = sball("x1sq", [128, 2 * 512], BF16, ph2)
    xn1b = sball("xn1b", [128, NCH * BT], BF16, ph2)
    l1scr = sball("l1scr", [128, 512], F32, ph2)
    g1st = sball("g1st", [128, 512], F32, ph2)
    pM0 = ph2.enter_context(nc.psum_tensor("pM0", [128, 512], F32))
    pM1 = ph2.enter_context(nc.psum_tensor("pM1", [128, 512], F32))
    pGa = ph2.enter_context(nc.psum_tensor("pGa", [128, 512], F32))
    pGb = ph2.enter_context(nc.psum_tensor("pGb", [128, 512], F32))

    @on("sync")
    def _(sync):
        D = mkdma(sync, "sy", "LD")
        D(MTl[:].rearrange("p (a n) -> p a n", a=NCH),
          DT["mt_d"].rearrange("(a p) n -> p a n", p=128))

    for tt in range(4):
        for ca in range(NCH):
            buf = ca % 2
            xc = xcb[:, buf * 512:(buf + 1) * 512]
            sq = x1sq[:, buf * 512:(buf + 1) * 512]

            @on("vector")
            def _(vector, tt=tt, ca=ca, xc=xc):
                S = mkstep(vector, "v", "VS", deps=("TS",))
                S(vector.tensor_copy, xc,
                  x1f[:, ca * BT + tt * 512:ca * BT + (tt + 1) * 512])

            @on("scalar")
            def _(scalar, xc=xc, sq=sq):
                S = mkstep(scalar, "s", "SS", deps=("VS", "TS"))
                S(scalar.activation, sq, xc, AF.Square)

            @on("tensor")
            def _(tensor, ca=ca, xc=xc, sq=sq):
                S = mkstep(tensor, "t", "TS", deps=("VS", "SS"))
                S(tensor.matmul, pM0[:], sb["c_onesb"][:], xc,
                  start=(ca == 0), stop=(ca == NCH - 1))
                S(tensor.matmul, pM1[:], sb["c_onesb"][:], sq,
                  start=(ca == 0), stop=(ca == NCH - 1))

        @on("scalar")
        def _(scalar, tt=tt):
            S = mkstep(scalar, "s", "SS")
            S(scalar.activation, l1scr[:], pM0[:], AF.Square, scale=1.0 / C)

        @on("vector")
        def _(vector, tt=tt):
            S = mkstep(vector, "v", "VS", deps=("TS", "SS"))
            sl = slice(tt * 512, (tt + 1) * 512)
            S(vector.tensor_scalar, sb["abc"][:, sl], pM1[:], 1.0 / C, EPS,
              ALU.mult, ALU.add)
            S.c(vector.tensor_sub, sb["abc"][:, sl], sb["abc"][:, sl], l1scr[:])

        @on("scalar")
        def _(scalar, tt=tt):
            S = mkstep(scalar, "s", "SS")
            sl = slice(tt * 512, (tt + 1) * 512)
            S(scalar.activation, sb["bbc"][:, sl], sb["abc"][:, sl], AF.Sqrt)

        @on("vector")
        def _(vector, tt=tt):
            S = mkstep(vector, "v", "VS", deps=("SS",))
            sl = slice(tt * 512, (tt + 1) * 512)
            S(vector.reciprocal, sb["abc"][:, sl], sb["bbc"][:, sl])
            S.c(vector.scalar_tensor_tensor, sb["bbc"][:, sl], pM0[:], -1.0 / C,
                sb["abc"][:, sl], ALU.mult, ALU.mult)

    @on("vector")
    def _(vector):
        S = mkstep(vector, "v", "VS")
        for ca in range(NCH):
            sl = slice(ca * BT, (ca + 1) * BT)
            S.c(vector.tensor_mul, x1f[:, sl], x1f[:, sl], sb["abc"][:])
            S.c(vector.tensor_add, xn1b[:, sl], x1f[:, sl], sb["bbc"][:])

    @on("gpsimd")
    def _(g):
        D = mkdma(g, "g", "GD")
        S = mkstep(g, "g", "CC")
        D(DT["cc_xn1_in"].rearrange("(a p) n -> p a n", p=128),
          xn1b[:].rearrange("p (a n) -> p a n", a=NCH))
        cx.wait(g, "g", ("GD",))
        S(g.collective_compute, "AllGather", ALU.bypass,
          replica_groups=[[0, 1], [2, 3], [4, 5], [6, 7]],
          ins=[DT["cc_xn1_in"].opt()], outs=[DT["cc_xn1_out"].opt()])

    for dc in range(NCH):
        for tt in range(4):
            pG = (pGa, pGb)[tt % 2]

            @on("tensor")
            def _(tensor, dc=dc, tt=tt, pG=pG):
                S = mkstep(tensor, "t", "TS", deps=("VS", "SS", "LD"))
                for ca in range(NCH):
                    S(tensor.matmul, pG[:],
                      MTl[:, ca * C + dc * 128:ca * C + (dc + 1) * 128],
                      xn1b[:, ca * BT + tt * 512:ca * BT + (tt + 1) * 512],
                      start=(ca == 0), stop=(ca == NCH - 1))

            @on("scalar")
            def _(scalar, dc=dc, tt=tt, pG=pG):
                S = mkstep(scalar, "s", "SS", deps=("TS", "LD"))
                S(scalar.activation, g1st[:], pG[:], AF.Silu,
                  bias=sb["v_bTTc"][:, dc:dc + 1])

            @on("sync")
            def _(sync, dc=dc, tt=tt):
                D = mkdma(sync, "sy", "LD", deps=("SS",))
                D(DT["g1_d"][dc * 128:(dc + 1) * 128, tt * 512:(tt + 1) * 512],
                  g1st[:])

    barrier()
    ph2.close()
    s_x1.close()
    s_carry.close()

    s_att = contextlib.ExitStack()
    Qh = sball("Qh", [128, 4 * T], BF16, s_att)
    Kh = sball("Kh", [128, 4 * T], BF16, s_att)
    Vsb = sball("Vsb", [128, 32 * 520], BF16, s_att)
    ph3 = contextlib.ExitStack()
    wqs = sball("wqs", [128, 2 * 1536], F32, ph3)
    wqb = sball("wqb", [128, NCH * 1536], BF16, ph3)
    cosQs = sball("cosQs", [128, T], BF16, ph3)
    sinQs = sball("sinQs", [128, T], BF16, ph3)
    cosKs = sball("cosKs", [128, T], BF16, ph3)
    sinKs = sball("sinKs", [128, T], BF16, ph3)
    xg = sball("xg", [128, NCH * 512], BF16, ph3)
    qraw = sball("qraw", [128, 512], BF16, ph3)
    t1b = sball("t1b", [128, 512], BF16, ph3)
    t2b = sball("t2b", [128, 512], BF16, ph3)
    pQ0 = ph3.enter_context(nc.psum_tensor("pQ0", [128, 512], F32))
    pQ1 = ph3.enter_context(nc.psum_tensor("pQ1", [128, 512], F32))
    pRot = ph3.enter_context(nc.psum_tensor("pRot", [128, 512], F32))
    pVv = ph3.enter_context(nc.psum_tensor("pVv", [128, 512], F32))

    @on("sync")
    def _(sync):
        D = mkdma(sync, "sy", "LD")
        for nm, dst in (("cosQ", cosQs), ("sinQ", sinQs),
                        ("cosK", cosKs), ("sinK", sinKs)):
            D(dst[:], DT[nm])

    for half in range(4):
        @on("sync")
        def _(sync, half=half):
            D = mkdma(sync, "sy", "LD")
            D(wqs[:].rearrange("p (a n) -> p a n", a=2),
              DT["WqkvT_sl"][half * 256:(half + 1) * 256, :].rearrange(
                  "(a p) n -> p a n", p=128))

        @on("vector")
        def _(vector, half=half):
            S = mkstep(vector, "v", "VS")
            S(vector.tensor_copy, wqb[:, half * 2 * 1536:(half + 1) * 2 * 1536],
              wqs[:])

    @on("vector")
    def _(vector):
        S = mkstep(vector, "v", "VS")
        S(vector.memset,
          Vsb[:].rearrange("p (k h e) -> p k h e", k=32, h=8)[:, :, :, 64:65], 1.0)

    for tt in range(8):
        r, tl = tt // 4, tt % 4

        @on("sync")
        def _(sync, r=r, tl=tl):
            D = mkdma(sync, "sy", "LD", deps=("CC", "TS"))
            for ca in range(NCH):
                D(xg[:, ca * 512:(ca + 1) * 512],
                  DT["cc_xn1_out"][r * C + ca * 128:r * C + (ca + 1) * 128,
                                   tl * 512:(tl + 1) * 512])

        for qc in range(8):
            pQ = (pQ0, pQ1)[qc % 2]
            isq = qc < 4
            hp = qc % 4
            cosT, sinT = (cosQs, sinQs) if isq else (cosKs, sinKs)
            dstT = Qh if isq else Kh

            @on("tensor")
            def _(tensor, tt=tt, qc=qc, pQ=pQ):
                S = mkstep(tensor, "t", "TS", deps=("LD", "SS", "VS"))
                for ca in range(NCH):
                    S(tensor.matmul, pQ[:],
                      wqb[:, ca * 1536 + qc * 128:ca * 1536 + (qc + 1) * 128],
                      xg[:, ca * 512:(ca + 1) * 512],
                      start=(ca == 0), stop=(ca == NCH - 1))

            @on("scalar")
            def _(scalar, pQ=pQ):
                S = mkstep(scalar, "s", "SS", deps=("TS", "VS"))
                S(scalar.copy, qraw[:], pQ[:])

            @on("tensor")
            def _(tensor):
                S = mkstep(tensor, "t", "TS", deps=("SS", "VS"))
                S(tensor.matmul, pRot[:], sb["c_ropeR"][:], qraw[:],
                  start=True, stop=True)

            @on("vector")
            def _(vector, tt=tt, hp=hp, cosT=cosT, sinT=sinT, dstT=dstT):
                S = mkstep(vector, "v", "VS", deps=("SS", "TS"))
                sl = slice(tt * 512, (tt + 1) * 512)
                S(vector.tensor_mul, t1b[:], qraw[:], cosT[:, sl])
                S(vector.tensor_mul, t2b[:], pRot[:], sinT[:, sl])
                S.c(vector.tensor_add,
                    dstT[:, hp * T + tt * 512:hp * T + (tt + 1) * 512], t1b[:],
                    t2b[:])

        for tb in range(4):
            kt = tt * 4 + tb

            @on("tensor")
            def _(tensor, tb=tb):
                S = mkstep(tensor, "t", "TS", deps=("LD", "SS", "VS"))
                for ca in range(NCH):
                    S(tensor.matmul, pVv[:],
                      xg[:, ca * 512 + tb * 128:ca * 512 + (tb + 1) * 128],
                      wqb[:, ca * 1536 + 1024:ca * 1536 + 1536],
                      start=(ca == 0), stop=(ca == NCH - 1))

            @on("scalar")
            def _(scalar, kt=kt):
                S = mkstep(scalar, "s", "SS", deps=("TS", "VS"))
                S(scalar.copy,
                  Vsb[:, kt * 520:(kt + 1) * 520].rearrange(
                      "p (h e) -> p h e", h=8)[:, :, 0:64],
                  pVv[:].rearrange("p (h e) -> p h e", h=8))

    barrier()
    ph3.close()

    ph4 = contextlib.ExitStack()
    pTb = sball("pTb", [128, 2048], BF16, ph4)
    dmaskP = sball("dmaskP_s", [128, 4096], BF16, ph4)
    lst = sball("lst", [128, 1024], F32, ph4)
    l0r = sball("l0r", [1, 1024], F32, ph4)
    bcs = sball("bcs", [64, 1024], F32, ph4)
    outn = sball("outn", [64, 1024], BF16, ph4)
    pSmA = ph4.enter_context(nc.psum_tensor("pSmA", [128, 1024], F32))
    pSmB = ph4.enter_context(nc.psum_tensor("pSmB", [128, 1024], F32))
    pAV0 = ph4.enter_context(nc.psum_tensor("pAV0", [128, 512], F32))
    pAV1 = ph4.enter_context(nc.psum_tensor("pAV1", [128, 512], F32))
    pDV = ph4.enter_context(nc.psum_tensor("pDV", [128, 512], F32))

    @on("sync")
    def _(sync):
        D = mkdma(sync, "sy", "LD")
        D(dmaskP[:], DT["dmaskP"])

    amk = {}
    for qt in range(8):
        for hp in range(4):
            ngroups = 4 * qt + 4
            S0 = cx.cnt["SS"]
            V0 = cx.cnt["VS"]
            ts_after_st = {}
            ts_after_av = {}

            def emit_st(kg, qt=qt, hp=hp, S0=S0, ts_after_st=ts_after_st):
                sm = kg % 2
                pSm = (pSmA, pSmB)[sm]

                @on("tensor")
                def _(tensor, kg=kg, sm=sm, pSm=pSm):
                    cx.wait(tensor, "t", ("LD",))
                    cx.wait_value(tensor, "t", "SS", S0 + kg - 1)
                    S = mkstep(tensor, "t", "TS", deps=())
                    for sub in range(2):
                        prt = slice(64 * sub, 64 * sub + 64)
                        S(tensor.matmul, pSm[:, sub * 512:(sub + 1) * 512],
                          Kh[prt, hp * T + kg * 128:hp * T + (kg + 1) * 128],
                          Qh[prt, hp * T + qt * 512:hp * T + (qt + 1) * 512],
                          start=True, stop=True)
                ts_after_st[kg] = cx.cnt["TS"]

            emit_st(0)
            for kg in range(ngroups):
                sm = kg % 2
                last = (kg == ngroups - 1)
                if not last:
                    emit_st(kg + 1)

                @on("tensor")
                def _(tensor, qt=qt, hp=hp, kg=kg, last=last, sm=sm,
                      ngroups=ngroups, S0=S0, V0=V0):
                    cx.wait_value(tensor, "t", "SS", S0 + kg + 1)
                    if kg >= ngroups - 4:
                        cx.wait_value(tensor, "t", "VS",
                                      V0 + (kg - (ngroups - 4)) + 1)
                    if kg == 0:
                        cx.wait_mark(tensor, "t", amk.get(("pAV", 0)))
                        cx.wait_mark(tensor, "t", amk.get(("pAV", 1)))
                    S = mkstep(tensor, "t", "TS", deps=())
                    for sub in range(2):
                        h = 2 * hp + sub
                        pAV = (pAV0, pAV1)[sub]
                        S(tensor.matmul, pAV[0:65, :],
                          Vsb[:, kg * 520 + h * 65:kg * 520 + (h + 1) * 65],
                          pTb[:, sm * 1024 + sub * 512:sm * 1024 + (sub + 1) * 512],
                          start=(kg == 0), stop=last)
                ts_after_av[kg] = cx.cnt["TS"]

            @on("scalar")
            def _(scalar, ngroups=ngroups, ts_after_st=ts_after_st,
                  ts_after_av=ts_after_av):
                for kg in range(ngroups):
                    sm = kg % 2
                    pSm = (pSmA, pSmB)[sm]
                    cx.wait_value(scalar, "s", "TS", ts_after_st[kg])
                    if kg >= 2:
                        cx.wait_value(scalar, "s", "TS", ts_after_av[kg - 2])
                    ins = scalar.activation(pTb[:, sm * 1024:(sm + 1) * 1024],
                                            pSm[:, 0:1024], AF.Exp,
                                            bias=sb["c_nb30"][:, 0:1])
                    cx.inc(ins, "SS")

            @on("vector")
            def _(vector, ngroups=ngroups, S0=S0):
                cx.wait(vector, "v", ("LD",))
                for kg in range(ngroups - 4, ngroups):
                    sm = kg % 2
                    moff = (kg - (ngroups - 4)) * 1024
                    cx.wait_value(vector, "v", "SS", S0 + kg + 1)
                    ins = vector.tensor_mul(pTb[:, sm * 1024:(sm + 1) * 1024],
                                            pTb[:, sm * 1024:(sm + 1) * 1024],
                                            dmaskP[:, moff:moff + 1024])
                    cx.inc(ins, "VS")

            for sub in range(2):
                h = 2 * hp + sub
                hb = sub
                pAV = (pAV0, pAV1)[sub]

                @on("vector")
                def _(vector, hb=hb, pAV=pAV):
                    cx.wait(vector, "v", ("TS",))
                    cx.wait_mark(vector, "v", amk.get(("lst", hb)))
                    S = mkstep(vector, "v", "VS", deps=())
                    S(vector.tensor_copy, lst[64:65, hb * 512:(hb + 1) * 512],
                      pAV[64:65, :])

                @on("sync")
                def _(sync, hb=hb):
                    D = mkdma(sync, "sy", "LD", deps=("VS",))
                    D(l0r[0:1, hb * 512:(hb + 1) * 512],
                      lst[64:65, hb * 512:(hb + 1) * 512])
                amk[("lst", hb)] = cx.mark("LD")

                @on("vector")
                def _(vector, hb=hb):
                    S = mkstep(vector, "v", "VS", deps=("LD",))
                    S(vector.reciprocal, l0r[0:1, hb * 512:(hb + 1) * 512],
                      l0r[0:1, hb * 512:(hb + 1) * 512])

                @on("tensor")
                def _(tensor, hb=hb):
                    cx.wait(tensor, "t", ("VS",))
                    cx.wait_mark(tensor, "t", amk.get(("pDV",)))
                    S = mkstep(tensor, "t", "TS", deps=())
                    S(tensor.matmul, pDV[0:64, :], sb["c_onesrowf"][0:1, 0:64],
                      l0r[0:1, hb * 512:(hb + 1) * 512], start=True, stop=True)

                @on("vector")
                def _(vector, hb=hb, pAV=pAV):
                    cx.wait(vector, "v", ("TS",))
                    cx.wait_mark(vector, "v", amk.get(("outn", hb)))
                    S = mkstep(vector, "v", "VS", deps=())
                    S(vector.tensor_copy, bcs[:, hb * 512:(hb + 1) * 512],
                      pDV[0:64, :])
                    S.c(vector.tensor_mul, outn[:, hb * 512:(hb + 1) * 512],
                        pAV[0:64, :], bcs[:, hb * 512:(hb + 1) * 512])
                amk[("pDV",)] = cx.mark("VS")
                amk[("pAV", hb)] = cx.mark("VS")

                @on("sync")
                def _(sync, qt=qt, h=h, hb=hb):
                    D = mkdma(sync, "sy", "LD", deps=("VS",))
                    D(DT["cc_out_in"][h * 64:(h + 1) * 64, qt * 512:(qt + 1) * 512],
                      outn[:, hb * 512:(hb + 1) * 512])
                amk[("outn", hb)] = cx.mark("LD")

    @on("gpsimd")
    def _(g):
        S = mkstep(g, "g", "CC")
        S(g.collective_compute, "AllGather", ALU.bypass,
          replica_groups=[[0, 1], [2, 3], [4, 5], [6, 7]],
          ins=[DT["cc_out_in"].opt()], outs=[DT["cc_out_out"].opt()])

    barrier()
    ph4.close()
    s_att.close()

    ph5 = contextlib.ExitStack()
    wo1 = sball("wo1", [128, NCH * C], F32, ph5)
    wo1b = sball("wo1b", [128, NCH * C], BF16, ph5)
    gA = sball("gA", [128, BT], BF16, ph5)
    gB = sball("gB", [128, BT], BF16, ph5)
    omy = sball("omy", [128, NCH * BT], BF16, ph5)
    oscr = sball("oscr", [128, BT], F32, ph5)
    g1t = sball("g1t", [128, 2 * 512], F32, ph5)
    x1t = sball("x1t", [128, 2 * 512], F32, ph5)
    x2st = sball("x2st", [128, 2 * 512], F32, ph5)
    pZ0 = ph5.enter_context(nc.psum_tensor("pZ0", [128, 512], F32))
    pZ1 = ph5.enter_context(nc.psum_tensor("pZ1", [128, 512], F32))

    @on("sync")
    def _(sync):
        D = mkdma(sync, "sy", "LD")
        D(wo1[:].rearrange("p (a n) -> p a n", a=NCH),
          DT["Wo1T"].rearrange("(a p) n -> p a n", p=128))

    @on("vector")
    def _(vector):
        S = mkstep(vector, "v", "VS")
        S(vector.tensor_copy, wo1b[:], wo1[:])

    for r in range(2):
        for cc4 in range(4):
            dc = r * 4 + cc4
            row = r * 512 + cc4 * 128

            @on("sync")
            def _(sync, row=row):
                D = mkdma(sync, "sy", "LD", deps=("VS", "CC"))
                D(gA[:], DT["cc_out_out"][row:row + 128, 0:BT])
                D(gB[:], DT["cc_out_out"][row:row + 128, BT:T])

            @on("vector")
            def _(vector, dc=dc):
                S = mkstep(vector, "v", "VS", deps=("LD",))
                S(vector.tensor_scalar, oscr[:], gA[:], sb["c_selw"][:, 0:1], None,
                  ALU.mult)
                S.c(vector.scalar_tensor_tensor, oscr[:], gB[:], sb["c_selw"][:, 1:2],
                    oscr[:], ALU.mult, ALU.add)
                S.c(vector.tensor_copy, omy[:, dc * BT:(dc + 1) * BT], oscr[:])

    for dc in range(NCH):
        for tt in range(4):
            pZ = (pZ0, pZ1)[tt % 2]
            buf = tt % 2

            @on("sync")
            def _(sync, dc=dc, tt=tt, buf=buf):
                D = mkdma(sync, "sy", "LD", deps=("VS",))
                D(g1t[:, buf * 512:(buf + 1) * 512],
                  DT["g1_d"][dc * 128:(dc + 1) * 128, tt * 512:(tt + 1) * 512])
                D(x1t[:, buf * 512:(buf + 1) * 512],
                  DT["x1_d"][dc * 128:(dc + 1) * 128, tt * 512:(tt + 1) * 512])

            @on("tensor")
            def _(tensor, dc=dc, tt=tt, pZ=pZ):
                S = mkstep(tensor, "t", "TS", deps=("VS",))
                for ca in range(NCH):
                    S(tensor.matmul, pZ[:],
                      wo1b[:, ca * C + dc * 128:ca * C + (dc + 1) * 128],
                      omy[:, ca * BT + tt * 512:ca * BT + (tt + 1) * 512],
                      start=(ca == 0), stop=(ca == NCH - 1))

            @on("vector")
            def _(vector, dc=dc, tt=tt, pZ=pZ, buf=buf):
                S = mkstep(vector, "v", "VS", deps=("TS", "LD"))
                d = x2st[:, buf * 512:(buf + 1) * 512]
                S(vector.tensor_mul, d, pZ[:], g1t[:, buf * 512:(buf + 1) * 512])
                S.c(vector.tensor_add, d, d, x1t[:, buf * 512:(buf + 1) * 512])

            @on("sync")
            def _(sync, dc=dc, tt=tt, buf=buf):
                D = mkdma(sync, "sy", "LD", deps=("VS",))
                D(DT["x2_fm"][dc * 128:(dc + 1) * 128, tt * 512:(tt + 1) * 512],
                  x2st[:, buf * 512:(buf + 1) * 512])

    @on("sync")
    def _(sync):
        cx.wait_all(sync, "sy")

    ph5.close()
    es.close()
    return nc


def _host_prep(inputs):
    f32 = np.float32
    x = np.asarray(inputs["L0_attn_x"], f32)
    WkT = np.ascontiguousarray(np.asarray(inputs["L0_attn_Wk"], f32).T)
    WvT = np.ascontiguousarray(np.asarray(inputs["L0_attn_Wv"], f32).T)
    WrT = np.ascontiguousarray(np.asarray(inputs["L0_attn_Wr"], f32).T)
    WoT = np.ascontiguousarray(np.asarray(inputs["L0_attn_Wo"], f32).T)
    w1T = np.ascontiguousarray(np.asarray(inputs["L0_ffn_w1"], f32).T)
    w2T = np.ascontiguousarray(np.asarray(inputs["L0_ffn_w2"], f32).T)
    Wqkv = np.asarray(inputs["L1_attn_Wqkv"], f32)
    Wo1T = np.ascontiguousarray(np.asarray(inputs["L1_attn_Wo"], f32).T)

    def colpack(v, n):
        return np.ascontiguousarray(np.asarray(v, f32).reshape(n, 128).T)

    def rowbc(v):
        v = np.asarray(v, f32)
        return np.ascontiguousarray(np.broadcast_to(v, (128, v.shape[0])))

    gamA = colpack(inputs["L0_attn_ln_gamma"], 8)
    betA = colpack(inputs["L0_attn_ln_beta"], 8)
    decay = colpack(inputs["L0_attn_time_decay"], 8)
    first = colpack(inputs["L0_attn_time_first"], 8)
    b2c = colpack(inputs["L0_ffn_b2"], 8)
    bTT = np.concatenate([np.asarray(inputs["L1_ffn_bias_real"], f32),
                          np.asarray(inputs["L1_ffn_bias_imag"], f32)])
    bTTc = colpack(bTT, 8)
    g1bc = rowbc(inputs["L0_ffn_ln1_g"])
    b1bc = rowbc(inputs["L0_ffn_ln1_b"])
    b1fbc = rowbc(inputs["L0_ffn_b1"])
    g2bc = rowbc(inputs["L0_ffn_ln2_g"])
    be2bc = rowbc(inputs["L0_ffn_ln2_b"])

    identf = np.eye(128, dtype=f32)
    identb = np.eye(128, dtype=f32).astype(BF)
    onesb = np.ones((128, 128), f32).astype(BF)
    onescolf = np.ones((128, 1), f32)
    onesrowf = np.ones((1, 128), f32)
    R = np.zeros((128, 128), f32)
    for i in range(64):
        R[2 * i, 2 * i + 1] = -1.0
        R[2 * i + 1, 2 * i] = 1.0
    ropeRT = np.ascontiguousarray(R.T).astype(BF)
    freqs = np.exp(-np.log(10000.0) * np.arange(0, DH, 2, dtype=f32) / DH)
    ang = np.arange(T, dtype=f32)[:, None] * freqs[None, :]
    ang = np.repeat(ang, 2, axis=1)
    cos64 = np.cos(ang).T.astype(f32)
    sin64 = np.sin(ang).T.astype(f32)
    cos128 = np.concatenate([cos64, cos64], 0)
    sin128 = np.concatenate([sin64, sin64], 0)
    cosQ = (cos128 / 8.0).astype(BF)
    sinQ = (sin128 / 8.0).astype(BF)
    cosK = cos128.astype(BF)
    sinK = sin128.astype(BF)
    k_ = np.arange(128)[:, None]
    q_ = np.arange(512)[None, :]
    dmask = np.concatenate(
        [((k_ + 128 * j) <= q_).astype(f32) for j in range(4)], axis=1).astype(BF)
    dmaskP = np.concatenate(
        [np.concatenate([((k_ + 128 * j) <= q_).astype(f32)] * 2, axis=1)
         for j in range(4)], axis=1).astype(BF)

    G1R = np.asarray(inputs["L1_ffn_G1_R"], f32)
    G2R = np.asarray(inputs["L1_ffn_G2_R"], f32)
    G1I = np.asarray(inputs["L1_ffn_G1_I"], f32)
    G2I = np.asarray(inputs["L1_ffn_G2_I"], f32)

    def ttrep(G1, G2):
        T1 = np.zeros((512, 4, 512), f32)
        T2 = np.zeros((512, 4, 512), f32)
        d_ = np.arange(512) // 16
        g_ = np.arange(512) % 16
        b_ = np.arange(512) // 16
        e_ = np.arange(512) % 16
        for c in range(4):
            A = G1[0, :, c, :]
            Bm = G2[c, :, 0, :]
            T1[:, c, :] = A.T[d_][:, b_]
            T2[:, c, :] = Bm.T[g_][:, e_]
        return (np.ascontiguousarray(T1.reshape(512, 2048)),
                np.ascontiguousarray(T2.reshape(512, 2048)))

    T1r, T2r = ttrep(G1R, G2R)
    T1i, T2i = ttrep(G1I, G2I)

    base = dict(
        WkT=WkT, WvT=WvT, WrT=WrT, WoT=WoT, Wo1T=Wo1T,
        gamA=gamA, betA=betA, decay=decay, first=first, b2c=b2c, bTTc=bTTc,
        g1bc=g1bc, b1bc=b1bc, b1fbc=b1fbc, g2bc=g2bc, be2bc=be2bc,
        identf=identf, identb=identb, onesb=onesb, ropeR=ropeRT,
        onescolf=onescolf, onesrowf=onesrowf,
        nb30=np.full((128, 1), EXPB, f32),
        cosQ=cosQ, sinQ=sinQ, cosK=cosK, sinK=sinK, dmask=dmask,
        T1r=T1r, T2r=T2r, T1i=T1i, T2i=T2i, dmaskP=dmaskP,
    )

    in_maps = []
    for core in range(8):
        b, p = core // 2, core % 2
        m = dict(base)
        xt = np.ascontiguousarray(x[b, p * BT:(p + 1) * BT, :])
        m["x_tok"] = xt
        m["x_fm"] = np.ascontiguousarray(xt.T)
        m["w1T_sl"] = np.ascontiguousarray(w1T[:, core * 512:(core + 1) * 512])
        m["w2T_sl"] = np.ascontiguousarray(w2T[core * 512:(core + 1) * 512, :])
        hrows = np.concatenate([np.arange(p * 512, (p + 1) * 512),
                                C + np.arange(p * 512, (p + 1) * 512),
                                2 * C + np.arange(p * 512, (p + 1) * 512)])
        m["WqkvT_sl"] = np.ascontiguousarray(Wqkv[hrows, :].T)
        m["tvals"] = np.broadcast_to(
            np.arange(p * BT, (p + 1) * BT, dtype=f32), (128, BT)).copy()
        m["selw"] = np.broadcast_to(
            np.array([1.0 - p, float(p)], f32), (128, 2)).copy()
        m["carryw"] = np.broadcast_to(
            np.array([float(p), 0.0], f32), (128, 2)).copy()
        in_maps.append(m)
    return in_maps


def kernel(**inputs):
    global _last_res
    in_maps = _host_prep(inputs)
    nc = build_program()
    res = run_bass_kernel_spmd(nc, in_maps, core_ids=list(range(8)))
    _last_res = res
    out = np.zeros((B, T, C), np.float32)
    for core in range(8):
        b, p = core // 2, core % 2
        out[b, p * BT:(p + 1) * BT, :] = np.asarray(res.results[core]["x2_fm"]).T
    return out


# version 14
# speedup vs baseline: 1.3726x; 1.0660x over previous
import contextlib
import numpy as np
import ml_dtypes

import concourse.bass as bass
import concourse.mybir as mybir
from concourse.bass_utils import run_bass_kernel_spmd

F32 = mybir.dt.float32
BF16 = mybir.dt.bfloat16
AF = mybir.ActivationFunctionType
ALU = mybir.AluOpType
AX = mybir.AxisListType.X
BF = ml_dtypes.bfloat16

B, T, C = 4, 4096, 1024
BT = 2048
H4 = 4096
DH = 64
NCH = 8
NBLK = 16
MYH = 8
EPS = 1e-5
RC = 12582912.0
EXPB = -30.0
ALLSEM = ("LD", "GD", "TS", "VS", "SS", "CC")


class Ctx:
    def __init__(self, nc, es):
        self.sem = {}
        self.cnt = {}
        self.waited = {}
        for name in ALLSEM:
            self.sem[name] = es.enter_context(nc.semaphore(name))
            self.cnt[name] = 0

    def inc(self, instr, name, n=1):
        instr.then_inc(self.sem[name], n)
        self.cnt[name] += n
        return instr

    def wait(self, eng, ename, names):
        for name in names:
            v = self.cnt[name]
            key = (ename, name)
            if v == 0 or self.waited.get(key, -1) >= v:
                continue
            eng.wait_ge(self.sem[name], v)
            self.waited[key] = v

    def wait_all(self, eng, ename):
        self.wait(eng, ename, ALLSEM)

    def mark(self, name):
        return (name, self.cnt[name])

    def wait_value(self, eng, ename, name, v):
        key = (ename, name)
        if v <= 0 or self.waited.get(key, -1) >= v:
            return
        eng.wait_ge(self.sem[name], v)
        self.waited[key] = v

    def wait_mark(self, eng, ename, mk):
        if mk is None:
            return
        name, v = mk
        key = (ename, name)
        if v == 0 or self.waited.get(key, -1) >= v:
            return
        eng.wait_ge(self.sem[name], v)
        self.waited[key] = v


DEBUG = False


def build_program():
    nc = bass.Bass("TRN2", target_bir_lowering=False, debug=False)
    es = contextlib.ExitStack()
    DT = {}

    def din(name, shape, dt=F32):
        DT[name] = nc.dram_tensor(name, shape, dt, kind="ExternalInput").ap()

    def dout(name, shape, dt=F32):
        DT[name] = nc.dram_tensor(name, shape, dt, kind="ExternalOutput").ap()

    def dint(name, shape, dt=F32):
        DT[name] = nc.dram_tensor(name, shape, dt).ap()

    din("x_fm", [C, BT]); din("x_tok", [BT, C])
    din("WkT", [C, C]); din("WvT", [C, C]); din("WrT", [C, C]); din("WoT", [C, C])
    din("w1T_sl", [C, 512]); din("w2T_sl", [512, C])
    din("WqkvT_sl", [C, 1536]); din("Wo1T", [C, C])
    din("gamA", [128, 8]); din("betA", [128, 8])
    din("decay", [128, 8]); din("first", [128, 8])
    din("b2c", [128, 8]); din("bTTc", [128, 8])
    din("g1bc", [128, C]); din("b1bc", [128, C])
    din("b1fbc", [128, H4]); din("g2bc", [128, H4]); din("be2bc", [128, H4])
    din("tvals", [128, BT])
    din("identf", [128, 128], F32)
    din("identb", [128, 128], BF16)
    din("onesb", [128, 128], BF16)
    din("ropeR", [128, 128], BF16)
    din("onescolf", [128, 1]); din("onesrowf", [1, 128])
    din("selw", [128, 2]); din("carryw", [128, 2]); din("nb30", [128, 1])
    din("cosQ", [128, T], BF16); din("sinQ", [128, T], BF16)
    din("cosK", [128, T], BF16); din("sinK", [128, T], BF16)
    din("dmask", [128, 2048], BF16); din("dmaskP", [128, 4096], BF16)
    din("T1r", [512, 2048]); din("T2r", [512, 2048])
    din("T1i", [512, 2048]); din("T2i", [512, 2048])

    dout("x2_fm", [C, BT])
    if DEBUG:
        dout("dbg_xn2a", [C, BT], BF16)
        dout("dbg_int2", [H4, BT], BF16)
        dout("dbg_g0", [C, BT])
        dout("dbg_x1", [C, BT])
        dout("dbg_xn1", [C, BT], BF16)
        dout("dbg_g1", [C, BT])
        dout("dbg_att", [512, T], BF16)

    dint("cc_wsc_in", [1, 2]); dint("cc_wsc_out", [8, 2])
    dint("cc_wq1_in", [C, 512], BF16); dint("cc_wq1_out", [8 * C, 512], BF16)
    dint("cc_wq2_in", [512, C], BF16); dint("cc_wq2_out", [8 * 512, C], BF16)
    dint("cc_carry_in", [1, C]); dint("cc_carry_out", [2, C])
    dint("cc_xn1_in", [C, BT], BF16); dint("cc_xn1_out", [2 * C, BT], BF16)
    dint("cc_out_in", [512, T], BF16); dint("cc_out_out", [1024, T], BF16)
    dint("xn2a_d", [C, BT], BF16)
    dint("int2_d", [H4, BT], BF16)
    dint("g0_d", [C, BT]); dint("g1_d", [C, BT]); dint("x1_d", [C, BT])
    dint("mt_d", [C, C], BF16)

    cx = Ctx(nc, es)
    sb = {}

    def sball(name, shape, dt=F32, stack=None):
        sb[name] = (stack or es).enter_context(nc.sbuf_tensor(name, shape, dt))
        return sb[name]

    for nm, shp, dt in [
        ("c_idf", [128, 128], F32), ("c_idb", [128, 128], BF16),
        ("c_onesb", [128, 128], BF16), ("c_ropeR", [128, 128], BF16),
        ("c_onescolf", [128, 1], F32), ("c_onesrowf", [1, 128], F32),
        ("c_selw", [128, 2], F32), ("c_carryw", [128, 2], F32),
        ("c_nb30", [128, 1], F32),
        ("v_gamA", [128, 8], F32), ("v_betA", [128, 8], F32),
        ("v_decay", [128, 8], F32), ("v_first", [128, 8], F32),
        ("v_b2c", [128, 8], F32), ("v_bTTc", [128, 8], F32),
        ("v_td", [128, 8], F32), ("v_tf", [128, 8], F32), ("v_sp", [128, 8], F32),
        ("st_A", [128, 16], F32), ("st_B", [128, 16], F32),
        ("st_s1", [128, 16], F32), ("st_s1s", [128, 16], F32),
        ("st_s2i", [128, 16], F32),
        ("v_mw", [128, 2], F32), ("v_swc", [128, 2], F32),
        ("row_a", [1, BT], F32),
        ("carry_fix", [128, 8], F32),
        ("colscr", [128, 64], F32),
        ("st16", [16, 256], F32),
        ("abc", [128, BT], F32), ("bbc", [128, BT], F32),
    ]:
        sball(nm, shp, dt)

    block = es.enter_context(nc.Block())

    def on(engname):
        def deco(fn):
            getattr(block, engname)(fn)
        return deco

    def barrier():
        nc.all_engine_barrier()

    def mkstep(eng, ename, own, deps=ALLSEM):
        state = {"first": True}

        def step(f, *a, **k):
            if state["first"]:
                cx.wait(eng, ename, deps)
                state["first"] = False
            return cx.inc(f(*a, **k), own)

        def chain(f, *a, **k):
            if not state["first"]:
                cx.wait(eng, ename, (own,))
            return step(f, *a, **k)

        step.c = chain
        return step

    def mkdma(eng, ename, own, deps=ALLSEM):
        state = {"first": True}

        def dma(out, in_, **k):
            if state["first"]:
                cx.wait(eng, ename, deps)
                state["first"] = False
            return cx.inc(eng.dma_start(out=out, in_=in_, **k), own, 16)
        return dma

    def emit_bc_from_pack(pack_name, pack_sl, dst_name, ppsum):
        @on("tensor")
        def _(tensor):
            S = mkstep(tensor, "t", "TS")
            S(tensor.transpose, ppsum[0:16, 0:128], sb[pack_name][:, pack_sl],
              sb["c_idf"][:])

        @on("scalar")
        def _(scalar):
            S = mkstep(scalar, "s", "SS")
            S(scalar.copy, sb["st16"][:, 0:128], ppsum[0:16, 0:128])

        @on("sync")
        def _(sync):
            D = mkdma(sync, "sy", "LD")
            for tb in range(NBLK):
                D(sb["row_a"][0:1, tb * 128:(tb + 1) * 128],
                  sb["st16"][tb:tb + 1, 0:128])

        @on("tensor")
        def _(tensor):
            S = mkstep(tensor, "t", "TS")
            for tt in range(4):
                S(tensor.matmul, ppsum[:, tt * 512:(tt + 1) * 512],
                  sb["c_onesrowf"][:], sb["row_a"][0:1, tt * 512:(tt + 1) * 512],
                  start=True, stop=True)

        @on("scalar")
        def _(scalar):
            S = mkstep(scalar, "s", "SS")
            S(scalar.copy, sb[dst_name][:], ppsum[:, 0:2048])

    ph0 = contextlib.ExitStack()
    w1s = sball("w1s", [128, 8 * 512], F32, ph0)
    w2s = sball("w2s", [128, 4 * C], F32, ph0)
    wq1 = sball("wq1", [128, 8 * 512], BF16, ph0)
    wq2 = sball("wq2", [128, 4 * C], BF16, ph0)
    wscr = sball("wscr", [128, 4096], F32, ph0)
    wcol = sball("wcol", [128, 8], F32, ph0)
    wg8 = sball("wg8", [8, 2], F32, ph0)
    t1sb = sball("t1sb", [128, 4 * 2048], F32, ph0)
    t2sb = sball("t2sb", [128, 4 * 2048], F32, ph0)
    wTT = sball("wTT", [128, 4 * 512], F32, ph0)
    MT0 = sball("MT0", [128, 8 * C], BF16, ph0)
    pw = ph0.enter_context(nc.psum_tensor("pw", [128, 512], F32))

    @on("sync")
    def _(sync):
        D = mkdma(sync, "sy", "LD")
        for nm, dst in [("identf", "c_idf"), ("identb", "c_idb"),
                        ("onesb", "c_onesb"), ("ropeR", "c_ropeR"),
                        ("onescolf", "c_onescolf"), ("onesrowf", "c_onesrowf"),
                        ("selw", "c_selw"), ("carryw", "c_carryw"), ("nb30", "c_nb30"),
                        ("gamA", "v_gamA"), ("betA", "v_betA"),
                        ("decay", "v_decay"), ("first", "v_first"),
                        ("b2c", "v_b2c"), ("bTTc", "v_bTTc")]:
            D(sb[dst][:], DT[nm])
        D(w1s[:].rearrange("p (a n) -> p a n", a=8),
          DT["w1T_sl"].rearrange("(a p) n -> p a n", p=128))
        D(w2s[:].rearrange("p (a n) -> p a n", a=4),
          DT["w2T_sl"].rearrange("(a p) n -> p a n", p=128))
        D(t1sb[:].rearrange("p (a n) -> p a n", a=4),
          DT["T1r"].rearrange("(a p) n -> p a n", p=128))
        D(t2sb[:].rearrange("p (a n) -> p a n", a=4),
          DT["T2r"].rearrange("(a p) n -> p a n", p=128))

    @on("scalar")
    def _(scalar):
        S = mkstep(scalar, "s", "SS")
        S(scalar.activation, sb["v_td"][:], sb["v_decay"][:], AF.Sigmoid)
        S(scalar.activation, sb["v_tf"][:], sb["v_first"][:], AF.Sigmoid)

    @on("vector")
    def _(vector):
        S = mkstep(vector, "v", "VS")
        S(vector.tensor_scalar, sb["v_sp"][:], sb["v_td"][:], 1e-38, None, ALU.max)

    @on("scalar")
    def _(scalar):
        S = mkstep(scalar, "s", "SS")
        S(scalar.activation, sb["v_sp"][:], sb["v_sp"][:], AF.Ln)

    @on("vector")
    def _(vector):
        S = mkstep(vector, "v", "VS")
        S(vector.tensor_scalar, sb["v_sp"][:], sb["v_sp"][:], -1.0, None, ALU.mult)

    @on("vector")
    def _(vector):
        S = mkstep(vector, "v", "VS")
        S(vector.reduce_sum, wcol[:, 0:1], w1s[:], axis=AX, apply_absolute_value=True)
        S(vector.reduce_sum, wcol[:, 1:2], w2s[:], axis=AX, apply_absolute_value=True)

    @on("tensor")
    def _(tensor):
        S = mkstep(tensor, "t", "TS")
        S(tensor.matmul, pw[0:1, 0:2], sb["c_onescolf"][:], wcol[:, 0:2],
          start=True, stop=True)

    @on("scalar")
    def _(scalar):
        S = mkstep(scalar, "s", "SS")
        S(scalar.copy, wcol[0:1, 2:4], pw[0:1, 0:2])

    @on("gpsimd")
    def _(g):
        D = mkdma(g, "g", "GD")
        S = mkstep(g, "g", "CC")
        D(DT["cc_wsc_in"], wcol[0:1, 2:4])
        cx.wait(g, "g", ("GD",))
        S(g.collective_compute, "AllGather", ALU.bypass,
          replica_groups=[list(range(8))],
          ins=[DT["cc_wsc_in"].opt()], outs=[DT["cc_wsc_out"].opt()])
        cx.wait(g, "g", ("CC",))
        D(wg8[:], DT["cc_wsc_out"])

    @on("vector")
    def _(vector):
        S = mkstep(vector, "v", "VS")
        S(vector.tensor_copy, wcol[0:8, 4:6], wg8[:, 0:2])

    @on("tensor")
    def _(tensor):
        S = mkstep(tensor, "t", "TS")
        S(tensor.matmul, pw[0:1, 2:4], sb["c_onescolf"][0:8, 0:1], wcol[0:8, 4:6],
          start=True, stop=True)

    @on("vector")
    def _(vector):
        S = mkstep(vector, "v", "VS")
        S(vector.tensor_scalar, wcol[0:1, 6:8], pw[0:1, 2:4],
          1.0 / (C * H4), 1e-5, ALU.mult, ALU.max)

    @on("tensor")
    def _(tensor):
        S = mkstep(tensor, "t", "TS")
        S(tensor.matmul, pw[:, 4:6], sb["c_onesrowf"][:], wcol[0:1, 6:8],
          start=True, stop=True)

    @on("scalar")
    def _(scalar):
        S = mkstep(scalar, "s", "SS")
        S(scalar.copy, sb["v_mw"][:], pw[:, 4:6])

    @on("vector")
    def _(vector):
        S = mkstep(vector, "v", "VS")
        S(vector.reciprocal, sb["v_swc"][:], sb["v_mw"][:])
        for wsrc, dst, sc, n in ((w1s, wq1, 0, 8 * 512), (w2s, wq2, 1, 4 * C)):
            S.c(vector.tensor_scalar, wscr[:, 0:n], wsrc[:], sb["v_swc"][:, sc:sc + 1],
                RC, ALU.mult, ALU.add)
            S.c(vector.tensor_scalar, wscr[:, 0:n], wscr[:, 0:n], RC, None,
                ALU.subtract)
            S.c(vector.tensor_scalar, dst[:], wscr[:, 0:n], 1.0, -1.0, ALU.min,
                ALU.max)

    @on("gpsimd")
    def _(g):
        D = mkdma(g, "g", "GD")
        S = mkstep(g, "g", "CC")
        D(DT["cc_wq1_in"].rearrange("(a p) n -> p a n", p=128),
          wq1[:].rearrange("p (a n) -> p a n", a=8))
        D(DT["cc_wq2_in"].rearrange("(a p) n -> p a n", p=128),
          wq2[:].rearrange("p (a n) -> p a n", a=4))
        cx.wait(g, "g", ("GD",))
        S(g.collective_compute, "AllGather", ALU.bypass,
          replica_groups=[list(range(8))],
          ins=[DT["cc_wq1_in"].opt()], outs=[DT["cc_wq1_out"].opt()])
        S(g.collective_compute, "AllGather", ALU.bypass,
          replica_groups=[list(range(8))],
          ins=[DT["cc_wq2_in"].opt()], outs=[DT["cc_wq2_out"].opt()])

    def emit_tt_build(which):
        @on("vector")
        def _(vector):
            S = mkstep(vector, "v", "VS")
            for pc in range(4):
                base = pc * 2048
                S.c(vector.tensor_mul, wTT[:, pc * 512:(pc + 1) * 512],
                  t1sb[:, base:base + 512], t2sb[:, base:base + 512])
                for c in range(1, 4):
                    s0 = base + c * 512
                    S.c(vector.tensor_mul, wscr[:, 0:512], t1sb[:, s0:s0 + 512],
                      t2sb[:, s0:s0 + 512])
                    S.c(vector.tensor_add, wTT[:, pc * 512:(pc + 1) * 512],
                      wTT[:, pc * 512:(pc + 1) * 512], wscr[:, 0:512])
            for pc in range(4):
                w = wTT[:, pc * 512:(pc + 1) * 512]
                if which == 0:
                    S.c(vector.tensor_copy, MT0[:, pc * C + 0:pc * C + 512], w)
                    S.c(vector.tensor_copy,
                      MT0[:, (4 + pc) * C + 512:(4 + pc) * C + 1024], w)
                else:
                    S.c(vector.tensor_copy, MT0[:, pc * C + 512:pc * C + 1024], w)
                    S.c(vector.tensor_scalar, wscr[:, 0:512], w, -1.0, None, ALU.mult)
                    S.c(vector.tensor_copy,
                      MT0[:, (4 + pc) * C + 0:(4 + pc) * C + 512], wscr[:, 0:512])

    emit_tt_build(0)

    @on("sync")
    def _(sync):
        D = mkdma(sync, "sy", "LD")
        D(t1sb[:].rearrange("p (a n) -> p a n", a=4),
          DT["T1i"].rearrange("(a p) n -> p a n", p=128))
        D(t2sb[:].rearrange("p (a n) -> p a n", a=4),
          DT["T2i"].rearrange("(a p) n -> p a n", p=128))

    emit_tt_build(1)

    @on("sync")
    def _(sync):
        D = mkdma(sync, "sy", "LD")
        D(DT["mt_d"].rearrange("(a p) n -> p a n", p=128),
          MT0[:].rearrange("p (a n) -> p a n", a=NCH))

    barrier()
    ph0.close()

    s_int1f = contextlib.ExitStack()
    int1f = sball("int1f", [128, NCH * BT], BF16, s_int1f)
    ph1 = contextlib.ExitStack()
    xtb = sball("xtb", [128, 2 * C], F32, ph1)
    xfb = sball("xfb", [128, 2 * BT], F32, ph1)
    g1bcs = sball("g1bcs", [128, C], F32, ph1)
    b1bcs = sball("b1bcs", [128, C], F32, ph1)
    scr1 = sball("scr1", [128, C], F32, ph1)
    scr2 = sball("scr2", [128, C], F32, ph1)
    int1t = sball("int1t", [128, NBLK * C], BF16, ph1)
    xn2a = sball("xn2a", [128, NCH * BT], BF16, ph1)
    ph1p = contextlib.ExitStack()
    pstA = ph1p.enter_context(nc.psum_tensor("pstA", [128, 2048], F32))
    pstB = ph1p.enter_context(nc.psum_tensor("pstB", [128, 2048], F32))

    @on("sync")
    def _(sync):
        D = mkdma(sync, "sy", "LD")
        D(g1bcs[:], DT["g1bc"])
        D(b1bcs[:], DT["b1bc"])

    h1mk = {}
    for tb in range(NBLK):
        buf = tb % 2

        @on("sync")
        def _(sync, tb=tb, buf=buf):
            cx.wait_mark(sync, "sy", h1mk.get(("xtb_s", buf)))
            cx.wait_mark(sync, "sy", h1mk.get(("xtb_v", buf)))
            D = mkdma(sync, "sy", "LD", deps=())
            D(xtb[:, buf * C:(buf + 1) * C],
              DT["x_tok"][tb * 128:(tb + 1) * 128, :])

        @on("scalar")
        def _(scalar, tb=tb, buf=buf):
            S = mkstep(scalar, "s", "SS", deps=("LD", "SS"))
            S(scalar.activation, scr1[:], xtb[:, buf * C:(buf + 1) * C], AF.Square,
              accum_out=sb["colscr"][:, tb:tb + 1])
        h1mk[("xtb_s", buf)] = cx.mark("SS")

        @on("vector")
        def _(vector, tb=tb, buf=buf):
            S = mkstep(vector, "v", "VS", deps=("LD",))
            S(vector.reduce_sum, sb["colscr"][:, 16 + tb:17 + tb],
              xtb[:, buf * C:(buf + 1) * C], axis=AX)
        h1mk[("xtb_v", buf)] = cx.mark("VS")

    @on("vector")
    def _(vector):
        S = mkstep(vector, "v", "VS")
        mu = sb["colscr"][:, 32:48]
        var = sb["colscr"][:, 48:64]
        S(vector.tensor_scalar, mu, sb["colscr"][:, 16:32], 1.0 / C, None, ALU.mult)
        S.c(vector.tensor_mul, var, mu, mu)
        S.c(vector.tensor_scalar, sb["st_A"][:], sb["colscr"][:, 0:16], 1.0 / C, None,
            ALU.mult)
        S.c(vector.tensor_sub, var, sb["st_A"][:], var)
        S.c(vector.tensor_scalar, var, var, 1.0 + EPS, EPS * EPS, ALU.mult, ALU.add)

    @on("scalar")
    def _(scalar):
        S = mkstep(scalar, "s", "SS")
        S(scalar.activation, sb["st_B"][:], sb["colscr"][:, 48:64], AF.Sqrt)

    @on("vector")
    def _(vector):
        S = mkstep(vector, "v", "VS")
        S(vector.reciprocal, sb["st_A"][:], sb["st_B"][:])
        S.c(vector.scalar_tensor_tensor, sb["st_B"][:], sb["colscr"][:, 32:48], -1.0,
            sb["st_A"][:], ALU.mult, ALU.mult)

    for tb in range(NBLK):
        buf = tb % 2

        @on("sync")
        def _(sync, tb=tb, buf=buf):
            cx.wait_mark(sync, "sy", h1mk.get(("xtb2", buf)))
            D = mkdma(sync, "sy", "LD", deps=("SS", "VS"))
            D(xtb[:, buf * C:(buf + 1) * C],
              DT["x_tok"][tb * 128:(tb + 1) * 128, :])

        @on("vector")
        def _(vector, tb=tb, buf=buf):
            S = mkstep(vector, "v", "VS", deps=("LD",))
            xb = xtb[:, buf * C:(buf + 1) * C]
            S(vector.tensor_scalar, scr1[:], xb, sb["st_A"][:, tb:tb + 1],
              sb["st_B"][:, tb:tb + 1], ALU.mult, ALU.add)
            S.c(vector.scalar_tensor_tensor, scr2[:], scr1[:], 1.0, g1bcs[:],
                ALU.bypass, ALU.mult)
            S.c(vector.tensor_add, scr2[:], scr2[:], b1bcs[:])
            S.c(vector.reduce_max, sb["colscr"][:, tb:tb + 1], scr2[:], axis=AX,
                apply_absolute_value=True)
            S.c(vector.tensor_scalar, sb["colscr"][:, 16 + tb:17 + tb],
                sb["colscr"][:, tb:tb + 1], 1e-5, None, ALU.max)
            S.c(vector.reciprocal, sb["st_s1"][:, tb:tb + 1],
                sb["colscr"][:, 16 + tb:17 + tb])
            S.c(vector.tensor_scalar, sb["st_s1"][:, tb:tb + 1],
                sb["st_s1"][:, tb:tb + 1], 127.0, None, ALU.mult)
            S.c(vector.scalar_tensor_tensor, sb["st_s1s"][:, tb:tb + 1],
                sb["colscr"][:, 16 + tb:17 + tb], 1.0 / 127.0, sb["v_mw"][:, 0:1],
                ALU.mult, ALU.mult)
            S.c(vector.tensor_scalar, scr2[:], scr2[:], sb["st_s1"][:, tb:tb + 1], RC,
                ALU.mult, ALU.add)
            S.c(vector.tensor_scalar, int1t[:, tb * C:(tb + 1) * C], scr2[:], RC,
                None, ALU.subtract)
        h1mk[("xtb2", buf)] = cx.mark("VS")

    emit_bc_from_pack("st_A", slice(0, 16), "abc", pstA)
    emit_bc_from_pack("st_B", slice(0, 16), "bbc", pstB)

    for ca in range(NCH):
        buf = ca % 2

        @on("sync")
        def _(sync, ca=ca, buf=buf):
            cx.wait_mark(sync, "sy", h1mk.get(("xfb", buf)))
            D = mkdma(sync, "sy", "LD", deps=("SS",))
            D(xfb[:, buf * BT:(buf + 1) * BT],
              DT["x_fm"][ca * 128:(ca + 1) * 128, :])

        @on("vector")
        def _(vector, ca=ca, buf=buf):
            S = mkstep(vector, "v", "VS", deps=("LD",))
            xc = xfb[:, buf * BT:(buf + 1) * BT]
            dst = xn2a[:, ca * BT:(ca + 1) * BT]
            S(vector.tensor_mul, xc, xc, sb["abc"][:])
            S.c(vector.tensor_add, xc, xc, sb["bbc"][:])
            S.c(vector.tensor_scalar, dst, xc, sb["v_gamA"][:, ca:ca + 1],
                sb["v_betA"][:, ca:ca + 1], ALU.mult, ALU.add)
        h1mk[("xfb", buf)] = cx.mark("VS")

    @on("sync")
    def _(sync):
        D = mkdma(sync, "sy", "LD")
        D(DT["xn2a_d"].rearrange("(a p) n -> p a n", p=128),
          xn2a[:].rearrange("p (a n) -> p a n", a=NCH))

    barrier()
    ph1p.close()
    pT0 = ph1.enter_context(nc.psum_tensor("pT0", [128, 512], BF16))
    pT1 = ph1.enter_context(nc.psum_tensor("pT1", [128, 512], BF16))

    for ca in range(NCH):
        for tg in range(4):
            pT = (pT0, pT1)[tg % 2]

            @on("tensor")
            def _(tensor, ca=ca, tg=tg, pT=pT):
                cx.wait(tensor, "t", ("VS",))
                cx.wait_mark(tensor, "t", h1mk.get(("pT", tg % 2)))
                S = mkstep(tensor, "t", "TS", deps=())
                for j in range(4):
                    tb = tg * 4 + j
                    S(tensor.transpose, pT[:, j * 128:(j + 1) * 128],
                      int1t[:, tb * C + ca * 128:tb * C + (ca + 1) * 128],
                      sb["c_idb"][:])

            @on("scalar")
            def _(scalar, ca=ca, tg=tg, pT=pT):
                S = mkstep(scalar, "s", "SS", deps=("TS",))
                S(scalar.copy,
                  int1f[:, ca * BT + tg * 512:ca * BT + (tg + 1) * 512], pT[:])
            h1mk[("pT", tg % 2)] = cx.mark("SS")

    barrier()
    ph1.close()

    phf = contextlib.ExitStack()
    wq1f = sball("wq1f", [128, 64 * 512], BF16, phf)
    b1fs = sball("b1fs", [128, H4], BF16, phf)
    g2s = sball("g2s", [128, H4], BF16, phf)
    be2s = sball("be2s", [128, H4], BF16, phf)
    h2b = sball("h2b", [128, 2 * H4], BF16, phf)
    ln2o = sball("ln2o", [128, H4], F32, phf)
    sqs = sball("sqs", [128, 512], F32, phf)
    fscr = sball("fscr", [128, 4096], BF16, phf)
    int2s = sball("int2s", [128, H4], BF16, phf)
    i2st = sball("i2st", [128, 1024], BF16, phf)
    pF0 = phf.enter_context(nc.psum_tensor("pF0", [128, 512], F32))
    pF1 = phf.enter_context(nc.psum_tensor("pF1", [128, 512], F32))
    pX0 = phf.enter_context(nc.psum_tensor("pX0", [128, 512], BF16))
    pX1 = phf.enter_context(nc.psum_tensor("pX1", [128, 512], BF16))

    @on("sync")
    def _(sync):
        D = mkdma(sync, "sy", "LD")
        for r in range(8):
            for ca in range(NCH):
                D(wq1f[:, (ca * 8 + r) * 512:(ca * 8 + r + 1) * 512],
                  DT["cc_wq1_out"][r * C + ca * 128:r * C + (ca + 1) * 128, :])
    @on("gpsimd")
    def _(g):
        D = mkdma(g, "g", "GD")
        D(b1fs[:], DT["b1fbc"])
        D(g2s[:], DT["g2bc"])
        D(be2s[:], DT["be2bc"])

    fmk = {}
    for tb in range(NBLK):
        h2o = (tb % 2) * H4

        vs_after_stt = {}
        for hc in range(8):
            pF = (pF0, pF1)[hc % 2]

            @on("tensor")
            def _(tensor, tb=tb, hc=hc, pF=pF):
                cx.wait(tensor, "t", ("LD", "GD"))
                cx.wait_mark(tensor, "t", fmk.get(("pF", hc % 2)))
                S = mkstep(tensor, "t", "TS", deps=())
                for ca in range(NCH):
                    S(tensor.matmul, pF[:],
                      int1f[:, ca * BT + tb * 128:ca * BT + (tb + 1) * 128],
                      wq1f[:, (ca * 8 + hc) * 512:(ca * 8 + hc + 1) * 512],
                      start=(ca == 0), stop=(ca == NCH - 1))

            @on("vector")
            def _(vector, tb=tb, hc=hc, pF=pF):
                cx.wait(vector, "v", ("TS", "GD"))
                cx.wait_mark(vector, "v", fmk.get(("fscr",)))
                S = mkstep(vector, "v", "VS", deps=())
                S(vector.scalar_tensor_tensor, fscr[:, hc * 512:(hc + 1) * 512],
                  pF[:], sb["st_s1s"][:, tb:tb + 1],
                  b1fs[:, hc * 512:(hc + 1) * 512], ALU.mult, ALU.add)
            fmk[("pF", hc % 2)] = cx.mark("VS")
            vs_after_stt[hc] = cx.cnt["VS"]

        @on("scalar")
        def _(scalar, tb=tb, h2o=h2o, vs_after_stt=vs_after_stt):
            cx.wait_mark(scalar, "s", fmk.get(("h2b", tb % 2)))
            for hc in range(8):
                cx.wait_value(scalar, "s", "VS", vs_after_stt[hc])
                ins = scalar.activation(
                    h2b[:, h2o + hc * 512:h2o + (hc + 1) * 512],
                    fscr[:, hc * 512:(hc + 1) * 512], AF.Gelu)
                cx.inc(ins, "SS")
        fmk[("fscr",)] = cx.mark("SS")

        @on("vector")
        def _(vector, tb=tb, h2o=h2o):
            cx.wait(vector, "v", ("SS",))
            cx.wait_mark(vector, "v", fmk.get(("ln2o",)))
            S = mkstep(vector, "v", "VS", deps=())
            h2 = h2b[:, h2o:h2o + H4]
            cc = sb["colscr"]
            S(vector.reduce_sum, cc[:, 0:1], h2, axis=AX)
            S.c(vector.tensor_mul, ln2o[:], h2, h2)
            S.c(vector.reduce_sum, cc[:, 8:9], ln2o[:], axis=AX)
            S.c(vector.tensor_scalar, cc[:, 1:2], cc[:, 0:1], 1.0 / H4, None,
                ALU.mult)
            S.c(vector.tensor_mul, cc[:, 2:3], cc[:, 1:2], cc[:, 1:2])
            S.c(vector.tensor_scalar, cc[:, 3:4], cc[:, 8:9], 1.0 / H4, None,
                ALU.mult)
            S.c(vector.tensor_sub, cc[:, 3:4], cc[:, 3:4], cc[:, 2:3])
            S.c(vector.tensor_scalar, cc[:, 3:4], cc[:, 3:4], 1.0, EPS, ALU.mult,
                ALU.add)

        @on("scalar")
        def _(scalar, tb=tb):
            S = mkstep(scalar, "s", "SS", deps=("VS",))
            S(scalar.activation, sb["colscr"][:, 4:5], sb["colscr"][:, 3:4], AF.Sqrt)

        @on("vector")
        def _(vector, tb=tb, h2o=h2o):
            S = mkstep(vector, "v", "VS", deps=("SS",))
            h2 = h2b[:, h2o:h2o + H4]
            cc = sb["colscr"]
            S(vector.reciprocal, cc[:, 5:6], cc[:, 4:5])
            S.c(vector.scalar_tensor_tensor, cc[:, 6:7], cc[:, 1:2], -1.0, cc[:, 5:6],
                ALU.mult, ALU.mult)
            S.c(vector.tensor_scalar, ln2o[:], h2, cc[:, 5:6], cc[:, 6:7],
                ALU.mult, ALU.add)
            S.c(vector.scalar_tensor_tensor, ln2o[:], ln2o[:], 1.0, g2s[:],
                ALU.bypass, ALU.mult)
            S.c(vector.tensor_add, ln2o[:], ln2o[:], be2s[:])
            S.c(vector.reduce_max, cc[:, 7:8], ln2o[:], axis=AX,
                apply_absolute_value=True)
            S.c(vector.tensor_scalar, cc[:, 7:8], cc[:, 7:8], 1e-5, None, ALU.max)
            S.c(vector.reciprocal, cc[:, 2:3], cc[:, 7:8])
            S.c(vector.tensor_scalar, cc[:, 2:3], cc[:, 2:3], 127.0, None, ALU.mult)
            S.c(vector.scalar_tensor_tensor, sb["st_s2i"][:, tb:tb + 1], cc[:, 7:8],
                1.0 / 127.0, sb["v_mw"][:, 1:2], ALU.mult, ALU.mult)
            S.c(vector.tensor_scalar, ln2o[:], ln2o[:], cc[:, 2:3], RC, ALU.mult,
                ALU.add)
            S.c(vector.tensor_scalar, int2s[:], ln2o[:], RC, None, ALU.subtract)
        fmk[("ln2o",)] = cx.mark("VS")
        fmk[("h2b", tb % 2)] = cx.mark("VS")

        for hg in range(8):
            pX = (pX0, pX1)[hg % 2]
            xb = hg % 2

            @on("tensor")
            def _(tensor, tb=tb, hg=hg, pX=pX, xb=xb):
                cx.wait(tensor, "t", ("VS",))
                cx.wait_mark(tensor, "t", fmk.get(("pX", xb)))
                S = mkstep(tensor, "t", "TS", deps=())
                for j in range(4):
                    hh = hg * 4 + j
                    S(tensor.transpose, pX[:, j * 128:(j + 1) * 128],
                      int2s[:, hh * 128:(hh + 1) * 128], sb["c_idb"][:])

            @on("scalar")
            def _(scalar, tb=tb, hg=hg, pX=pX, xb=xb):
                cx.wait(scalar, "s", ("TS",))
                cx.wait_mark(scalar, "s", fmk.get(("i2st", xb)))
                S = mkstep(scalar, "s", "SS", deps=())
                S(scalar.copy, i2st[:, xb * 512:(xb + 1) * 512], pX[:])
            fmk[("pX", xb)] = cx.mark("SS")

            @on("sync")
            def _(sync, tb=tb, hg=hg, xb=xb):
                D = mkdma(sync, "sy", "LD", deps=("SS",))
                for j in range(4):
                    hh = hg * 4 + j
                    D(DT["int2_d"][hh * 128:(hh + 1) * 128, tb * 128:(tb + 1) * 128],
                      i2st[:, xb * 512 + j * 128:xb * 512 + (j + 1) * 128])
            fmk[("i2st", xb)] = cx.mark("LD")

    barrier()
    phf.close()
    s_int1f.close()

    phg = contextlib.ExitStack()
    wq2f = sball("wq2f", [128, 32 * C], BF16, phg)
    i2b = sball("i2b", [128, 32 * 512], BF16, phg)
    s2ib = sball("s2ib", [128, BT], F32, phg)
    gscr = sball("gscr", [128, 4096], F32, phg)
    gst = sball("gst", [128, 4096], F32, phg)
    pG0 = phg.enter_context(nc.psum_tensor("pG0", [128, 512], F32))
    pG1 = phg.enter_context(nc.psum_tensor("pG1", [128, 512], F32))
    pBC = phg.enter_context(nc.psum_tensor("pBC", [128, 2048], F32))

    @on("sync")
    def _(sync):
        D = mkdma(sync, "sy", "LD")
        for hh in range(32):
            r, loc = hh // 4, hh % 4
            D(wq2f[:, hh * C:(hh + 1) * C],
              DT["cc_wq2_out"][r * 512 + loc * 128:r * 512 + (loc + 1) * 128, :])

    emit_bc_from_pack("st_s2i", slice(0, 16), "abc", pBC)

    @on("vector")
    def _(vector):
        S = mkstep(vector, "v", "VS")
        S(vector.tensor_copy, s2ib[:], sb["abc"][:])

    gmk = {}
    for tt in range(4):
        @on("sync")
        def _(sync, tt=tt):
            D = mkdma(sync, "sy", "LD", deps=("TS", "LD"))
            for hh in range(32):
                D(i2b[:, hh * 512:(hh + 1) * 512],
                  DT["int2_d"][hh * 128:(hh + 1) * 128, tt * 512:(tt + 1) * 512])

        vs_after_g = {}
        for dc in range(NCH):
            pG = (pG0, pG1)[dc % 2]
            gb = dc % 2

            @on("tensor")
            def _(tensor, tt=tt, dc=dc, pG=pG, gb=gb):
                cx.wait(tensor, "t", ("LD",))
                cx.wait_mark(tensor, "t", gmk.get(("pG", gb)))
                S = mkstep(tensor, "t", "TS", deps=())
                for hh in range(32):
                    S(tensor.matmul, pG[:],
                      wq2f[:, hh * C + dc * 128:hh * C + (dc + 1) * 128],
                      i2b[:, hh * 512:(hh + 1) * 512],
                      start=(hh == 0), stop=(hh == 31))

            @on("vector")
            def _(vector, tt=tt, dc=dc, pG=pG):
                cx.wait(vector, "v", ("TS",))
                cx.wait_mark(vector, "v", gmk.get(("gscr",)))
                S = mkstep(vector, "v", "VS", deps=())
                S(vector.scalar_tensor_tensor, gscr[:, dc * 512:(dc + 1) * 512],
                  pG[:], 1.0, s2ib[:, tt * 512:(tt + 1) * 512], ALU.bypass, ALU.mult)
            gmk[("pG", dc % 2)] = cx.mark("VS")
            vs_after_g[dc] = cx.cnt["VS"]

        @on("scalar")
        def _(scalar, tt=tt, vs_after_g=vs_after_g):
            cx.wait_mark(scalar, "s", gmk.get(("gst",)))
            for dc in range(NCH):
                cx.wait_value(scalar, "s", "VS", vs_after_g[dc])
                ins = scalar.activation(gst[:, dc * 512:(dc + 1) * 512],
                                        gscr[:, dc * 512:(dc + 1) * 512], AF.Silu,
                                        bias=sb["v_b2c"][:, dc:dc + 1])
                cx.inc(ins, "SS")
        gmk[("gscr",)] = cx.mark("SS")

        @on("sync")
        def _(sync, tt=tt):
            D = mkdma(sync, "sy", "LD", deps=("SS",))
            for dc in range(NCH):
                D(DT["g0_d"][dc * 128:(dc + 1) * 128, tt * 512:(tt + 1) * 512],
                  gst[:, dc * 512:(dc + 1) * 512])
        gmk[("gst",)] = cx.mark("LD")

    barrier()
    phg.close()

    s_carry = contextlib.ExitStack()
    rwkv = sball("rwkv", [128, NCH * BT], BF16, s_carry)
    pha = contextlib.ExitStack()
    xn2r = sball("xn2r", [128, NCH * BT], BF16, pha)
    wst = sball("wst", [128, 2 * 1024], F32, pha)
    wbt = sball("wbt", [128, 2 * 1024], BF16, pha)
    kbuf = sball("kbuf", [128, BT], BF16, pha)
    vbuf = sball("vbuf", [128, BT], BF16, pha)
    kvb = sball("kvb", [128, BT], F32, pha)
    ubuf = sball("ubuf", [128, BT], F32, pha)
    pbuf = sball("pbuf", [128, BT], F32, pha)
    sS = sball("sS", [128, BT + 1], F32, pha)
    wkvb = sball("wkvb", [128, BT], F32, pha)
    tvalsb = sball("tvalsb", [128, BT], F32, pha)
    sigr = sball("sigr", [128, NCH * BT], BF16, pha)
    totc = sball("totc", [128, 8], F32, pha)
    cgat = sball("cgat", [128, 16], F32, pha)
    pK = pha.enter_context(nc.psum_tensor("pK", [128, 512], F32))
    pV = pha.enter_context(nc.psum_tensor("pV", [128, 512], F32))
    pR = pha.enter_context(nc.psum_tensor("pR", [128, 512], F32))

    @on("sync")
    def _(sync):
        D = mkdma(sync, "sy", "LD")
        D(xn2r[:].rearrange("p (a n) -> p a n", a=NCH),
          DT["xn2a_d"].rearrange("(a p) n -> p a n", p=128))
        D(tvalsb[:], DT["tvals"])

    pmk = {}

    @on("vector")
    def _(vector):
        S = mkstep(vector, "v", "VS")
        S(vector.memset, sS[:, 0:1], 0.0)

    for dc in range(NCH):
        @on("vector")
        def _(vector, dc=dc):
            S = mkstep(vector, "v", "VS")
            S(vector.memset, wkvb[:], 0.0)

        for wi, wname in enumerate(("WkT", "WvT", "WrT")):
            wb = (dc * 3 + wi) % 2

            @on("sync")
            def _(sync, dc=dc, wname=wname, wb=wb):
                cx.wait_mark(sync, "sy", pmk.get(("wst", wb)))
                D = mkdma(sync, "sy", "LD", deps=())
                D(wst[:, wb * 1024:(wb + 1) * 1024].rearrange(
                      "p (a n) -> p a n", a=8),
                  DT[wname][:, dc * 128:(dc + 1) * 128].rearrange(
                      "(a p) n -> p a n", p=128))

            @on("vector")
            def _(vector, wb=wb):
                cx.wait(vector, "v", ("LD",))
                cx.wait_mark(vector, "v", pmk.get(("wbt", wb)))
                S = mkstep(vector, "v", "VS", deps=())
                S(vector.tensor_copy, wbt[:, wb * 1024:(wb + 1) * 1024],
                  wst[:, wb * 1024:(wb + 1) * 1024])
            pmk[("wst", wb)] = cx.mark("VS")

            pdst = (pK, pV, pR)[wi]
            for tt in range(4):
                @on("tensor")
                def _(tensor, dc=dc, tt=tt, pdst=pdst, wb=wb, wi=wi):
                    cx.wait(tensor, "t", ("VS", "SS"))
                    S = mkstep(tensor, "t", "TS", deps=())
                    for ca in range(NCH):
                        S(tensor.matmul, pdst[:],
                          wbt[:, wb * 1024 + ca * 128:wb * 1024 + (ca + 1) * 128],
                          xn2r[:, ca * BT + tt * 512:ca * BT + (tt + 1) * 512],
                          start=(ca == 0), stop=(ca == NCH - 1))

                @on("scalar")
                def _(scalar, dc=dc, tt=tt, wi=wi, pdst=pdst):
                    S = mkstep(scalar, "s", "SS", deps=("TS", "VS"))
                    if wi == 0:
                        S(scalar.copy, kbuf[:, tt * 512:(tt + 1) * 512], pdst[:])
                    elif wi == 1:
                        S(scalar.copy, vbuf[:, tt * 512:(tt + 1) * 512], pdst[:])
                    else:
                        S(scalar.activation,
                          sigr[:, dc * BT + tt * 512:dc * BT + (tt + 1) * 512],
                          pdst[:], AF.Sigmoid)
            pmk[("wbt", wb)] = cx.mark("TS")

        for tt in range(4):
            sl = slice(tt * 512, (tt + 1) * 512)

            @on("vector")
            def _(vector, dc=dc, sl=sl):
                S = mkstep(vector, "v", "VS", deps=("SS", "LD", "VS"))
                S(vector.tensor_mul, kvb[:, sl], kbuf[:, sl], vbuf[:, sl])
                S(vector.tensor_scalar, pbuf[:, sl], tvalsb[:, sl],
                  sb["v_sp"][:, dc:dc + 1], None, ALU.mult)

            @on("scalar")
            def _(scalar, sl=sl):
                S = mkstep(scalar, "s", "SS", deps=("VS",))
                S(scalar.activation, pbuf[:, sl], pbuf[:, sl], AF.Exp, scale=-1.0)

            @on("vector")
            def _(vector, sl=sl):
                S = mkstep(vector, "v", "VS", deps=("SS",))
                S(vector.tensor_mul, ubuf[:, sl], kvb[:, sl], pbuf[:, sl])

        @on("vector")
        def _(vector, dc=dc):
            S = mkstep(vector, "v", "VS", deps=("SS",))
            S(vector.tensor_tensor_scan, sS[:, 1:BT + 1], wkvb[:], ubuf[:], 0.0,
              ALU.add, ALU.add)
            S.c(vector.tensor_copy, totc[:, dc:dc + 1], sS[:, BT:BT + 1])
            S.c(vector.tensor_scalar, wkvb[:], kvb[:], sb["v_tf"][:, dc:dc + 1], None,
                ALU.mult)
            S.c(vector.scalar_tensor_tensor, wkvb[:], sS[:, 0:BT],
                sb["v_td"][:, dc:dc + 1], wkvb[:], ALU.mult, ALU.add)
            S.c(vector.tensor_mul, rwkv[:, dc * BT:(dc + 1) * BT],
                sigr[:, dc * BT:(dc + 1) * BT], wkvb[:])

    @on("gpsimd")
    def _(g):
        D = mkdma(g, "g", "GD")
        S = mkstep(g, "g", "CC")
        D(DT["cc_carry_in"].rearrange("o (a p) -> (o p) a", p=128), totc[:],
          allow_slow_non_contiguous=True)
        cx.wait(g, "g", ("GD",))
        S(g.collective_compute, "AllGather", ALU.bypass,
          replica_groups=[[0, 1], [2, 3], [4, 5], [6, 7]],
          ins=[DT["cc_carry_in"].opt()], outs=[DT["cc_carry_out"].opt()])
        cx.wait(g, "g", ("CC",))
        D(cgat[:, 0:8],
          DT["cc_carry_out"][0:1, :].rearrange("o (a p) -> (o p) a", p=128),
          allow_slow_non_contiguous=True)
        D(cgat[:, 8:16],
          DT["cc_carry_out"][1:2, :].rearrange("o (a p) -> (o p) a", p=128),
          allow_slow_non_contiguous=True)

    @on("vector")
    def _(vector):
        S = mkstep(vector, "v", "VS")
        S(vector.tensor_scalar, sb["carry_fix"][:], cgat[:, 0:8],
          sb["c_carryw"][:, 0:1], None, ALU.mult)
        S.c(vector.scalar_tensor_tensor, sb["carry_fix"][:], cgat[:, 8:16],
            sb["c_carryw"][:, 1:2], sb["carry_fix"][:], ALU.mult, ALU.add)
        S.c(vector.tensor_mul, sb["carry_fix"][:], sb["carry_fix"][:], sb["v_td"][:])
        for dc in range(NCH):
            sl = slice(dc * BT, (dc + 1) * BT)
            S.c(vector.scalar_tensor_tensor, rwkv[:, sl], sigr[:, sl],
                sb["carry_fix"][:, dc:dc + 1], rwkv[:, sl], ALU.mult, ALU.add)

    barrier()
    pha.close()

    s_x1 = contextlib.ExitStack()
    x1f = sball("x1f", [128, NCH * BT], F32, s_x1)
    phr = contextlib.ExitStack()
    woT = sball("woT", [128, NCH * C], F32, phr)
    woTb = sball("woTb", [128, NCH * C], BF16, phr)
    g0t = sball("g0t", [128, 2 * 512], F32, phr)
    xft = sball("xft", [128, 2 * 512], F32, phr)
    pS0 = phr.enter_context(nc.psum_tensor("pS0", [128, 512], F32))
    pS1 = phr.enter_context(nc.psum_tensor("pS1", [128, 512], F32))

    @on("sync")
    def _(sync):
        D = mkdma(sync, "sy", "LD")
        D(woT[:].rearrange("p (a n) -> p a n", a=NCH),
          DT["WoT"].rearrange("(a p) n -> p a n", p=128))

    @on("vector")
    def _(vector):
        S = mkstep(vector, "v", "VS")
        S(vector.tensor_copy, woTb[:], woT[:])

    for dc in range(NCH):
        for tt in range(4):
            pS = (pS0, pS1)[tt % 2]
            buf = tt % 2

            @on("sync")
            def _(sync, dc=dc, tt=tt, buf=buf):
                D = mkdma(sync, "sy", "LD", deps=("VS", "LD"))
                D(g0t[:, buf * 512:(buf + 1) * 512],
                  DT["g0_d"][dc * 128:(dc + 1) * 128, tt * 512:(tt + 1) * 512])
                D(xft[:, buf * 512:(buf + 1) * 512],
                  DT["x_fm"][dc * 128:(dc + 1) * 128, tt * 512:(tt + 1) * 512])

            @on("tensor")
            def _(tensor, dc=dc, tt=tt, pS=pS):
                S = mkstep(tensor, "t", "TS", deps=("VS",))
                for ca in range(NCH):
                    S(tensor.matmul, pS[:],
                      woTb[:, ca * C + dc * 128:ca * C + (dc + 1) * 128],
                      rwkv[:, ca * BT + tt * 512:ca * BT + (tt + 1) * 512],
                      start=(ca == 0), stop=(ca == NCH - 1))

            @on("vector")
            def _(vector, dc=dc, tt=tt, pS=pS, buf=buf):
                S = mkstep(vector, "v", "VS", deps=("TS", "LD"))
                dst = x1f[:, dc * BT + tt * 512:dc * BT + (tt + 1) * 512]
                S(vector.tensor_mul, dst, pS[:], g0t[:, buf * 512:(buf + 1) * 512])
                S.c(vector.tensor_add, dst, dst, xft[:, buf * 512:(buf + 1) * 512])

    @on("sync")
    def _(sync):
        D = mkdma(sync, "sy", "LD")
        D(DT["x1_d"].rearrange("(a p) n -> p a n", p=128),
          x1f[:].rearrange("p (a n) -> p a n", a=NCH))

    barrier()
    phr.close()

    ph2 = contextlib.ExitStack()
    xcb = sball("xcb", [128, 2 * 512], BF16, ph2)
    MTl = sball("MTl", [128, 8 * C], BF16, ph2)
    x1sq = sball("x1sq", [128, 2 * 512], BF16, ph2)
    xn1b = sball("xn1b", [128, NCH * BT], BF16, ph2)
    l1scr = sball("l1scr", [128, 512], F32, ph2)
    g1st = sball("g1st", [128, 512], F32, ph2)
    pM0 = ph2.enter_context(nc.psum_tensor("pM0", [128, 512], F32))
    pM1 = ph2.enter_context(nc.psum_tensor("pM1", [128, 512], F32))
    pGa = ph2.enter_context(nc.psum_tensor("pGa", [128, 512], F32))
    pGb = ph2.enter_context(nc.psum_tensor("pGb", [128, 512], F32))

    @on("sync")
    def _(sync):
        D = mkdma(sync, "sy", "LD")
        D(MTl[:].rearrange("p (a n) -> p a n", a=NCH),
          DT["mt_d"].rearrange("(a p) n -> p a n", p=128))

    for tt in range(4):
        for ca in range(NCH):
            buf = ca % 2
            xc = xcb[:, buf * 512:(buf + 1) * 512]
            sq = x1sq[:, buf * 512:(buf + 1) * 512]

            @on("vector")
            def _(vector, tt=tt, ca=ca, xc=xc):
                S = mkstep(vector, "v", "VS", deps=("TS",))
                S(vector.tensor_copy, xc,
                  x1f[:, ca * BT + tt * 512:ca * BT + (tt + 1) * 512])

            @on("scalar")
            def _(scalar, xc=xc, sq=sq):
                S = mkstep(scalar, "s", "SS", deps=("VS", "TS"))
                S(scalar.activation, sq, xc, AF.Square)

            @on("tensor")
            def _(tensor, ca=ca, xc=xc, sq=sq):
                S = mkstep(tensor, "t", "TS", deps=("VS", "SS"))
                S(tensor.matmul, pM0[:], sb["c_onesb"][:], xc,
                  start=(ca == 0), stop=(ca == NCH - 1))
                S(tensor.matmul, pM1[:], sb["c_onesb"][:], sq,
                  start=(ca == 0), stop=(ca == NCH - 1))

        @on("scalar")
        def _(scalar, tt=tt):
            S = mkstep(scalar, "s", "SS")
            S(scalar.activation, l1scr[:], pM0[:], AF.Square, scale=1.0 / C)

        @on("vector")
        def _(vector, tt=tt):
            S = mkstep(vector, "v", "VS", deps=("TS", "SS"))
            sl = slice(tt * 512, (tt + 1) * 512)
            S(vector.tensor_scalar, sb["abc"][:, sl], pM1[:], 1.0 / C, EPS,
              ALU.mult, ALU.add)
            S.c(vector.tensor_sub, sb["abc"][:, sl], sb["abc"][:, sl], l1scr[:])

        @on("scalar")
        def _(scalar, tt=tt):
            S = mkstep(scalar, "s", "SS")
            sl = slice(tt * 512, (tt + 1) * 512)
            S(scalar.activation, sb["bbc"][:, sl], sb["abc"][:, sl], AF.Sqrt)

        @on("vector")
        def _(vector, tt=tt):
            S = mkstep(vector, "v", "VS", deps=("SS",))
            sl = slice(tt * 512, (tt + 1) * 512)
            S(vector.reciprocal, sb["abc"][:, sl], sb["bbc"][:, sl])
            S.c(vector.scalar_tensor_tensor, sb["bbc"][:, sl], pM0[:], -1.0 / C,
                sb["abc"][:, sl], ALU.mult, ALU.mult)

    @on("vector")
    def _(vector):
        S = mkstep(vector, "v", "VS")
        for ca in range(NCH):
            sl = slice(ca * BT, (ca + 1) * BT)
            S.c(vector.tensor_mul, x1f[:, sl], x1f[:, sl], sb["abc"][:])
            S.c(vector.tensor_add, xn1b[:, sl], x1f[:, sl], sb["bbc"][:])

    @on("gpsimd")
    def _(g):
        D = mkdma(g, "g", "GD")
        S = mkstep(g, "g", "CC")
        D(DT["cc_xn1_in"].rearrange("(a p) n -> p a n", p=128),
          xn1b[:].rearrange("p (a n) -> p a n", a=NCH))
        cx.wait(g, "g", ("GD",))
        S(g.collective_compute, "AllGather", ALU.bypass,
          replica_groups=[[0, 1], [2, 3], [4, 5], [6, 7]],
          ins=[DT["cc_xn1_in"].opt()], outs=[DT["cc_xn1_out"].opt()])

    for dc in range(NCH):
        for tt in range(4):
            pG = (pGa, pGb)[tt % 2]

            @on("tensor")
            def _(tensor, dc=dc, tt=tt, pG=pG):
                S = mkstep(tensor, "t", "TS", deps=("VS", "SS", "LD"))
                for ca in range(NCH):
                    S(tensor.matmul, pG[:],
                      MTl[:, ca * C + dc * 128:ca * C + (dc + 1) * 128],
                      xn1b[:, ca * BT + tt * 512:ca * BT + (tt + 1) * 512],
                      start=(ca == 0), stop=(ca == NCH - 1))

            @on("scalar")
            def _(scalar, dc=dc, tt=tt, pG=pG):
                S = mkstep(scalar, "s", "SS", deps=("TS", "LD"))
                S(scalar.activation, g1st[:], pG[:], AF.Silu,
                  bias=sb["v_bTTc"][:, dc:dc + 1])

            @on("sync")
            def _(sync, dc=dc, tt=tt):
                D = mkdma(sync, "sy", "LD", deps=("SS",))
                D(DT["g1_d"][dc * 128:(dc + 1) * 128, tt * 512:(tt + 1) * 512],
                  g1st[:])

    barrier()
    ph2.close()
    s_x1.close()
    s_carry.close()

    s_att = contextlib.ExitStack()
    Qh = sball("Qh", [128, 4 * T], BF16, s_att)
    Kh = sball("Kh", [128, 4 * T], BF16, s_att)
    Vsb = sball("Vsb", [128, 32 * 520], BF16, s_att)
    ph3 = contextlib.ExitStack()
    wqs = sball("wqs", [128, 2 * 1536], F32, ph3)
    wqb = sball("wqb", [128, NCH * 1536], BF16, ph3)
    cosQs = sball("cosQs", [128, T], BF16, ph3)
    sinQs = sball("sinQs", [128, T], BF16, ph3)
    cosKs = sball("cosKs", [128, T], BF16, ph3)
    sinKs = sball("sinKs", [128, T], BF16, ph3)
    xg = sball("xg", [128, NCH * 512], BF16, ph3)
    qraw = sball("qraw", [128, 512], BF16, ph3)
    t1b = sball("t1b", [128, 512], BF16, ph3)
    t2b = sball("t2b", [128, 512], BF16, ph3)
    pQ0 = ph3.enter_context(nc.psum_tensor("pQ0", [128, 512], F32))
    pQ1 = ph3.enter_context(nc.psum_tensor("pQ1", [128, 512], F32))
    pRot = ph3.enter_context(nc.psum_tensor("pRot", [128, 512], F32))
    pVv = ph3.enter_context(nc.psum_tensor("pVv", [128, 512], F32))

    @on("sync")
    def _(sync):
        D = mkdma(sync, "sy", "LD")
        for nm, dst in (("cosQ", cosQs), ("sinQ", sinQs),
                        ("cosK", cosKs), ("sinK", sinKs)):
            D(dst[:], DT[nm])

    for half in range(4):
        @on("sync")
        def _(sync, half=half):
            D = mkdma(sync, "sy", "LD")
            D(wqs[:].rearrange("p (a n) -> p a n", a=2),
              DT["WqkvT_sl"][half * 256:(half + 1) * 256, :].rearrange(
                  "(a p) n -> p a n", p=128))

        @on("vector")
        def _(vector, half=half):
            S = mkstep(vector, "v", "VS")
            S(vector.tensor_copy, wqb[:, half * 2 * 1536:(half + 1) * 2 * 1536],
              wqs[:])

    @on("vector")
    def _(vector):
        S = mkstep(vector, "v", "VS")
        S(vector.memset,
          Vsb[:].rearrange("p (k h e) -> p k h e", k=32, h=8)[:, :, :, 64:65], 1.0)

    for tt in range(8):
        r, tl = tt // 4, tt % 4

        @on("sync")
        def _(sync, r=r, tl=tl):
            D = mkdma(sync, "sy", "LD", deps=("CC", "TS"))
            for ca in range(NCH):
                D(xg[:, ca * 512:(ca + 1) * 512],
                  DT["cc_xn1_out"][r * C + ca * 128:r * C + (ca + 1) * 128,
                                   tl * 512:(tl + 1) * 512])

        for qc in range(8):
            pQ = (pQ0, pQ1)[qc % 2]
            isq = qc < 4
            hp = qc % 4
            cosT, sinT = (cosQs, sinQs) if isq else (cosKs, sinKs)
            dstT = Qh if isq else Kh

            @on("tensor")
            def _(tensor, tt=tt, qc=qc, pQ=pQ):
                S = mkstep(tensor, "t", "TS", deps=("LD", "SS", "VS"))
                for ca in range(NCH):
                    S(tensor.matmul, pQ[:],
                      wqb[:, ca * 1536 + qc * 128:ca * 1536 + (qc + 1) * 128],
                      xg[:, ca * 512:(ca + 1) * 512],
                      start=(ca == 0), stop=(ca == NCH - 1))

            @on("scalar")
            def _(scalar, pQ=pQ):
                S = mkstep(scalar, "s", "SS", deps=("TS", "VS"))
                S(scalar.copy, qraw[:], pQ[:])

            @on("tensor")
            def _(tensor):
                S = mkstep(tensor, "t", "TS", deps=("SS", "VS"))
                S(tensor.matmul, pRot[:], sb["c_ropeR"][:], qraw[:],
                  start=True, stop=True)

            @on("vector")
            def _(vector, tt=tt, hp=hp, cosT=cosT, sinT=sinT, dstT=dstT):
                S = mkstep(vector, "v", "VS", deps=("SS", "TS"))
                sl = slice(tt * 512, (tt + 1) * 512)
                S(vector.tensor_mul, t1b[:], qraw[:], cosT[:, sl])
                S(vector.tensor_mul, t2b[:], pRot[:], sinT[:, sl])
                S.c(vector.tensor_add,
                    dstT[:, hp * T + tt * 512:hp * T + (tt + 1) * 512], t1b[:],
                    t2b[:])

        for tb in range(4):
            kt = tt * 4 + tb

            @on("tensor")
            def _(tensor, tb=tb):
                S = mkstep(tensor, "t", "TS", deps=("LD", "SS", "VS"))
                for ca in range(NCH):
                    S(tensor.matmul, pVv[:],
                      xg[:, ca * 512 + tb * 128:ca * 512 + (tb + 1) * 128],
                      wqb[:, ca * 1536 + 1024:ca * 1536 + 1536],
                      start=(ca == 0), stop=(ca == NCH - 1))

            @on("scalar")
            def _(scalar, kt=kt):
                S = mkstep(scalar, "s", "SS", deps=("TS", "VS"))
                S(scalar.copy,
                  Vsb[:, kt * 520:(kt + 1) * 520].rearrange(
                      "p (h e) -> p h e", h=8)[:, :, 0:64],
                  pVv[:].rearrange("p (h e) -> p h e", h=8))

    barrier()
    ph3.close()

    ph4 = contextlib.ExitStack()
    pTb = sball("pTb", [128, 2048], BF16, ph4)
    dmaskP = sball("dmaskP_s", [128, 4096], BF16, ph4)
    lst = sball("lst", [128, 1024], F32, ph4)
    l0r = sball("l0r", [1, 1024], F32, ph4)
    bcs = sball("bcs", [64, 1024], F32, ph4)
    outn = sball("outn", [64, 1024], BF16, ph4)
    pSmA = ph4.enter_context(nc.psum_tensor("pSmA", [128, 1024], F32))
    pSmB = ph4.enter_context(nc.psum_tensor("pSmB", [128, 1024], F32))
    pAV0 = ph4.enter_context(nc.psum_tensor("pAV0", [128, 512], F32))
    pAV1 = ph4.enter_context(nc.psum_tensor("pAV1", [128, 512], F32))
    pDV = ph4.enter_context(nc.psum_tensor("pDV", [128, 512], F32))

    @on("sync")
    def _(sync):
        D = mkdma(sync, "sy", "LD")
        D(dmaskP[:], DT["dmaskP"])

    amk = {}
    for qt in range(8):
        for hp in range(4):
            ngroups = 4 * qt + 4
            S0 = cx.cnt["SS"]
            V0 = cx.cnt["VS"]
            ts_after_st = {}
            ts_after_av = {}

            def emit_st(kg, qt=qt, hp=hp, S0=S0, ts_after_st=ts_after_st):
                sm = kg % 2
                pSm = (pSmA, pSmB)[sm]

                @on("tensor")
                def _(tensor, kg=kg, sm=sm, pSm=pSm):
                    cx.wait(tensor, "t", ("LD",))
                    cx.wait_value(tensor, "t", "SS", S0 + kg - 1)
                    S = mkstep(tensor, "t", "TS", deps=())
                    for sub in range(2):
                        prt = slice(64 * sub, 64 * sub + 64)
                        S(tensor.matmul, pSm[:, sub * 512:(sub + 1) * 512],
                          Kh[prt, hp * T + kg * 128:hp * T + (kg + 1) * 128],
                          Qh[prt, hp * T + qt * 512:hp * T + (qt + 1) * 512],
                          start=True, stop=True)
                ts_after_st[kg] = cx.cnt["TS"]

            emit_st(0)
            for kg in range(ngroups):
                sm = kg % 2
                last = (kg == ngroups - 1)
                if not last:
                    emit_st(kg + 1)

                @on("tensor")
                def _(tensor, qt=qt, hp=hp, kg=kg, last=last, sm=sm,
                      ngroups=ngroups, S0=S0, V0=V0):
                    cx.wait_value(tensor, "t", "SS", S0 + kg + 1)
                    if kg >= ngroups - 4:
                        cx.wait_value(tensor, "t", "VS",
                                      V0 + (kg - (ngroups - 4)) + 1)
                    if kg == 0:
                        cx.wait_mark(tensor, "t", amk.get(("pAV", 0)))
                        cx.wait_mark(tensor, "t", amk.get(("pAV", 1)))
                    S = mkstep(tensor, "t", "TS", deps=())
                    for sub in range(2):
                        h = 2 * hp + sub
                        pAV = (pAV0, pAV1)[sub]
                        S(tensor.matmul, pAV[0:65, :],
                          Vsb[:, kg * 520 + h * 65:kg * 520 + (h + 1) * 65],
                          pTb[:, sm * 1024 + sub * 512:sm * 1024 + (sub + 1) * 512],
                          start=(kg == 0), stop=last)
                ts_after_av[kg] = cx.cnt["TS"]

            @on("scalar")
            def _(scalar, ngroups=ngroups, ts_after_st=ts_after_st,
                  ts_after_av=ts_after_av):
                for kg in range(ngroups):
                    sm = kg % 2
                    pSm = (pSmA, pSmB)[sm]
                    cx.wait_value(scalar, "s", "TS", ts_after_st[kg])
                    if kg >= 2:
                        cx.wait_value(scalar, "s", "TS", ts_after_av[kg - 2])
                    ins = scalar.activation(pTb[:, sm * 1024:(sm + 1) * 1024],
                                            pSm[:, 0:1024], AF.Exp,
                                            bias=sb["c_nb30"][:, 0:1])
                    cx.inc(ins, "SS")

            @on("vector")
            def _(vector, ngroups=ngroups, S0=S0):
                cx.wait(vector, "v", ("LD",))
                for kg in range(ngroups - 4, ngroups):
                    sm = kg % 2
                    moff = (kg - (ngroups - 4)) * 1024
                    cx.wait_value(vector, "v", "SS", S0 + kg + 1)
                    ins = vector.tensor_mul(pTb[:, sm * 1024:(sm + 1) * 1024],
                                            pTb[:, sm * 1024:(sm + 1) * 1024],
                                            dmaskP[:, moff:moff + 1024])
                    cx.inc(ins, "VS")

            for sub in range(2):
                h = 2 * hp + sub
                hb = sub
                pAV = (pAV0, pAV1)[sub]

                @on("vector")
                def _(vector, hb=hb, pAV=pAV):
                    cx.wait(vector, "v", ("TS",))
                    cx.wait_mark(vector, "v", amk.get(("lst", hb)))
                    S = mkstep(vector, "v", "VS", deps=())
                    S(vector.tensor_copy, lst[64:65, hb * 512:(hb + 1) * 512],
                      pAV[64:65, :])

                @on("sync")
                def _(sync, hb=hb):
                    D = mkdma(sync, "sy", "LD", deps=("VS",))
                    D(l0r[0:1, hb * 512:(hb + 1) * 512],
                      lst[64:65, hb * 512:(hb + 1) * 512])
                amk[("lst", hb)] = cx.mark("LD")

                @on("vector")
                def _(vector, hb=hb):
                    S = mkstep(vector, "v", "VS", deps=("LD",))
                    S(vector.reciprocal, l0r[0:1, hb * 512:(hb + 1) * 512],
                      l0r[0:1, hb * 512:(hb + 1) * 512])

                @on("tensor")
                def _(tensor, hb=hb):
                    cx.wait(tensor, "t", ("VS",))
                    cx.wait_mark(tensor, "t", amk.get(("pDV",)))
                    S = mkstep(tensor, "t", "TS", deps=())
                    S(tensor.matmul, pDV[0:64, :], sb["c_onesrowf"][0:1, 0:64],
                      l0r[0:1, hb * 512:(hb + 1) * 512], start=True, stop=True)

                @on("vector")
                def _(vector, hb=hb, pAV=pAV):
                    cx.wait(vector, "v", ("TS",))
                    cx.wait_mark(vector, "v", amk.get(("outn", hb)))
                    S = mkstep(vector, "v", "VS", deps=())
                    S(vector.tensor_copy, bcs[:, hb * 512:(hb + 1) * 512],
                      pDV[0:64, :])
                    S.c(vector.tensor_mul, outn[:, hb * 512:(hb + 1) * 512],
                        pAV[0:64, :], bcs[:, hb * 512:(hb + 1) * 512])
                amk[("pDV",)] = cx.mark("VS")
                amk[("pAV", hb)] = cx.mark("VS")

                @on("sync")
                def _(sync, qt=qt, h=h, hb=hb):
                    D = mkdma(sync, "sy", "LD", deps=("VS",))
                    D(DT["cc_out_in"][h * 64:(h + 1) * 64, qt * 512:(qt + 1) * 512],
                      outn[:, hb * 512:(hb + 1) * 512])
                amk[("outn", hb)] = cx.mark("LD")

    @on("gpsimd")
    def _(g):
        S = mkstep(g, "g", "CC")
        S(g.collective_compute, "AllGather", ALU.bypass,
          replica_groups=[[0, 1], [2, 3], [4, 5], [6, 7]],
          ins=[DT["cc_out_in"].opt()], outs=[DT["cc_out_out"].opt()])

    barrier()
    ph4.close()
    s_att.close()

    ph5 = contextlib.ExitStack()
    wo1 = sball("wo1", [128, NCH * C], F32, ph5)
    wo1b = sball("wo1b", [128, NCH * C], BF16, ph5)
    gA = sball("gA", [128, BT], BF16, ph5)
    gB = sball("gB", [128, BT], BF16, ph5)
    omy = sball("omy", [128, NCH * BT], BF16, ph5)
    oscr = sball("oscr", [128, BT], F32, ph5)
    g1t = sball("g1t", [128, 2 * 512], F32, ph5)
    x1t = sball("x1t", [128, 2 * 512], F32, ph5)
    x2st = sball("x2st", [128, 2 * 512], F32, ph5)
    pZ0 = ph5.enter_context(nc.psum_tensor("pZ0", [128, 512], F32))
    pZ1 = ph5.enter_context(nc.psum_tensor("pZ1", [128, 512], F32))

    @on("sync")
    def _(sync):
        D = mkdma(sync, "sy", "LD")
        D(wo1[:].rearrange("p (a n) -> p a n", a=NCH),
          DT["Wo1T"].rearrange("(a p) n -> p a n", p=128))

    @on("vector")
    def _(vector):
        S = mkstep(vector, "v", "VS")
        S(vector.tensor_copy, wo1b[:], wo1[:])

    for r in range(2):
        for cc4 in range(4):
            dc = r * 4 + cc4
            row = r * 512 + cc4 * 128

            @on("sync")
            def _(sync, row=row):
                D = mkdma(sync, "sy", "LD", deps=("VS", "CC"))
                D(gA[:], DT["cc_out_out"][row:row + 128, 0:BT])
                D(gB[:], DT["cc_out_out"][row:row + 128, BT:T])

            @on("vector")
            def _(vector, dc=dc):
                S = mkstep(vector, "v", "VS", deps=("LD",))
                S(vector.tensor_scalar, oscr[:], gA[:], sb["c_selw"][:, 0:1], None,
                  ALU.mult)
                S.c(vector.scalar_tensor_tensor, oscr[:], gB[:], sb["c_selw"][:, 1:2],
                    oscr[:], ALU.mult, ALU.add)
                S.c(vector.tensor_copy, omy[:, dc * BT:(dc + 1) * BT], oscr[:])

    for dc in range(NCH):
        for tt in range(4):
            pZ = (pZ0, pZ1)[tt % 2]
            buf = tt % 2

            @on("sync")
            def _(sync, dc=dc, tt=tt, buf=buf):
                D = mkdma(sync, "sy", "LD", deps=("VS", "LD"))
                D(g1t[:, buf * 512:(buf + 1) * 512],
                  DT["g1_d"][dc * 128:(dc + 1) * 128, tt * 512:(tt + 1) * 512])
                D(x1t[:, buf * 512:(buf + 1) * 512],
                  DT["x1_d"][dc * 128:(dc + 1) * 128, tt * 512:(tt + 1) * 512])

            @on("tensor")
            def _(tensor, dc=dc, tt=tt, pZ=pZ):
                S = mkstep(tensor, "t", "TS", deps=("VS",))
                for ca in range(NCH):
                    S(tensor.matmul, pZ[:],
                      wo1b[:, ca * C + dc * 128:ca * C + (dc + 1) * 128],
                      omy[:, ca * BT + tt * 512:ca * BT + (tt + 1) * 512],
                      start=(ca == 0), stop=(ca == NCH - 1))

            @on("vector")
            def _(vector, dc=dc, tt=tt, pZ=pZ, buf=buf):
                S = mkstep(vector, "v", "VS", deps=("TS", "LD"))
                d = x2st[:, buf * 512:(buf + 1) * 512]
                S(vector.tensor_mul, d, pZ[:], g1t[:, buf * 512:(buf + 1) * 512])
                S.c(vector.tensor_add, d, d, x1t[:, buf * 512:(buf + 1) * 512])

            @on("sync")
            def _(sync, dc=dc, tt=tt, buf=buf):
                D = mkdma(sync, "sy", "LD", deps=("VS",))
                D(DT["x2_fm"][dc * 128:(dc + 1) * 128, tt * 512:(tt + 1) * 512],
                  x2st[:, buf * 512:(buf + 1) * 512])

    @on("sync")
    def _(sync):
        cx.wait_all(sync, "sy")

    ph5.close()
    es.close()
    return nc


def _host_prep(inputs):
    f32 = np.float32
    x = np.asarray(inputs["L0_attn_x"], f32)
    WkT = np.ascontiguousarray(np.asarray(inputs["L0_attn_Wk"], f32).T)
    WvT = np.ascontiguousarray(np.asarray(inputs["L0_attn_Wv"], f32).T)
    WrT = np.ascontiguousarray(np.asarray(inputs["L0_attn_Wr"], f32).T)
    WoT = np.ascontiguousarray(np.asarray(inputs["L0_attn_Wo"], f32).T)
    w1T = np.ascontiguousarray(np.asarray(inputs["L0_ffn_w1"], f32).T)
    w2T = np.ascontiguousarray(np.asarray(inputs["L0_ffn_w2"], f32).T)
    Wqkv = np.asarray(inputs["L1_attn_Wqkv"], f32)
    Wo1T = np.ascontiguousarray(np.asarray(inputs["L1_attn_Wo"], f32).T)

    def colpack(v, n):
        return np.ascontiguousarray(np.asarray(v, f32).reshape(n, 128).T)

    def rowbc(v):
        v = np.asarray(v, f32)
        return np.ascontiguousarray(np.broadcast_to(v, (128, v.shape[0])))

    gamA = colpack(inputs["L0_attn_ln_gamma"], 8)
    betA = colpack(inputs["L0_attn_ln_beta"], 8)
    decay = colpack(inputs["L0_attn_time_decay"], 8)
    first = colpack(inputs["L0_attn_time_first"], 8)
    b2c = colpack(inputs["L0_ffn_b2"], 8)
    bTT = np.concatenate([np.asarray(inputs["L1_ffn_bias_real"], f32),
                          np.asarray(inputs["L1_ffn_bias_imag"], f32)])
    bTTc = colpack(bTT, 8)
    g1bc = rowbc(inputs["L0_ffn_ln1_g"])
    b1bc = rowbc(inputs["L0_ffn_ln1_b"])
    b1fbc = rowbc(inputs["L0_ffn_b1"])
    g2bc = rowbc(inputs["L0_ffn_ln2_g"])
    be2bc = rowbc(inputs["L0_ffn_ln2_b"])

    identf = np.eye(128, dtype=f32)
    identb = np.eye(128, dtype=f32).astype(BF)
    onesb = np.ones((128, 128), f32).astype(BF)
    onescolf = np.ones((128, 1), f32)
    onesrowf = np.ones((1, 128), f32)
    R = np.zeros((128, 128), f32)
    for i in range(64):
        R[2 * i, 2 * i + 1] = -1.0
        R[2 * i + 1, 2 * i] = 1.0
    ropeRT = np.ascontiguousarray(R.T).astype(BF)
    freqs = np.exp(-np.log(10000.0) * np.arange(0, DH, 2, dtype=f32) / DH)
    ang = np.arange(T, dtype=f32)[:, None] * freqs[None, :]
    ang = np.repeat(ang, 2, axis=1)
    cos64 = np.cos(ang).T.astype(f32)
    sin64 = np.sin(ang).T.astype(f32)
    cos128 = np.concatenate([cos64, cos64], 0)
    sin128 = np.concatenate([sin64, sin64], 0)
    cosQ = (cos128 / 8.0).astype(BF)
    sinQ = (sin128 / 8.0).astype(BF)
    cosK = cos128.astype(BF)
    sinK = sin128.astype(BF)
    k_ = np.arange(128)[:, None]
    q_ = np.arange(512)[None, :]
    dmask = np.concatenate(
        [((k_ + 128 * j) <= q_).astype(f32) for j in range(4)], axis=1).astype(BF)
    dmaskP = np.concatenate(
        [np.concatenate([((k_ + 128 * j) <= q_).astype(f32)] * 2, axis=1)
         for j in range(4)], axis=1).astype(BF)

    G1R = np.asarray(inputs["L1_ffn_G1_R"], f32)
    G2R = np.asarray(inputs["L1_ffn_G2_R"], f32)
    G1I = np.asarray(inputs["L1_ffn_G1_I"], f32)
    G2I = np.asarray(inputs["L1_ffn_G2_I"], f32)

    def ttrep(G1, G2):
        T1 = np.zeros((512, 4, 512), f32)
        T2 = np.zeros((512, 4, 512), f32)
        d_ = np.arange(512) // 16
        g_ = np.arange(512) % 16
        b_ = np.arange(512) // 16
        e_ = np.arange(512) % 16
        for c in range(4):
            A = G1[0, :, c, :]
            Bm = G2[c, :, 0, :]
            T1[:, c, :] = A.T[d_][:, b_]
            T2[:, c, :] = Bm.T[g_][:, e_]
        return (np.ascontiguousarray(T1.reshape(512, 2048)),
                np.ascontiguousarray(T2.reshape(512, 2048)))

    T1r, T2r = ttrep(G1R, G2R)
    T1i, T2i = ttrep(G1I, G2I)

    base = dict(
        WkT=WkT, WvT=WvT, WrT=WrT, WoT=WoT, Wo1T=Wo1T,
        gamA=gamA, betA=betA, decay=decay, first=first, b2c=b2c, bTTc=bTTc,
        g1bc=g1bc, b1bc=b1bc, b1fbc=b1fbc, g2bc=g2bc, be2bc=be2bc,
        identf=identf, identb=identb, onesb=onesb, ropeR=ropeRT,
        onescolf=onescolf, onesrowf=onesrowf,
        nb30=np.full((128, 1), EXPB, f32),
        cosQ=cosQ, sinQ=sinQ, cosK=cosK, sinK=sinK, dmask=dmask,
        T1r=T1r, T2r=T2r, T1i=T1i, T2i=T2i, dmaskP=dmaskP,
    )

    in_maps = []
    for core in range(8):
        b, p = core // 2, core % 2
        m = dict(base)
        xt = np.ascontiguousarray(x[b, p * BT:(p + 1) * BT, :])
        m["x_tok"] = xt
        m["x_fm"] = np.ascontiguousarray(xt.T)
        m["w1T_sl"] = np.ascontiguousarray(w1T[:, core * 512:(core + 1) * 512])
        m["w2T_sl"] = np.ascontiguousarray(w2T[core * 512:(core + 1) * 512, :])
        hrows = np.concatenate([np.arange(p * 512, (p + 1) * 512),
                                C + np.arange(p * 512, (p + 1) * 512),
                                2 * C + np.arange(p * 512, (p + 1) * 512)])
        m["WqkvT_sl"] = np.ascontiguousarray(Wqkv[hrows, :].T)
        m["tvals"] = np.broadcast_to(
            np.arange(p * BT, (p + 1) * BT, dtype=f32), (128, BT)).copy()
        m["selw"] = np.broadcast_to(
            np.array([1.0 - p, float(p)], f32), (128, 2)).copy()
        m["carryw"] = np.broadcast_to(
            np.array([float(p), 0.0], f32), (128, 2)).copy()
        in_maps.append(m)
    return in_maps


def kernel(**inputs):
    global _last_res
    in_maps = _host_prep(inputs)
    nc = build_program()
    res = run_bass_kernel_spmd(nc, in_maps, core_ids=list(range(8)))
    _last_res = res
    out = np.zeros((B, T, C), np.float32)
    for core in range(8):
        b, p = core // 2, core % 2
        out[b, p * BT:(p + 1) * BT, :] = np.asarray(res.results[core]["x2_fm"]).T
    return out


# version 15
# speedup vs baseline: 1.4198x; 1.0344x over previous
import contextlib
import numpy as np
import ml_dtypes

import concourse.bass as bass
import concourse.mybir as mybir
from concourse.bass_utils import run_bass_kernel_spmd

F32 = mybir.dt.float32
BF16 = mybir.dt.bfloat16
AF = mybir.ActivationFunctionType
ALU = mybir.AluOpType
AX = mybir.AxisListType.X
BF = ml_dtypes.bfloat16

B, T, C = 4, 4096, 1024
BT = 2048
H4 = 4096
DH = 64
NCH = 8
NBLK = 16
MYH = 8
EPS = 1e-5
RC = 12582912.0
EXPB = -30.0
ALLSEM = ("LD", "GD", "TS", "VS", "SS", "CC")


class Ctx:
    def __init__(self, nc, es):
        self.sem = {}
        self.cnt = {}
        self.waited = {}
        for name in ALLSEM:
            self.sem[name] = es.enter_context(nc.semaphore(name))
            self.cnt[name] = 0

    def inc(self, instr, name, n=1):
        instr.then_inc(self.sem[name], n)
        self.cnt[name] += n
        return instr

    def wait(self, eng, ename, names):
        for name in names:
            v = self.cnt[name]
            key = (ename, name)
            if v == 0 or self.waited.get(key, -1) >= v:
                continue
            eng.wait_ge(self.sem[name], v)
            self.waited[key] = v

    def wait_all(self, eng, ename):
        self.wait(eng, ename, ALLSEM)

    def mark(self, name):
        return (name, self.cnt[name])

    def wait_value(self, eng, ename, name, v):
        key = (ename, name)
        if v <= 0 or self.waited.get(key, -1) >= v:
            return
        eng.wait_ge(self.sem[name], v)
        self.waited[key] = v

    def wait_mark(self, eng, ename, mk):
        if mk is None:
            return
        name, v = mk
        key = (ename, name)
        if v == 0 or self.waited.get(key, -1) >= v:
            return
        eng.wait_ge(self.sem[name], v)
        self.waited[key] = v


DEBUG = False


def build_program():
    nc = bass.Bass("TRN2", target_bir_lowering=False, debug=False)
    es = contextlib.ExitStack()
    DT = {}

    def din(name, shape, dt=F32):
        DT[name] = nc.dram_tensor(name, shape, dt, kind="ExternalInput").ap()

    def dout(name, shape, dt=F32):
        DT[name] = nc.dram_tensor(name, shape, dt, kind="ExternalOutput").ap()

    def dint(name, shape, dt=F32):
        DT[name] = nc.dram_tensor(name, shape, dt).ap()

    din("x_fm", [C, BT]); din("x_tok", [BT, C])
    din("WkT", [C, C]); din("WvT", [C, C]); din("WrT", [C, C]); din("WoT", [C, C])
    din("w1T_sl", [C, 512]); din("w2T_sl", [512, C])
    din("WqkvT_sl", [C, 1536]); din("Wo1T", [C, C])
    din("gamA", [128, 8]); din("betA", [128, 8])
    din("decay", [128, 8]); din("first", [128, 8])
    din("b2c", [128, 8]); din("bTTc", [128, 8])
    din("g1bc", [128, C]); din("b1bc", [128, C])
    din("b1fbc", [128, H4]); din("g2bc", [128, H4]); din("be2bc", [128, H4])
    din("tvals", [128, BT])
    din("identf", [128, 128], F32)
    din("identb", [128, 128], BF16)
    din("onesb", [128, 128], BF16)
    din("ropeR", [128, 128], BF16)
    din("onescolf", [128, 1]); din("onesrowf", [1, 128])
    din("selw", [128, 2]); din("carryw", [128, 2]); din("nb30", [128, 1])
    din("cosQ", [128, T], BF16); din("sinQ", [128, T], BF16)
    din("cosK", [128, T], BF16); din("sinK", [128, T], BF16)
    din("dmask", [128, 2048], BF16); din("dmaskP", [128, 4096], BF16)
    din("T1r", [512, 2048]); din("T2r", [512, 2048])
    din("T1i", [512, 2048]); din("T2i", [512, 2048])

    dout("x2_fm", [C, BT])
    if DEBUG:
        dout("dbg_xn2a", [C, BT], BF16)
        dout("dbg_int2", [H4, BT], BF16)
        dout("dbg_g0", [C, BT])
        dout("dbg_x1", [C, BT])
        dout("dbg_xn1", [C, BT], BF16)
        dout("dbg_g1", [C, BT])
        dout("dbg_att", [512, T], BF16)

    dint("cc_wsc_in", [1, 2]); dint("cc_wsc_out", [8, 2])
    dint("cc_wq1_in", [C, 512], BF16); dint("cc_wq1_out", [8 * C, 512], BF16)
    dint("cc_wq2_in", [512, C], BF16); dint("cc_wq2_out", [8 * 512, C], BF16)
    dint("cc_carry_in", [1, C]); dint("cc_carry_out", [2, C])
    dint("cc_xn1_in", [C, BT], BF16); dint("cc_xn1_out", [2 * C, BT], BF16)
    dint("cc_out_in", [512, T], BF16); dint("cc_out_out", [1024, T], BF16)
    dint("xn2a_d", [C, BT], BF16)
    dint("int2_d", [H4, BT], BF16)
    dint("g0_d", [C, BT]); dint("g1_d", [C, BT]); dint("x1_d", [C, BT])
    dint("mt_d", [C, C], BF16)

    cx = Ctx(nc, es)
    sb = {}

    def sball(name, shape, dt=F32, stack=None):
        sb[name] = (stack or es).enter_context(nc.sbuf_tensor(name, shape, dt))
        return sb[name]

    for nm, shp, dt in [
        ("c_idf", [128, 128], F32), ("c_idb", [128, 128], BF16),
        ("c_onesb", [128, 128], BF16), ("c_ropeR", [128, 128], BF16),
        ("c_onescolf", [128, 1], F32), ("c_onesrowf", [1, 128], F32),
        ("c_selw", [128, 2], F32), ("c_carryw", [128, 2], F32),
        ("c_nb30", [128, 1], F32),
        ("v_gamA", [128, 8], F32), ("v_betA", [128, 8], F32),
        ("v_decay", [128, 8], F32), ("v_first", [128, 8], F32),
        ("v_b2c", [128, 8], F32), ("v_bTTc", [128, 8], F32),
        ("v_td", [128, 8], F32), ("v_tf", [128, 8], F32), ("v_sp", [128, 8], F32),
        ("st_A", [128, 16], F32), ("st_B", [128, 16], F32),
        ("st_s1", [128, 16], F32), ("st_s1s", [128, 16], F32),
        ("st_s2i", [128, 16], F32),
        ("v_mw", [128, 2], F32), ("v_swc", [128, 2], F32),
        ("row_a", [1, BT], F32),
        ("carry_fix", [128, 8], F32),
        ("colscr", [128, 64], F32),
        ("st16", [16, 256], F32),
        ("abc", [128, BT], F32), ("bbc", [128, BT], F32),
    ]:
        sball(nm, shp, dt)

    block = es.enter_context(nc.Block())

    def on(engname):
        def deco(fn):
            getattr(block, engname)(fn)
        return deco

    def barrier():
        nc.all_engine_barrier()

    def mkstep(eng, ename, own, deps=ALLSEM):
        state = {"first": True}

        def step(f, *a, **k):
            if state["first"]:
                cx.wait(eng, ename, deps)
                state["first"] = False
            return cx.inc(f(*a, **k), own)

        def chain(f, *a, **k):
            if not state["first"]:
                cx.wait(eng, ename, (own,))
            return step(f, *a, **k)

        step.c = chain
        return step

    def mkdma(eng, ename, own, deps=ALLSEM):
        state = {"first": True}

        def dma(out, in_, **k):
            if state["first"]:
                cx.wait(eng, ename, deps)
                state["first"] = False
            return cx.inc(eng.dma_start(out=out, in_=in_, **k), own, 16)
        return dma

    def emit_bc_from_pack(pack_name, pack_sl, dst_name, ppsum):
        @on("tensor")
        def _(tensor):
            S = mkstep(tensor, "t", "TS")
            S(tensor.transpose, ppsum[0:16, 0:128], sb[pack_name][:, pack_sl],
              sb["c_idf"][:])

        @on("scalar")
        def _(scalar):
            S = mkstep(scalar, "s", "SS")
            S(scalar.copy, sb["st16"][:, 0:128], ppsum[0:16, 0:128])

        @on("sync")
        def _(sync):
            D = mkdma(sync, "sy", "LD")
            for tb in range(NBLK):
                D(sb["row_a"][0:1, tb * 128:(tb + 1) * 128],
                  sb["st16"][tb:tb + 1, 0:128])

        @on("tensor")
        def _(tensor):
            S = mkstep(tensor, "t", "TS")
            for tt in range(4):
                S(tensor.matmul, ppsum[:, tt * 512:(tt + 1) * 512],
                  sb["c_onesrowf"][:], sb["row_a"][0:1, tt * 512:(tt + 1) * 512],
                  start=True, stop=True)

        @on("scalar")
        def _(scalar):
            S = mkstep(scalar, "s", "SS")
            S(scalar.copy, sb[dst_name][:], ppsum[:, 0:2048])

    ph0 = contextlib.ExitStack()
    w1s = sball("w1s", [128, 8 * 512], F32, ph0)
    w2s = sball("w2s", [128, 4 * C], F32, ph0)
    wq1 = sball("wq1", [128, 8 * 512], BF16, ph0)
    wq2 = sball("wq2", [128, 4 * C], BF16, ph0)
    wscr = sball("wscr", [128, 4096], F32, ph0)
    wcol = sball("wcol", [128, 8], F32, ph0)
    wg8 = sball("wg8", [8, 2], F32, ph0)
    t1sb = sball("t1sb", [128, 4 * 2048], F32, ph0)
    t2sb = sball("t2sb", [128, 4 * 2048], F32, ph0)
    wTT = sball("wTT", [128, 4 * 512], F32, ph0)
    MT0 = sball("MT0", [128, 8 * C], BF16, ph0)
    pw = ph0.enter_context(nc.psum_tensor("pw", [128, 512], F32))

    @on("sync")
    def _(sync):
        D = mkdma(sync, "sy", "LD")
        for nm, dst in [("identf", "c_idf"), ("identb", "c_idb"),
                        ("onesb", "c_onesb"), ("ropeR", "c_ropeR"),
                        ("onescolf", "c_onescolf"), ("onesrowf", "c_onesrowf"),
                        ("selw", "c_selw"), ("carryw", "c_carryw"), ("nb30", "c_nb30"),
                        ("gamA", "v_gamA"), ("betA", "v_betA"),
                        ("decay", "v_decay"), ("first", "v_first"),
                        ("b2c", "v_b2c"), ("bTTc", "v_bTTc")]:
            D(sb[dst][:], DT[nm])
        D(w1s[:].rearrange("p (a n) -> p a n", a=8),
          DT["w1T_sl"].rearrange("(a p) n -> p a n", p=128))
        D(w2s[:].rearrange("p (a n) -> p a n", a=4),
          DT["w2T_sl"].rearrange("(a p) n -> p a n", p=128))
        D(t1sb[:].rearrange("p (a n) -> p a n", a=4),
          DT["T1r"].rearrange("(a p) n -> p a n", p=128))
        D(t2sb[:].rearrange("p (a n) -> p a n", a=4),
          DT["T2r"].rearrange("(a p) n -> p a n", p=128))

    @on("scalar")
    def _(scalar):
        S = mkstep(scalar, "s", "SS")
        S(scalar.activation, sb["v_td"][:], sb["v_decay"][:], AF.Sigmoid)
        S(scalar.activation, sb["v_tf"][:], sb["v_first"][:], AF.Sigmoid)

    @on("vector")
    def _(vector):
        S = mkstep(vector, "v", "VS")
        S(vector.tensor_scalar, sb["v_sp"][:], sb["v_td"][:], 1e-38, None, ALU.max)

    @on("scalar")
    def _(scalar):
        S = mkstep(scalar, "s", "SS")
        S(scalar.activation, sb["v_sp"][:], sb["v_sp"][:], AF.Ln)

    @on("vector")
    def _(vector):
        S = mkstep(vector, "v", "VS")
        S(vector.tensor_scalar, sb["v_sp"][:], sb["v_sp"][:], -1.0, None, ALU.mult)

    @on("vector")
    def _(vector):
        S = mkstep(vector, "v", "VS")
        S(vector.reduce_sum, wcol[:, 0:1], w1s[:], axis=AX, apply_absolute_value=True)
        S(vector.reduce_sum, wcol[:, 1:2], w2s[:], axis=AX, apply_absolute_value=True)

    @on("tensor")
    def _(tensor):
        S = mkstep(tensor, "t", "TS")
        S(tensor.matmul, pw[0:1, 0:2], sb["c_onescolf"][:], wcol[:, 0:2],
          start=True, stop=True)

    @on("scalar")
    def _(scalar):
        S = mkstep(scalar, "s", "SS")
        S(scalar.copy, wcol[0:1, 2:4], pw[0:1, 0:2])

    @on("gpsimd")
    def _(g):
        D = mkdma(g, "g", "GD")
        S = mkstep(g, "g", "CC")
        D(DT["cc_wsc_in"], wcol[0:1, 2:4])
        cx.wait(g, "g", ("GD",))
        S(g.collective_compute, "AllGather", ALU.bypass,
          replica_groups=[list(range(8))],
          ins=[DT["cc_wsc_in"].opt()], outs=[DT["cc_wsc_out"].opt()])
        cx.wait(g, "g", ("CC",))
        D(wg8[:], DT["cc_wsc_out"])

    @on("vector")
    def _(vector):
        S = mkstep(vector, "v", "VS")
        S(vector.tensor_copy, wcol[0:8, 4:6], wg8[:, 0:2])

    @on("tensor")
    def _(tensor):
        S = mkstep(tensor, "t", "TS")
        S(tensor.matmul, pw[0:1, 2:4], sb["c_onescolf"][0:8, 0:1], wcol[0:8, 4:6],
          start=True, stop=True)

    @on("vector")
    def _(vector):
        S = mkstep(vector, "v", "VS")
        S(vector.tensor_scalar, wcol[0:1, 6:8], pw[0:1, 2:4],
          1.0 / (C * H4), 1e-5, ALU.mult, ALU.max)

    @on("tensor")
    def _(tensor):
        S = mkstep(tensor, "t", "TS")
        S(tensor.matmul, pw[:, 4:6], sb["c_onesrowf"][:], wcol[0:1, 6:8],
          start=True, stop=True)

    @on("scalar")
    def _(scalar):
        S = mkstep(scalar, "s", "SS")
        S(scalar.copy, sb["v_mw"][:], pw[:, 4:6])

    @on("vector")
    def _(vector):
        S = mkstep(vector, "v", "VS")
        S(vector.reciprocal, sb["v_swc"][:], sb["v_mw"][:])
        for wsrc, dst, sc, n in ((w1s, wq1, 0, 8 * 512), (w2s, wq2, 1, 4 * C)):
            S.c(vector.tensor_scalar, wscr[:, 0:n], wsrc[:], sb["v_swc"][:, sc:sc + 1],
                RC, ALU.mult, ALU.add)
            S.c(vector.tensor_scalar, wscr[:, 0:n], wscr[:, 0:n], RC, None,
                ALU.subtract)
            S.c(vector.tensor_scalar, dst[:], wscr[:, 0:n], 1.0, -1.0, ALU.min,
                ALU.max)

    @on("gpsimd")
    def _(g):
        D = mkdma(g, "g", "GD")
        S = mkstep(g, "g", "CC")
        D(DT["cc_wq1_in"].rearrange("(a p) n -> p a n", p=128),
          wq1[:].rearrange("p (a n) -> p a n", a=8))
        D(DT["cc_wq2_in"].rearrange("(a p) n -> p a n", p=128),
          wq2[:].rearrange("p (a n) -> p a n", a=4))
        cx.wait(g, "g", ("GD",))
        S(g.collective_compute, "AllGather", ALU.bypass,
          replica_groups=[list(range(8))],
          ins=[DT["cc_wq1_in"].opt()], outs=[DT["cc_wq1_out"].opt()])
        S(g.collective_compute, "AllGather", ALU.bypass,
          replica_groups=[list(range(8))],
          ins=[DT["cc_wq2_in"].opt()], outs=[DT["cc_wq2_out"].opt()])

    def emit_tt_build(which):
        @on("vector")
        def _(vector):
            S = mkstep(vector, "v", "VS")
            for pc in range(4):
                base = pc * 2048
                S.c(vector.tensor_mul, wTT[:, pc * 512:(pc + 1) * 512],
                  t1sb[:, base:base + 512], t2sb[:, base:base + 512])
                for c in range(1, 4):
                    s0 = base + c * 512
                    S.c(vector.tensor_mul, wscr[:, 0:512], t1sb[:, s0:s0 + 512],
                      t2sb[:, s0:s0 + 512])
                    S.c(vector.tensor_add, wTT[:, pc * 512:(pc + 1) * 512],
                      wTT[:, pc * 512:(pc + 1) * 512], wscr[:, 0:512])
            for pc in range(4):
                w = wTT[:, pc * 512:(pc + 1) * 512]
                if which == 0:
                    S.c(vector.tensor_copy, MT0[:, pc * C + 0:pc * C + 512], w)
                    S.c(vector.tensor_copy,
                      MT0[:, (4 + pc) * C + 512:(4 + pc) * C + 1024], w)
                else:
                    S.c(vector.tensor_copy, MT0[:, pc * C + 512:pc * C + 1024], w)
                    S.c(vector.tensor_scalar, wscr[:, 0:512], w, -1.0, None, ALU.mult)
                    S.c(vector.tensor_copy,
                      MT0[:, (4 + pc) * C + 0:(4 + pc) * C + 512], wscr[:, 0:512])

    emit_tt_build(0)

    @on("sync")
    def _(sync):
        D = mkdma(sync, "sy", "LD")
        D(t1sb[:].rearrange("p (a n) -> p a n", a=4),
          DT["T1i"].rearrange("(a p) n -> p a n", p=128))
        D(t2sb[:].rearrange("p (a n) -> p a n", a=4),
          DT["T2i"].rearrange("(a p) n -> p a n", p=128))

    emit_tt_build(1)

    @on("sync")
    def _(sync):
        D = mkdma(sync, "sy", "LD")
        D(DT["mt_d"].rearrange("(a p) n -> p a n", p=128),
          MT0[:].rearrange("p (a n) -> p a n", a=NCH))

    barrier()
    ph0.close()

    s_int1f = contextlib.ExitStack()
    int1f = sball("int1f", [128, NCH * BT], BF16, s_int1f)
    ph1 = contextlib.ExitStack()
    xtb = sball("xtb", [128, 2 * C], F32, ph1)
    xfb = sball("xfb", [128, 2 * BT], F32, ph1)
    g1bcs = sball("g1bcs", [128, C], F32, ph1)
    b1bcs = sball("b1bcs", [128, C], F32, ph1)
    scr1 = sball("scr1", [128, C], F32, ph1)
    scr2 = sball("scr2", [128, C], F32, ph1)
    int1t = sball("int1t", [128, NBLK * C], BF16, ph1)
    xn2a = sball("xn2a", [128, NCH * BT], BF16, ph1)
    ph1p = contextlib.ExitStack()
    pstA = ph1p.enter_context(nc.psum_tensor("pstA", [128, 2048], F32))
    pstB = ph1p.enter_context(nc.psum_tensor("pstB", [128, 2048], F32))

    @on("sync")
    def _(sync):
        D = mkdma(sync, "sy", "LD")
        D(g1bcs[:], DT["g1bc"])
        D(b1bcs[:], DT["b1bc"])

    h1mk = {}
    for tb in range(NBLK):
        buf = tb % 2

        @on("sync")
        def _(sync, tb=tb, buf=buf):
            cx.wait_mark(sync, "sy", h1mk.get(("xtb_s", buf)))
            cx.wait_mark(sync, "sy", h1mk.get(("xtb_v", buf)))
            D = mkdma(sync, "sy", "LD", deps=())
            D(xtb[:, buf * C:(buf + 1) * C],
              DT["x_tok"][tb * 128:(tb + 1) * 128, :])

        @on("scalar")
        def _(scalar, tb=tb, buf=buf):
            S = mkstep(scalar, "s", "SS", deps=("LD", "SS"))
            S(scalar.activation, scr1[:], xtb[:, buf * C:(buf + 1) * C], AF.Square,
              accum_out=sb["colscr"][:, tb:tb + 1])
        h1mk[("xtb_s", buf)] = cx.mark("SS")

        @on("vector")
        def _(vector, tb=tb, buf=buf):
            S = mkstep(vector, "v", "VS", deps=("LD",))
            S(vector.reduce_sum, sb["colscr"][:, 16 + tb:17 + tb],
              xtb[:, buf * C:(buf + 1) * C], axis=AX)
        h1mk[("xtb_v", buf)] = cx.mark("VS")

    @on("vector")
    def _(vector):
        S = mkstep(vector, "v", "VS")
        mu = sb["colscr"][:, 32:48]
        var = sb["colscr"][:, 48:64]
        S(vector.tensor_scalar, mu, sb["colscr"][:, 16:32], 1.0 / C, None, ALU.mult)
        S.c(vector.tensor_mul, var, mu, mu)
        S.c(vector.tensor_scalar, sb["st_A"][:], sb["colscr"][:, 0:16], 1.0 / C, None,
            ALU.mult)
        S.c(vector.tensor_sub, var, sb["st_A"][:], var)
        S.c(vector.tensor_scalar, var, var, 1.0 + EPS, EPS * EPS, ALU.mult, ALU.add)

    @on("scalar")
    def _(scalar):
        S = mkstep(scalar, "s", "SS")
        S(scalar.activation, sb["st_B"][:], sb["colscr"][:, 48:64], AF.Sqrt)

    @on("vector")
    def _(vector):
        S = mkstep(vector, "v", "VS")
        S(vector.reciprocal, sb["st_A"][:], sb["st_B"][:])
        S.c(vector.scalar_tensor_tensor, sb["st_B"][:], sb["colscr"][:, 32:48], -1.0,
            sb["st_A"][:], ALU.mult, ALU.mult)

    for tb in range(NBLK):
        buf = tb % 2

        @on("sync")
        def _(sync, tb=tb, buf=buf):
            cx.wait_mark(sync, "sy", h1mk.get(("xtb2", buf)))
            D = mkdma(sync, "sy", "LD", deps=("SS", "VS"))
            D(xtb[:, buf * C:(buf + 1) * C],
              DT["x_tok"][tb * 128:(tb + 1) * 128, :])

        @on("vector")
        def _(vector, tb=tb, buf=buf):
            S = mkstep(vector, "v", "VS", deps=("LD",))
            xb = xtb[:, buf * C:(buf + 1) * C]
            S(vector.tensor_scalar, scr1[:], xb, sb["st_A"][:, tb:tb + 1],
              sb["st_B"][:, tb:tb + 1], ALU.mult, ALU.add)
            S.c(vector.scalar_tensor_tensor, scr2[:], scr1[:], 1.0, g1bcs[:],
                ALU.bypass, ALU.mult)
            S.c(vector.tensor_add, scr2[:], scr2[:], b1bcs[:])
            S.c(vector.reduce_max, sb["colscr"][:, tb:tb + 1], scr2[:], axis=AX,
                apply_absolute_value=True)
            S.c(vector.tensor_scalar, sb["colscr"][:, 16 + tb:17 + tb],
                sb["colscr"][:, tb:tb + 1], 1e-5, None, ALU.max)
            S.c(vector.reciprocal, sb["st_s1"][:, tb:tb + 1],
                sb["colscr"][:, 16 + tb:17 + tb])
            S.c(vector.tensor_scalar, sb["st_s1"][:, tb:tb + 1],
                sb["st_s1"][:, tb:tb + 1], 127.0, None, ALU.mult)
            S.c(vector.scalar_tensor_tensor, sb["st_s1s"][:, tb:tb + 1],
                sb["colscr"][:, 16 + tb:17 + tb], 1.0 / 127.0, sb["v_mw"][:, 0:1],
                ALU.mult, ALU.mult)
            S.c(vector.tensor_scalar, scr2[:], scr2[:], sb["st_s1"][:, tb:tb + 1], RC,
                ALU.mult, ALU.add)
            S.c(vector.tensor_scalar, int1t[:, tb * C:(tb + 1) * C], scr2[:], RC,
                None, ALU.subtract)
        h1mk[("xtb2", buf)] = cx.mark("VS")

    emit_bc_from_pack("st_A", slice(0, 16), "abc", pstA)
    emit_bc_from_pack("st_B", slice(0, 16), "bbc", pstB)

    for ca in range(NCH):
        buf = ca % 2

        @on("sync")
        def _(sync, ca=ca, buf=buf):
            cx.wait_mark(sync, "sy", h1mk.get(("xfb", buf)))
            D = mkdma(sync, "sy", "LD", deps=("SS",))
            D(xfb[:, buf * BT:(buf + 1) * BT],
              DT["x_fm"][ca * 128:(ca + 1) * 128, :])

        @on("vector")
        def _(vector, ca=ca, buf=buf):
            S = mkstep(vector, "v", "VS", deps=("LD",))
            xc = xfb[:, buf * BT:(buf + 1) * BT]
            dst = xn2a[:, ca * BT:(ca + 1) * BT]
            S(vector.tensor_mul, xc, xc, sb["abc"][:])
            S.c(vector.tensor_add, xc, xc, sb["bbc"][:])
            S.c(vector.tensor_scalar, dst, xc, sb["v_gamA"][:, ca:ca + 1],
                sb["v_betA"][:, ca:ca + 1], ALU.mult, ALU.add)
        h1mk[("xfb", buf)] = cx.mark("VS")

    @on("sync")
    def _(sync):
        D = mkdma(sync, "sy", "LD")
        D(DT["xn2a_d"].rearrange("(a p) n -> p a n", p=128),
          xn2a[:].rearrange("p (a n) -> p a n", a=NCH))

    barrier()
    ph1p.close()
    pT0 = ph1.enter_context(nc.psum_tensor("pT0", [128, 512], BF16))
    pT1 = ph1.enter_context(nc.psum_tensor("pT1", [128, 512], BF16))

    for ca in range(NCH):
        for tg in range(4):
            pT = (pT0, pT1)[tg % 2]

            @on("tensor")
            def _(tensor, ca=ca, tg=tg, pT=pT):
                cx.wait(tensor, "t", ("VS",))
                cx.wait_mark(tensor, "t", h1mk.get(("pT", tg % 2)))
                S = mkstep(tensor, "t", "TS", deps=())
                for j in range(4):
                    tb = tg * 4 + j
                    S(tensor.transpose, pT[:, j * 128:(j + 1) * 128],
                      int1t[:, tb * C + ca * 128:tb * C + (ca + 1) * 128],
                      sb["c_idb"][:])

            @on("scalar")
            def _(scalar, ca=ca, tg=tg, pT=pT):
                S = mkstep(scalar, "s", "SS", deps=("TS",))
                S(scalar.copy,
                  int1f[:, ca * BT + tg * 512:ca * BT + (tg + 1) * 512], pT[:])
            h1mk[("pT", tg % 2)] = cx.mark("SS")

    barrier()
    ph1.close()

    phf = contextlib.ExitStack()
    wq1f = sball("wq1f", [128, 64 * 512], BF16, phf)
    b1fs = sball("b1fs", [128, H4], BF16, phf)
    g2s = sball("g2s", [128, H4], BF16, phf)
    be2s = sball("be2s", [128, H4], BF16, phf)
    h2b = sball("h2b", [128, 2 * H4], BF16, phf)
    ln2o = sball("ln2o", [128, H4], F32, phf)
    sqs = sball("sqs", [128, 512], F32, phf)
    fscr = sball("fscr", [128, 4096], BF16, phf)
    int2s = sball("int2s", [128, H4], BF16, phf)
    i2st = sball("i2st", [128, 1024], BF16, phf)
    pF0 = phf.enter_context(nc.psum_tensor("pF0", [128, 512], F32))
    pF1 = phf.enter_context(nc.psum_tensor("pF1", [128, 512], F32))
    pX0 = phf.enter_context(nc.psum_tensor("pX0", [128, 512], BF16))
    pX1 = phf.enter_context(nc.psum_tensor("pX1", [128, 512], BF16))

    @on("sync")
    def _(sync):
        D = mkdma(sync, "sy", "LD")
        for r in range(8):
            for ca in range(NCH):
                D(wq1f[:, (ca * 8 + r) * 512:(ca * 8 + r + 1) * 512],
                  DT["cc_wq1_out"][r * C + ca * 128:r * C + (ca + 1) * 128, :])
    @on("gpsimd")
    def _(g):
        D = mkdma(g, "g", "GD")
        D(b1fs[:], DT["b1fbc"])
        D(g2s[:], DT["g2bc"])
        D(be2s[:], DT["be2bc"])

    fmk = {}
    for tb in range(NBLK):
        h2o = (tb % 2) * H4

        vs_after_stt = {}
        for hc in range(8):
            pF = (pF0, pF1)[hc % 2]

            @on("tensor")
            def _(tensor, tb=tb, hc=hc, pF=pF):
                cx.wait(tensor, "t", ("LD", "GD"))
                cx.wait_mark(tensor, "t", fmk.get(("pF", hc % 2)))
                S = mkstep(tensor, "t", "TS", deps=())
                for ca in range(NCH):
                    S(tensor.matmul, pF[:],
                      int1f[:, ca * BT + tb * 128:ca * BT + (tb + 1) * 128],
                      wq1f[:, (ca * 8 + hc) * 512:(ca * 8 + hc + 1) * 512],
                      start=(ca == 0), stop=(ca == NCH - 1))

            @on("vector")
            def _(vector, tb=tb, hc=hc, pF=pF):
                cx.wait(vector, "v", ("TS", "GD"))
                cx.wait_mark(vector, "v", fmk.get(("fscr",)))
                S = mkstep(vector, "v", "VS", deps=())
                S(vector.scalar_tensor_tensor, fscr[:, hc * 512:(hc + 1) * 512],
                  pF[:], sb["st_s1s"][:, tb:tb + 1],
                  b1fs[:, hc * 512:(hc + 1) * 512], ALU.mult, ALU.add)
            fmk[("pF", hc % 2)] = cx.mark("VS")
            vs_after_stt[hc] = cx.cnt["VS"]

        @on("scalar")
        def _(scalar, tb=tb, h2o=h2o, vs_after_stt=vs_after_stt):
            cx.wait_mark(scalar, "s", fmk.get(("h2b", tb % 2)))
            for hc in range(8):
                cx.wait_value(scalar, "s", "VS", vs_after_stt[hc])
                ins = scalar.activation(
                    h2b[:, h2o + hc * 512:h2o + (hc + 1) * 512],
                    fscr[:, hc * 512:(hc + 1) * 512], AF.Gelu)
                cx.inc(ins, "SS")
        fmk[("fscr",)] = cx.mark("SS")

        @on("vector")
        def _(vector, tb=tb, h2o=h2o):
            cx.wait(vector, "v", ("SS",))
            cx.wait_mark(vector, "v", fmk.get(("ln2o",)))
            S = mkstep(vector, "v", "VS", deps=())
            h2 = h2b[:, h2o:h2o + H4]
            cc = sb["colscr"]
            S(vector.reduce_sum, cc[:, 0:1], h2, axis=AX)
            S.c(vector.tensor_mul, ln2o[:], h2, h2)
            S.c(vector.reduce_sum, cc[:, 8:9], ln2o[:], axis=AX)
            S.c(vector.tensor_scalar, cc[:, 1:2], cc[:, 0:1], 1.0 / H4, None,
                ALU.mult)
            S.c(vector.tensor_mul, cc[:, 2:3], cc[:, 1:2], cc[:, 1:2])
            S.c(vector.tensor_scalar, cc[:, 3:4], cc[:, 8:9], 1.0 / H4, None,
                ALU.mult)
            S.c(vector.tensor_sub, cc[:, 3:4], cc[:, 3:4], cc[:, 2:3])
            S.c(vector.tensor_scalar, cc[:, 3:4], cc[:, 3:4], 1.0, EPS, ALU.mult,
                ALU.add)

        @on("scalar")
        def _(scalar, tb=tb):
            S = mkstep(scalar, "s", "SS", deps=("VS",))
            S(scalar.activation, sb["colscr"][:, 4:5], sb["colscr"][:, 3:4], AF.Sqrt)

        @on("vector")
        def _(vector, tb=tb, h2o=h2o):
            S = mkstep(vector, "v", "VS", deps=("SS",))
            h2 = h2b[:, h2o:h2o + H4]
            cc = sb["colscr"]
            S(vector.reciprocal, cc[:, 5:6], cc[:, 4:5])
            S.c(vector.scalar_tensor_tensor, cc[:, 6:7], cc[:, 1:2], -1.0, cc[:, 5:6],
                ALU.mult, ALU.mult)
            S.c(vector.tensor_scalar, ln2o[:], h2, cc[:, 5:6], cc[:, 6:7],
                ALU.mult, ALU.add)
            S.c(vector.scalar_tensor_tensor, ln2o[:], ln2o[:], 1.0, g2s[:],
                ALU.bypass, ALU.mult)
            S.c(vector.tensor_add, ln2o[:], ln2o[:], be2s[:])
            S.c(vector.reduce_max, cc[:, 7:8], ln2o[:], axis=AX,
                apply_absolute_value=True)
            S.c(vector.tensor_scalar, cc[:, 7:8], cc[:, 7:8], 1e-5, None, ALU.max)
            S.c(vector.reciprocal, cc[:, 2:3], cc[:, 7:8])
            S.c(vector.tensor_scalar, cc[:, 2:3], cc[:, 2:3], 127.0, None, ALU.mult)
            S.c(vector.scalar_tensor_tensor, sb["st_s2i"][:, tb:tb + 1], cc[:, 7:8],
                1.0 / 127.0, sb["v_mw"][:, 1:2], ALU.mult, ALU.mult)
            S.c(vector.tensor_scalar, ln2o[:], ln2o[:], cc[:, 2:3], RC, ALU.mult,
                ALU.add)
            S.c(vector.tensor_scalar, int2s[:], ln2o[:], RC, None, ALU.subtract)
        fmk[("ln2o",)] = cx.mark("VS")
        fmk[("h2b", tb % 2)] = cx.mark("VS")

        for hg in range(8):
            pX = (pX0, pX1)[hg % 2]
            xb = hg % 2

            @on("tensor")
            def _(tensor, tb=tb, hg=hg, pX=pX, xb=xb):
                cx.wait(tensor, "t", ("VS",))
                cx.wait_mark(tensor, "t", fmk.get(("pX", xb)))
                S = mkstep(tensor, "t", "TS", deps=())
                for j in range(4):
                    hh = hg * 4 + j
                    S(tensor.transpose, pX[:, j * 128:(j + 1) * 128],
                      int2s[:, hh * 128:(hh + 1) * 128], sb["c_idb"][:])

            @on("scalar")
            def _(scalar, tb=tb, hg=hg, pX=pX, xb=xb):
                cx.wait(scalar, "s", ("TS",))
                cx.wait_mark(scalar, "s", fmk.get(("i2st", xb)))
                S = mkstep(scalar, "s", "SS", deps=())
                S(scalar.copy, i2st[:, xb * 512:(xb + 1) * 512], pX[:])
            fmk[("pX", xb)] = cx.mark("SS")

            @on("sync")
            def _(sync, tb=tb, hg=hg, xb=xb):
                D = mkdma(sync, "sy", "LD", deps=("SS",))
                for j in range(4):
                    hh = hg * 4 + j
                    D(DT["int2_d"][hh * 128:(hh + 1) * 128, tb * 128:(tb + 1) * 128],
                      i2st[:, xb * 512 + j * 128:xb * 512 + (j + 1) * 128])
            fmk[("i2st", xb)] = cx.mark("LD")

    barrier()
    phf.close()
    s_int1f.close()

    phg = contextlib.ExitStack()
    wq2f = sball("wq2f", [128, 32 * C], BF16, phg)
    i2b = sball("i2b", [128, 32 * 512], BF16, phg)
    s2ib = sball("s2ib", [128, BT], F32, phg)
    gscr = sball("gscr", [128, 4096], F32, phg)
    gst = sball("gst", [128, 4096], F32, phg)
    pG0 = phg.enter_context(nc.psum_tensor("pG0", [128, 512], F32))
    pG1 = phg.enter_context(nc.psum_tensor("pG1", [128, 512], F32))
    pBC = phg.enter_context(nc.psum_tensor("pBC", [128, 2048], F32))

    @on("sync")
    def _(sync):
        D = mkdma(sync, "sy", "LD")
        for hh in range(32):
            r, loc = hh // 4, hh % 4
            D(wq2f[:, hh * C:(hh + 1) * C],
              DT["cc_wq2_out"][r * 512 + loc * 128:r * 512 + (loc + 1) * 128, :])

    emit_bc_from_pack("st_s2i", slice(0, 16), "abc", pBC)

    @on("vector")
    def _(vector):
        S = mkstep(vector, "v", "VS")
        S(vector.tensor_copy, s2ib[:], sb["abc"][:])

    gmk = {}
    for tt in range(4):
        @on("sync")
        def _(sync, tt=tt):
            D = mkdma(sync, "sy", "LD", deps=("TS", "LD"))
            for hh in range(32):
                D(i2b[:, hh * 512:(hh + 1) * 512],
                  DT["int2_d"][hh * 128:(hh + 1) * 128, tt * 512:(tt + 1) * 512])

        vs_after_g = {}
        for dc in range(NCH):
            pG = (pG0, pG1)[dc % 2]
            gb = dc % 2

            @on("tensor")
            def _(tensor, tt=tt, dc=dc, pG=pG, gb=gb):
                cx.wait(tensor, "t", ("LD",))
                cx.wait_mark(tensor, "t", gmk.get(("pG", gb)))
                S = mkstep(tensor, "t", "TS", deps=())
                for hh in range(32):
                    S(tensor.matmul, pG[:],
                      wq2f[:, hh * C + dc * 128:hh * C + (dc + 1) * 128],
                      i2b[:, hh * 512:(hh + 1) * 512],
                      start=(hh == 0), stop=(hh == 31))

            @on("vector")
            def _(vector, tt=tt, dc=dc, pG=pG):
                cx.wait(vector, "v", ("TS",))
                cx.wait_mark(vector, "v", gmk.get(("gscr",)))
                S = mkstep(vector, "v", "VS", deps=())
                S(vector.scalar_tensor_tensor, gscr[:, dc * 512:(dc + 1) * 512],
                  pG[:], 1.0, s2ib[:, tt * 512:(tt + 1) * 512], ALU.bypass, ALU.mult)
            gmk[("pG", dc % 2)] = cx.mark("VS")
            vs_after_g[dc] = cx.cnt["VS"]

        @on("scalar")
        def _(scalar, tt=tt, vs_after_g=vs_after_g):
            cx.wait_mark(scalar, "s", gmk.get(("gst",)))
            for dc in range(NCH):
                cx.wait_value(scalar, "s", "VS", vs_after_g[dc])
                ins = scalar.activation(gst[:, dc * 512:(dc + 1) * 512],
                                        gscr[:, dc * 512:(dc + 1) * 512], AF.Silu,
                                        bias=sb["v_b2c"][:, dc:dc + 1])
                cx.inc(ins, "SS")
        gmk[("gscr",)] = cx.mark("SS")

        @on("sync")
        def _(sync, tt=tt):
            D = mkdma(sync, "sy", "LD", deps=("SS",))
            for dc in range(NCH):
                D(DT["g0_d"][dc * 128:(dc + 1) * 128, tt * 512:(tt + 1) * 512],
                  gst[:, dc * 512:(dc + 1) * 512])
        gmk[("gst",)] = cx.mark("LD")

    barrier()
    phg.close()

    s_carry = contextlib.ExitStack()
    rwkv = sball("rwkv", [128, NCH * BT], BF16, s_carry)
    pha = contextlib.ExitStack()
    xn2r = sball("xn2r", [128, NCH * BT], BF16, pha)
    wst = sball("wst", [128, 2 * 1024], F32, pha)
    wbt = sball("wbt", [128, 2 * 1024], BF16, pha)
    kbuf = sball("kbuf", [128, BT], BF16, pha)
    vbuf = sball("vbuf", [128, BT], BF16, pha)
    kvb = sball("kvb", [128, BT], F32, pha)
    ubuf = sball("ubuf", [128, BT], F32, pha)
    pbuf = sball("pbuf", [128, BT], F32, pha)
    sS = sball("sS", [128, BT + 1], F32, pha)
    wkvb = sball("wkvb", [128, BT], F32, pha)
    tvalsb = sball("tvalsb", [128, BT], F32, pha)
    sigr = sball("sigr", [128, NCH * BT], BF16, pha)
    totc = sball("totc", [128, 8], F32, pha)
    cgat = sball("cgat", [128, 16], F32, pha)
    pK = pha.enter_context(nc.psum_tensor("pK", [128, 512], F32))
    pV = pha.enter_context(nc.psum_tensor("pV", [128, 512], F32))
    pR = pha.enter_context(nc.psum_tensor("pR", [128, 512], F32))

    @on("sync")
    def _(sync):
        D = mkdma(sync, "sy", "LD")
        D(xn2r[:].rearrange("p (a n) -> p a n", a=NCH),
          DT["xn2a_d"].rearrange("(a p) n -> p a n", p=128))
        D(tvalsb[:], DT["tvals"])

    pmk = {}

    @on("vector")
    def _(vector):
        S = mkstep(vector, "v", "VS")
        S(vector.memset, sS[:, 0:1], 0.0)

    for dc in range(NCH):
        @on("vector")
        def _(vector, dc=dc):
            S = mkstep(vector, "v", "VS")
            S(vector.memset, wkvb[:], 0.0)

        for wi, wname in enumerate(("WkT", "WvT", "WrT")):
            wb = (dc * 3 + wi) % 2

            @on("sync")
            def _(sync, dc=dc, wname=wname, wb=wb):
                cx.wait_mark(sync, "sy", pmk.get(("wst", wb)))
                D = mkdma(sync, "sy", "LD", deps=())
                D(wst[:, wb * 1024:(wb + 1) * 1024].rearrange(
                      "p (a n) -> p a n", a=8),
                  DT[wname][:, dc * 128:(dc + 1) * 128].rearrange(
                      "(a p) n -> p a n", p=128))

            @on("vector")
            def _(vector, wb=wb):
                cx.wait(vector, "v", ("LD",))
                cx.wait_mark(vector, "v", pmk.get(("wbt", wb)))
                S = mkstep(vector, "v", "VS", deps=())
                S(vector.tensor_copy, wbt[:, wb * 1024:(wb + 1) * 1024],
                  wst[:, wb * 1024:(wb + 1) * 1024])
            pmk[("wst", wb)] = cx.mark("VS")

            pdst = (pK, pV, pR)[wi]
            ts_after_mm = {}
            SS0 = cx.cnt["SS"]
            for tt in range(4):
                @on("tensor")
                def _(tensor, dc=dc, tt=tt, pdst=pdst, wb=wb, wi=wi, SS0=SS0):
                    cx.wait(tensor, "t", ("VS",))
                    if tt > 0:
                        cx.wait_value(tensor, "t", "SS", SS0 + tt)
                    else:
                        cx.wait_mark(tensor, "t", pmk.get(("pkevac", wi)))
                    S = mkstep(tensor, "t", "TS", deps=())
                    for ca in range(NCH):
                        S(tensor.matmul, pdst[:],
                          wbt[:, wb * 1024 + ca * 128:wb * 1024 + (ca + 1) * 128],
                          xn2r[:, ca * BT + tt * 512:ca * BT + (tt + 1) * 512],
                          start=(ca == 0), stop=(ca == NCH - 1))
                ts_after_mm[tt] = cx.cnt["TS"]

            @on("scalar")
            def _(scalar, dc=dc, wi=wi, pdst=pdst, ts_after_mm=ts_after_mm):
                cx.wait(scalar, "s", ("VS",))
                for tt in range(4):
                    cx.wait_value(scalar, "s", "TS", ts_after_mm[tt])
                    if wi == 0:
                        ins = scalar.copy(kbuf[:, tt * 512:(tt + 1) * 512], pdst[:])
                    elif wi == 1:
                        ins = scalar.copy(vbuf[:, tt * 512:(tt + 1) * 512], pdst[:])
                    else:
                        ins = scalar.activation(
                            sigr[:, dc * BT + tt * 512:dc * BT + (tt + 1) * 512],
                            pdst[:], AF.Sigmoid)
                    cx.inc(ins, "SS")
            pmk[("pkevac", wi)] = cx.mark("SS")
            pmk[("wbt", wb)] = cx.mark("TS")

        vs_after_p = {}

        @on("vector")
        def _(vector, dc=dc):
            S = mkstep(vector, "v", "VS", deps=("SS", "LD", "VS"))
            for tt in range(4):
                sl = slice(tt * 512, (tt + 1) * 512)
                S(vector.tensor_mul, kvb[:, sl], kbuf[:, sl], vbuf[:, sl])
            for tt in range(4):
                sl = slice(tt * 512, (tt + 1) * 512)
                S(vector.tensor_scalar, pbuf[:, sl], tvalsb[:, sl],
                  sb["v_sp"][:, dc:dc + 1], None, ALU.mult)
                vs_after_p[tt] = cx.cnt["VS"]

        @on("scalar")
        def _(scalar, dc=dc, vs_after_p=vs_after_p):
            cx.wait(scalar, "s", ("VS",)) if False else None
            for tt in range(4):
                sl = slice(tt * 512, (tt + 1) * 512)
                cx.wait_value(scalar, "s", "VS", vs_after_p[tt])
                ins = scalar.activation(pbuf[:, sl], pbuf[:, sl], AF.Exp, scale=-1.0)
                cx.inc(ins, "SS")

        @on("vector")
        def _(vector, dc=dc):
            S = mkstep(vector, "v", "VS", deps=("SS",))
            for tt in range(4):
                sl = slice(tt * 512, (tt + 1) * 512)
                S(vector.tensor_mul, ubuf[:, sl], kvb[:, sl], pbuf[:, sl])
            S.c(vector.tensor_tensor_scan, sS[:, 1:BT + 1], wkvb[:], ubuf[:], 0.0,
                ALU.add, ALU.add)
            S.c(vector.tensor_copy, totc[:, dc:dc + 1], sS[:, BT:BT + 1])
            S.c(vector.tensor_scalar, wkvb[:], kvb[:], sb["v_tf"][:, dc:dc + 1], None,
                ALU.mult)
            S.c(vector.scalar_tensor_tensor, wkvb[:], sS[:, 0:BT],
                sb["v_td"][:, dc:dc + 1], wkvb[:], ALU.mult, ALU.add)
            S.c(vector.tensor_mul, rwkv[:, dc * BT:(dc + 1) * BT],
                sigr[:, dc * BT:(dc + 1) * BT], wkvb[:])

    @on("gpsimd")
    def _(g):
        D = mkdma(g, "g", "GD")
        S = mkstep(g, "g", "CC")
        D(DT["cc_carry_in"].rearrange("o (a p) -> (o p) a", p=128), totc[:],
          allow_slow_non_contiguous=True)
        cx.wait(g, "g", ("GD",))
        S(g.collective_compute, "AllGather", ALU.bypass,
          replica_groups=[[0, 1], [2, 3], [4, 5], [6, 7]],
          ins=[DT["cc_carry_in"].opt()], outs=[DT["cc_carry_out"].opt()])
        cx.wait(g, "g", ("CC",))
        D(cgat[:, 0:8],
          DT["cc_carry_out"][0:1, :].rearrange("o (a p) -> (o p) a", p=128),
          allow_slow_non_contiguous=True)
        D(cgat[:, 8:16],
          DT["cc_carry_out"][1:2, :].rearrange("o (a p) -> (o p) a", p=128),
          allow_slow_non_contiguous=True)

    @on("vector")
    def _(vector):
        S = mkstep(vector, "v", "VS")
        S(vector.tensor_scalar, sb["carry_fix"][:], cgat[:, 0:8],
          sb["c_carryw"][:, 0:1], None, ALU.mult)
        S.c(vector.scalar_tensor_tensor, sb["carry_fix"][:], cgat[:, 8:16],
            sb["c_carryw"][:, 1:2], sb["carry_fix"][:], ALU.mult, ALU.add)
        S.c(vector.tensor_mul, sb["carry_fix"][:], sb["carry_fix"][:], sb["v_td"][:])
        for dc in range(NCH):
            sl = slice(dc * BT, (dc + 1) * BT)
            S.c(vector.scalar_tensor_tensor, rwkv[:, sl], sigr[:, sl],
                sb["carry_fix"][:, dc:dc + 1], rwkv[:, sl], ALU.mult, ALU.add)

    barrier()
    pha.close()

    s_x1 = contextlib.ExitStack()
    x1f = sball("x1f", [128, NCH * BT], F32, s_x1)
    phr = contextlib.ExitStack()
    woT = sball("woT", [128, NCH * C], F32, phr)
    woTb = sball("woTb", [128, NCH * C], BF16, phr)
    g0t = sball("g0t", [128, 2 * 512], F32, phr)
    xft = sball("xft", [128, 2 * 512], F32, phr)
    pS0 = phr.enter_context(nc.psum_tensor("pS0", [128, 512], F32))
    pS1 = phr.enter_context(nc.psum_tensor("pS1", [128, 512], F32))

    @on("sync")
    def _(sync):
        D = mkdma(sync, "sy", "LD")
        D(woT[:].rearrange("p (a n) -> p a n", a=NCH),
          DT["WoT"].rearrange("(a p) n -> p a n", p=128))

    @on("vector")
    def _(vector):
        S = mkstep(vector, "v", "VS")
        S(vector.tensor_copy, woTb[:], woT[:])

    for dc in range(NCH):
        for tt in range(4):
            pS = (pS0, pS1)[tt % 2]
            buf = tt % 2

            @on("sync")
            def _(sync, dc=dc, tt=tt, buf=buf):
                D = mkdma(sync, "sy", "LD", deps=("VS", "LD"))
                D(g0t[:, buf * 512:(buf + 1) * 512],
                  DT["g0_d"][dc * 128:(dc + 1) * 128, tt * 512:(tt + 1) * 512])
                D(xft[:, buf * 512:(buf + 1) * 512],
                  DT["x_fm"][dc * 128:(dc + 1) * 128, tt * 512:(tt + 1) * 512])

            @on("tensor")
            def _(tensor, dc=dc, tt=tt, pS=pS):
                S = mkstep(tensor, "t", "TS", deps=("VS",))
                for ca in range(NCH):
                    S(tensor.matmul, pS[:],
                      woTb[:, ca * C + dc * 128:ca * C + (dc + 1) * 128],
                      rwkv[:, ca * BT + tt * 512:ca * BT + (tt + 1) * 512],
                      start=(ca == 0), stop=(ca == NCH - 1))

            @on("vector")
            def _(vector, dc=dc, tt=tt, pS=pS, buf=buf):
                S = mkstep(vector, "v", "VS", deps=("TS", "LD"))
                dst = x1f[:, dc * BT + tt * 512:dc * BT + (tt + 1) * 512]
                S(vector.tensor_mul, dst, pS[:], g0t[:, buf * 512:(buf + 1) * 512])
                S.c(vector.tensor_add, dst, dst, xft[:, buf * 512:(buf + 1) * 512])

    @on("sync")
    def _(sync):
        D = mkdma(sync, "sy", "LD")
        D(DT["x1_d"].rearrange("(a p) n -> p a n", p=128),
          x1f[:].rearrange("p (a n) -> p a n", a=NCH))

    barrier()
    phr.close()

    ph2 = contextlib.ExitStack()
    xcb = sball("xcb", [128, 2 * 512], BF16, ph2)
    MTl = sball("MTl", [128, 8 * C], BF16, ph2)
    x1sq = sball("x1sq", [128, 2 * 512], BF16, ph2)
    xn1b = sball("xn1b", [128, NCH * BT], BF16, ph2)
    l1scr = sball("l1scr", [128, 512], F32, ph2)
    g1st = sball("g1st", [128, 512], F32, ph2)
    pM0 = ph2.enter_context(nc.psum_tensor("pM0", [128, 512], F32))
    pM1 = ph2.enter_context(nc.psum_tensor("pM1", [128, 512], F32))
    pGa = ph2.enter_context(nc.psum_tensor("pGa", [128, 512], F32))
    pGb = ph2.enter_context(nc.psum_tensor("pGb", [128, 512], F32))

    @on("sync")
    def _(sync):
        D = mkdma(sync, "sy", "LD")
        D(MTl[:].rearrange("p (a n) -> p a n", a=NCH),
          DT["mt_d"].rearrange("(a p) n -> p a n", p=128))

    for tt in range(4):
        for ca in range(NCH):
            buf = ca % 2
            xc = xcb[:, buf * 512:(buf + 1) * 512]
            sq = x1sq[:, buf * 512:(buf + 1) * 512]

            @on("vector")
            def _(vector, tt=tt, ca=ca, xc=xc):
                S = mkstep(vector, "v", "VS", deps=("TS",))
                S(vector.tensor_copy, xc,
                  x1f[:, ca * BT + tt * 512:ca * BT + (tt + 1) * 512])

            @on("scalar")
            def _(scalar, xc=xc, sq=sq):
                S = mkstep(scalar, "s", "SS", deps=("VS", "TS"))
                S(scalar.activation, sq, xc, AF.Square)

            @on("tensor")
            def _(tensor, ca=ca, xc=xc, sq=sq):
                S = mkstep(tensor, "t", "TS", deps=("VS", "SS"))
                S(tensor.matmul, pM0[:], sb["c_onesb"][:], xc,
                  start=(ca == 0), stop=(ca == NCH - 1))
                S(tensor.matmul, pM1[:], sb["c_onesb"][:], sq,
                  start=(ca == 0), stop=(ca == NCH - 1))

        @on("scalar")
        def _(scalar, tt=tt):
            S = mkstep(scalar, "s", "SS")
            S(scalar.activation, l1scr[:], pM0[:], AF.Square, scale=1.0 / C)

        @on("vector")
        def _(vector, tt=tt):
            S = mkstep(vector, "v", "VS", deps=("TS", "SS"))
            sl = slice(tt * 512, (tt + 1) * 512)
            S(vector.tensor_scalar, sb["abc"][:, sl], pM1[:], 1.0 / C, EPS,
              ALU.mult, ALU.add)
            S.c(vector.tensor_sub, sb["abc"][:, sl], sb["abc"][:, sl], l1scr[:])

        @on("scalar")
        def _(scalar, tt=tt):
            S = mkstep(scalar, "s", "SS")
            sl = slice(tt * 512, (tt + 1) * 512)
            S(scalar.activation, sb["bbc"][:, sl], sb["abc"][:, sl], AF.Sqrt)

        @on("vector")
        def _(vector, tt=tt):
            S = mkstep(vector, "v", "VS", deps=("SS",))
            sl = slice(tt * 512, (tt + 1) * 512)
            S(vector.reciprocal, sb["abc"][:, sl], sb["bbc"][:, sl])
            S.c(vector.scalar_tensor_tensor, sb["bbc"][:, sl], pM0[:], -1.0 / C,
                sb["abc"][:, sl], ALU.mult, ALU.mult)

    @on("vector")
    def _(vector):
        S = mkstep(vector, "v", "VS")
        for ca in range(NCH):
            sl = slice(ca * BT, (ca + 1) * BT)
            S.c(vector.tensor_mul, x1f[:, sl], x1f[:, sl], sb["abc"][:])
            S.c(vector.tensor_add, xn1b[:, sl], x1f[:, sl], sb["bbc"][:])

    @on("gpsimd")
    def _(g):
        D = mkdma(g, "g", "GD")
        S = mkstep(g, "g", "CC")
        D(DT["cc_xn1_in"].rearrange("(a p) n -> p a n", p=128),
          xn1b[:].rearrange("p (a n) -> p a n", a=NCH))
        cx.wait(g, "g", ("GD",))
        S(g.collective_compute, "AllGather", ALU.bypass,
          replica_groups=[[0, 1], [2, 3], [4, 5], [6, 7]],
          ins=[DT["cc_xn1_in"].opt()], outs=[DT["cc_xn1_out"].opt()])

    for dc in range(NCH):
        for tt in range(4):
            pG = (pGa, pGb)[tt % 2]

            @on("tensor")
            def _(tensor, dc=dc, tt=tt, pG=pG):
                S = mkstep(tensor, "t", "TS", deps=("VS", "SS", "LD"))
                for ca in range(NCH):
                    S(tensor.matmul, pG[:],
                      MTl[:, ca * C + dc * 128:ca * C + (dc + 1) * 128],
                      xn1b[:, ca * BT + tt * 512:ca * BT + (tt + 1) * 512],
                      start=(ca == 0), stop=(ca == NCH - 1))

            @on("scalar")
            def _(scalar, dc=dc, tt=tt, pG=pG):
                S = mkstep(scalar, "s", "SS", deps=("TS", "LD"))
                S(scalar.activation, g1st[:], pG[:], AF.Silu,
                  bias=sb["v_bTTc"][:, dc:dc + 1])

            @on("sync")
            def _(sync, dc=dc, tt=tt):
                D = mkdma(sync, "sy", "LD", deps=("SS",))
                D(DT["g1_d"][dc * 128:(dc + 1) * 128, tt * 512:(tt + 1) * 512],
                  g1st[:])

    barrier()
    ph2.close()
    s_x1.close()
    s_carry.close()

    s_att = contextlib.ExitStack()
    Qh = sball("Qh", [128, 4 * T], BF16, s_att)
    Kh = sball("Kh", [128, 4 * T], BF16, s_att)
    Vsb = sball("Vsb", [128, 32 * 520], BF16, s_att)
    ph3 = contextlib.ExitStack()
    wqs = sball("wqs", [128, 2 * 1536], F32, ph3)
    wqb = sball("wqb", [128, NCH * 1536], BF16, ph3)
    cosQs = sball("cosQs", [128, T], BF16, ph3)
    sinQs = sball("sinQs", [128, T], BF16, ph3)
    cosKs = sball("cosKs", [128, T], BF16, ph3)
    sinKs = sball("sinKs", [128, T], BF16, ph3)
    xg = sball("xg", [128, NCH * 512], BF16, ph3)
    qraw = sball("qraw", [128, 512], BF16, ph3)
    t1b = sball("t1b", [128, 512], BF16, ph3)
    t2b = sball("t2b", [128, 512], BF16, ph3)
    pQ0 = ph3.enter_context(nc.psum_tensor("pQ0", [128, 512], F32))
    pQ1 = ph3.enter_context(nc.psum_tensor("pQ1", [128, 512], F32))
    pRot = ph3.enter_context(nc.psum_tensor("pRot", [128, 512], F32))
    pVv = ph3.enter_context(nc.psum_tensor("pVv", [128, 512], F32))

    @on("sync")
    def _(sync):
        D = mkdma(sync, "sy", "LD")
        for nm, dst in (("cosQ", cosQs), ("sinQ", sinQs),
                        ("cosK", cosKs), ("sinK", sinKs)):
            D(dst[:], DT[nm])

    for half in range(4):
        @on("sync")
        def _(sync, half=half):
            D = mkdma(sync, "sy", "LD")
            D(wqs[:].rearrange("p (a n) -> p a n", a=2),
              DT["WqkvT_sl"][half * 256:(half + 1) * 256, :].rearrange(
                  "(a p) n -> p a n", p=128))

        @on("vector")
        def _(vector, half=half):
            S = mkstep(vector, "v", "VS")
            S(vector.tensor_copy, wqb[:, half * 2 * 1536:(half + 1) * 2 * 1536],
              wqs[:])

    @on("vector")
    def _(vector):
        S = mkstep(vector, "v", "VS")
        S(vector.memset,
          Vsb[:].rearrange("p (k h e) -> p k h e", k=32, h=8)[:, :, :, 64:65], 1.0)

    for tt in range(8):
        r, tl = tt // 4, tt % 4

        @on("sync")
        def _(sync, r=r, tl=tl):
            D = mkdma(sync, "sy", "LD", deps=("CC", "TS"))
            for ca in range(NCH):
                D(xg[:, ca * 512:(ca + 1) * 512],
                  DT["cc_xn1_out"][r * C + ca * 128:r * C + (ca + 1) * 128,
                                   tl * 512:(tl + 1) * 512])

        for qc in range(8):
            pQ = (pQ0, pQ1)[qc % 2]
            isq = qc < 4
            hp = qc % 4
            cosT, sinT = (cosQs, sinQs) if isq else (cosKs, sinKs)
            dstT = Qh if isq else Kh

            @on("tensor")
            def _(tensor, tt=tt, qc=qc, pQ=pQ):
                S = mkstep(tensor, "t", "TS", deps=("LD", "SS", "VS"))
                for ca in range(NCH):
                    S(tensor.matmul, pQ[:],
                      wqb[:, ca * 1536 + qc * 128:ca * 1536 + (qc + 1) * 128],
                      xg[:, ca * 512:(ca + 1) * 512],
                      start=(ca == 0), stop=(ca == NCH - 1))

            @on("scalar")
            def _(scalar, pQ=pQ):
                S = mkstep(scalar, "s", "SS", deps=("TS", "VS"))
                S(scalar.copy, qraw[:], pQ[:])

            @on("tensor")
            def _(tensor):
                S = mkstep(tensor, "t", "TS", deps=("SS", "VS"))
                S(tensor.matmul, pRot[:], sb["c_ropeR"][:], qraw[:],
                  start=True, stop=True)

            @on("vector")
            def _(vector, tt=tt, hp=hp, cosT=cosT, sinT=sinT, dstT=dstT):
                S = mkstep(vector, "v", "VS", deps=("SS", "TS"))
                sl = slice(tt * 512, (tt + 1) * 512)
                S(vector.tensor_mul, t1b[:], qraw[:], cosT[:, sl])
                S(vector.tensor_mul, t2b[:], pRot[:], sinT[:, sl])
                S.c(vector.tensor_add,
                    dstT[:, hp * T + tt * 512:hp * T + (tt + 1) * 512], t1b[:],
                    t2b[:])

        for tb in range(4):
            kt = tt * 4 + tb

            @on("tensor")
            def _(tensor, tb=tb):
                S = mkstep(tensor, "t", "TS", deps=("LD", "SS", "VS"))
                for ca in range(NCH):
                    S(tensor.matmul, pVv[:],
                      xg[:, ca * 512 + tb * 128:ca * 512 + (tb + 1) * 128],
                      wqb[:, ca * 1536 + 1024:ca * 1536 + 1536],
                      start=(ca == 0), stop=(ca == NCH - 1))

            @on("scalar")
            def _(scalar, kt=kt):
                S = mkstep(scalar, "s", "SS", deps=("TS", "VS"))
                S(scalar.copy,
                  Vsb[:, kt * 520:(kt + 1) * 520].rearrange(
                      "p (h e) -> p h e", h=8)[:, :, 0:64],
                  pVv[:].rearrange("p (h e) -> p h e", h=8))

    barrier()
    ph3.close()

    ph4 = contextlib.ExitStack()
    pTb = sball("pTb", [128, 2048], BF16, ph4)
    dmaskP = sball("dmaskP_s", [128, 4096], BF16, ph4)
    lst = sball("lst", [128, 1024], F32, ph4)
    l0r = sball("l0r", [1, 1024], F32, ph4)
    bcs = sball("bcs", [64, 1024], F32, ph4)
    outn = sball("outn", [64, 1024], BF16, ph4)
    pSmA = ph4.enter_context(nc.psum_tensor("pSmA", [128, 1024], F32))
    pSmB = ph4.enter_context(nc.psum_tensor("pSmB", [128, 1024], F32))
    pAV0 = ph4.enter_context(nc.psum_tensor("pAV0", [128, 512], F32))
    pAV1 = ph4.enter_context(nc.psum_tensor("pAV1", [128, 512], F32))
    pDV = ph4.enter_context(nc.psum_tensor("pDV", [128, 512], F32))

    @on("sync")
    def _(sync):
        D = mkdma(sync, "sy", "LD")
        D(dmaskP[:], DT["dmaskP"])

    amk = {}
    for qt in range(8):
        for hp in range(4):
            ngroups = 4 * qt + 4
            S0 = cx.cnt["SS"]
            V0 = cx.cnt["VS"]
            ts_after_st = {}
            ts_after_av = {}

            def emit_st(kg, qt=qt, hp=hp, S0=S0, ts_after_st=ts_after_st):
                sm = kg % 2
                pSm = (pSmA, pSmB)[sm]

                @on("tensor")
                def _(tensor, kg=kg, sm=sm, pSm=pSm):
                    cx.wait(tensor, "t", ("LD",))
                    cx.wait_value(tensor, "t", "SS", S0 + kg - 1)
                    S = mkstep(tensor, "t", "TS", deps=())
                    for sub in range(2):
                        prt = slice(64 * sub, 64 * sub + 64)
                        S(tensor.matmul, pSm[:, sub * 512:(sub + 1) * 512],
                          Kh[prt, hp * T + kg * 128:hp * T + (kg + 1) * 128],
                          Qh[prt, hp * T + qt * 512:hp * T + (qt + 1) * 512],
                          start=True, stop=True)
                ts_after_st[kg] = cx.cnt["TS"]

            emit_st(0)
            for kg in range(ngroups):
                sm = kg % 2
                last = (kg == ngroups - 1)
                if not last:
                    emit_st(kg + 1)

                @on("tensor")
                def _(tensor, qt=qt, hp=hp, kg=kg, last=last, sm=sm,
                      ngroups=ngroups, S0=S0, V0=V0):
                    cx.wait_value(tensor, "t", "SS", S0 + kg + 1)
                    if kg >= ngroups - 4:
                        cx.wait_value(tensor, "t", "VS",
                                      V0 + (kg - (ngroups - 4)) + 1)
                    if kg == 0:
                        cx.wait_mark(tensor, "t", amk.get(("pAV", 0)))
                        cx.wait_mark(tensor, "t", amk.get(("pAV", 1)))
                    S = mkstep(tensor, "t", "TS", deps=())
                    for sub in range(2):
                        h = 2 * hp + sub
                        pAV = (pAV0, pAV1)[sub]
                        S(tensor.matmul, pAV[0:65, :],
                          Vsb[:, kg * 520 + h * 65:kg * 520 + (h + 1) * 65],
                          pTb[:, sm * 1024 + sub * 512:sm * 1024 + (sub + 1) * 512],
                          start=(kg == 0), stop=last)
                ts_after_av[kg] = cx.cnt["TS"]

            @on("scalar")
            def _(scalar, ngroups=ngroups, ts_after_st=ts_after_st,
                  ts_after_av=ts_after_av):
                for kg in range(ngroups):
                    sm = kg % 2
                    pSm = (pSmA, pSmB)[sm]
                    cx.wait_value(scalar, "s", "TS", ts_after_st[kg])
                    if kg >= 2:
                        cx.wait_value(scalar, "s", "TS", ts_after_av[kg - 2])
                    ins = scalar.activation(pTb[:, sm * 1024:(sm + 1) * 1024],
                                            pSm[:, 0:1024], AF.Exp,
                                            bias=sb["c_nb30"][:, 0:1])
                    cx.inc(ins, "SS")

            @on("vector")
            def _(vector, ngroups=ngroups, S0=S0):
                cx.wait(vector, "v", ("LD",))
                for kg in range(ngroups - 4, ngroups):
                    sm = kg % 2
                    moff = (kg - (ngroups - 4)) * 1024
                    cx.wait_value(vector, "v", "SS", S0 + kg + 1)
                    ins = vector.tensor_mul(pTb[:, sm * 1024:(sm + 1) * 1024],
                                            pTb[:, sm * 1024:(sm + 1) * 1024],
                                            dmaskP[:, moff:moff + 1024])
                    cx.inc(ins, "VS")

            for sub in range(2):
                h = 2 * hp + sub
                hb = sub
                pAV = (pAV0, pAV1)[sub]

                @on("vector")
                def _(vector, hb=hb, pAV=pAV):
                    cx.wait(vector, "v", ("TS",))
                    cx.wait_mark(vector, "v", amk.get(("lst", hb)))
                    S = mkstep(vector, "v", "VS", deps=())
                    S(vector.tensor_copy, lst[64:65, hb * 512:(hb + 1) * 512],
                      pAV[64:65, :])

                @on("sync")
                def _(sync, hb=hb):
                    D = mkdma(sync, "sy", "LD", deps=("VS",))
                    D(l0r[0:1, hb * 512:(hb + 1) * 512],
                      lst[64:65, hb * 512:(hb + 1) * 512])
                amk[("lst", hb)] = cx.mark("LD")

                @on("vector")
                def _(vector, hb=hb):
                    S = mkstep(vector, "v", "VS", deps=("LD",))
                    S(vector.reciprocal, l0r[0:1, hb * 512:(hb + 1) * 512],
                      l0r[0:1, hb * 512:(hb + 1) * 512])

                @on("tensor")
                def _(tensor, hb=hb):
                    cx.wait(tensor, "t", ("VS",))
                    cx.wait_mark(tensor, "t", amk.get(("pDV",)))
                    S = mkstep(tensor, "t", "TS", deps=())
                    S(tensor.matmul, pDV[0:64, :], sb["c_onesrowf"][0:1, 0:64],
                      l0r[0:1, hb * 512:(hb + 1) * 512], start=True, stop=True)

                @on("vector")
                def _(vector, hb=hb, pAV=pAV):
                    cx.wait(vector, "v", ("TS",))
                    cx.wait_mark(vector, "v", amk.get(("outn", hb)))
                    S = mkstep(vector, "v", "VS", deps=())
                    S(vector.tensor_copy, bcs[:, hb * 512:(hb + 1) * 512],
                      pDV[0:64, :])
                    S.c(vector.tensor_mul, outn[:, hb * 512:(hb + 1) * 512],
                        pAV[0:64, :], bcs[:, hb * 512:(hb + 1) * 512])
                amk[("pDV",)] = cx.mark("VS")
                amk[("pAV", hb)] = cx.mark("VS")

                @on("sync")
                def _(sync, qt=qt, h=h, hb=hb):
                    D = mkdma(sync, "sy", "LD", deps=("VS",))
                    D(DT["cc_out_in"][h * 64:(h + 1) * 64, qt * 512:(qt + 1) * 512],
                      outn[:, hb * 512:(hb + 1) * 512])
                amk[("outn", hb)] = cx.mark("LD")

    @on("gpsimd")
    def _(g):
        S = mkstep(g, "g", "CC")
        S(g.collective_compute, "AllGather", ALU.bypass,
          replica_groups=[[0, 1], [2, 3], [4, 5], [6, 7]],
          ins=[DT["cc_out_in"].opt()], outs=[DT["cc_out_out"].opt()])

    barrier()
    ph4.close()
    s_att.close()

    ph5 = contextlib.ExitStack()
    wo1 = sball("wo1", [128, NCH * C], F32, ph5)
    wo1b = sball("wo1b", [128, NCH * C], BF16, ph5)
    gA = sball("gA", [128, BT], BF16, ph5)
    gB = sball("gB", [128, BT], BF16, ph5)
    omy = sball("omy", [128, NCH * BT], BF16, ph5)
    oscr = sball("oscr", [128, BT], F32, ph5)
    g1t = sball("g1t", [128, 2 * 512], F32, ph5)
    x1t = sball("x1t", [128, 2 * 512], F32, ph5)
    x2st = sball("x2st", [128, 2 * 512], F32, ph5)
    pZ0 = ph5.enter_context(nc.psum_tensor("pZ0", [128, 512], F32))
    pZ1 = ph5.enter_context(nc.psum_tensor("pZ1", [128, 512], F32))

    @on("sync")
    def _(sync):
        D = mkdma(sync, "sy", "LD")
        D(wo1[:].rearrange("p (a n) -> p a n", a=NCH),
          DT["Wo1T"].rearrange("(a p) n -> p a n", p=128))

    @on("vector")
    def _(vector):
        S = mkstep(vector, "v", "VS")
        S(vector.tensor_copy, wo1b[:], wo1[:])

    for r in range(2):
        for cc4 in range(4):
            dc = r * 4 + cc4
            row = r * 512 + cc4 * 128

            @on("sync")
            def _(sync, row=row):
                D = mkdma(sync, "sy", "LD", deps=("VS", "CC"))
                D(gA[:], DT["cc_out_out"][row:row + 128, 0:BT])
                D(gB[:], DT["cc_out_out"][row:row + 128, BT:T])

            @on("vector")
            def _(vector, dc=dc):
                S = mkstep(vector, "v", "VS", deps=("LD",))
                S(vector.tensor_scalar, oscr[:], gA[:], sb["c_selw"][:, 0:1], None,
                  ALU.mult)
                S.c(vector.scalar_tensor_tensor, oscr[:], gB[:], sb["c_selw"][:, 1:2],
                    oscr[:], ALU.mult, ALU.add)
                S.c(vector.tensor_copy, omy[:, dc * BT:(dc + 1) * BT], oscr[:])

    for dc in range(NCH):
        for tt in range(4):
            pZ = (pZ0, pZ1)[tt % 2]
            buf = tt % 2

            @on("sync")
            def _(sync, dc=dc, tt=tt, buf=buf):
                D = mkdma(sync, "sy", "LD", deps=("VS", "LD"))
                D(g1t[:, buf * 512:(buf + 1) * 512],
                  DT["g1_d"][dc * 128:(dc + 1) * 128, tt * 512:(tt + 1) * 512])
                D(x1t[:, buf * 512:(buf + 1) * 512],
                  DT["x1_d"][dc * 128:(dc + 1) * 128, tt * 512:(tt + 1) * 512])

            @on("tensor")
            def _(tensor, dc=dc, tt=tt, pZ=pZ):
                S = mkstep(tensor, "t", "TS", deps=("VS",))
                for ca in range(NCH):
                    S(tensor.matmul, pZ[:],
                      wo1b[:, ca * C + dc * 128:ca * C + (dc + 1) * 128],
                      omy[:, ca * BT + tt * 512:ca * BT + (tt + 1) * 512],
                      start=(ca == 0), stop=(ca == NCH - 1))

            @on("vector")
            def _(vector, dc=dc, tt=tt, pZ=pZ, buf=buf):
                S = mkstep(vector, "v", "VS", deps=("TS", "LD"))
                d = x2st[:, buf * 512:(buf + 1) * 512]
                S(vector.tensor_mul, d, pZ[:], g1t[:, buf * 512:(buf + 1) * 512])
                S.c(vector.tensor_add, d, d, x1t[:, buf * 512:(buf + 1) * 512])

            @on("sync")
            def _(sync, dc=dc, tt=tt, buf=buf):
                D = mkdma(sync, "sy", "LD", deps=("VS",))
                D(DT["x2_fm"][dc * 128:(dc + 1) * 128, tt * 512:(tt + 1) * 512],
                  x2st[:, buf * 512:(buf + 1) * 512])

    @on("sync")
    def _(sync):
        cx.wait_all(sync, "sy")

    ph5.close()
    es.close()
    return nc


def _host_prep(inputs):
    f32 = np.float32
    x = np.asarray(inputs["L0_attn_x"], f32)
    WkT = np.ascontiguousarray(np.asarray(inputs["L0_attn_Wk"], f32).T)
    WvT = np.ascontiguousarray(np.asarray(inputs["L0_attn_Wv"], f32).T)
    WrT = np.ascontiguousarray(np.asarray(inputs["L0_attn_Wr"], f32).T)
    WoT = np.ascontiguousarray(np.asarray(inputs["L0_attn_Wo"], f32).T)
    w1T = np.ascontiguousarray(np.asarray(inputs["L0_ffn_w1"], f32).T)
    w2T = np.ascontiguousarray(np.asarray(inputs["L0_ffn_w2"], f32).T)
    Wqkv = np.asarray(inputs["L1_attn_Wqkv"], f32)
    Wo1T = np.ascontiguousarray(np.asarray(inputs["L1_attn_Wo"], f32).T)

    def colpack(v, n):
        return np.ascontiguousarray(np.asarray(v, f32).reshape(n, 128).T)

    def rowbc(v):
        v = np.asarray(v, f32)
        return np.ascontiguousarray(np.broadcast_to(v, (128, v.shape[0])))

    gamA = colpack(inputs["L0_attn_ln_gamma"], 8)
    betA = colpack(inputs["L0_attn_ln_beta"], 8)
    decay = colpack(inputs["L0_attn_time_decay"], 8)
    first = colpack(inputs["L0_attn_time_first"], 8)
    b2c = colpack(inputs["L0_ffn_b2"], 8)
    bTT = np.concatenate([np.asarray(inputs["L1_ffn_bias_real"], f32),
                          np.asarray(inputs["L1_ffn_bias_imag"], f32)])
    bTTc = colpack(bTT, 8)
    g1bc = rowbc(inputs["L0_ffn_ln1_g"])
    b1bc = rowbc(inputs["L0_ffn_ln1_b"])
    b1fbc = rowbc(inputs["L0_ffn_b1"])
    g2bc = rowbc(inputs["L0_ffn_ln2_g"])
    be2bc = rowbc(inputs["L0_ffn_ln2_b"])

    identf = np.eye(128, dtype=f32)
    identb = np.eye(128, dtype=f32).astype(BF)
    onesb = np.ones((128, 128), f32).astype(BF)
    onescolf = np.ones((128, 1), f32)
    onesrowf = np.ones((1, 128), f32)
    R = np.zeros((128, 128), f32)
    for i in range(64):
        R[2 * i, 2 * i + 1] = -1.0
        R[2 * i + 1, 2 * i] = 1.0
    ropeRT = np.ascontiguousarray(R.T).astype(BF)
    freqs = np.exp(-np.log(10000.0) * np.arange(0, DH, 2, dtype=f32) / DH)
    ang = np.arange(T, dtype=f32)[:, None] * freqs[None, :]
    ang = np.repeat(ang, 2, axis=1)
    cos64 = np.cos(ang).T.astype(f32)
    sin64 = np.sin(ang).T.astype(f32)
    cos128 = np.concatenate([cos64, cos64], 0)
    sin128 = np.concatenate([sin64, sin64], 0)
    cosQ = (cos128 / 8.0).astype(BF)
    sinQ = (sin128 / 8.0).astype(BF)
    cosK = cos128.astype(BF)
    sinK = sin128.astype(BF)
    k_ = np.arange(128)[:, None]
    q_ = np.arange(512)[None, :]
    dmask = np.concatenate(
        [((k_ + 128 * j) <= q_).astype(f32) for j in range(4)], axis=1).astype(BF)
    dmaskP = np.concatenate(
        [np.concatenate([((k_ + 128 * j) <= q_).astype(f32)] * 2, axis=1)
         for j in range(4)], axis=1).astype(BF)

    G1R = np.asarray(inputs["L1_ffn_G1_R"], f32)
    G2R = np.asarray(inputs["L1_ffn_G2_R"], f32)
    G1I = np.asarray(inputs["L1_ffn_G1_I"], f32)
    G2I = np.asarray(inputs["L1_ffn_G2_I"], f32)

    def ttrep(G1, G2):
        T1 = np.zeros((512, 4, 512), f32)
        T2 = np.zeros((512, 4, 512), f32)
        d_ = np.arange(512) // 16
        g_ = np.arange(512) % 16
        b_ = np.arange(512) // 16
        e_ = np.arange(512) % 16
        for c in range(4):
            A = G1[0, :, c, :]
            Bm = G2[c, :, 0, :]
            T1[:, c, :] = A.T[d_][:, b_]
            T2[:, c, :] = Bm.T[g_][:, e_]
        return (np.ascontiguousarray(T1.reshape(512, 2048)),
                np.ascontiguousarray(T2.reshape(512, 2048)))

    T1r, T2r = ttrep(G1R, G2R)
    T1i, T2i = ttrep(G1I, G2I)

    base = dict(
        WkT=WkT, WvT=WvT, WrT=WrT, WoT=WoT, Wo1T=Wo1T,
        gamA=gamA, betA=betA, decay=decay, first=first, b2c=b2c, bTTc=bTTc,
        g1bc=g1bc, b1bc=b1bc, b1fbc=b1fbc, g2bc=g2bc, be2bc=be2bc,
        identf=identf, identb=identb, onesb=onesb, ropeR=ropeRT,
        onescolf=onescolf, onesrowf=onesrowf,
        nb30=np.full((128, 1), EXPB, f32),
        cosQ=cosQ, sinQ=sinQ, cosK=cosK, sinK=sinK, dmask=dmask,
        T1r=T1r, T2r=T2r, T1i=T1i, T2i=T2i, dmaskP=dmaskP,
    )

    in_maps = []
    for core in range(8):
        b, p = core // 2, core % 2
        m = dict(base)
        xt = np.ascontiguousarray(x[b, p * BT:(p + 1) * BT, :])
        m["x_tok"] = xt
        m["x_fm"] = np.ascontiguousarray(xt.T)
        m["w1T_sl"] = np.ascontiguousarray(w1T[:, core * 512:(core + 1) * 512])
        m["w2T_sl"] = np.ascontiguousarray(w2T[core * 512:(core + 1) * 512, :])
        hrows = np.concatenate([np.arange(p * 512, (p + 1) * 512),
                                C + np.arange(p * 512, (p + 1) * 512),
                                2 * C + np.arange(p * 512, (p + 1) * 512)])
        m["WqkvT_sl"] = np.ascontiguousarray(Wqkv[hrows, :].T)
        m["tvals"] = np.broadcast_to(
            np.arange(p * BT, (p + 1) * BT, dtype=f32), (128, BT)).copy()
        m["selw"] = np.broadcast_to(
            np.array([1.0 - p, float(p)], f32), (128, 2)).copy()
        m["carryw"] = np.broadcast_to(
            np.array([float(p), 0.0], f32), (128, 2)).copy()
        in_maps.append(m)
    return in_maps


def kernel(**inputs):
    global _last_res
    in_maps = _host_prep(inputs)
    nc = build_program()
    res = run_bass_kernel_spmd(nc, in_maps, core_ids=list(range(8)))
    _last_res = res
    out = np.zeros((B, T, C), np.float32)
    for core in range(8):
        b, p = core // 2, core % 2
        out[b, p * BT:(p + 1) * BT, :] = np.asarray(res.results[core]["x2_fm"]).T
    return out
